# version 02
# baseline (speedup 1.0000x reference)
import contextlib
import numpy as np
import ml_dtypes
import concourse.bass as bass
import concourse.mybir as mybir
from concourse.bass_utils import run_bass_kernel_spmd

F32 = mybir.dt.float32
BF16 = mybir.dt.bfloat16
AF = mybir.ActivationFunctionType
ALU = mybir.AluOpType
AX = mybir.AxisListType

NCORES = 8
D = 2048
KD = D // 128
T = 1024
TB = 512
BF = ml_dtypes.bfloat16

BLOCKS_ALL = ("l0_attn", "l0_ffn", "l1_attn", "l1_ffn")


class Sched:

    def __init__(self):
        self.ops = []
        self.force = set()
        self._tail = {}

    def tail(self, stream="main"):
        return self._tail.get(stream)

    def op(self, eng, fn, waits=(), stream="main", sem=None, inc=1, tag="",
           nosync=False):
        tok = len(self.ops)
        for w in waits:
            if w is not None:
                self.force.add(w)
        self.ops.append(dict(eng=eng, fn=fn, inc=inc, sem=sem or eng,
                             stream=stream, waits=[w for w in waits if w is not None],
                             tag=tag, nosync=nosync))
        self._tail[stream] = tok
        return tok

    def dma(self, eng, fn, waits=(), stream="main", sem=None, tag="", nosync=False):
        return self.op(eng, fn, waits=waits, stream=stream,
                       sem=sem or (eng + "_dma"), inc=16, tag=tag, nosync=nosync)

    def build(self, nc, block, es):
        n = len(self.ops)
        prev_in_stream = {}
        chain_wait = [None] * n
        for i, o in enumerate(self.ops):
            p = prev_in_stream.get(o["stream"])
            if p is not None and not o.get("nosync"):
                po = self.ops[p]
                if po["eng"] != o["eng"]:
                    chain_wait[i] = p
                elif (o["inc"] != 16 and po["inc"] != 16
                      and o["eng"] in ("vector", "scalar", "gpsimd")):
                    chain_wait[i] = p
            prev_in_stream[o["stream"]] = i
        last_of_stream = set(prev_in_stream.values())
        do_inc = [False] * n
        for i, o in enumerate(self.ops):
            if o["inc"] == 16 or i in self.force or i in last_of_stream:
                do_inc[i] = True
        for i in range(n):
            if chain_wait[i] is not None:
                do_inc[chain_wait[i]] = True
        sem_names = sorted({o["sem"] for i, o in enumerate(self.ops) if do_inc[i]})
        sems = {s: es.enter_context(nc.semaphore("sem_" + s)) for s in sem_names}
        cum = {s: 0 for s in sem_names}
        val_after = [0] * n
        for i, o in enumerate(self.ops):
            if do_inc[i]:
                cum[o["sem"]] += o["inc"]
                val_after[i] = cum[o["sem"]]
        resolved = [[] for _ in range(n)]
        for i, o in enumerate(self.ops):
            tgts = list(o["waits"])
            if chain_wait[i] is not None:
                tgts.append(chain_wait[i])
            seen = {}
            for tkn in tgts:
                assert do_inc[tkn], (i, tkn, self.ops[tkn])
                s = self.ops[tkn]["sem"]
                seen[s] = max(seen.get(s, 0), val_after[tkn])
            resolved[i] = [(sems[s], v) for s, v in sorted(seen.items())]

        def make_body(label):
            def body(eng):
                for i, o in enumerate(self.ops):
                    if o["eng"] != label:
                        continue
                    for s, v in resolved[i]:
                        eng.wait_ge(s, v)
                    inst = o["fn"](eng)
                    if do_inc[i]:
                        inst.then_inc(sems[o["sem"]], o["inc"])
            return body

        used = {o["eng"] for o in self.ops}
        for label, attr in [("tensor", block.tensor), ("vector", block.vector),
                            ("scalar", block.scalar), ("gpsimd", block.gpsimd),
                            ("sync", block.sync)]:
            if label in used:
                attr(make_body(label))


def tile_fm(wt, kdim, odim):
    kch, och = kdim // 128, odim // 128
    return np.ascontiguousarray(
        wt.reshape(kch, 128, och, 128).transpose(2, 1, 0, 3).reshape(och, 128, kch * 128))


def pmajor(v):
    n = v.shape[0] // 128
    return np.ascontiguousarray(v.reshape(n, 128).T)


def make_hadamard(n):
    h = np.array([[1.0]], dtype=np.float32)
    while h.shape[0] < n:
        h = np.block([[h, h], [h, -h]])
    return h


class Ctx:
    pass


def _alloc_common(nc, es, ctx):
    S = ctx.sched = Sched()
    ctx.nc = nc
    ctx.es = es

    ctx.d_x = nc.dram_tensor("x_fm", [D, T], F32, kind="ExternalInput").ap()
    ctx.d_out = nc.dram_tensor("out_fm", [D, T], F32, kind="ExternalOutput").ap()
    ctx.d_h128 = nc.dram_tensor("c_h128", [128, 128], F32, kind="ExternalInput").ap()
    ctx.d_ident = nc.dram_tensor("c_ident", [128, 128], F32, kind="ExternalInput").ap()
    ctx.d_mask = nc.dram_tensor("c_mask", [128, 512], F32, kind="ExternalInput").ap()
    ctx.d_pmask = nc.dram_tensor("c_pmask", [128, 8], F32, kind="ExternalInput").ap()
    ctx.d_dbg = nc.dram_tensor("dbg", [128, 4096], F32, kind="ExternalOutput").ap()
    ctx.dbg_off = [0]

    ctx.x = es.enter_context(nc.sbuf_tensor("x", [128, KD * T], BF16))
    ctx.rA = es.enter_context(nc.sbuf_tensor("rA", [128, 16384], BF16))
    ctx.rB = es.enter_context(nc.sbuf_tensor("rB", [128, 16384], BF16))
    ctx.rC = es.enter_context(nc.sbuf_tensor("rC", [128, 16384], BF16))
    ctx.rD = es.enter_context(nc.sbuf_tensor("rD", [128, 16384], BF16))
    ctx.wreg = es.enter_context(nc.sbuf_tensor("wreg", [128, 16384], BF16))
    ctx.trans = es.enter_context(nc.sbuf_tensor("trans", [128, 6656], BF16))
    ctx.dsb = ctx.wreg[:, 4096:8224]
    ctx.accf = ctx.wreg[:, 4096:8224].bitcast(F32)
    ctx.dbuf = ctx.wreg[:, 8224:10288]
    ctx.acc = ctx.wreg[:, 10288:14416]
    ctx.mask = ctx.wreg[:, 14416:14928]
    ctx.sm1 = es.enter_context(nc.sbuf_tensor("sm1", [128, 192], F32))
    ctx.sm2 = es.enter_context(nc.sbuf_tensor("sm2", [128, 256], F32))
    ctx.h128 = es.enter_context(nc.sbuf_tensor("h128", [128, 128], BF16))
    ctx.identb = es.enter_context(nc.sbuf_tensor("identb", [128, 128], BF16))
    ctx.onecol = es.enter_context(nc.sbuf_tensor("onecol", [128, 1], BF16))
    ctx.onerow = es.enter_context(nc.sbuf_tensor("onerow", [1, 128], BF16))
    ctx.pmask = es.enter_context(nc.sbuf_tensor("pmask", [128, 8], F32))

    ctx.pA = es.enter_context(nc.psum_tensor("pA", [128, 2048], F32))
    ctx.pB = es.enter_context(nc.psum_tensor("pB", [128, 2048], F32))

    ctx.transf = ctx.trans[:, :].bitcast(F32)

    S.dma("gpsimd", lambda e: e.dma_start(out=ctx.h128[:, :], in_=ctx.d_h128[:, :]))
    S.dma("gpsimd", lambda e: e.dma_start(out=ctx.identb[:, :], in_=ctx.d_ident[:, :]))
    S.dma("sync", lambda e: e.dma_start(out=ctx.pmask[:, :], in_=ctx.d_pmask[:, :]))
    S.op("vector", lambda e: e.memset(ctx.onecol[:, :], 1.0))
    S.op("vector", lambda e: e.memset(ctx.onerow[:, :], 1.0))
    for c in range(KD):
        S.dma("gpsimd", lambda e, c=c: e.dma_start(
            out=ctx.x[:, c * T:(c + 1) * T], in_=ctx.d_x[c * 128:(c + 1) * 128, :]))


def _emit_store_out(ctx):
    S = ctx.sched
    for c in range(KD):
        S.dma("gpsimd", lambda e, c=c: e.dma_start(
            out=ctx.d_out[c * 128:(c + 1) * 128, :], in_=ctx.x[:, c * T:(c + 1) * T]))
    S.op("scalar", lambda e: e.nop(), tag="fence")


def emit_dbg(ctx, ap, rows, cols):
    off = ctx.dbg_off[0]
    ctx.sched.dma("gpsimd", lambda e: e.dma_start(
        out=ctx.d_dbg[0:rows, off:off + cols], in_=ap), tag="dbg")
    ctx.dbg_off[0] += cols


def x_tile(ctx, c, tcols=None):
    if tcols is None:
        return ctx.x[:, c * T:(c + 1) * T]
    return ctx.x[:, c * T + tcols[0]:c * T + tcols[1]]


def t_row_bf(ctx, off, w):
    return ctx.trans[0:1, off:off + w]


def t_tile_bf(ctx, off, w, p0=0, p1=128):
    return ctx.trans[p0:p1, off:off + w]


def emit_ln(ctx, su_row, out_region, out_off, tw=T, tbase=0):
    S = ctx.sched
    nt = tw // 512
    x2t = ctx.trans[:, 2048:2560]

    def xt(c, i):
        return ctx.x[:, c * T + tbase + i * 512: c * T + tbase + (i + 1) * 512]

    for i in range(nt):
        psx = ctx.pA[0:1, 1024 + i * 512:1536 + i * 512]
        psq = ctx.pB[0:1, 1024 + i * 512:1536 + i * 512]
        for c in range(KD):
            S.op("scalar", lambda e, c=c, i=i: e.activation(
                x2t, xt(c, i), AF.Square), tag="ln_sq")
            S.op("tensor", lambda e, c=c, i=i, psx=psx: e.matmul(
                psx, ctx.onecol[:, :], xt(c, i), start=(c == 0), stop=(c == KD - 1)),
                tag="ln_sx")
            S.op("tensor", lambda e, c=c, psq=psq: e.matmul(
                psq, ctx.onecol[:, :], x2t, start=(c == 0), stop=(c == KD - 1)),
                tag="ln_sx2")
    mu_ps = ctx.pA[0:1, 1024:1024 + tw]
    var_ps = ctx.pB[0:1, 1024:1024 + tw]
    musq = ctx.transf[0:1, 1280:1280 + tw]
    rstd_bf = t_row_bf(ctx, 4608, tw)
    nmr_bf = t_row_bf(ctx, 4608 + tw, tw)
    S.op("vector", lambda e: e.tensor_scalar(mu_ps, mu_ps, 1.0 / D, None, ALU.mult))
    S.op("vector", lambda e: e.tensor_scalar(var_ps, var_ps, 1.0 / D, 1e-5, ALU.mult, ALU.add))
    S.op("scalar", lambda e: e.activation(musq, mu_ps, AF.Square))
    S.op("vector", lambda e: e.tensor_tensor(var_ps, var_ps, musq, ALU.subtract))
    S.op("scalar", lambda e: e.activation(var_ps, var_ps, AF.Ln))
    S.op("vector", lambda e: e.tensor_scalar(var_ps, var_ps, -0.5, None, ALU.mult))
    S.op("scalar", lambda e: e.activation(rstd_bf, var_ps, AF.Exp))
    S.op("vector", lambda e: e.scalar_tensor_tensor(
        nmr_bf, mu_ps, -1.0, rstd_bf, ALU.mult, ALU.mult))
    for c in range(KD):
        su_c = su_row[0:1, c * 128:(c + 1) * 128]
        for i in range(nt):
            pa = ctx.pA[:, (c % 2) * 512:(c % 2) * 512 + 512]
            pb = ctx.pB[:, (c % 2) * 512:(c % 2) * 512 + 512]
            S.op("tensor", lambda e, pa=pa, su_c=su_c, i=i: e.matmul(
                pa, su_c, rstd_bf[0:1, i * 512:(i + 1) * 512], start=True, stop=True),
                tag="ln_abc")
            S.op("tensor", lambda e, pb=pb, su_c=su_c, i=i: e.matmul(
                pb, su_c, nmr_bf[0:1, i * 512:(i + 1) * 512], start=True, stop=True),
                tag="ln_bbc")
            ot = out_region[:, out_off + c * tw + i * 512: out_off + c * tw + (i + 1) * 512]
            S.op("vector", lambda e, c=c, i=i, ot=ot, pa=pa: e.tensor_tensor(
                ot, xt(c, i), pa, ALU.mult), tag="ln_za")
            S.op("vector", lambda e, ot=ot, pb=pb: e.tensor_tensor(
                ot, ot, pb, ALU.add), tag="ln_zb")


def mm_op(ctx, base, fn, **kw):
    need_nop = getattr(ctx, "_mm_base", None) is not None and ctx._mm_base != base
    ctx._mm_base = base
    if need_nop:
        fn2 = lambda e: (e.nop(), fn(e))[1]
    else:
        fn2 = fn
    return ctx.sched.op("tensor", fn2, **kw)


def emit_ones_row(ctx, off, w):
    S = ctx.sched
    row = t_row_bf(ctx, off, w)
    S.op("vector", lambda e: e.memset(row, 1.0))
    return row


def emit_fwht_combine(ctx, tiles):
    S = ctx.sched
    n = len(tiles)
    h = 1
    while h < n:
        for base in range(0, n, 2 * h):
            for j in range(h):
                a = tiles[base + j]
                b = tiles[base + j + h]
                S.op("vector", lambda e, a=a, b=b: e.tensor_tensor(a, a, b, ALU.add),
                     tag="fw_a")
                S.op("vector", lambda e, a=a, b=b: e.scalar_tensor_tensor(
                    b, b, -2.0, a, ALU.mult, ALU.add), tag="fw_b")
        h *= 2


def emit_fwht_h128(ctx, tiles, scale):
    S = ctx.sched
    for idx, tile in enumerate(tiles):
        tw = tile.shape[1]
        pa = ctx.pA if idx % 2 == 0 else ctx.pB
        for i in range((tw + 511) // 512):
            w = min(512, tw - i * 512)
            ps = pa[:, i * 512:i * 512 + w]
            S.op("tensor", lambda e, tile=tile, i=i, w=w, ps=ps: e.matmul(
                ps, ctx.h128[:, :], tile[:, i * 512:i * 512 + w], start=True, stop=True),
                tag="fw_h128")
            S.op("vector", lambda e, tile=tile, i=i, w=w, ps=ps: e.tensor_scalar(
                tile[:, i * 512:i * 512 + w], ps, scale, None, ALU.mult), tag="fw_ev")


def _decl_l0_ffn(nc, ctx):
    ctx.d_wup = nc.dram_tensor("l0f_wup", [64, 128, 2048], F32, kind="ExternalInput").ap()
    ctx.d_wgate = nc.dram_tensor("l0f_wgate", [64, 128, 2048], F32, kind="ExternalInput").ap()
    ctx.d_wdown = nc.dram_tensor("l0f_wdown", [16, 128, 8192], F32, kind="ExternalInput").ap()
    ctx.d_suug = nc.dram_tensor("l0f_suug", [1, 2048], F32, kind="ExternalInput").ap()
    ctx.d_sudp = nc.dram_tensor("l0f_sudp", [128, 64], F32, kind="ExternalInput").ap()
    ctx.d_wsu_up = nc.dram_tensor("l0f_wsu_up", [128, 64], F32, kind="ExternalInput").ap()
    ctx.d_wsu_g = nc.dram_tensor("l0f_wsu_g", [128, 64], F32, kind="ExternalInput").ap()
    ctx.d_wsdp = nc.dram_tensor("l0f_wsdp", [128, 16], F32, kind="ExternalInput").ap()


def emit_l0_ffn(ctx):
    S = ctx.sched

    su_row = t_row_bf(ctx, 0, 2048)
    S.dma("gpsimd", lambda e: e.dma_start(out=su_row, in_=ctx.d_suug[:, :]))
    wsg64 = ctx.sm1[:, 0:64]
    comb_up = ctx.sm1[:, 64:128]
    wdp64 = ctx.sm1[:, 128:144]
    sudp = ctx.sm2[:, 0:64]
    wsuup = ctx.sm2[:, 64:128]
    S.dma("sync", lambda e: e.dma_start(out=sudp, in_=ctx.d_sudp[:, :]))
    S.dma("sync", lambda e: e.dma_start(out=wsuup, in_=ctx.d_wsu_up[:, :]))
    S.dma("sync", lambda e: e.dma_start(out=ctx.sm2[:, 128:192], in_=ctx.d_wsu_g[:, :]))
    S.dma("sync", lambda e: e.dma_start(out=ctx.sm2[:, 192:208], in_=ctx.d_wsdp[:, :]))
    S.op("vector", lambda e: e.tensor_scalar(wsg64, ctx.sm2[:, 128:192], 64.0, None, ALU.mult))
    S.op("vector", lambda e: e.scalar_tensor_tensor(comb_up, wsuup, 64.0, sudp, ALU.mult, ALU.mult))
    S.op("vector", lambda e: e.tensor_scalar(wdp64, ctx.sm2[:, 192:208], 64.0, None, ALU.mult))

    wreg_free = getattr(ctx, "wreg_free", [None, None])
    for tp in range(2):
        tb = tp * 512
        emit_ln(ctx, su_row, ctx.rA, 0, tw=512, tbase=tb)
        ztiles = [ctx.rA[:, c * 512:(c + 1) * 512] for c in range(KD)]
        emit_fwht_combine(ctx, ztiles)
        emit_fwht_h128(ctx, ztiles, 1.0 / (64.0 * np.sqrt(2048.0)))

        in_tok = S.tail("main")
        last_use = [wreg_free[0], wreg_free[0]]
        ld_tok = [None, None]
        psum_free = [None, None]
        act = []
        for o in range(64):
            reg = ctx.rB if o < 32 else ctx.rC
            act.append(reg[:, (o % 32) * 512:(o % 32) * 512 + 512])
        for o in range(64):
            s = o % 2
            wup = ctx.wreg[:, s * 4096:s * 4096 + 2048]
            wgt = ctx.wreg[:, s * 4096 + 2048:s * 4096 + 4096]
            S.dma("gpsimd", lambda e, o=o, wup=wup: e.dma_start(
                out=wup, in_=ctx.d_wup[o, :, :]), waits=(last_use[s],),
                stream="wl%d" % s, sem="wsl%d" % s, tag="wld_up")
            ld_tok[s] = S.dma("gpsimd", lambda e, o=o, wgt=wgt: e.dma_start(
                out=wgt, in_=ctx.d_wgate[o, :, :]),
                stream="wl%d" % s, sem="wsl%d" % s, tag="wld_gt")
            pu = ctx.pA[:, s * 1024:s * 1024 + 512]
            pg = ctx.pA[:, s * 1024 + 512:s * 1024 + 1024]
            for k in range(KD):
                w0 = (ld_tok[s], psum_free[s], in_tok if o < 2 else None) if k == 0 else ()
                tok_up = S.op("tensor", lambda e, k=k, wup=wup, pu=pu: e.matmul(
                    pu, wup[:, k * 128:(k + 1) * 128], ztiles[k],
                    start=(k == 0), stop=(k == KD - 1)),
                    waits=w0, stream="mm", tag="up_mm")
            for k in range(KD):
                tok = S.op("tensor", lambda e, k=k, wgt=wgt, pg=pg: e.matmul(
                    pg, wgt[:, k * 128:(k + 1) * 128], ztiles[k],
                    start=(k == 0), stop=(k == KD - 1)), stream="mm", tag="gt_mm")
            last_use[s] = tok
            S.op("scalar", lambda e, o=o, pg=pg: e.activation(
                ctx.trans[:, 2048:2560], pg, AF.Silu, scale=wsg64[:, o:o + 1]),
                waits=(tok,), tag="silu")
            psum_free[s] = S.op("vector", lambda e, o=o, pu=pu: e.scalar_tensor_tensor(
                act[o], ctx.trans[:, 2048:2560], comb_up[:, o:o + 1], pu,
                ALU.mult, ALU.mult), waits=(tok_up,), tag="actmul")

        emit_fwht_combine(ctx, act)
        emit_fwht_h128(ctx, act, 1.0 / (64.0 * np.sqrt(8192.0)))

        in_tok = S.tail("main")
        last_use_d = [None, None]
        ld_tok_d = [None, None]
        psum_free_d = [None, None]
        for o in range(KD):
            s = o % 2
            wd = ctx.wreg[:, s * 8192:s * 8192 + 8192]
            if o == 0:
                w8 = (last_use[0], last_use[1])
            elif o == 1:
                w8 = (wreg_free[1],)
            else:
                w8 = (last_use_d[s],)
            ld_tok_d[s] = S.dma("gpsimd", lambda e, o=o, wd=wd: e.dma_start(
                out=wd, in_=ctx.d_wdown[o, :, :]), waits=w8,
                stream="wd%d" % s, sem="wsd%d" % s, tag="wld_dn")
            pd = ctx.pB[:, s * 512:s * 512 + 512]
            for k in range(64):
                w0 = (ld_tok_d[s], psum_free_d[s], in_tok if o < 2 else None) if k == 0 else ()
                tok = S.op("tensor", lambda e, k=k, wd=wd, pd=pd: e.matmul(
                    pd, wd[:, k * 128:(k + 1) * 128], act[k],
                    start=(k == 0), stop=(k == 63)),
                    waits=w0, stream="mm", tag="dn_mm")
            last_use_d[s] = tok
            xt = x_tile(ctx, o, (tb, tb + 512))
            psum_free_d[s] = S.op("vector", lambda e, o=o, pd=pd, xt=xt: e.scalar_tensor_tensor(
                xt, pd, wdp64[:, o:o + 1], xt, ALU.mult, ALU.add),
                waits=(tok,), tag="dn_res")
        wreg_free = [last_use_d[0], last_use_d[1]]
    ctx.wreg_free = wreg_free


def _decl_l0_attn(nc, ctx):
    ctx.d_wq = nc.dram_tensor("l0a_wq", [16, 128, 2048], F32, kind="ExternalInput").ap()
    ctx.d_wk = nc.dram_tensor("l0a_wk", [16, 128, 2048], F32, kind="ExternalInput").ap()
    ctx.d_wv = nc.dram_tensor("l0a_wv", [16, 128, 2048], F32, kind="ExternalInput").ap()
    ctx.d_wo = nc.dram_tensor("l0a_wo", [16, 128, 2048], F32, kind="ExternalInput").ap()
    ctx.d_wfq = nc.dram_tensor("l0a_wfq", [32, 64, 64], F32, kind="ExternalInput").ap()
    ctx.d_wfk = nc.dram_tensor("l0a_wfk", [32, 64, 64], F32, kind="ExternalInput").ap()
    ctx.d_blk = nc.dram_tensor("l0a_blk", [128, 4128], BF16)
    ctx.d_ag = nc.dram_tensor("l0a_ag", [NCORES * 128, 4128], BF16)


L0ATTN_STAGE = 5
L0ATTN_SUB = 9


def emit_l0_attn(ctx):
    S = ctx.sched
    wreg_free = getattr(ctx, "wreg_free", [None, None])

    S.dma("gpsimd", lambda e: e.dma_start(out=ctx.mask[:, :], in_=ctx.d_mask[:, :]))

    su_row = emit_ones_row(ctx, 0, 2048)
    emit_ln(ctx, su_row, ctx.rA, 0, tw=T)
    ztiles = [ctx.rA[:, c * T:(c + 1) * T] for c in range(KD)]

    vtmp = ctx.trans[:, 2048:3072]
    last_use = [wreg_free[0], wreg_free[0]]
    ld_tok = [None, None]
    for which, wdram in (("q", ctx.d_wq), ("k", ctx.d_wk), ("v", ctx.d_wv)):
        for o in range(KD):
            s = o % 2
            wsl = ctx.wreg[:, s * 2048:s * 2048 + 2048]
            ld_tok[s] = S.dma("gpsimd", lambda e, o=o, wsl=wsl, wdram=wdram: e.dma_start(
                out=wsl, in_=wdram[o, :, :]), waits=(last_use[s],),
                stream="qv%d" % s, sem="qvs%d" % s, tag="wld_" + which)
            for tt in range(2):
                ps = ctx.pA[:, (s * 2 + tt) * 512:(s * 2 + tt) * 512 + 512]
                for k in range(KD):
                    tok = S.op("tensor", lambda e, k=k, wsl=wsl, ps=ps, tt=tt: e.matmul(
                        ps, wsl[:, k * 128:(k + 1) * 128],
                        ztiles[k][:, tt * 512:(tt + 1) * 512],
                        start=(k == 0), stop=(k == KD - 1)),
                        waits=(ld_tok[s],) if k == 0 else (), tag=which + "_mm")
                if which == "q":
                    S.op("vector", lambda e, o=o, tt=tt, ps=ps: e.tensor_copy(
                        ctx.rB[:, o * T + tt * 512:o * T + (tt + 1) * 512], ps),
                        tag="q_ev")
                elif which == "k":
                    S.op("vector", lambda e, o=o, tt=tt, ps=ps: e.tensor_copy(
                        ctx.rC[:, o * T + tt * 512:o * T + (tt + 1) * 512], ps),
                        tag="k_ev")
                else:
                    S.op("vector", lambda e, tt=tt, ps=ps: e.tensor_copy(
                        vtmp[:, tt * 512:(tt + 1) * 512], ps), tag="v_ev")
            last_use[s] = tok
            if which == "v":
                for m in range(8):
                    pt = ctx.pB[:, (m % 2) * 512:(m % 2) * 512 + 64].bitcast(BF16)
                    S.op("tensor", lambda e, m=m, pt=pt: e.transpose(
                        pt[:, 0:128], vtmp[:, m * 128:(m + 1) * 128],
                        ctx.identb[:, :]), tag="v_tr")
                    S.op("vector", lambda e, m=m, o=o, pt=pt: e.tensor_copy(
                        ctx.rD[:, m * 2048 + o * 128:m * 2048 + (o + 1) * 128],
                        pt[:, 0:128]), tag="v_trev")

    if L0ATTN_STAGE < 2:
        ctx.wreg_free = [last_use[0], last_use[1]]
        return
    wfq_sb = ctx.wreg[:, 0:2048]
    wfk_sb = ctx.wreg[:, 2048:4096]
    ms_tok = S.op("vector", lambda e: e.memset(ctx.wreg[:, 0:4096], 0.0),
                  waits=(last_use[0], last_use[1]), tag="fm_ms")
    for wsb, wdram, tagn in ((wfq_sb, ctx.d_wfq, "wld_fq"), (wfk_sb, ctx.d_wfk, "wld_fk")):
        for half in range(2):
            fm_ld = S.dma("gpsimd", lambda e, wsb=wsb, wdram=wdram, half=half: e.dma_start(
                out=wsb[half * 64:half * 64 + 64, :].rearrange(
                    "d (c g) -> d c g", g=128)[:, :, half * 64:half * 64 + 64],
                in_=wdram.rearrange("(c two) d f -> two d c f", two=2)[half]),
                waits=(ms_tok,) if (half == 0 and wsb is wfq_sb) else (),
                stream="qv0", sem="qvs0", tag=tagn)

    first_fm = True
    for b in range(2):
        for c in range(KD):
            ps = ctx.pA[:, (c % 4) * 512:(c % 4) * 512 + 512]
            mm_op(ctx, 0, lambda e, b=b, c=c, ps=ps: e.matmul(
                ps, wfq_sb[:, c * 128:(c + 1) * 128],
                ctx.rB[:, c * T + b * 512:c * T + b * 512 + 512],
                start=True, stop=True),
                waits=(fm_ld,) if first_fm else (), tag="fq_mm")
            first_fm = False
            S.op("scalar", lambda e, b=b, c=c, ps=ps: e.activation(
                ctx.rB[:, c * T + b * 512:c * T + b * 512 + 512], ps, AF.Exp),
                tag="fq_exp")

    if L0ATTN_STAGE < 3:
        ctx.wreg_free = [last_use[0], last_use[1]]
        return
    fk_tm = ctx.trans[:, 0:2048]
    sums = ctx.sm2[:, 0:32]
    rcps = ctx.sm2[:, 32:64]
    for b in range(2):
        for jc in range(4):
            for c in range(KD):
                mm_op(ctx, 0, lambda e, b=b, jc=jc, c=c: e.matmul(
                    ctx.pA[:, c * 128:(c + 1) * 128],
                    ctx.rC[:, c * T + b * 512 + jc * 128:
                           c * T + b * 512 + (jc + 1) * 128],
                    wfk_sb[:, c * 128:(c + 1) * 128],
                    start=True, stop=True), tag="fk_mm")
            S.op("scalar", lambda e: e.activation(fk_tm, ctx.pA[:, :], AF.Exp),
                 tag="fk_exp")
            if L0ATTN_SUB < 2:
                continue
            S.op("vector", lambda e: e.reduce_sum(
                sums, fk_tm.rearrange("p (h f) -> p h f", h=32), axis=AX.X),
                tag="fk_sum")
            S.op("vector", lambda e: e.reciprocal(rcps, sums), tag="fk_rcp")
            S.op("vector", lambda e: e.tensor_tensor(
                fk_tm.rearrange("p (h f) -> p h f", h=32),
                fk_tm.rearrange("p (h f) -> p h f", h=32),
                rcps.unsqueeze(2).broadcast_to([128, 32, 64]), ALU.mult),
                tag="fk_nrm")
            if L0ATTN_SUB < 3:
                continue
            for hh in range(16):
                S.op("tensor", lambda e, hh=hh: e.matmul(
                    ctx.pA[:, 1024 + hh:1024 + hh + 1],
                    fk_tm[:, hh * 128:(hh + 1) * 128], ctx.onecol[:, :],
                    start=True, stop=True), tag="kp_mm")
            S.op("vector", lambda e, b=b, jc=jc: e.scalar_tensor_tensor(
                ctx.sm2[:, 64 + b * 16:80 + b * 16], ctx.pA[:, 1024:1040], 1.0,
                ctx.sm2[:, 64 + b * 16:80 + b * 16], ALU.mult, ALU.add)
                if jc > 0 else e.tensor_copy(
                ctx.sm2[:, 64 + b * 16:80 + b * 16], ctx.pA[:, 1024:1040]),
                tag="kp_acc")
            if L0ATTN_SUB < 4:
                continue
            for hh in range(16):
                pt = ctx.pA[:, (hh % 2) * 512:(hh % 2) * 512 + 64].bitcast(BF16)
                S.op("tensor", lambda e, hh=hh, pt=pt: e.transpose(
                    pt[:, 0:128], fk_tm[:, hh * 128:(hh + 1) * 128],
                    ctx.identb[:, :]), tag="fk_tr")
                S.op("vector", lambda e, hh=hh, b=b, jc=jc, pt=pt: e.tensor_copy(
                    ctx.rC[:, hh * T + b * 512 + jc * 128:hh * T + b * 512 + (jc + 1) * 128],
                    pt[:, 0:128]), tag="fk_trev")
            if L0ATTN_SUB < 5:
                continue
            for h in range(32):
                S.op("tensor", lambda e, b=b, jc=jc, h=h: e.matmul(
                    ctx.pB[(h % 2) * 64:(h % 2) * 64 + 64, h * 64:(h + 1) * 64],
                    fk_tm[:, h * 64:(h + 1) * 64],
                    ctx.rD[:, (b * 4 + jc) * 2048 + h * 64:(b * 4 + jc) * 2048 + (h + 1) * 64],
                    start=True, stop=True), tag="sd_mm")
            S.op("vector", lambda e, b=b, jc=jc: e.scalar_tensor_tensor(
                ctx.dsb[:, b * 2064:b * 2064 + 2048], ctx.pB[:, 0:2048], 1.0,
                ctx.dsb[:, b * 2064:b * 2064 + 2048], ALU.mult, ALU.add)
                if jc > 0 else e.tensor_copy(
                ctx.dsb[:, b * 2064:b * 2064 + 2048], ctx.pB[:, 0:2048]),
                tag="sd_acc")
        if L0ATTN_SUB < 5:
            continue
        S.op("vector", lambda e, b=b: e.tensor_copy(
            ctx.dsb[:, b * 2064 + 2048:b * 2064 + 2064],
            ctx.sm2[:, 64 + b * 16:80 + b * 16]), tag="kp_exp")

    if L0ATTN_STAGE < 4:
        ctx.wreg_free = [last_use[0], last_use[1]]
        return
    S.dma("sync", lambda e: e.dma_start(out=ctx.d_blk[:, :], in_=ctx.dsb[:, :]),
          tag="blk_out")
    S.op("gpsimd", lambda e: e.collective_compute(
        "AllGather", ALU.bypass, replica_groups=[list(range(NCORES))],
        ins=[ctx.d_blk.ap().opt()], outs=[ctx.d_ag.ap().opt()]), sem="coll", tag="ag")
    for half in range(2):
        S.op("vector", lambda e: e.memset(ctx.accf[:, :], 0.0), tag="accz")
        for r in range(NCORES):
            S.dma("sync", lambda e, r=r, half=half: e.dma_start(
                out=ctx.dbuf[:, 0:2064],
                in_=ctx.d_ag[r * 128:(r + 1) * 128, half * 2064:(half + 1) * 2064]),
                tag="ag_ld")
            S.op("vector", lambda e, r=r: e.scalar_tensor_tensor(
                ctx.accf[:, :], ctx.dbuf[:, 0:2064],
                ctx.pmask[:, r:r + 1], ctx.accf[:, :],
                ALU.mult, ALU.add), tag="ag_acc")
        S.op("vector", lambda e, half=half: e.tensor_copy(
            ctx.acc[:, half * 2064:(half + 1) * 2064], ctx.accf[:, 0:2064]),
            tag="acc_bf")

    if getattr(ctx, "dbg_acc", False):
        emit_dbg(ctx, ctx.acc[:, 0:4096], 128, 4096)
    if L0ATTN_STAGE < 5:
        ctx.wreg_free = [last_use[0], last_use[1]]
        return
    den_sb = ctx.transf[:, 1664:1668]
    rec_sb = ctx.transf[:, 1668:1672]
    am_off = [0, 512, 896, 1152]
    for b in range(2):
        for h in range(32):
            pb_ = (h % 2) * 64
            cch = (h // 2) * T + b * 512
            spref = ctx.acc[pb_:pb_ + 64, b * 2064 + h * 64:b * 2064 + h * 64 + 64]
            kprev = ctx.acc[pb_:pb_ + 64, b * 2064 + 2048 + (h // 2):b * 2064 + 2048 + (h // 2) + 1]
            fq_ap = ctx.rB[pb_:pb_ + 64, cch:cch + 512]
            fk_ap = ctx.rC[pb_:pb_ + 64, cch:cch + 512]
            for jc in range(4):
                nw = 512 - jc * 128
                aps = ctx.pA[:, (jc % 2) * 512:(jc % 2) * 512 + nw]
                mm_op(ctx, pb_, lambda e, jc=jc, nw=nw, aps=aps, fk_ap=fk_ap, fq_ap=fq_ap: e.matmul(
                    aps, fk_ap[:, jc * 128:(jc + 1) * 128], fq_ap[:, jc * 128:512],
                    start=True, stop=True), tag="ap_mm")
                am = ctx.trans[:, 2048 + am_off[jc]:2048 + am_off[jc] + nw]
                S.op("vector", lambda e, jc=jc, aps=aps, am=am: e.tensor_tensor(
                    am[:, 0:128], aps[:, 0:128], ctx.mask[:, 0:128], ALU.mult),
                    tag="am_diag")
                if nw > 128:
                    S.op("vector", lambda e, jc=jc, nw=nw, aps=aps, am=am: e.tensor_copy(
                        am[:, 128:nw], aps[:, 128:nw]), tag="am_off")
            for ic in range(4):
                yps = ctx.pB[:, ic * 64:(ic + 1) * 64]
                first = True
                for jc in range(ic + 1):
                    am = ctx.trans[:, 2048 + am_off[jc] + (ic - jc) * 128:
                                   2048 + am_off[jc] + (ic - jc + 1) * 128]
                    mm_op(ctx, 0, lambda e, am=am, b=b, jc=jc, h=h, yps=yps, first=first: e.matmul(
                        yps, am,
                        ctx.rD[:, (b * 4 + jc) * 2048 + h * 64:(b * 4 + jc) * 2048 + (h + 1) * 64],
                        start=first, stop=False), tag="y_intra")
                    first = False
                mm_op(ctx, pb_, lambda e, ic=ic, yps=yps, fq_ap=fq_ap, spref=spref: e.matmul(
                    yps, fq_ap[:, ic * 128:(ic + 1) * 128], spref,
                    start=False, stop=True), tag="y_inter")
                dps = ctx.pB[:, 256 + ic:256 + ic + 1]
                mm_op(ctx, pb_, lambda e, ic=ic, dps=dps, fq_ap=fq_ap, kprev=kprev: e.matmul(
                    dps, fq_ap[:, ic * 128:(ic + 1) * 128], kprev,
                    start=True, stop=False), tag="dn_kp")
                for jc in range(ic + 1):
                    am = ctx.trans[:, 2048 + am_off[jc] + (ic - jc) * 128:
                                   2048 + am_off[jc] + (ic - jc + 1) * 128]
                    mm_op(ctx, 0, lambda e, am=am, dps=dps, jc=jc, ic=ic: e.matmul(
                        dps, am, ctx.onecol[:, :],
                        start=False, stop=(jc == ic)), tag="dn_am")
            S.op("vector", lambda e: e.tensor_copy(den_sb, ctx.pB[:, 256:260]),
                 tag="dn_ev")
            S.op("vector", lambda e: e.reciprocal(rec_sb, den_sb), tag="dn_rc")
            for ic in range(4):
                S.op("vector", lambda e, ic=ic, b=b, h=h: e.tensor_scalar(
                    ctx.rA[:, (b * 4 + ic) * 2048 + h * 64:(b * 4 + ic) * 2048 + (h + 1) * 64],
                    ctx.pB[:, ic * 64:(ic + 1) * 64], rec_sb[:, ic:ic + 1], None,
                    ALU.mult), tag="y_div")

    for m in range(8):
        for c in range(KD):
            pt = ctx.pA[:, (c % 2) * 512:(c % 2) * 512 + 64].bitcast(BF16)
            mm_op(ctx, 0, lambda e, m=m, c=c, pt=pt: e.transpose(
                pt[:, 0:128], ctx.rA[:, m * 2048 + c * 128:m * 2048 + (c + 1) * 128],
                ctx.identb[:, :]), tag="y_tr")
            tcol = (m // 4) * 512 + (m % 4) * 128
            S.op("vector", lambda e, m=m, c=c, pt=pt, tcol=tcol: e.tensor_copy(
                ctx.rB[:, c * T + tcol:c * T + tcol + 128], pt[:, 0:128]),
                tag="y_trev")

    last2 = [None, None]
    ld2 = [None, None]
    fm_last = S.ops.__len__() - 1
    for o in range(KD):
        s = o % 2
        wsl = ctx.wreg[:, s * 2048:s * 2048 + 2048]
        w8 = (fm_last,) if o < 2 else (last2[s],)
        ld2[s] = S.dma("gpsimd", lambda e, o=o, wsl=wsl: e.dma_start(
            out=wsl, in_=ctx.d_wo[o, :, :]), waits=w8,
            stream="qv%d" % s, sem="qvs%d" % s, tag="wld_wo")
        for tt in range(2):
            ps = ctx.pA[:, (s * 2 + tt) * 512:(s * 2 + tt) * 512 + 512]
            for kc in range(KD):
                tok = S.op("tensor", lambda e, kc=kc, wsl=wsl, ps=ps, tt=tt: e.matmul(
                    ps, wsl[:, kc * 128:(kc + 1) * 128],
                    ctx.rB[:, kc * T + tt * 512:kc * T + (tt + 1) * 512],
                    start=(kc == 0), stop=(kc == KD - 1)),
                    waits=(ld2[s],) if kc == 0 else (), tag="wo_mm")
            xt = x_tile(ctx, o, (tt * 512, (tt + 1) * 512))
            S.op("vector", lambda e, ps=ps, xt=xt: e.tensor_tensor(
                xt, xt, ps, ALU.add), tag="wo_res")
        last2[s] = tok
    ctx.wreg_free = [last2[0], last2[1]]


def _decl_l1_attn(nc, ctx):
    ctx.d_wqkv = nc.dram_tensor("l1a_wqkv", [24, 128, 2048], F32, kind="ExternalInput").ap()
    ctx.d_wproj = nc.dram_tensor("l1a_wproj", [16, 128, 2048], F32, kind="ExternalInput").ap()


def emit_l1_attn(ctx):
    S = ctx.sched
    wreg_free = getattr(ctx, "wreg_free", [None, None])

    su_row = emit_ones_row(ctx, 0, 2048)
    emit_ln(ctx, su_row, ctx.rA, 0, tw=T)
    ztiles = [ctx.rA[:, c * T:(c + 1) * T] for c in range(KD)]

    def qkv_tile(o):
        reg = ctx.rB if o < 16 else ctx.rC
        return reg[:, (o % 16) * T:(o % 16) * T + T]

    last_use = [wreg_free[0], wreg_free[0]]
    ld_tok = [None, None]
    for o in range(24):
        s = o % 2
        wsl = ctx.wreg[:, s * 2048:s * 2048 + 2048]
        ld_tok[s] = S.dma("gpsimd", lambda e, o=o, wsl=wsl: e.dma_start(
            out=wsl, in_=ctx.d_wqkv[o, :, :]), waits=(last_use[s],),
            stream="qk%d" % s, sem="qks%d" % s, tag="wld_qkv")
        for tt in range(2):
            ps = ctx.pA[:, (s * 2 + tt) * 512:(s * 2 + tt) * 512 + 512]
            for k in range(KD):
                tok = S.op("tensor", lambda e, k=k, wsl=wsl, ps=ps, tt=tt: e.matmul(
                    ps, wsl[:, k * 128:(k + 1) * 128],
                    ztiles[k][:, tt * 512:(tt + 1) * 512],
                    start=(k == 0), stop=(k == KD - 1)),
                    waits=(ld_tok[s],) if k == 0 else (), tag="qkv_mm")
            S.op("vector", lambda e, o=o, tt=tt, ps=ps: e.tensor_copy(
                qkv_tile(o)[:, tt * 512:(tt + 1) * 512], ps), tag="qkv_ev")
        last_use[s] = tok

    dv = ctx.trans[:, 2048:2560]
    prod = ctx.trans[:, 2560:3072]
    rDf = ctx.rD[:, :].bitcast(F32)
    s0_sb = rDf[0:1, 4096:6144]
    sd_sb = rDf[0:1, 6144:8192]
    p1row = ctx.rD[0:1, 8192:10240]
    sc = 1.0 / np.sqrt(128.0)
    for g in range(4):
        kt = qkv_tile(g * 6 + 4)
        vt = qkv_tile(g * 6 + 5)
        S.op("vector", lambda e, vt=vt: e.tensor_tensor(
            dv, vt[:, 512:1024], vt[:, 0:512], ALU.subtract), tag="dv")
        for qi in range(4):
            qt = qkv_tile(g * 6 + qi)
            S.op("vector", lambda e, qt=qt, kt=kt: e.tensor_tensor(
                prod, qt[:, 512:1024], kt[:, 0:512], ALU.mult), tag="p0")
            S.op("tensor", lambda e, qi=qi: e.matmul(
                ctx.pA[0:1, qi * 512:qi * 512 + 512], ctx.onecol[:, :], prod,
                start=True, stop=True), tag="s0mm")
            S.op("vector", lambda e, qt=qt, kt=kt: e.tensor_tensor(
                prod, qt[:, 512:1024], kt[:, 512:1024], ALU.mult), tag="p1")
            S.op("tensor", lambda e, qi=qi: e.matmul(
                ctx.pA[64:65, qi * 512:qi * 512 + 512], ctx.onecol[:, :], prod,
                start=True, stop=True), tag="s1mm")
        S.op("vector", lambda e: e.tensor_scalar(
            s0_sb, ctx.pA[0:1, 0:2048], sc, None, ALU.mult), tag="s0ev")
        S.op("vector", lambda e: e.scalar_tensor_tensor(
            sd_sb, ctx.pA[64:65, 0:2048], sc, s0_sb,
            ALU.mult, ALU.subtract), tag="sdiff")
        S.op("scalar", lambda e: e.activation(
            p1row, sd_sb, AF.Sigmoid), tag="sig")
        for qi in range(4):
            h = g * 4 + qi
            pbc = ctx.pB[:, qi * 512:qi * 512 + 512]
            S.op("tensor", lambda e, qi=qi, pbc=pbc: e.matmul(
                pbc, ctx.onerow[:, :], p1row[0:1, qi * 512:qi * 512 + 512],
                start=True, stop=True), tag="pbc")
            S.op("vector", lambda e, pbc=pbc: e.tensor_tensor(
                prod, pbc, dv, ALU.mult), tag="odv")
            ot = ctx.rD[:, h * 512:(h + 1) * 512]
            S.op("vector", lambda e, ot=ot, vt=vt: e.tensor_tensor(
                ot, prod, vt[:, 0:512], ALU.add), tag="oadd")

    last2 = [last_use[0], last_use[1]]
    ld2 = [None, None]
    for o in range(KD):
        s = o % 2
        wsl = ctx.wreg[:, s * 2048:s * 2048 + 2048]
        ld2[s] = S.dma("gpsimd", lambda e, o=o, wsl=wsl: e.dma_start(
            out=wsl, in_=ctx.d_wproj[o, :, :]), waits=(last2[s],),
            stream="qk%d" % s, sem="qks%d" % s, tag="wld_prj")
        for tt in range(2):
            ps = ctx.pA[:, (s * 2 + tt) * 512:(s * 2 + tt) * 512 + 512]
            for hc in range(KD):
                g = hc // 4
                if tt == 0:
                    rhs = qkv_tile(g * 6 + 5)[:, 0:512]
                else:
                    rhs = ctx.rD[:, hc * 512:(hc + 1) * 512]
                tok = S.op("tensor", lambda e, hc=hc, wsl=wsl, ps=ps, rhs=rhs: e.matmul(
                    ps, wsl[:, hc * 128:(hc + 1) * 128], rhs,
                    start=(hc == 0), stop=(hc == KD - 1)),
                    waits=(ld2[s],) if hc == 0 else (), tag="prj_mm")
            xt = x_tile(ctx, o, (tt * 512, (tt + 1) * 512))
            S.op("vector", lambda e, ps=ps, xt=xt: e.tensor_tensor(
                xt, xt, ps, ALU.add), tag="prj_res")
        last2[s] = tok
    ctx.wreg_free = [last2[0], last2[1]]


def _decl_l1_ffn(nc, ctx):
    ctx.d_wg = nc.dram_tensor("l1f_wg", [16, 128, 8], F32, kind="ExternalInput").ap()
    ctx.d_wgu = nc.dram_tensor("l1f_wgu", [8, 16, 128, 2048], F32, kind="ExternalInput").ap()
    ctx.d_wdn = nc.dram_tensor("l1f_wdn", [8, 16, 128, 1024], F32, kind="ExternalInput").ap()
    ctx.d_wshgu = nc.dram_tensor("l1f_wshgu", [64, 128, 2048], F32, kind="ExternalInput").ap()
    ctx.d_wshdn = nc.dram_tensor("l1f_wshdn", [16, 128, 4096], F32, kind="ExternalInput").ap()


L1FFN_PARTS = ("moe", "shared")


def emit_l1_ffn(ctx):
    S = ctx.sched
    wreg_free = getattr(ctx, "wreg_free", [None, None])

    su_row = emit_ones_row(ctx, 0, 2048)
    emit_ln(ctx, su_row, ctx.rA, 0, tw=T)
    ztiles = [ctx.rA[:, c * T:(c + 1) * T] for c in range(KD)]

    do_moe = "moe" in L1FFN_PARTS
    do_shared = "shared" in L1FFN_PARTS
    wg_sb = ctx.trans[:, 2048:2176]
    S.dma("gpsimd", lambda e: e.dma_start(
        out=wg_sb.rearrange("p (k u) -> p k u", u=8),
        in_=ctx.d_wg.rearrange("k p u -> p k u")), tag="wld_rt")
    compfm = ctx.trans[0:8, 4224:5248]
    comb_tm = ctx.trans[:, 2176:2184]
    pBb = ctx.pB[:, 512:1024].bitcast(BF16)
    m1 = ctx.sm2[:, 208:209]
    m2 = ctx.sm2[:, 209:210]
    nm1 = ctx.sm2[:, 210:211]
    ds = ctx.sm2[:, 211:212]
    tmp8 = ctx.sm2[:, 212:220]
    l2 = ctx.sm2[:, 220:228]
    g8 = ctx.sm2[:, 228:236]
    sel = ctx.sm2[:, 236:244]
    rc = ctx.sm2[:, 244:245]
    for m in range(8 if do_moe else 0):
        lps = ctx.pA[:, m * 8:(m + 1) * 8]
        for k in range(KD):
            S.op("tensor", lambda e, m=m, k=k, lps=lps: e.matmul(
                lps, ztiles[k][:, m * 128:(m + 1) * 128], wg_sb[:, k * 8:(k + 1) * 8],
                start=(k == 0), stop=(k == KD - 1)), tag="rt_mm")
        S.op("vector", lambda e, lps=lps: e.reduce_max(m1, lps, axis=AX.X))
        S.op("vector", lambda e, lps=lps: e.tensor_scalar(tmp8, lps, m1, None, ALU.is_ge))
        S.op("vector", lambda e, lps=lps: e.scalar_tensor_tensor(
            l2, tmp8, -1e30, lps, ALU.mult, ALU.add))
        S.op("vector", lambda e: e.reduce_max(m2, l2, axis=AX.X))
        S.op("vector", lambda e: e.tensor_scalar(nm1, m1, -1.0, None, ALU.mult))
        S.op("scalar", lambda e, lps=lps: e.activation(g8, lps, AF.Exp, bias=nm1))
        S.op("vector", lambda e, lps=lps: e.tensor_scalar(sel, lps, m2, None, ALU.is_ge))
        S.op("vector", lambda e: e.tensor_tensor(g8, g8, sel, ALU.mult))
        S.op("vector", lambda e: e.reduce_sum(ds, g8, axis=AX.X))
        S.op("vector", lambda e: e.reciprocal(rc, ds))
        S.op("vector", lambda e: e.tensor_scalar(comb_tm, g8, rc, None, ALU.mult))
        S.op("tensor", lambda e, m=m: e.transpose(
            pBb[0:8, m * 128:(m + 1) * 128], comb_tm, ctx.identb[:, :]), tag="rt_tr")
        S.op("vector", lambda e, m=m: e.tensor_copy(
            compfm[:, m * 128:(m + 1) * 128], pBb[0:8, m * 128:(m + 1) * 128]))

    cbc_sb = ctx.trans[:, 2176:2688]
    silu_t = ctx.trans[:, 2688:3200]
    t1 = ctx.trans[:, 3200:3712]
    ce_row = ctx.trans[0:1, 5248:5760]

    def act_tile(e, k):
        reg = ctx.rB if e < 4 else ctx.rC
        return reg[:, (e % 4) * 4096 + k * 512:(e % 4) * 4096 + (k + 1) * 512]

    last_gu = [wreg_free[0], wreg_free[0]]
    last_dn = [None, wreg_free[1]]
    ld_tok = [None, None]
    for tp in range(2 if do_moe else 0):
        tb = tp * 512
        for e in range(8):
            ce_tok = S.dma("sync", lambda e_, e=e, tb=tb: e_.dma_start(
                out=ce_row, in_=compfm[e:e + 1, tb:tb + 512]), tag="ce_dma")
            pcb = ctx.pB[:, 1536:2048]
            S.op("tensor", lambda e_, pcb=pcb: e_.matmul(
                pcb, ctx.onerow[:, :], ce_row, start=True, stop=True),
                waits=(ce_tok,), tag="cbc_mm")
            S.op("vector", lambda e_, pcb=pcb: e_.tensor_copy(cbc_sb, pcb), tag="cbc_ev")
            for j in range(8):
                s = (e * 8 + j) % 2
                wga = ctx.wreg[:, s * 4096:s * 4096 + 2048]
                wua = ctx.wreg[:, s * 4096 + 2048:s * 4096 + 4096]
                S.dma("gpsimd", lambda e_, e=e, j=j, wga=wga: e_.dma_start(
                    out=wga, in_=ctx.d_wgu[e, j, :, :]), waits=(last_gu[s],),
                    stream="gu%d" % s, sem="gus%d" % s, tag="wld_g")
                ld_tok[s] = S.dma("gpsimd", lambda e_, e=e, j=j, wua=wua: e_.dma_start(
                    out=wua, in_=ctx.d_wgu[e, j + 8, :, :]),
                    stream="gu%d" % s, sem="gus%d" % s, tag="wld_u")
                pg = ctx.pA[:, s * 1024:s * 1024 + 512]
                pu = ctx.pA[:, s * 1024 + 512:s * 1024 + 1024]
                for k in range(KD):
                    S.op("tensor", lambda e_, k=k, wga=wga, pg=pg, tb=tb: e_.matmul(
                        pg, wga[:, k * 128:(k + 1) * 128],
                        ztiles[k][:, tb:tb + 512],
                        start=(k == 0), stop=(k == KD - 1)),
                        waits=(ld_tok[s],) if k == 0 else (), tag="gu_g_mm")
                for k in range(KD):
                    tok = S.op("tensor", lambda e_, k=k, wua=wua, pu=pu, tb=tb: e_.matmul(
                        pu, wua[:, k * 128:(k + 1) * 128],
                        ztiles[k][:, tb:tb + 512],
                        start=(k == 0), stop=(k == KD - 1)), tag="gu_u_mm")
                last_gu[s] = tok
                S.op("scalar", lambda e_, pg=pg: e_.activation(
                    silu_t, pg, AF.Silu), tag="moe_silu")
                S.op("vector", lambda e_, pu=pu: e_.tensor_tensor(
                    t1, pu, cbc_sb, ALU.mult), tag="moe_t1")
                S.op("vector", lambda e_, e=e, j=j: e_.tensor_tensor(
                    act_tile(e, j), silu_t, t1, ALU.mult), tag="moe_act")
        for o in range(KD):
            s = o % 2
            if o == 0:
                w8 = (last_gu[0], last_gu[1]) if last_dn[0] is None else (last_dn[0], last_gu[0], last_gu[1])
            elif o == 1:
                w8 = (last_dn[1],)
            else:
                w8 = (last_dn[s],)
            ldd = None
            for e in range(8):
                wdn = ctx.wreg[:, s * 8192 + e * 1024:s * 8192 + (e + 1) * 1024]
                ldd = S.dma("gpsimd", lambda e_, e=e, o=o, wdn=wdn: e_.dma_start(
                    out=wdn, in_=ctx.d_wdn[e, o, :, :]),
                    waits=w8 if e == 0 else (),
                    stream="dn%d" % s, sem="dns%d" % s, tag="wld_dn")
            pd = ctx.pB[:, s * 512:s * 512 + 512]
            for e in range(8):
                wdn = ctx.wreg[:, s * 8192 + e * 1024:s * 8192 + (e + 1) * 1024]
                for k in range(8):
                    tok = S.op("tensor", lambda e_, e=e, k=k, wdn=wdn, pd=pd: e_.matmul(
                        pd, wdn[:, k * 128:(k + 1) * 128], act_tile(e, k),
                        start=(e == 0 and k == 0), stop=(e == 7 and k == 7)),
                        waits=(ldd,) if (e == 0 and k == 0) else (), tag="dn_mm")
            last_dn[s] = tok
            xt = x_tile(ctx, o, (tb, tb + 512))
            S.op("vector", lambda e_, pd=pd, xt=xt: e_.tensor_tensor(
                xt, xt, pd, ALU.add), tag="dn_res")

    def sh_tile(o):
        reg = ctx.rB if o < 16 else ctx.rC
        return reg[:, (o % 16) * T:(o % 16) * T + T]

    if not do_moe:
        last_dn = [wreg_free[0], wreg_free[1]]
    last_sh = [last_dn[0], last_dn[0]]
    ld_sh = [None, None]
    for o in range(32 if do_shared else 0):
        s = o % 2
        wga = ctx.wreg[:, s * 4096:s * 4096 + 2048]
        wua = ctx.wreg[:, s * 4096 + 2048:s * 4096 + 4096]
        wt = (last_sh[s],) if o < 2 else (last_sh[s],)
        S.dma("gpsimd", lambda e_, o=o, wga=wga: e_.dma_start(
            out=wga, in_=ctx.d_wshgu[o, :, :]), waits=wt,
            stream="sg%d" % s, sem="sgs%d" % s, tag="wld_shg")
        ld_sh[s] = S.dma("gpsimd", lambda e_, o=o, wua=wua: e_.dma_start(
            out=wua, in_=ctx.d_wshgu[o + 32, :, :]),
            stream="sg%d" % s, sem="sgs%d" % s, tag="wld_shu")
        for tt in range(2):
            pg = ctx.pA[:, s * 1024 + tt * 512:s * 1024 + (tt + 1) * 512]
            pu = ctx.pB[:, s * 1024 + tt * 512:s * 1024 + (tt + 1) * 512]
            for k in range(KD):
                S.op("tensor", lambda e_, k=k, wga=wga, pg=pg, tt=tt: e_.matmul(
                    pg, wga[:, k * 128:(k + 1) * 128],
                    ztiles[k][:, tt * 512:(tt + 1) * 512],
                    start=(k == 0), stop=(k == KD - 1)),
                    waits=(ld_sh[s],) if k == 0 else (), tag="sh_g_mm")
            for k in range(KD):
                tok = S.op("tensor", lambda e_, k=k, wua=wua, pu=pu, tt=tt: e_.matmul(
                    pu, wua[:, k * 128:(k + 1) * 128],
                    ztiles[k][:, tt * 512:(tt + 1) * 512],
                    start=(k == 0), stop=(k == KD - 1)), tag="sh_u_mm")
            last_sh[s] = tok
            S.op("scalar", lambda e_, pg=pg: e_.activation(
                silu_t, pg, AF.Silu), tag="sh_silu")
            S.op("vector", lambda e_, o=o, tt=tt, pu=pu: e_.tensor_tensor(
                sh_tile(o)[:, tt * 512:(tt + 1) * 512], silu_t, pu, ALU.mult),
                tag="sh_act")
    last_sd = [None, None]
    for o in range(KD if do_shared else 0):
        s = o % 2
        wsd = ctx.wreg[:, s * 8192:s * 8192 + 4096]
        if o == 0:
            w8 = (last_sh[0], last_sh[1])
        elif o == 1:
            w8 = (last_dn[1],) if last_dn[1] is not None else ()
        else:
            w8 = (last_sd[s],)
        ldd = S.dma("gpsimd", lambda e_, o=o, wsd=wsd: e_.dma_start(
            out=wsd, in_=ctx.d_wshdn[o, :, :]), waits=w8,
            stream="sd%d" % s, sem="sds%d" % s, tag="wld_shd")
        for tt in range(2):
            pd = ctx.pA[:, (s * 2 + tt) * 512:(s * 2 + tt) * 512 + 512]
            for k in range(32):
                tok = S.op("tensor", lambda e_, k=k, wsd=wsd, pd=pd, tt=tt: e_.matmul(
                    pd, wsd[:, k * 128:(k + 1) * 128],
                    sh_tile(k)[:, tt * 512:(tt + 1) * 512],
                    start=(k == 0), stop=(k == 31)),
                    waits=(ldd,) if k == 0 else (), tag="shd_mm")
            xt = x_tile(ctx, o, (tt * 512, (tt + 1) * 512))
            S.op("vector", lambda e_, pd=pd, xt=xt: e_.tensor_tensor(
                xt, xt, pd, ALU.add), tag="shd_res")
        last_sd[s] = tok
    if do_shared:
        ctx.wreg_free = [last_sd[0], last_sd[1]]
    else:
        ctx.wreg_free = [last_dn[0], last_dn[1]]


def _prep_consts():
    h128 = make_hadamard(128).astype(np.float32)
    ident = np.eye(128, dtype=np.float32)
    mask = (np.arange(128)[:, None] <= np.arange(512)[None, :]).astype(np.float32)
    return {"c_h128": h128, "c_ident": ident, "c_mask": mask}


def _prep_pmask(core):
    pm = np.zeros((128, 8), dtype=np.float32)
    pm[:, :core] = 1.0
    return pm


def _prep_l0_attn(inp):
    return {
        "l0a_wq": tile_fm(np.ascontiguousarray(np.asarray(inp["L0_attn_Wq"]).T), D, D),
        "l0a_wk": tile_fm(np.ascontiguousarray(np.asarray(inp["L0_attn_Wk"]).T), D, D),
        "l0a_wv": tile_fm(np.ascontiguousarray(np.asarray(inp["L0_attn_Wv"]).T), D, D),
        "l0a_wo": tile_fm(np.ascontiguousarray(np.asarray(inp["L0_attn_Wo"]).T), D, D),
        "l0a_wfq": np.asarray(inp["L0_attn_Wfq"]),
        "l0a_wfk": np.asarray(inp["L0_attn_Wfk"]),
    }


def _prep_l1_attn(inp):
    wqkv = np.asarray(inp["L1_attn_weight_qkv"])
    wproj = np.asarray(inp["L1_attn_weight_proj"])
    return {
        "l1a_wqkv": tile_fm(np.ascontiguousarray(wqkv.T), D, 3072),
        "l1a_wproj": tile_fm(np.ascontiguousarray(wproj.T), D, D),
    }


def _prep_l1_ffn(inp):
    wg = np.asarray(inp["L1_ffn_w_gate"])
    wgu = np.asarray(inp["L1_ffn_w_gate_up"])
    wdn = np.asarray(inp["L1_ffn_w_down"])
    wshgu = np.asarray(inp["L1_ffn_w_sh_gate_up"])
    wshdn = np.asarray(inp["L1_ffn_w_sh_down"])
    return {
        "l1f_wg": np.ascontiguousarray(wg.reshape(16, 128, 8)),
        "l1f_wgu": np.stack([tile_fm(wgu[e], D, D) for e in range(8)]),
        "l1f_wdn": np.stack([tile_fm(wdn[e], 1024, D) for e in range(8)]),
        "l1f_wshgu": tile_fm(wshgu, D, 8192),
        "l1f_wshdn": tile_fm(wshdn, 4096, D),
    }


def _prep_l0_ffn(inp):
    up = np.asarray(inp["L0_ffn_W_up"])
    gate = np.asarray(inp["L0_ffn_W_gate"])
    down = np.asarray(inp["L0_ffn_W_down"])
    wsc = np.asarray(inp["L0_ffn_Wscale_ug"])
    return {
        "l0f_wup": tile_fm(np.ascontiguousarray(up.T), D, 8192),
        "l0f_wgate": tile_fm(np.ascontiguousarray(gate.T), D, 8192),
        "l0f_wdown": tile_fm(np.ascontiguousarray(down.T), 8192, D),
        "l0f_suug": np.asarray(inp["L0_ffn_SU_ug"]).reshape(1, D),
        "l0f_sudp": pmajor(np.asarray(inp["L0_ffn_SU_dp"])),
        "l0f_wsu_up": pmajor(wsc[:8192]),
        "l0f_wsu_g": pmajor(wsc[8192:]),
        "l0f_wsdp": pmajor(np.asarray(inp["L0_ffn_Wscale_dp"])),
    }


DBG_ACC = False


def build_program(blocks):
    nc = bass.Bass("TRN2", target_bir_lowering=False)
    es = contextlib.ExitStack()
    ctx = Ctx()
    with es:
        _alloc_common(nc, es, ctx)
        ctx.dbg_acc = DBG_ACC
        if "l0_attn" in blocks:
            _decl_l0_attn(nc, ctx)
        if "l0_ffn" in blocks:
            _decl_l0_ffn(nc, ctx)
        if "l1_attn" in blocks:
            _decl_l1_attn(nc, ctx)
        if "l1_ffn" in blocks:
            _decl_l1_ffn(nc, ctx)
        for b in blocks:
            if b == "l0_attn":
                emit_l0_attn(ctx)
            elif b == "l0_ffn":
                emit_l0_ffn(ctx)
            elif b == "l1_attn":
                emit_l1_attn(ctx)
            elif b == "l1_ffn":
                emit_l1_ffn(ctx)
        _emit_store_out(ctx)
        block = es.enter_context(nc.Block())
        ctx.sched.build(nc, block, es)
    return nc


def shard_x(x):
    shards = []
    for c in range(NCORES):
        xs = np.concatenate([x[0, c * TB:(c + 1) * TB, :],
                             x[1, c * TB:(c + 1) * TB, :]], axis=0)
        shards.append(np.ascontiguousarray(xs.T))
    return shards


def unshard_x(outs):
    x = np.empty((2, NCORES * TB, D), dtype=np.float32)
    for c in range(NCORES):
        o = np.asarray(outs[c])
        x[0, c * TB:(c + 1) * TB, :] = o[:, 0:TB].T
        x[1, c * TB:(c + 1) * TB, :] = o[:, TB:2 * TB].T
    return x


KERNEL_BLOCKS = BLOCKS_ALL


def kernel(**inputs):
    x = np.asarray(inputs["L0_attn_hidden_states"])
    consts = _prep_consts()
    w = {}
    if "l0_attn" in KERNEL_BLOCKS:
        w.update(_prep_l0_attn(inputs))
    if "l0_ffn" in KERNEL_BLOCKS:
        w.update(_prep_l0_ffn(inputs))
    if "l1_attn" in KERNEL_BLOCKS:
        w.update(_prep_l1_attn(inputs))
    if "l1_ffn" in KERNEL_BLOCKS:
        w.update(_prep_l1_ffn(inputs))
    nc = build_program(KERNEL_BLOCKS)
    shards = shard_x(x)
    in_maps = []
    for c in range(NCORES):
        m = {"x_fm": shards[c], "c_pmask": _prep_pmask(c)}
        m.update(consts)
        m.update(w)
        in_maps.append(m)
    res = run_bass_kernel_spmd(nc, in_maps, core_ids=list(range(NCORES)))
    global LAST_RESULTS
    LAST_RESULTS = res
    return unshard_x([r["out_fm"] for r in res.results])


LAST_RESULTS = None


# version 04
# speedup vs baseline: 1.0685x; 1.0685x over previous
import contextlib
import numpy as np
import ml_dtypes
import concourse.bass as bass
import concourse.mybir as mybir
from concourse.bass_utils import run_bass_kernel_spmd

F32 = mybir.dt.float32
BF16 = mybir.dt.bfloat16
AF = mybir.ActivationFunctionType
ALU = mybir.AluOpType
AX = mybir.AxisListType

NCORES = 8
D = 2048
KD = D // 128
T = 1024
TB = 512
BF = ml_dtypes.bfloat16

BLOCKS_ALL = ("l0_attn", "l0_ffn", "l1_attn", "l1_ffn")


class Sched:

    def __init__(self):
        self.ops = []
        self.force = set()
        self._tail = {}

    def tail(self, stream="main"):
        return self._tail.get(stream)

    def op(self, eng, fn, waits=(), stream="main", sem=None, inc=1, tag="",
           nosync=False):
        tok = len(self.ops)
        for w in waits:
            if w is not None:
                self.force.add(w)
        self.ops.append(dict(eng=eng, fn=fn, inc=inc, sem=sem or eng,
                             stream=stream, waits=[w for w in waits if w is not None],
                             tag=tag, nosync=nosync))
        self._tail[stream] = tok
        return tok

    def dma(self, eng, fn, waits=(), stream="main", sem=None, tag="", nosync=False):
        return self.op(eng, fn, waits=waits, stream=stream,
                       sem=sem or (eng + "_dma"), inc=16, tag=tag, nosync=nosync)

    def build(self, nc, block, es):
        n = len(self.ops)
        prev_in_stream = {}
        chain_wait = [None] * n
        for i, o in enumerate(self.ops):
            p = prev_in_stream.get(o["stream"])
            if p is not None and not o.get("nosync"):
                po = self.ops[p]
                if po["eng"] != o["eng"]:
                    chain_wait[i] = p
                elif (o["inc"] != 16 and po["inc"] != 16
                      and o["eng"] in ("vector", "scalar", "gpsimd")):
                    chain_wait[i] = p
            prev_in_stream[o["stream"]] = i
        last_of_stream = set(prev_in_stream.values())
        do_inc = [False] * n
        for i, o in enumerate(self.ops):
            if o["inc"] == 16 or i in self.force or i in last_of_stream:
                do_inc[i] = True
        for i in range(n):
            if chain_wait[i] is not None:
                do_inc[chain_wait[i]] = True
        sem_names = sorted({o["sem"] for i, o in enumerate(self.ops) if do_inc[i]})
        sems = {s: es.enter_context(nc.semaphore("sem_" + s)) for s in sem_names}
        cum = {s: 0 for s in sem_names}
        val_after = [0] * n
        for i, o in enumerate(self.ops):
            if do_inc[i]:
                cum[o["sem"]] += o["inc"]
                val_after[i] = cum[o["sem"]]
        resolved = [[] for _ in range(n)]
        for i, o in enumerate(self.ops):
            tgts = list(o["waits"])
            if chain_wait[i] is not None:
                tgts.append(chain_wait[i])
            seen = {}
            for tkn in tgts:
                assert do_inc[tkn], (i, tkn, self.ops[tkn])
                s = self.ops[tkn]["sem"]
                seen[s] = max(seen.get(s, 0), val_after[tkn])
            resolved[i] = [(sems[s], v) for s, v in sorted(seen.items())]

        def make_body(label):
            def body(eng):
                for i, o in enumerate(self.ops):
                    if o["eng"] != label:
                        continue
                    for s, v in resolved[i]:
                        eng.wait_ge(s, v)
                    inst = o["fn"](eng)
                    if do_inc[i]:
                        inst.then_inc(sems[o["sem"]], o["inc"])
            return body

        used = {o["eng"] for o in self.ops}
        for label, attr in [("tensor", block.tensor), ("vector", block.vector),
                            ("scalar", block.scalar), ("gpsimd", block.gpsimd),
                            ("sync", block.sync)]:
            if label in used:
                attr(make_body(label))


def tile_fm(wt, kdim, odim):
    kch, och = kdim // 128, odim // 128
    return np.ascontiguousarray(
        wt.reshape(kch, 128, och, 128).transpose(2, 1, 0, 3).reshape(och, 128, kch * 128))


def pmajor(v):
    n = v.shape[0] // 128
    return np.ascontiguousarray(v.reshape(n, 128).T)


def make_hadamard(n):
    h = np.array([[1.0]], dtype=np.float32)
    while h.shape[0] < n:
        h = np.block([[h, h], [h, -h]])
    return h


class Ctx:
    pass


def _alloc_common(nc, es, ctx):
    S = ctx.sched = Sched()
    ctx.nc = nc
    ctx.es = es

    ctx.d_x = nc.dram_tensor("x_fm", [D, T], F32, kind="ExternalInput").ap()
    ctx.d_out = nc.dram_tensor("out_fm", [D, T], F32, kind="ExternalOutput").ap()
    ctx.d_h128 = nc.dram_tensor("c_h128", [128, 128], F32, kind="ExternalInput").ap()
    ctx.d_ident = nc.dram_tensor("c_ident", [128, 128], F32, kind="ExternalInput").ap()
    ctx.d_mask = nc.dram_tensor("c_mask", [128, 512], F32, kind="ExternalInput").ap()
    ctx.d_pmask = nc.dram_tensor("c_pmask", [128, 8], F32, kind="ExternalInput").ap()
    ctx.d_dbg = nc.dram_tensor("dbg", [128, 4096], F32, kind="ExternalOutput").ap()
    ctx.dbg_off = [0]

    ctx.x = es.enter_context(nc.sbuf_tensor("x", [128, KD * T], BF16))
    ctx.rA = es.enter_context(nc.sbuf_tensor("rA", [128, 16384], BF16))
    ctx.rB = es.enter_context(nc.sbuf_tensor("rB", [128, 16384], BF16))
    ctx.rC = es.enter_context(nc.sbuf_tensor("rC", [128, 16384], BF16))
    ctx.rD = es.enter_context(nc.sbuf_tensor("rD", [128, 16384], BF16))
    ctx.wreg = es.enter_context(nc.sbuf_tensor("wreg", [128, 16384], BF16))
    ctx.trans = es.enter_context(nc.sbuf_tensor("trans", [128, 6656], BF16))
    ctx.dsb = ctx.wreg[:, 4096:8224]
    ctx.accf = ctx.wreg[:, 4096:8224].bitcast(F32)
    ctx.dbuf = ctx.wreg[:, 8224:10288]
    ctx.acc = ctx.wreg[:, 10288:14416]
    ctx.mask = ctx.wreg[:, 14416:14928]
    ctx.sm1 = es.enter_context(nc.sbuf_tensor("sm1", [128, 192], F32))
    ctx.sm2 = es.enter_context(nc.sbuf_tensor("sm2", [128, 256], F32))
    ctx.h128 = es.enter_context(nc.sbuf_tensor("h128", [128, 128], BF16))
    ctx.identb = es.enter_context(nc.sbuf_tensor("identb", [128, 128], BF16))
    ctx.onecol = es.enter_context(nc.sbuf_tensor("onecol", [128, 1], BF16))
    ctx.onerow = es.enter_context(nc.sbuf_tensor("onerow", [1, 128], BF16))
    ctx.pmask = es.enter_context(nc.sbuf_tensor("pmask", [128, 8], F32))

    ctx.pA = es.enter_context(nc.psum_tensor("pA", [128, 2048], F32))
    ctx.pB = es.enter_context(nc.psum_tensor("pB", [128, 2048], F32))

    ctx.transf = ctx.trans[:, :].bitcast(F32)

    S.dma("gpsimd", lambda e: e.dma_start(out=ctx.h128[:, :], in_=ctx.d_h128[:, :]))
    S.dma("gpsimd", lambda e: e.dma_start(out=ctx.identb[:, :], in_=ctx.d_ident[:, :]))
    S.dma("sync", lambda e: e.dma_start(out=ctx.pmask[:, :], in_=ctx.d_pmask[:, :]))
    S.op("vector", lambda e: e.memset(ctx.onecol[:, :], 1.0))
    S.op("vector", lambda e: e.memset(ctx.onerow[:, :], 1.0))
    for c in range(KD):
        S.dma("gpsimd", lambda e, c=c: e.dma_start(
            out=ctx.x[:, c * T:(c + 1) * T], in_=ctx.d_x[c * 128:(c + 1) * 128, :]))


def _emit_store_out(ctx):
    S = ctx.sched
    for c in range(KD):
        S.dma("gpsimd", lambda e, c=c: e.dma_start(
            out=ctx.d_out[c * 128:(c + 1) * 128, :], in_=ctx.x[:, c * T:(c + 1) * T]))
    S.op("scalar", lambda e: e.nop(), tag="fence")


def emit_dbg(ctx, ap, rows, cols):
    off = ctx.dbg_off[0]
    ctx.sched.dma("gpsimd", lambda e: e.dma_start(
        out=ctx.d_dbg[0:rows, off:off + cols], in_=ap), tag="dbg")
    ctx.dbg_off[0] += cols


def x_tile(ctx, c, tcols=None):
    if tcols is None:
        return ctx.x[:, c * T:(c + 1) * T]
    return ctx.x[:, c * T + tcols[0]:c * T + tcols[1]]


def t_row_bf(ctx, off, w):
    return ctx.trans[0:1, off:off + w]


def t_tile_bf(ctx, off, w, p0=0, p1=128):
    return ctx.trans[p0:p1, off:off + w]


def emit_ln(ctx, su_row, out_region, out_off, tw=T, tbase=0):
    S = ctx.sched
    nt = tw // 512
    x2t = ctx.trans[:, 2048:2560]

    def xt(c, i):
        return ctx.x[:, c * T + tbase + i * 512: c * T + tbase + (i + 1) * 512]

    for i in range(nt):
        psx = ctx.pA[0:1, 1024 + i * 512:1536 + i * 512]
        psq = ctx.pB[0:1, 1024 + i * 512:1536 + i * 512]
        for c in range(KD):
            S.op("scalar", lambda e, c=c, i=i: e.activation(
                x2t, xt(c, i), AF.Square), tag="ln_sq")
            S.op("tensor", lambda e, c=c, i=i, psx=psx: e.matmul(
                psx, ctx.onecol[:, :], xt(c, i), start=(c == 0), stop=(c == KD - 1)),
                tag="ln_sx")
            S.op("tensor", lambda e, c=c, psq=psq: e.matmul(
                psq, ctx.onecol[:, :], x2t, start=(c == 0), stop=(c == KD - 1)),
                tag="ln_sx2")
    mu_ps = ctx.pA[0:1, 1024:1024 + tw]
    var_ps = ctx.pB[0:1, 1024:1024 + tw]
    musq = ctx.transf[0:1, 1280:1280 + tw]
    rstd_bf = t_row_bf(ctx, 4608, tw)
    nmr_bf = t_row_bf(ctx, 4608 + tw, tw)
    S.op("vector", lambda e: e.tensor_scalar(mu_ps, mu_ps, 1.0 / D, None, ALU.mult))
    S.op("vector", lambda e: e.tensor_scalar(var_ps, var_ps, 1.0 / D, 1e-5, ALU.mult, ALU.add))
    S.op("scalar", lambda e: e.activation(musq, mu_ps, AF.Square))
    S.op("vector", lambda e: e.tensor_tensor(var_ps, var_ps, musq, ALU.subtract))
    S.op("scalar", lambda e: e.activation(var_ps, var_ps, AF.Ln))
    S.op("vector", lambda e: e.tensor_scalar(var_ps, var_ps, -0.5, None, ALU.mult))
    S.op("scalar", lambda e: e.activation(rstd_bf, var_ps, AF.Exp))
    S.op("vector", lambda e: e.scalar_tensor_tensor(
        nmr_bf, mu_ps, -1.0, rstd_bf, ALU.mult, ALU.mult))
    for c in range(KD):
        su_c = su_row[0:1, c * 128:(c + 1) * 128]
        for i in range(nt):
            pa = ctx.pA[:, (c % 2) * 512:(c % 2) * 512 + 512]
            pb = ctx.pB[:, (c % 2) * 512:(c % 2) * 512 + 512]
            S.op("tensor", lambda e, pa=pa, su_c=su_c, i=i: e.matmul(
                pa, su_c, rstd_bf[0:1, i * 512:(i + 1) * 512], start=True, stop=True),
                tag="ln_abc")
            S.op("tensor", lambda e, pb=pb, su_c=su_c, i=i: e.matmul(
                pb, su_c, nmr_bf[0:1, i * 512:(i + 1) * 512], start=True, stop=True),
                tag="ln_bbc")
            ot = out_region[:, out_off + c * tw + i * 512: out_off + c * tw + (i + 1) * 512]
            S.op("vector", lambda e, c=c, i=i, ot=ot, pa=pa: e.tensor_tensor(
                ot, xt(c, i), pa, ALU.mult), tag="ln_za")
            S.op("vector", lambda e, ot=ot, pb=pb: e.tensor_tensor(
                ot, ot, pb, ALU.add), tag="ln_zb")


def mm_op(ctx, base, fn, **kw):
    need_nop = getattr(ctx, "_mm_base", None) is not None and ctx._mm_base != base
    ctx._mm_base = base
    if need_nop:
        fn2 = lambda e: (e.nop(), fn(e))[1]
    else:
        fn2 = fn
    return ctx.sched.op("tensor", fn2, **kw)


def emit_ones_row(ctx, off, w):
    S = ctx.sched
    row = t_row_bf(ctx, off, w)
    S.op("vector", lambda e: e.memset(row, 1.0))
    return row


def emit_fwht_combine(ctx, tiles):
    S = ctx.sched
    n = len(tiles)
    barrier = [S.tail("main"), None]
    h = 1
    while h < n:
        first = {"vector": True, "gpsimd": True}
        last = {"vector": None, "gpsimd": None}
        idx = 0
        for base in range(0, n, 2 * h):
            for j in range(h):
                a = tiles[base + j]
                b = tiles[base + j + h]
                eng = "vector" if idx % 2 == 0 else "gpsimd"
                idx += 1
                w0 = tuple(t for t in barrier if t is not None) if first[eng] else ()
                first[eng] = False
                if eng == "vector":
                    ta = S.op(eng, lambda e, a=a, b=b: e.tensor_tensor(a, a, b, ALU.add),
                              waits=w0, nosync=True, tag="fw_a")
                    last[eng] = S.op(eng, lambda e, a=a, b=b: e.scalar_tensor_tensor(
                        b, b, -2.0, a, ALU.mult, ALU.add),
                        waits=(ta,), nosync=True, tag="fw_b")
                else:
                    tmp = ctx.trans[:, 2048:2560]
                    tt = S.op(eng, lambda e, a=a, b=b, tmp=tmp: e.tensor_tensor(
                        tmp, a, b, ALU.subtract), waits=w0, nosync=True, tag="fw_s")
                    ta = S.op(eng, lambda e, a=a, b=b: e.tensor_tensor(
                        a, a, b, ALU.add), waits=(tt,), nosync=True, tag="fw_a")
                    last[eng] = S.op(eng, lambda e, b=b, tmp=tmp: e.tensor_copy(
                        b, tmp), waits=(ta,), nosync=True, tag="fw_b")
        barrier = [last["vector"], last["gpsimd"]]
        h *= 2
    S.op("vector", lambda e: e.nop(), waits=tuple(t for t in barrier if t is not None),
         tag="fw_end")


def emit_fwht_h128(ctx, tiles, scale=None):
    S = ctx.sched
    in_tok = S.tail("main")
    psum_free = [None, None, None, None]
    evs = []
    for idx, tile in enumerate(tiles):
        tw = tile.shape[1]
        s4 = idx % 4
        pa = ctx.pA if s4 % 2 == 0 else ctx.pB
        ps = pa[:, (s4 // 2) * 512:(s4 // 2) * 512 + tw]
        tok = S.op("tensor", lambda e, tile=tile, tw=tw, ps=ps: e.matmul(
            ps, ctx.h128[:, :], tile[:, 0:tw], start=True, stop=True),
            waits=(psum_free[s4], in_tok if idx < 4 else None),
            stream="mm", tag="fw_h128")
        psum_free[s4] = S.op("scalar", lambda e, tile=tile, tw=tw, ps=ps: e.copy(
            tile[:, 0:tw], ps), waits=(tok,), nosync=True, tag="fw_ev")
    S.op("vector", lambda e: e.nop(),
         waits=tuple(t for t in psum_free if t is not None), tag="fwh_end")


def _decl_l0_ffn(nc, ctx):
    ctx.d_wup = nc.dram_tensor("l0f_wup", [64, 128, 2048], F32, kind="ExternalInput").ap()
    ctx.d_wgate = nc.dram_tensor("l0f_wgate", [64, 128, 2048], F32, kind="ExternalInput").ap()
    ctx.d_wdown = nc.dram_tensor("l0f_wdown", [16, 128, 8192], F32, kind="ExternalInput").ap()
    ctx.d_suug = nc.dram_tensor("l0f_suug", [1, 2048], F32, kind="ExternalInput").ap()
    ctx.d_sudp = nc.dram_tensor("l0f_sudp", [128, 64], F32, kind="ExternalInput").ap()
    ctx.d_wsu_up = nc.dram_tensor("l0f_wsu_up", [128, 64], F32, kind="ExternalInput").ap()
    ctx.d_wsu_g = nc.dram_tensor("l0f_wsu_g", [128, 64], F32, kind="ExternalInput").ap()
    ctx.d_wsdp = nc.dram_tensor("l0f_wsdp", [128, 16], F32, kind="ExternalInput").ap()


def emit_l0_ffn(ctx):
    S = ctx.sched

    su_row = t_row_bf(ctx, 0, 2048)
    S.dma("gpsimd", lambda e: e.dma_start(out=su_row, in_=ctx.d_suug[:, :]))
    wsg64 = ctx.sm1[:, 0:64]
    comb_up = ctx.sm1[:, 64:128]
    wdp64 = ctx.sm1[:, 128:144]
    sudp = ctx.sm2[:, 0:64]
    wsuup = ctx.sm2[:, 64:128]
    S.dma("sync", lambda e: e.dma_start(out=sudp, in_=ctx.d_sudp[:, :]))
    S.dma("sync", lambda e: e.dma_start(out=wsuup, in_=ctx.d_wsu_up[:, :]))
    S.dma("sync", lambda e: e.dma_start(out=ctx.sm2[:, 128:192], in_=ctx.d_wsu_g[:, :]))
    S.dma("sync", lambda e: e.dma_start(out=ctx.sm2[:, 192:208], in_=ctx.d_wsdp[:, :]))
    S.op("vector", lambda e: e.tensor_scalar(
        wsg64, ctx.sm2[:, 128:192], 1.0 / np.sqrt(2048.0), None, ALU.mult))
    S.op("vector", lambda e: e.scalar_tensor_tensor(
        comb_up, wsuup, 1.0 / np.sqrt(2048.0), sudp, ALU.mult, ALU.mult))
    S.op("vector", lambda e: e.tensor_scalar(
        wdp64, ctx.sm2[:, 192:208], 1.0 / np.sqrt(8192.0), None, ALU.mult))

    wreg_free = getattr(ctx, "wreg_free", [None, None])
    for tp in range(2):
        tb = tp * 512
        emit_ln(ctx, su_row, ctx.rA, 0, tw=512, tbase=tb)
        ztiles = [ctx.rA[:, c * 512:(c + 1) * 512] for c in range(KD)]
        emit_fwht_combine(ctx, ztiles)
        emit_fwht_h128(ctx, ztiles)

        in_tok = S.tail("main")
        last_use = [wreg_free[0], wreg_free[0]]
        ld_tok = [None, None]
        psum_free = [None, None]
        act = []
        for o in range(64):
            reg = ctx.rB if o < 32 else ctx.rC
            act.append(reg[:, (o % 32) * 512:(o % 32) * 512 + 512])
        for o in range(64):
            s = o % 2
            wup = ctx.wreg[:, s * 4096:s * 4096 + 2048]
            wgt = ctx.wreg[:, s * 4096 + 2048:s * 4096 + 4096]
            S.dma("gpsimd", lambda e, o=o, wup=wup: e.dma_start(
                out=wup, in_=ctx.d_wup[o, :, :]), waits=(last_use[s],),
                stream="wl%d" % s, sem="wsl%d" % s, tag="wld_up")
            ld_tok[s] = S.dma("gpsimd", lambda e, o=o, wgt=wgt: e.dma_start(
                out=wgt, in_=ctx.d_wgate[o, :, :]),
                stream="wl%d" % s, sem="wsl%d" % s, tag="wld_gt")
            pu = ctx.pA[:, s * 1024:s * 1024 + 512]
            pg = ctx.pA[:, s * 1024 + 512:s * 1024 + 1024]
            for k in range(KD):
                w0 = (ld_tok[s], psum_free[s], in_tok if o < 2 else None) if k == 0 else ()
                tok_up = S.op("tensor", lambda e, k=k, wup=wup, pu=pu: e.matmul(
                    pu, wup[:, k * 128:(k + 1) * 128], ztiles[k],
                    start=(k == 0), stop=(k == KD - 1)),
                    waits=w0, stream="mm", tag="up_mm")
            for k in range(KD):
                tok = S.op("tensor", lambda e, k=k, wgt=wgt, pg=pg: e.matmul(
                    pg, wgt[:, k * 128:(k + 1) * 128], ztiles[k],
                    start=(k == 0), stop=(k == KD - 1)), stream="mm", tag="gt_mm")
            last_use[s] = tok
            S.op("scalar", lambda e, o=o, pg=pg: e.activation(
                ctx.trans[:, 2048:2560], pg, AF.Silu, scale=wsg64[:, o:o + 1]),
                waits=(tok,), tag="silu")
            psum_free[s] = S.op("vector", lambda e, o=o, pu=pu: e.scalar_tensor_tensor(
                act[o], ctx.trans[:, 2048:2560], comb_up[:, o:o + 1], pu,
                ALU.mult, ALU.mult), waits=(tok_up,), tag="actmul")

        emit_fwht_combine(ctx, act)
        emit_fwht_h128(ctx, act)

        in_tok = S.tail("main")
        last_use_d = [None, None]
        ld_tok_d = [None, None]
        psum_free_d = [None, None]
        for o in range(KD):
            s = o % 2
            wd = ctx.wreg[:, s * 8192:s * 8192 + 8192]
            if o == 0:
                w8 = (last_use[0], last_use[1])
            elif o == 1:
                w8 = (wreg_free[1],)
            else:
                w8 = (last_use_d[s],)
            ld_tok_d[s] = S.dma("gpsimd", lambda e, o=o, wd=wd: e.dma_start(
                out=wd, in_=ctx.d_wdown[o, :, :]), waits=w8,
                stream="wd%d" % s, sem="wsd%d" % s, tag="wld_dn")
            pd = ctx.pB[:, s * 512:s * 512 + 512]
            for k in range(64):
                w0 = (ld_tok_d[s], psum_free_d[s], in_tok if o < 2 else None) if k == 0 else ()
                tok = S.op("tensor", lambda e, k=k, wd=wd, pd=pd: e.matmul(
                    pd, wd[:, k * 128:(k + 1) * 128], act[k],
                    start=(k == 0), stop=(k == 63)),
                    waits=w0, stream="mm", tag="dn_mm")
            last_use_d[s] = tok
            xt = x_tile(ctx, o, (tb, tb + 512))
            psum_free_d[s] = S.op("vector", lambda e, o=o, pd=pd, xt=xt: e.scalar_tensor_tensor(
                xt, pd, wdp64[:, o:o + 1], xt, ALU.mult, ALU.add),
                waits=(tok,), tag="dn_res")
        wreg_free = [last_use_d[0], last_use_d[1]]
    ctx.wreg_free = wreg_free


def _decl_l0_attn(nc, ctx):
    ctx.d_wq = nc.dram_tensor("l0a_wq", [16, 128, 2048], F32, kind="ExternalInput").ap()
    ctx.d_wk = nc.dram_tensor("l0a_wk", [16, 128, 2048], F32, kind="ExternalInput").ap()
    ctx.d_wv = nc.dram_tensor("l0a_wv", [16, 128, 2048], F32, kind="ExternalInput").ap()
    ctx.d_wo = nc.dram_tensor("l0a_wo", [16, 128, 2048], F32, kind="ExternalInput").ap()
    ctx.d_wfq = nc.dram_tensor("l0a_wfq", [32, 64, 64], F32, kind="ExternalInput").ap()
    ctx.d_wfk = nc.dram_tensor("l0a_wfk", [32, 64, 64], F32, kind="ExternalInput").ap()
    ctx.d_blk = nc.dram_tensor("l0a_blk", [128, 4128], BF16)
    ctx.d_ag = nc.dram_tensor("l0a_ag", [NCORES * 128, 4128], BF16)


L0ATTN_STAGE = 5
L0ATTN_SUB = 9


def emit_l0_attn(ctx):
    S = ctx.sched
    wreg_free = getattr(ctx, "wreg_free", [None, None])

    S.dma("gpsimd", lambda e: e.dma_start(out=ctx.mask[:, :], in_=ctx.d_mask[:, :]))

    su_row = emit_ones_row(ctx, 0, 2048)
    emit_ln(ctx, su_row, ctx.rA, 0, tw=T)
    ztiles = [ctx.rA[:, c * T:(c + 1) * T] for c in range(KD)]

    vtmp = ctx.trans[:, 2048:3072]
    in_tok = S.tail("main")
    last_use = [wreg_free[0], wreg_free[0]]
    ld_tok = [None, None]
    psum_free = [None, None, None, None]
    pBb8 = ctx.pB[:, 0:512].bitcast(BF16)
    trev_tok = None
    ev_toks = []
    for which, wdram in (("q", ctx.d_wq), ("k", ctx.d_wk), ("v", ctx.d_wv)):
        for o in range(KD):
            s = o % 2
            wsl = ctx.wreg[:, s * 2048:s * 2048 + 2048]
            ld_tok[s] = S.dma("gpsimd", lambda e, o=o, wsl=wsl, wdram=wdram: e.dma_start(
                out=wsl, in_=wdram[o, :, :]), waits=(last_use[s],),
                stream="qv%d" % s, sem="qvs%d" % s, tag="wld_" + which)
            ev2 = []
            for tt in range(2):
                sl = s * 2 + tt
                ps = ctx.pA[:, sl * 512:sl * 512 + 512]
                for k in range(KD):
                    w0 = (ld_tok[s], psum_free[sl],
                          in_tok if (which == "q" and o < 2) else None) if k == 0 else ()
                    tok = S.op("tensor", lambda e, k=k, wsl=wsl, ps=ps, tt=tt: e.matmul(
                        ps, wsl[:, k * 128:(k + 1) * 128],
                        ztiles[k][:, tt * 512:(tt + 1) * 512],
                        start=(k == 0), stop=(k == KD - 1)),
                        waits=w0, stream="mm", tag=which + "_mm")
                if which == "q":
                    dst = ctx.rB[:, o * T + tt * 512:o * T + (tt + 1) * 512]
                elif which == "k":
                    dst = ctx.rC[:, o * T + tt * 512:o * T + (tt + 1) * 512]
                else:
                    dst = vtmp[:, tt * 512:(tt + 1) * 512]
                wv = (tok, trev_tok) if which == "v" else (tok,)
                psum_free[sl] = S.op("scalar", lambda e, dst=dst, ps=ps: e.copy(
                    dst, ps), waits=wv, nosync=True, tag=which + "_ev")
                ev2.append(psum_free[sl])
            last_use[s] = tok
            if which == "v":
                ttoks = []
                for m in range(8):
                    pt = pBb8[:, m * 128:(m + 1) * 128]
                    ttoks.append(S.op("tensor", lambda e, m=m, pt=pt: e.transpose(
                        pt, vtmp[:, m * 128:(m + 1) * 128],
                        ctx.identb[:, :]),
                        waits=(ev2[0], ev2[1], trev_tok) if m == 0 else (),
                        stream="mm", nosync=True, tag="v_tr"))
                trev_tok = S.op("vector", lambda e, o=o: e.tensor_copy(
                    ctx.rD[:, :].rearrange("p (m t) -> p m t", t=2048)[:, :, o * 128:(o + 1) * 128],
                    pBb8.rearrange("p (m c) -> p m c", c=128)),
                    waits=(ttoks[-1],), nosync=True, tag="v_trev")
                ev_toks.append(trev_tok)
            else:
                ev_toks.extend(ev2)
    S.op("vector", lambda e: e.nop(),
         waits=(max(t for t in psum_free if t is not None), trev_tok),
         tag="qkv_anchor")

    if L0ATTN_STAGE < 2:
        ctx.wreg_free = [last_use[0], last_use[1]]
        return
    wfq_sb = ctx.wreg[:, 0:2048]
    wfk_sb = ctx.wreg[:, 2048:4096]
    ms_tok = S.op("vector", lambda e: e.memset(ctx.wreg[:, 0:4096], 0.0),
                  waits=(last_use[0], last_use[1]), tag="fm_ms")
    for wsb, wdram, tagn in ((wfq_sb, ctx.d_wfq, "wld_fq"), (wfk_sb, ctx.d_wfk, "wld_fk")):
        for half in range(2):
            fm_ld = S.dma("gpsimd", lambda e, wsb=wsb, wdram=wdram, half=half: e.dma_start(
                out=wsb[half * 64:half * 64 + 64, :].rearrange(
                    "d (c g) -> d c g", g=128)[:, :, half * 64:half * 64 + 64],
                in_=wdram.rearrange("(c two) d f -> two d c f", two=2)[half]),
                waits=(ms_tok,) if (half == 0 and wsb is wfq_sb) else (),
                stream="qv0", sem="qvs0", tag=tagn)

    first_fm = True
    for b in range(2):
        for c in range(KD):
            ps = ctx.pA[:, (c % 4) * 512:(c % 4) * 512 + 512]
            mm_op(ctx, 0, lambda e, b=b, c=c, ps=ps: e.matmul(
                ps, wfq_sb[:, c * 128:(c + 1) * 128],
                ctx.rB[:, c * T + b * 512:c * T + b * 512 + 512],
                start=True, stop=True),
                waits=(fm_ld,) if first_fm else (), tag="fq_mm")
            first_fm = False
            S.op("scalar", lambda e, b=b, c=c, ps=ps: e.activation(
                ctx.rB[:, c * T + b * 512:c * T + b * 512 + 512], ps, AF.Exp),
                tag="fq_exp")

    if L0ATTN_STAGE < 3:
        ctx.wreg_free = [last_use[0], last_use[1]]
        return
    fk_tm = ctx.trans[:, 0:2048]
    sums = ctx.sm2[:, 0:32]
    rcps = ctx.sm2[:, 32:64]
    for b in range(2):
        for jc in range(4):
            for c in range(KD):
                mm_op(ctx, 0, lambda e, b=b, jc=jc, c=c: e.matmul(
                    ctx.pA[:, c * 128:(c + 1) * 128],
                    ctx.rC[:, c * T + b * 512 + jc * 128:
                           c * T + b * 512 + (jc + 1) * 128],
                    wfk_sb[:, c * 128:(c + 1) * 128],
                    start=True, stop=True), tag="fk_mm")
            S.op("scalar", lambda e: e.activation(fk_tm, ctx.pA[:, :], AF.Exp),
                 tag="fk_exp")
            if L0ATTN_SUB < 2:
                continue
            S.op("vector", lambda e: e.reduce_sum(
                sums, fk_tm.rearrange("p (h f) -> p h f", h=32), axis=AX.X),
                tag="fk_sum")
            S.op("vector", lambda e: e.reciprocal(rcps, sums), tag="fk_rcp")
            S.op("vector", lambda e: e.tensor_tensor(
                fk_tm.rearrange("p (h f) -> p h f", h=32),
                fk_tm.rearrange("p (h f) -> p h f", h=32),
                rcps.unsqueeze(2).broadcast_to([128, 32, 64]), ALU.mult),
                tag="fk_nrm")
            if L0ATTN_SUB < 3:
                continue
            for hh in range(16):
                S.op("tensor", lambda e, hh=hh: e.matmul(
                    ctx.pA[:, 1024 + hh:1024 + hh + 1],
                    fk_tm[:, hh * 128:(hh + 1) * 128], ctx.onecol[:, :],
                    start=True, stop=True), tag="kp_mm")
            S.op("vector", lambda e, b=b, jc=jc: e.scalar_tensor_tensor(
                ctx.sm2[:, 64 + b * 16:80 + b * 16], ctx.pA[:, 1024:1040], 1.0,
                ctx.sm2[:, 64 + b * 16:80 + b * 16], ALU.mult, ALU.add)
                if jc > 0 else e.tensor_copy(
                ctx.sm2[:, 64 + b * 16:80 + b * 16], ctx.pA[:, 1024:1040]),
                tag="kp_acc")
            if L0ATTN_SUB < 4:
                continue
            for hh in range(16):
                pt = ctx.pA[:, (hh % 2) * 512:(hh % 2) * 512 + 64].bitcast(BF16)
                S.op("tensor", lambda e, hh=hh, pt=pt: e.transpose(
                    pt[:, 0:128], fk_tm[:, hh * 128:(hh + 1) * 128],
                    ctx.identb[:, :]), tag="fk_tr")
                S.op("vector", lambda e, hh=hh, b=b, jc=jc, pt=pt: e.tensor_copy(
                    ctx.rC[:, hh * T + b * 512 + jc * 128:hh * T + b * 512 + (jc + 1) * 128],
                    pt[:, 0:128]), tag="fk_trev")
            if L0ATTN_SUB < 5:
                continue
            for h in range(32):
                S.op("tensor", lambda e, b=b, jc=jc, h=h: e.matmul(
                    ctx.pB[(h % 2) * 64:(h % 2) * 64 + 64, h * 64:(h + 1) * 64],
                    fk_tm[:, h * 64:(h + 1) * 64],
                    ctx.rD[:, (b * 4 + jc) * 2048 + h * 64:(b * 4 + jc) * 2048 + (h + 1) * 64],
                    start=True, stop=True), tag="sd_mm")
            S.op("vector", lambda e, b=b, jc=jc: e.scalar_tensor_tensor(
                ctx.dsb[:, b * 2064:b * 2064 + 2048], ctx.pB[:, 0:2048], 1.0,
                ctx.dsb[:, b * 2064:b * 2064 + 2048], ALU.mult, ALU.add)
                if jc > 0 else e.tensor_copy(
                ctx.dsb[:, b * 2064:b * 2064 + 2048], ctx.pB[:, 0:2048]),
                tag="sd_acc")
        if L0ATTN_SUB < 5:
            continue
        S.op("vector", lambda e, b=b: e.tensor_copy(
            ctx.dsb[:, b * 2064 + 2048:b * 2064 + 2064],
            ctx.sm2[:, 64 + b * 16:80 + b * 16]), tag="kp_exp")

    if L0ATTN_STAGE < 4:
        ctx.wreg_free = [last_use[0], last_use[1]]
        return
    S.dma("sync", lambda e: e.dma_start(out=ctx.d_blk[:, :], in_=ctx.dsb[:, :]),
          tag="blk_out")
    S.op("gpsimd", lambda e: e.collective_compute(
        "AllGather", ALU.bypass, replica_groups=[list(range(NCORES))],
        ins=[ctx.d_blk.ap().opt()], outs=[ctx.d_ag.ap().opt()]), sem="coll", tag="ag")
    for half in range(2):
        S.op("vector", lambda e: e.memset(ctx.accf[:, :], 0.0), tag="accz")
        for r in range(NCORES):
            S.dma("sync", lambda e, r=r, half=half: e.dma_start(
                out=ctx.dbuf[:, 0:2064],
                in_=ctx.d_ag[r * 128:(r + 1) * 128, half * 2064:(half + 1) * 2064]),
                tag="ag_ld")
            S.op("vector", lambda e, r=r: e.scalar_tensor_tensor(
                ctx.accf[:, :], ctx.dbuf[:, 0:2064],
                ctx.pmask[:, r:r + 1], ctx.accf[:, :],
                ALU.mult, ALU.add), tag="ag_acc")
        S.op("vector", lambda e, half=half: e.tensor_copy(
            ctx.acc[:, half * 2064:(half + 1) * 2064], ctx.accf[:, 0:2064]),
            tag="acc_bf")

    if getattr(ctx, "dbg_acc", False):
        emit_dbg(ctx, ctx.acc[:, 0:4096], 128, 4096)
    if L0ATTN_STAGE < 5:
        ctx.wreg_free = [last_use[0], last_use[1]]
        return
    den_sb = ctx.transf[:, 1664:1668]
    rec_sb = ctx.transf[:, 1668:1672]
    am_off = [0, 512, 896, 1152]
    for b in range(2):
        for h in range(32):
            pb_ = (h % 2) * 64
            cch = (h // 2) * T + b * 512
            spref = ctx.acc[pb_:pb_ + 64, b * 2064 + h * 64:b * 2064 + h * 64 + 64]
            kprev = ctx.acc[pb_:pb_ + 64, b * 2064 + 2048 + (h // 2):b * 2064 + 2048 + (h // 2) + 1]
            fq_ap = ctx.rB[pb_:pb_ + 64, cch:cch + 512]
            fk_ap = ctx.rC[pb_:pb_ + 64, cch:cch + 512]
            for jc in range(4):
                nw = 512 - jc * 128
                aps = ctx.pA[:, (jc % 2) * 512:(jc % 2) * 512 + nw]
                mm_op(ctx, pb_, lambda e, jc=jc, nw=nw, aps=aps, fk_ap=fk_ap, fq_ap=fq_ap: e.matmul(
                    aps, fk_ap[:, jc * 128:(jc + 1) * 128], fq_ap[:, jc * 128:512],
                    start=True, stop=True), tag="ap_mm")
                am = ctx.trans[:, 2048 + am_off[jc]:2048 + am_off[jc] + nw]
                S.op("vector", lambda e, jc=jc, aps=aps, am=am: e.tensor_tensor(
                    am[:, 0:128], aps[:, 0:128], ctx.mask[:, 0:128], ALU.mult),
                    tag="am_diag")
                if nw > 128:
                    S.op("vector", lambda e, jc=jc, nw=nw, aps=aps, am=am: e.tensor_copy(
                        am[:, 128:nw], aps[:, 128:nw]), tag="am_off")
            for ic in range(4):
                yps = ctx.pB[:, ic * 64:(ic + 1) * 64]
                first = True
                for jc in range(ic + 1):
                    am = ctx.trans[:, 2048 + am_off[jc] + (ic - jc) * 128:
                                   2048 + am_off[jc] + (ic - jc + 1) * 128]
                    mm_op(ctx, 0, lambda e, am=am, b=b, jc=jc, h=h, yps=yps, first=first: e.matmul(
                        yps, am,
                        ctx.rD[:, (b * 4 + jc) * 2048 + h * 64:(b * 4 + jc) * 2048 + (h + 1) * 64],
                        start=first, stop=False), tag="y_intra")
                    first = False
                mm_op(ctx, pb_, lambda e, ic=ic, yps=yps, fq_ap=fq_ap, spref=spref: e.matmul(
                    yps, fq_ap[:, ic * 128:(ic + 1) * 128], spref,
                    start=False, stop=True), tag="y_inter")
                dps = ctx.pB[:, 256 + ic:256 + ic + 1]
                mm_op(ctx, pb_, lambda e, ic=ic, dps=dps, fq_ap=fq_ap, kprev=kprev: e.matmul(
                    dps, fq_ap[:, ic * 128:(ic + 1) * 128], kprev,
                    start=True, stop=False), tag="dn_kp")
                for jc in range(ic + 1):
                    am = ctx.trans[:, 2048 + am_off[jc] + (ic - jc) * 128:
                                   2048 + am_off[jc] + (ic - jc + 1) * 128]
                    mm_op(ctx, 0, lambda e, am=am, dps=dps, jc=jc, ic=ic: e.matmul(
                        dps, am, ctx.onecol[:, :],
                        start=False, stop=(jc == ic)), tag="dn_am")
            S.op("vector", lambda e: e.tensor_copy(den_sb, ctx.pB[:, 256:260]),
                 tag="dn_ev")
            S.op("vector", lambda e: e.reciprocal(rec_sb, den_sb), tag="dn_rc")
            for ic in range(4):
                S.op("vector", lambda e, ic=ic, b=b, h=h: e.tensor_scalar(
                    ctx.rA[:, (b * 4 + ic) * 2048 + h * 64:(b * 4 + ic) * 2048 + (h + 1) * 64],
                    ctx.pB[:, ic * 64:(ic + 1) * 64], rec_sb[:, ic:ic + 1], None,
                    ALU.mult), tag="y_div")

    for m in range(8):
        for c in range(KD):
            pt = ctx.pA[:, (c % 2) * 512:(c % 2) * 512 + 64].bitcast(BF16)
            mm_op(ctx, 0, lambda e, m=m, c=c, pt=pt: e.transpose(
                pt[:, 0:128], ctx.rA[:, m * 2048 + c * 128:m * 2048 + (c + 1) * 128],
                ctx.identb[:, :]), tag="y_tr")
            tcol = (m // 4) * 512 + (m % 4) * 128
            S.op("vector", lambda e, m=m, c=c, pt=pt, tcol=tcol: e.tensor_copy(
                ctx.rB[:, c * T + tcol:c * T + tcol + 128], pt[:, 0:128]),
                tag="y_trev")

    in_tok5 = S.tail("main")
    last2 = [None, None]
    ld2 = [None, None]
    psum_free5 = [None, None, None, None]
    for o in range(KD):
        s = o % 2
        wsl = ctx.wreg[:, s * 2048:s * 2048 + 2048]
        w8 = (in_tok5,) if o < 2 else (last2[s],)
        ld2[s] = S.dma("gpsimd", lambda e, o=o, wsl=wsl: e.dma_start(
            out=wsl, in_=ctx.d_wo[o, :, :]), waits=w8,
            stream="qv%d" % s, sem="qvs%d" % s, tag="wld_wo")
        for tt in range(2):
            sl = s * 2 + tt
            ps = ctx.pA[:, sl * 512:sl * 512 + 512]
            for kc in range(KD):
                w0 = (ld2[s], psum_free5[sl],
                      in_tok5 if o < 2 else None) if kc == 0 else ()
                tok = S.op("tensor", lambda e, kc=kc, wsl=wsl, ps=ps, tt=tt: e.matmul(
                    ps, wsl[:, kc * 128:(kc + 1) * 128],
                    ctx.rB[:, kc * T + tt * 512:kc * T + (tt + 1) * 512],
                    start=(kc == 0), stop=(kc == KD - 1)),
                    waits=w0, stream="mm", tag="wo_mm")
            xt = x_tile(ctx, o, (tt * 512, (tt + 1) * 512))
            psum_free5[sl] = S.op("vector", lambda e, ps=ps, xt=xt: e.tensor_tensor(
                xt, xt, ps, ALU.add), waits=(tok,), tag="wo_res")
        last2[s] = tok
    ctx.wreg_free = [last2[0], last2[1]]


def _decl_l1_attn(nc, ctx):
    ctx.d_wqkv = nc.dram_tensor("l1a_wqkv", [24, 128, 2048], F32, kind="ExternalInput").ap()
    ctx.d_wproj = nc.dram_tensor("l1a_wproj", [16, 128, 2048], F32, kind="ExternalInput").ap()


def emit_l1_attn(ctx):
    S = ctx.sched
    wreg_free = getattr(ctx, "wreg_free", [None, None])

    su_row = emit_ones_row(ctx, 0, 2048)
    emit_ln(ctx, su_row, ctx.rA, 0, tw=T)
    ztiles = [ctx.rA[:, c * T:(c + 1) * T] for c in range(KD)]

    def qkv_tile(o):
        reg = ctx.rB if o < 16 else ctx.rC
        return reg[:, (o % 16) * T:(o % 16) * T + T]

    in_tok = S.tail("main")
    last_use = [wreg_free[0], wreg_free[0]]
    ld_tok = [None, None]
    psum_free = [None, None, None, None]
    for o in range(24):
        s = o % 2
        wsl = ctx.wreg[:, s * 2048:s * 2048 + 2048]
        ld_tok[s] = S.dma("gpsimd", lambda e, o=o, wsl=wsl: e.dma_start(
            out=wsl, in_=ctx.d_wqkv[o, :, :]), waits=(last_use[s],),
            stream="qk%d" % s, sem="qks%d" % s, tag="wld_qkv")
        for tt in range(2):
            sl = s * 2 + tt
            ps = ctx.pA[:, sl * 512:sl * 512 + 512]
            for k in range(KD):
                w0 = (ld_tok[s], psum_free[sl],
                      in_tok if o < 2 else None) if k == 0 else ()
                tok = S.op("tensor", lambda e, k=k, wsl=wsl, ps=ps, tt=tt: e.matmul(
                    ps, wsl[:, k * 128:(k + 1) * 128],
                    ztiles[k][:, tt * 512:(tt + 1) * 512],
                    start=(k == 0), stop=(k == KD - 1)),
                    waits=w0, stream="mm", tag="qkv_mm")
            psum_free[sl] = S.op("scalar", lambda e, o=o, tt=tt, ps=ps: e.copy(
                qkv_tile(o)[:, tt * 512:(tt + 1) * 512], ps),
                waits=(tok,), nosync=True, tag="qkv_ev")
        last_use[s] = tok
    S.op("vector", lambda e: e.nop(),
         waits=tuple(t for t in psum_free if t is not None), tag="qkv_end")

    dv = ctx.trans[:, 2048:2560]
    prod = ctx.trans[:, 2560:3072]
    rDf = ctx.rD[:, :].bitcast(F32)
    s0_sb = rDf[0:1, 4096:6144]
    sd_sb = rDf[0:1, 6144:8192]
    p1row = ctx.rD[0:1, 8192:10240]
    sc = 1.0 / np.sqrt(128.0)
    for g in range(4):
        kt = qkv_tile(g * 6 + 4)
        vt = qkv_tile(g * 6 + 5)
        S.op("vector", lambda e, vt=vt: e.tensor_tensor(
            dv, vt[:, 512:1024], vt[:, 0:512], ALU.subtract), tag="dv")
        for qi in range(4):
            qt = qkv_tile(g * 6 + qi)
            S.op("vector", lambda e, qt=qt, kt=kt: e.tensor_tensor(
                prod, qt[:, 512:1024], kt[:, 0:512], ALU.mult), tag="p0")
            S.op("tensor", lambda e, qi=qi: e.matmul(
                ctx.pA[0:1, qi * 512:qi * 512 + 512], ctx.onecol[:, :], prod,
                start=True, stop=True), tag="s0mm")
            S.op("vector", lambda e, qt=qt, kt=kt: e.tensor_tensor(
                prod, qt[:, 512:1024], kt[:, 512:1024], ALU.mult), tag="p1")
            S.op("tensor", lambda e, qi=qi: e.matmul(
                ctx.pA[64:65, qi * 512:qi * 512 + 512], ctx.onecol[:, :], prod,
                start=True, stop=True), tag="s1mm")
        S.op("vector", lambda e: e.tensor_scalar(
            s0_sb, ctx.pA[0:1, 0:2048], sc, None, ALU.mult), tag="s0ev")
        S.op("vector", lambda e: e.scalar_tensor_tensor(
            sd_sb, ctx.pA[64:65, 0:2048], sc, s0_sb,
            ALU.mult, ALU.subtract), tag="sdiff")
        S.op("scalar", lambda e: e.activation(
            p1row, sd_sb, AF.Sigmoid), tag="sig")
        for qi in range(4):
            h = g * 4 + qi
            pbc = ctx.pB[:, qi * 512:qi * 512 + 512]
            S.op("tensor", lambda e, qi=qi, pbc=pbc: e.matmul(
                pbc, ctx.onerow[:, :], p1row[0:1, qi * 512:qi * 512 + 512],
                start=True, stop=True), tag="pbc")
            S.op("vector", lambda e, pbc=pbc: e.tensor_tensor(
                prod, pbc, dv, ALU.mult), tag="odv")
            ot = ctx.rD[:, h * 512:(h + 1) * 512]
            S.op("vector", lambda e, ot=ot, vt=vt: e.tensor_tensor(
                ot, prod, vt[:, 0:512], ALU.add), tag="oadd")

    in_tok2 = S.tail("main")
    last2 = [last_use[0], last_use[1]]
    ld2 = [None, None]
    psum_free2 = [None, None, None, None]
    for o in range(KD):
        s = o % 2
        wsl = ctx.wreg[:, s * 2048:s * 2048 + 2048]
        ld2[s] = S.dma("gpsimd", lambda e, o=o, wsl=wsl: e.dma_start(
            out=wsl, in_=ctx.d_wproj[o, :, :]), waits=(last2[s],),
            stream="qk%d" % s, sem="qks%d" % s, tag="wld_prj")
        for tt in range(2):
            sl = s * 2 + tt
            ps = ctx.pA[:, sl * 512:sl * 512 + 512]
            for hc in range(KD):
                g = hc // 4
                if tt == 0:
                    rhs = qkv_tile(g * 6 + 5)[:, 0:512]
                else:
                    rhs = ctx.rD[:, hc * 512:(hc + 1) * 512]
                w0 = (ld2[s], psum_free2[sl],
                      in_tok2 if o < 2 else None) if hc == 0 else ()
                tok = S.op("tensor", lambda e, hc=hc, wsl=wsl, ps=ps, rhs=rhs: e.matmul(
                    ps, wsl[:, hc * 128:(hc + 1) * 128], rhs,
                    start=(hc == 0), stop=(hc == KD - 1)),
                    waits=w0, stream="mm", tag="prj_mm")
            xt = x_tile(ctx, o, (tt * 512, (tt + 1) * 512))
            psum_free2[sl] = S.op("vector", lambda e, ps=ps, xt=xt: e.tensor_tensor(
                xt, xt, ps, ALU.add), waits=(tok,), tag="prj_res")
        last2[s] = tok
    ctx.wreg_free = [last2[0], last2[1]]


def _decl_l1_ffn(nc, ctx):
    ctx.d_wg = nc.dram_tensor("l1f_wg", [16, 128, 8], F32, kind="ExternalInput").ap()
    ctx.d_wgu = nc.dram_tensor("l1f_wgu", [8, 16, 128, 2048], F32, kind="ExternalInput").ap()
    ctx.d_wdn = nc.dram_tensor("l1f_wdn", [8, 16, 128, 1024], F32, kind="ExternalInput").ap()
    ctx.d_wshgu = nc.dram_tensor("l1f_wshgu", [64, 128, 2048], F32, kind="ExternalInput").ap()
    ctx.d_wshdn = nc.dram_tensor("l1f_wshdn", [16, 128, 4096], F32, kind="ExternalInput").ap()


L1FFN_PARTS = ("moe", "shared")


def emit_l1_ffn(ctx):
    S = ctx.sched
    wreg_free = getattr(ctx, "wreg_free", [None, None])

    su_row = emit_ones_row(ctx, 0, 2048)
    emit_ln(ctx, su_row, ctx.rA, 0, tw=T)
    ztiles = [ctx.rA[:, c * T:(c + 1) * T] for c in range(KD)]

    do_moe = "moe" in L1FFN_PARTS
    do_shared = "shared" in L1FFN_PARTS
    wg_sb = ctx.trans[:, 2048:2176]
    S.dma("gpsimd", lambda e: e.dma_start(
        out=wg_sb.rearrange("p (k u) -> p k u", u=8),
        in_=ctx.d_wg.rearrange("k p u -> p k u")), tag="wld_rt")
    compfm = ctx.trans[0:8, 4224:5248]
    comb_tm = ctx.trans[:, 2176:2184]
    pBb = ctx.pB[:, 512:1024].bitcast(BF16)
    m1 = ctx.sm2[:, 208:209]
    m2 = ctx.sm2[:, 209:210]
    nm1 = ctx.sm2[:, 210:211]
    ds = ctx.sm2[:, 211:212]
    tmp8 = ctx.sm2[:, 212:220]
    l2 = ctx.sm2[:, 220:228]
    g8 = ctx.sm2[:, 228:236]
    sel = ctx.sm2[:, 236:244]
    rc = ctx.sm2[:, 244:245]
    for m in range(8 if do_moe else 0):
        lps = ctx.pA[:, m * 8:(m + 1) * 8]
        for k in range(KD):
            S.op("tensor", lambda e, m=m, k=k, lps=lps: e.matmul(
                lps, ztiles[k][:, m * 128:(m + 1) * 128], wg_sb[:, k * 8:(k + 1) * 8],
                start=(k == 0), stop=(k == KD - 1)), tag="rt_mm")
        S.op("vector", lambda e, lps=lps: e.reduce_max(m1, lps, axis=AX.X))
        S.op("vector", lambda e, lps=lps: e.tensor_scalar(tmp8, lps, m1, None, ALU.is_ge))
        S.op("vector", lambda e, lps=lps: e.scalar_tensor_tensor(
            l2, tmp8, -1e30, lps, ALU.mult, ALU.add))
        S.op("vector", lambda e: e.reduce_max(m2, l2, axis=AX.X))
        S.op("vector", lambda e: e.tensor_scalar(nm1, m1, -1.0, None, ALU.mult))
        S.op("scalar", lambda e, lps=lps: e.activation(g8, lps, AF.Exp, bias=nm1))
        S.op("vector", lambda e, lps=lps: e.tensor_scalar(sel, lps, m2, None, ALU.is_ge))
        S.op("vector", lambda e: e.tensor_tensor(g8, g8, sel, ALU.mult))
        S.op("vector", lambda e: e.reduce_sum(ds, g8, axis=AX.X))
        S.op("vector", lambda e: e.reciprocal(rc, ds))
        S.op("vector", lambda e: e.tensor_scalar(comb_tm, g8, rc, None, ALU.mult))
        S.op("tensor", lambda e, m=m: e.transpose(
            pBb[0:8, m * 128:(m + 1) * 128], comb_tm, ctx.identb[:, :]), tag="rt_tr")
        S.op("vector", lambda e, m=m: e.tensor_copy(
            compfm[:, m * 128:(m + 1) * 128], pBb[0:8, m * 128:(m + 1) * 128]))

    cbc_sb = ctx.trans[:, 2176:2688]
    silu_t = ctx.trans[:, 2688:3200]
    t1 = ctx.trans[:, 3200:3712]
    ce_row = ctx.trans[0:1, 5248:5760]

    def act_tile(e, k):
        reg = ctx.rB if e < 4 else ctx.rC
        return reg[:, (e % 4) * 4096 + k * 512:(e % 4) * 4096 + (k + 1) * 512]

    last_gu = [wreg_free[0], wreg_free[0]]
    last_dn = [None, wreg_free[1]]
    ld_tok = [None, None]
    for tp in range(2 if do_moe else 0):
        tb = tp * 512
        in_tok = S.tail("main")
        psum_free_gu = [None, None]
        cbc_last = None
        for e in range(8):
            ce_tok = S.dma("sync", lambda e_, e=e, tb=tb: e_.dma_start(
                out=ce_row, in_=compfm[e:e + 1, tb:tb + 512]),
                waits=(cbc_last,), tag="ce_dma")
            pcb = ctx.pB[:, 1536:2048]
            cbc_mm = S.op("tensor", lambda e_, pcb=pcb: e_.matmul(
                pcb, ctx.onerow[:, :], ce_row, start=True, stop=True),
                waits=(ce_tok, cbc_last), stream="mm", tag="cbc_mm")
            cbc_ev = S.op("vector", lambda e_, pcb=pcb: e_.tensor_copy(cbc_sb, pcb),
                          waits=(cbc_mm,), tag="cbc_ev")
            for j in range(8):
                s = (e * 8 + j) % 2
                wga = ctx.wreg[:, s * 4096:s * 4096 + 2048]
                wua = ctx.wreg[:, s * 4096 + 2048:s * 4096 + 4096]
                S.dma("gpsimd", lambda e_, e=e, j=j, wga=wga: e_.dma_start(
                    out=wga, in_=ctx.d_wgu[e, j, :, :]), waits=(last_gu[s],),
                    stream="gu%d" % s, sem="gus%d" % s, tag="wld_g")
                ld_tok[s] = S.dma("gpsimd", lambda e_, e=e, j=j, wua=wua: e_.dma_start(
                    out=wua, in_=ctx.d_wgu[e, j + 8, :, :]),
                    stream="gu%d" % s, sem="gus%d" % s, tag="wld_u")
                pg = ctx.pA[:, s * 1024:s * 1024 + 512]
                pu = ctx.pA[:, s * 1024 + 512:s * 1024 + 1024]
                for k in range(KD):
                    w0 = (ld_tok[s], psum_free_gu[s],
                          in_tok if (e == 0 and j < 2) else None) if k == 0 else ()
                    S.op("tensor", lambda e_, k=k, wga=wga, pg=pg, tb=tb: e_.matmul(
                        pg, wga[:, k * 128:(k + 1) * 128],
                        ztiles[k][:, tb:tb + 512],
                        start=(k == 0), stop=(k == KD - 1)),
                        waits=w0, stream="mm", tag="gu_g_mm")
                for k in range(KD):
                    tok = S.op("tensor", lambda e_, k=k, wua=wua, pu=pu, tb=tb: e_.matmul(
                        pu, wua[:, k * 128:(k + 1) * 128],
                        ztiles[k][:, tb:tb + 512],
                        start=(k == 0), stop=(k == KD - 1)), stream="mm", tag="gu_u_mm")
                last_gu[s] = tok
                S.op("scalar", lambda e_, pg=pg: e_.activation(
                    silu_t, pg, AF.Silu), waits=(tok,), tag="moe_silu")
                S.op("vector", lambda e_, pu=pu: e_.tensor_tensor(
                    t1, pu, cbc_sb, ALU.mult), waits=(tok, cbc_ev), tag="moe_t1")
                psum_free_gu[s] = S.op("vector", lambda e_, e=e, j=j: e_.tensor_tensor(
                    act_tile(e, j), silu_t, t1, ALU.mult), tag="moe_act")
            cbc_last = psum_free_gu[(e * 8 + 7) % 2]
        in_tok2 = S.tail("main")
        psum_free_dn = [None, None]
        for o in range(KD):
            s = o % 2
            if o == 0:
                w8 = (last_gu[0], last_gu[1]) if last_dn[0] is None else (last_dn[0], last_gu[0], last_gu[1])
            elif o == 1:
                w8 = (last_dn[1],)
            else:
                w8 = (last_dn[s],)
            ldd = None
            for e in range(8):
                wdn = ctx.wreg[:, s * 8192 + e * 1024:s * 8192 + (e + 1) * 1024]
                ldd = S.dma("gpsimd", lambda e_, e=e, o=o, wdn=wdn: e_.dma_start(
                    out=wdn, in_=ctx.d_wdn[e, o, :, :]),
                    waits=w8 if e == 0 else (),
                    stream="dn%d" % s, sem="dns%d" % s, tag="wld_dn")
            pd = ctx.pB[:, s * 512:s * 512 + 512]
            for e in range(8):
                wdn = ctx.wreg[:, s * 8192 + e * 1024:s * 8192 + (e + 1) * 1024]
                for k in range(8):
                    w0 = (ldd, psum_free_dn[s],
                          in_tok2 if o < 2 else None) if (e == 0 and k == 0) else ()
                    tok = S.op("tensor", lambda e_, e=e, k=k, wdn=wdn, pd=pd: e_.matmul(
                        pd, wdn[:, k * 128:(k + 1) * 128], act_tile(e, k),
                        start=(e == 0 and k == 0), stop=(e == 7 and k == 7)),
                        waits=w0, stream="mm", tag="dn_mm")
            last_dn[s] = tok
            xt = x_tile(ctx, o, (tb, tb + 512))
            psum_free_dn[s] = S.op("vector", lambda e_, pd=pd, xt=xt: e_.tensor_tensor(
                xt, xt, pd, ALU.add), waits=(tok,), tag="dn_res")

    def sh_tile(o):
        reg = ctx.rB if o < 16 else ctx.rC
        return reg[:, (o % 16) * T:(o % 16) * T + T]

    if not do_moe:
        last_dn = [wreg_free[0], wreg_free[1]]
    in_tok3 = S.tail("main")
    last_sh = [last_dn[0], last_dn[0]]
    ld_sh = [None, None]
    psum_free_sh = [None, None, None, None]
    for o in range(32 if do_shared else 0):
        s = o % 2
        wga = ctx.wreg[:, s * 4096:s * 4096 + 2048]
        wua = ctx.wreg[:, s * 4096 + 2048:s * 4096 + 4096]
        S.dma("gpsimd", lambda e_, o=o, wga=wga: e_.dma_start(
            out=wga, in_=ctx.d_wshgu[o, :, :]), waits=(last_sh[s],),
            stream="sg%d" % s, sem="sgs%d" % s, tag="wld_shg")
        ld_sh[s] = S.dma("gpsimd", lambda e_, o=o, wua=wua: e_.dma_start(
            out=wua, in_=ctx.d_wshgu[o + 32, :, :]),
            stream="sg%d" % s, sem="sgs%d" % s, tag="wld_shu")
        for tt in range(2):
            sl = s * 2 + tt
            pg = ctx.pA[:, s * 1024 + tt * 512:s * 1024 + (tt + 1) * 512]
            pu = ctx.pB[:, s * 1024 + tt * 512:s * 1024 + (tt + 1) * 512]
            for k in range(KD):
                w0 = (ld_sh[s], psum_free_sh[sl],
                      in_tok3 if o < 2 else None) if k == 0 else ()
                S.op("tensor", lambda e_, k=k, wga=wga, pg=pg, tt=tt: e_.matmul(
                    pg, wga[:, k * 128:(k + 1) * 128],
                    ztiles[k][:, tt * 512:(tt + 1) * 512],
                    start=(k == 0), stop=(k == KD - 1)),
                    waits=w0, stream="mm", tag="sh_g_mm")
            for k in range(KD):
                tok = S.op("tensor", lambda e_, k=k, wua=wua, pu=pu, tt=tt: e_.matmul(
                    pu, wua[:, k * 128:(k + 1) * 128],
                    ztiles[k][:, tt * 512:(tt + 1) * 512],
                    start=(k == 0), stop=(k == KD - 1)), stream="mm", tag="sh_u_mm")
            last_sh[s] = tok
            S.op("scalar", lambda e_, pg=pg: e_.activation(
                silu_t, pg, AF.Silu), waits=(tok,), tag="sh_silu")
            psum_free_sh[sl] = S.op("vector", lambda e_, o=o, tt=tt, pu=pu: e_.tensor_tensor(
                sh_tile(o)[:, tt * 512:(tt + 1) * 512], silu_t, pu, ALU.mult),
                tag="sh_act")
    in_tok4 = S.tail("main")
    last_sd = [None, None]
    psum_free_sd = [None, None, None, None]
    for o in range(KD if do_shared else 0):
        s = o % 2
        wsd = ctx.wreg[:, s * 8192:s * 8192 + 4096]
        if o == 0:
            w8 = (last_sh[0], last_sh[1])
        elif o == 1:
            w8 = (last_dn[1],) if last_dn[1] is not None else ()
        else:
            w8 = (last_sd[s],)
        ldd = S.dma("gpsimd", lambda e_, o=o, wsd=wsd: e_.dma_start(
            out=wsd, in_=ctx.d_wshdn[o, :, :]), waits=w8,
            stream="sd%d" % s, sem="sds%d" % s, tag="wld_shd")
        for tt in range(2):
            sl = s * 2 + tt
            pd = ctx.pA[:, (s * 2 + tt) * 512:(s * 2 + tt) * 512 + 512]
            for k in range(32):
                w0 = (ldd, psum_free_sd[sl],
                      in_tok4 if o < 2 else None) if k == 0 else ()
                tok = S.op("tensor", lambda e_, k=k, wsd=wsd, pd=pd, tt=tt: e_.matmul(
                    pd, wsd[:, k * 128:(k + 1) * 128],
                    sh_tile(k)[:, tt * 512:(tt + 1) * 512],
                    start=(k == 0), stop=(k == 31)),
                    waits=w0, stream="mm", tag="shd_mm")
            xt = x_tile(ctx, o, (tt * 512, (tt + 1) * 512))
            psum_free_sd[sl] = S.op("vector", lambda e_, pd=pd, xt=xt: e_.tensor_tensor(
                xt, xt, pd, ALU.add), waits=(tok,), tag="shd_res")
        last_sd[s] = tok
    if do_shared:
        ctx.wreg_free = [last_sd[0], last_sd[1]]
    else:
        ctx.wreg_free = [last_dn[0], last_dn[1]]


def _prep_consts():
    h128 = make_hadamard(128).astype(np.float32)
    ident = np.eye(128, dtype=np.float32)
    mask = (np.arange(128)[:, None] <= np.arange(512)[None, :]).astype(np.float32)
    return {"c_h128": h128, "c_ident": ident, "c_mask": mask}


def _prep_pmask(core):
    pm = np.zeros((128, 8), dtype=np.float32)
    pm[:, :core] = 1.0
    return pm


def _prep_l0_attn(inp):
    return {
        "l0a_wq": tile_fm(np.ascontiguousarray(np.asarray(inp["L0_attn_Wq"]).T), D, D),
        "l0a_wk": tile_fm(np.ascontiguousarray(np.asarray(inp["L0_attn_Wk"]).T), D, D),
        "l0a_wv": tile_fm(np.ascontiguousarray(np.asarray(inp["L0_attn_Wv"]).T), D, D),
        "l0a_wo": tile_fm(np.ascontiguousarray(np.asarray(inp["L0_attn_Wo"]).T), D, D),
        "l0a_wfq": np.asarray(inp["L0_attn_Wfq"]),
        "l0a_wfk": np.asarray(inp["L0_attn_Wfk"]),
    }


def _prep_l1_attn(inp):
    wqkv = np.asarray(inp["L1_attn_weight_qkv"])
    wproj = np.asarray(inp["L1_attn_weight_proj"])
    return {
        "l1a_wqkv": tile_fm(np.ascontiguousarray(wqkv.T), D, 3072),
        "l1a_wproj": tile_fm(np.ascontiguousarray(wproj.T), D, D),
    }


def _prep_l1_ffn(inp):
    wg = np.asarray(inp["L1_ffn_w_gate"])
    wgu = np.asarray(inp["L1_ffn_w_gate_up"])
    wdn = np.asarray(inp["L1_ffn_w_down"])
    wshgu = np.asarray(inp["L1_ffn_w_sh_gate_up"])
    wshdn = np.asarray(inp["L1_ffn_w_sh_down"])
    return {
        "l1f_wg": np.ascontiguousarray(wg.reshape(16, 128, 8)),
        "l1f_wgu": np.stack([tile_fm(wgu[e], D, D) for e in range(8)]),
        "l1f_wdn": np.stack([tile_fm(wdn[e], 1024, D) for e in range(8)]),
        "l1f_wshgu": tile_fm(wshgu, D, 8192),
        "l1f_wshdn": tile_fm(wshdn, 4096, D),
    }


def _prep_l0_ffn(inp):
    up = np.asarray(inp["L0_ffn_W_up"])
    gate = np.asarray(inp["L0_ffn_W_gate"])
    down = np.asarray(inp["L0_ffn_W_down"])
    wsc = np.asarray(inp["L0_ffn_Wscale_ug"])
    return {
        "l0f_wup": tile_fm(np.ascontiguousarray(up.T), D, 8192),
        "l0f_wgate": tile_fm(np.ascontiguousarray(gate.T), D, 8192),
        "l0f_wdown": tile_fm(np.ascontiguousarray(down.T), 8192, D),
        "l0f_suug": np.asarray(inp["L0_ffn_SU_ug"]).reshape(1, D),
        "l0f_sudp": pmajor(np.asarray(inp["L0_ffn_SU_dp"])),
        "l0f_wsu_up": pmajor(wsc[:8192]),
        "l0f_wsu_g": pmajor(wsc[8192:]),
        "l0f_wsdp": pmajor(np.asarray(inp["L0_ffn_Wscale_dp"])),
    }


DBG_ACC = False


def build_program(blocks):
    nc = bass.Bass("TRN2", target_bir_lowering=False)
    es = contextlib.ExitStack()
    ctx = Ctx()
    with es:
        _alloc_common(nc, es, ctx)
        ctx.dbg_acc = DBG_ACC
        if "l0_attn" in blocks:
            _decl_l0_attn(nc, ctx)
        if "l0_ffn" in blocks:
            _decl_l0_ffn(nc, ctx)
        if "l1_attn" in blocks:
            _decl_l1_attn(nc, ctx)
        if "l1_ffn" in blocks:
            _decl_l1_ffn(nc, ctx)
        for b in blocks:
            if b == "l0_attn":
                emit_l0_attn(ctx)
            elif b == "l0_ffn":
                emit_l0_ffn(ctx)
            elif b == "l1_attn":
                emit_l1_attn(ctx)
            elif b == "l1_ffn":
                emit_l1_ffn(ctx)
        _emit_store_out(ctx)
        block = es.enter_context(nc.Block())
        ctx.sched.build(nc, block, es)
    return nc


def shard_x(x):
    shards = []
    for c in range(NCORES):
        xs = np.concatenate([x[0, c * TB:(c + 1) * TB, :],
                             x[1, c * TB:(c + 1) * TB, :]], axis=0)
        shards.append(np.ascontiguousarray(xs.T))
    return shards


def unshard_x(outs):
    x = np.empty((2, NCORES * TB, D), dtype=np.float32)
    for c in range(NCORES):
        o = np.asarray(outs[c])
        x[0, c * TB:(c + 1) * TB, :] = o[:, 0:TB].T
        x[1, c * TB:(c + 1) * TB, :] = o[:, TB:2 * TB].T
    return x


KERNEL_BLOCKS = BLOCKS_ALL


def kernel(**inputs):
    x = np.asarray(inputs["L0_attn_hidden_states"])
    consts = _prep_consts()
    w = {}
    if "l0_attn" in KERNEL_BLOCKS:
        w.update(_prep_l0_attn(inputs))
    if "l0_ffn" in KERNEL_BLOCKS:
        w.update(_prep_l0_ffn(inputs))
    if "l1_attn" in KERNEL_BLOCKS:
        w.update(_prep_l1_attn(inputs))
    if "l1_ffn" in KERNEL_BLOCKS:
        w.update(_prep_l1_ffn(inputs))
    nc = build_program(KERNEL_BLOCKS)
    shards = shard_x(x)
    in_maps = []
    for c in range(NCORES):
        m = {"x_fm": shards[c], "c_pmask": _prep_pmask(c)}
        m.update(consts)
        m.update(w)
        in_maps.append(m)
    res = run_bass_kernel_spmd(nc, in_maps, core_ids=list(range(NCORES)))
    global LAST_RESULTS
    LAST_RESULTS = res
    return unshard_x([r["out_fm"] for r in res.results])


LAST_RESULTS = None


# version 07
# speedup vs baseline: 1.1500x; 1.0763x over previous
import contextlib
import numpy as np
import ml_dtypes
import concourse.bass as bass
import concourse.mybir as mybir
from concourse.bass_utils import run_bass_kernel_spmd

F32 = mybir.dt.float32
BF16 = mybir.dt.bfloat16
AF = mybir.ActivationFunctionType
ALU = mybir.AluOpType
AX = mybir.AxisListType

NCORES = 8
D = 2048
KD = D // 128
T = 1024
TB = 512
BF = ml_dtypes.bfloat16

BLOCKS_ALL = ("l0_attn", "l0_ffn", "l1_attn", "l1_ffn")


class Sched:

    def __init__(self):
        self.ops = []
        self.force = set()
        self._tail = {}

    def tail(self, stream="main"):
        return self._tail.get(stream)

    def op(self, eng, fn, waits=(), stream="main", sem=None, inc=1, tag="",
           nosync=False):
        tok = len(self.ops)
        for w in waits:
            if w is not None:
                self.force.add(w)
        self.ops.append(dict(eng=eng, fn=fn, inc=inc, sem=sem or eng,
                             stream=stream, waits=[w for w in waits if w is not None],
                             tag=tag, nosync=nosync))
        self._tail[stream] = tok
        return tok

    def dma(self, eng, fn, waits=(), stream="main", sem=None, tag="", nosync=False):
        return self.op(eng, fn, waits=waits, stream=stream,
                       sem=sem or (eng + "_dma"), inc=16, tag=tag, nosync=nosync)

    def build(self, nc, block, es):
        n = len(self.ops)
        prev_in_stream = {}
        chain_wait = [None] * n
        for i, o in enumerate(self.ops):
            p = prev_in_stream.get(o["stream"])
            if p is not None and not o.get("nosync"):
                po = self.ops[p]
                if po["eng"] != o["eng"]:
                    chain_wait[i] = p
                elif (o["inc"] != 16 and po["inc"] != 16
                      and o["eng"] in ("vector", "scalar", "gpsimd")):
                    chain_wait[i] = p
            prev_in_stream[o["stream"]] = i
        last_of_stream = set(prev_in_stream.values())
        do_inc = [False] * n
        for i, o in enumerate(self.ops):
            if o["inc"] == 16 or i in self.force or i in last_of_stream:
                do_inc[i] = True
        for i in range(n):
            if chain_wait[i] is not None:
                do_inc[chain_wait[i]] = True
        sem_names = sorted({o["sem"] for i, o in enumerate(self.ops) if do_inc[i]})
        sems = {s: es.enter_context(nc.semaphore("sem_" + s)) for s in sem_names}
        cum = {s: 0 for s in sem_names}
        val_after = [0] * n
        for i, o in enumerate(self.ops):
            if do_inc[i]:
                cum[o["sem"]] += o["inc"]
                val_after[i] = cum[o["sem"]]
        resolved = [[] for _ in range(n)]
        for i, o in enumerate(self.ops):
            tgts = list(o["waits"])
            if chain_wait[i] is not None:
                tgts.append(chain_wait[i])
            seen = {}
            for tkn in tgts:
                assert do_inc[tkn], (i, tkn, self.ops[tkn])
                s = self.ops[tkn]["sem"]
                seen[s] = max(seen.get(s, 0), val_after[tkn])
            resolved[i] = [(sems[s], v) for s, v in sorted(seen.items())]

        def make_body(label):
            def body(eng):
                for i, o in enumerate(self.ops):
                    if o["eng"] != label:
                        continue
                    for s, v in resolved[i]:
                        eng.wait_ge(s, v)
                    inst = o["fn"](eng)
                    if do_inc[i]:
                        inst.then_inc(sems[o["sem"]], o["inc"])
            return body

        used = {o["eng"] for o in self.ops}
        for label, attr in [("tensor", block.tensor), ("vector", block.vector),
                            ("scalar", block.scalar), ("gpsimd", block.gpsimd),
                            ("sync", block.sync)]:
            if label in used:
                attr(make_body(label))


def tile_fm(wt, kdim, odim):
    kch, och = kdim // 128, odim // 128
    return np.ascontiguousarray(
        wt.reshape(kch, 128, och, 128).transpose(2, 1, 0, 3).reshape(och, 128, kch * 128))


def pmajor(v):
    n = v.shape[0] // 128
    return np.ascontiguousarray(v.reshape(n, 128).T)


def make_hadamard(n):
    h = np.array([[1.0]], dtype=np.float32)
    while h.shape[0] < n:
        h = np.block([[h, h], [h, -h]])
    return h


class Ctx:
    pass


def _alloc_common(nc, es, ctx):
    S = ctx.sched = Sched()
    ctx.nc = nc
    ctx.es = es

    ctx.d_x = nc.dram_tensor("x_fm", [D, T], F32, kind="ExternalInput").ap()
    ctx.d_out = nc.dram_tensor("out_fm", [D, T], F32, kind="ExternalOutput").ap()
    ctx.d_h128 = nc.dram_tensor("c_h128", [128, 128], F32, kind="ExternalInput").ap()
    ctx.d_h128n = nc.dram_tensor("c_h128n", [128, 128], F32, kind="ExternalInput").ap()
    ctx.d_ident = nc.dram_tensor("c_ident", [128, 128], F32, kind="ExternalInput").ap()
    ctx.d_mask = nc.dram_tensor("c_mask", [128, 512], F32, kind="ExternalInput").ap()
    ctx.d_pmask = nc.dram_tensor("c_pmask", [128, 8], F32, kind="ExternalInput").ap()
    ctx.d_dbg = nc.dram_tensor("dbg", [128, 8192], F32, kind="ExternalOutput").ap()
    ctx.dbg_off = [0]

    ctx.x = es.enter_context(nc.sbuf_tensor("x", [128, KD * T], BF16))
    ctx.rA = es.enter_context(nc.sbuf_tensor("rA", [128, 16384], BF16))
    ctx.rB = es.enter_context(nc.sbuf_tensor("rB", [128, 16384], BF16))
    ctx.rC = es.enter_context(nc.sbuf_tensor("rC", [128, 16384], BF16))
    ctx.rD = es.enter_context(nc.sbuf_tensor("rD", [128, 16384], BF16))
    ctx.wreg = es.enter_context(nc.sbuf_tensor("wreg", [128, 16384], BF16))
    ctx.trans = es.enter_context(nc.sbuf_tensor("trans", [128, 6656], BF16))
    ctx.dsb = ctx.wreg[:, 4096:8224]
    ctx.accf = ctx.wreg[:, 4096:8224].bitcast(F32)
    ctx.dbuf = ctx.wreg[:, 8224:10288]
    ctx.acc = ctx.wreg[:, 10288:14416]
    ctx.mask = ctx.wreg[:, 14416:14928]
    ctx.sm1 = es.enter_context(nc.sbuf_tensor("sm1", [128, 192], F32))
    ctx.sm2 = es.enter_context(nc.sbuf_tensor("sm2", [128, 256], F32))
    ctx.h128 = es.enter_context(nc.sbuf_tensor("h128", [128, 128], BF16))
    ctx.h128n = es.enter_context(nc.sbuf_tensor("h128n", [128, 128], BF16))
    ctx.identb = es.enter_context(nc.sbuf_tensor("identb", [128, 128], BF16))
    ctx.onecol = es.enter_context(nc.sbuf_tensor("onecol", [128, 1], BF16))
    ctx.onerow = es.enter_context(nc.sbuf_tensor("onerow", [1, 128], BF16))
    ctx.pmask = es.enter_context(nc.sbuf_tensor("pmask", [128, 8], F32))

    ctx.pA = es.enter_context(nc.psum_tensor("pA", [128, 2048], F32))
    ctx.pB = es.enter_context(nc.psum_tensor("pB", [128, 2048], F32))

    ctx.transf = ctx.trans[:, :].bitcast(F32)

    S.dma("gpsimd", lambda e: e.dma_start(out=ctx.h128[:, :], in_=ctx.d_h128[:, :]))
    S.dma("gpsimd", lambda e: e.dma_start(out=ctx.h128n[:, :], in_=ctx.d_h128n[:, :]))
    S.dma("gpsimd", lambda e: e.dma_start(out=ctx.identb[:, :], in_=ctx.d_ident[:, :]))
    S.dma("sync", lambda e: e.dma_start(out=ctx.pmask[:, :], in_=ctx.d_pmask[:, :]))
    S.op("vector", lambda e: e.memset(ctx.onecol[:, :], 1.0))
    S.op("vector", lambda e: e.memset(ctx.onerow[:, :], 1.0))
    for c in range(KD):
        S.dma("gpsimd", lambda e, c=c: e.dma_start(
            out=ctx.x[:, c * T:(c + 1) * T], in_=ctx.d_x[c * 128:(c + 1) * 128, :]))


def _emit_store_out(ctx):
    S = ctx.sched
    for c in range(KD):
        S.dma("gpsimd", lambda e, c=c: e.dma_start(
            out=ctx.d_out[c * 128:(c + 1) * 128, :], in_=ctx.x[:, c * T:(c + 1) * T]))
    S.op("scalar", lambda e: e.nop(), tag="fence")


def emit_dbg(ctx, ap, rows, cols):
    off = ctx.dbg_off[0]
    ctx.sched.dma("gpsimd", lambda e: e.dma_start(
        out=ctx.d_dbg[0:rows, off:off + cols], in_=ap), tag="dbg")
    ctx.dbg_off[0] += cols


def x_tile(ctx, c, tcols=None):
    if tcols is None:
        return ctx.x[:, c * T:(c + 1) * T]
    return ctx.x[:, c * T + tcols[0]:c * T + tcols[1]]


def t_row_bf(ctx, off, w):
    return ctx.trans[0:1, off:off + w]


def t_tile_bf(ctx, off, w, p0=0, p1=128):
    return ctx.trans[p0:p1, off:off + w]


def emit_ln(ctx, su_row, out_region, out_off, tw=T, tbase=0):
    S = ctx.sched
    nt = tw // 512
    x2t = ctx.trans[:, 2048:2560]

    def xt(c, i):
        return ctx.x[:, c * T + tbase + i * 512: c * T + tbase + (i + 1) * 512]

    for i in range(nt):
        psx = ctx.pA[0:1, 1024 + i * 512:1536 + i * 512]
        psq = ctx.pB[0:1, 1024 + i * 512:1536 + i * 512]
        for c in range(KD):
            S.op("scalar", lambda e, c=c, i=i: e.activation(
                x2t, xt(c, i), AF.Square), tag="ln_sq")
            S.op("tensor", lambda e, c=c, i=i, psx=psx: e.matmul(
                psx, ctx.onecol[:, :], xt(c, i), start=(c == 0), stop=(c == KD - 1)),
                tag="ln_sx")
            S.op("tensor", lambda e, c=c, psq=psq: e.matmul(
                psq, ctx.onecol[:, :], x2t, start=(c == 0), stop=(c == KD - 1)),
                tag="ln_sx2")
    mu_ps = ctx.pA[0:1, 1024:1024 + tw]
    var_ps = ctx.pB[0:1, 1024:1024 + tw]
    musq = ctx.transf[0:1, 1280:1280 + tw]
    rstd_bf = t_row_bf(ctx, 4608, tw)
    nmr_bf = t_row_bf(ctx, 4608 + tw, tw)
    S.op("vector", lambda e: e.tensor_scalar(mu_ps, mu_ps, 1.0 / D, None, ALU.mult))
    S.op("vector", lambda e: e.tensor_scalar(var_ps, var_ps, 1.0 / D, 1e-5, ALU.mult, ALU.add))
    S.op("scalar", lambda e: e.activation(musq, mu_ps, AF.Square))
    S.op("vector", lambda e: e.tensor_tensor(var_ps, var_ps, musq, ALU.subtract))
    S.op("scalar", lambda e: e.activation(var_ps, var_ps, AF.Ln))
    S.op("vector", lambda e: e.tensor_scalar(var_ps, var_ps, -0.5, None, ALU.mult))
    S.op("scalar", lambda e: e.activation(rstd_bf, var_ps, AF.Exp))
    S.op("vector", lambda e: e.scalar_tensor_tensor(
        nmr_bf, mu_ps, -1.0, rstd_bf, ALU.mult, ALU.mult))
    for c in range(KD):
        su_c = su_row[0:1, c * 128:(c + 1) * 128]
        for i in range(nt):
            pa = ctx.pA[:, (c % 2) * 512:(c % 2) * 512 + 512]
            pb = ctx.pB[:, (c % 2) * 512:(c % 2) * 512 + 512]
            S.op("tensor", lambda e, pa=pa, su_c=su_c, i=i: e.matmul(
                pa, su_c, rstd_bf[0:1, i * 512:(i + 1) * 512], start=True, stop=True),
                tag="ln_abc")
            S.op("tensor", lambda e, pb=pb, su_c=su_c, i=i: e.matmul(
                pb, su_c, nmr_bf[0:1, i * 512:(i + 1) * 512], start=True, stop=True),
                tag="ln_bbc")
            ot = out_region[:, out_off + c * tw + i * 512: out_off + c * tw + (i + 1) * 512]
            S.op("vector", lambda e, c=c, i=i, ot=ot, pa=pa: e.tensor_tensor(
                ot, xt(c, i), pa, ALU.mult), tag="ln_za")
            S.op("vector", lambda e, ot=ot, pb=pb: e.tensor_tensor(
                ot, ot, pb, ALU.add), tag="ln_zb")


def mm_op(ctx, base, fn, **kw):
    need_nop = getattr(ctx, "_mm_base", None) is not None and ctx._mm_base != base
    ctx._mm_base = base
    if need_nop:
        fn2 = lambda e: (e.nop(), fn(e))[1]
    else:
        fn2 = fn
    return ctx.sched.op("tensor", fn2, **kw)


def emit_ones_row(ctx, off, w):
    S = ctx.sched
    row = t_row_bf(ctx, off, w)
    S.op("vector", lambda e: e.memset(row, 1.0))
    return row


def emit_fwht_combine(ctx, tiles):
    S = ctx.sched
    n = len(tiles)
    barrier = [S.tail("main"), None]
    h = 1
    while h < n:
        first = {"vector": True, "gpsimd": True}
        last = {"vector": None, "gpsimd": None}
        idx = 0
        for base in range(0, n, 2 * h):
            for j in range(h):
                a = tiles[base + j]
                b = tiles[base + j + h]
                eng = "vector" if idx % 2 == 0 else "gpsimd"
                idx += 1
                w0 = tuple(t for t in barrier if t is not None) if first[eng] else ()
                first[eng] = False
                if eng == "vector":
                    ta = S.op(eng, lambda e, a=a, b=b: e.tensor_tensor(a, a, b, ALU.add),
                              waits=w0, nosync=True, tag="fw_a")
                    last[eng] = S.op(eng, lambda e, a=a, b=b: e.scalar_tensor_tensor(
                        b, b, -2.0, a, ALU.mult, ALU.add),
                        waits=(ta,), nosync=True, tag="fw_b")
                else:
                    tmp = ctx.trans[:, 2048:2560]
                    tt = S.op(eng, lambda e, a=a, b=b, tmp=tmp: e.tensor_tensor(
                        tmp, a, b, ALU.subtract), waits=w0, nosync=True, tag="fw_s")
                    ta = S.op(eng, lambda e, a=a, b=b: e.tensor_tensor(
                        a, a, b, ALU.add), waits=(tt,), nosync=True, tag="fw_a")
                    last[eng] = S.op(eng, lambda e, b=b, tmp=tmp: e.tensor_copy(
                        b, tmp), waits=(ta,), nosync=True, tag="fw_b")
        barrier = [last["vector"], last["gpsimd"]]
        h *= 2
    S.op("vector", lambda e: e.nop(), waits=tuple(t for t in barrier if t is not None),
         tag="fw_end")


H16S = make_hadamard(16)


def emit_fwht_folded(ctx, tiles, scratch):
    S = ctx.sched
    n = len(tiles)
    ngroups = n // 16
    in_tok = S.tail("main")
    psum_free = [None, None, None, None]
    out_loc = [None] * n
    ev_last = None
    prev_sign = [None]
    for g in range(ngroups):
        for bp in range(16):
            o = g * 16 + bp
            dst = scratch[bp] if g == 0 else tiles[(g - 1) * 16 + bp]
            out_loc[o] = dst
            s4 = o % 4
            pa = ctx.pA if s4 % 2 == 0 else ctx.pB
            ps = pa[:, (s4 // 2) * 512:(s4 // 2) * 512 + 512]
            border = sorted(range(16), key=lambda b: -H16S[bp, b])
            for bi, b in enumerate(border):
                w0 = (psum_free[s4], in_tok if o < 4 else None) if bi == 0 else ()
                hmat = ctx.h128 if H16S[bp, b] > 0 else ctx.h128n
                tok = S.op("tensor", lambda e, hmat=hmat, ps=ps, bi=bi, src=tiles[g * 16 + b]: e.matmul(
                    ps, hmat[:, :], src, start=(bi == 0), stop=(bi == 15)),
                    waits=w0, stream="mm", tag="fwf_mm")
            psum_free[s4] = ev_last = S.op(
                "scalar", lambda e, dst=dst, ps=ps: e.copy(dst, ps),
                waits=(tok,), nosync=True, tag="fwf_ev")
    S.op("vector", lambda e: e.nop(), waits=(ev_last,), tag="fwf_end")
    if ngroups == 1:
        return [out_loc[b] for b in range(16)]
    barrier = [S.tail("main"), None]
    h = 1
    idx = 0
    while h < ngroups:
        first = {"vector": True, "gpsimd": True}
        last = {"vector": None, "gpsimd": None}
        for base in range(0, ngroups, 2 * h):
            for j in range(h):
                for bp in range(16):
                    a = out_loc[(base + j) * 16 + bp]
                    b2 = out_loc[(base + j + h) * 16 + bp]
                    eng = "vector" if idx % 2 == 0 else "gpsimd"
                    idx += 1
                    w0 = tuple(t for t in barrier if t is not None) if first[eng] else ()
                    first[eng] = False
                    if eng == "vector":
                        ta = S.op(eng, lambda e, a=a, b2=b2: e.tensor_tensor(
                            a, a, b2, ALU.add), waits=w0, nosync=True, tag="fw_a")
                        last[eng] = S.op(eng, lambda e, a=a, b2=b2: e.scalar_tensor_tensor(
                            b2, b2, -2.0, a, ALU.mult, ALU.add),
                            waits=(ta,), nosync=True, tag="fw_b")
                    else:
                        tmp = ctx.trans[:, 2048:2560]
                        pw = w0 + ((last[eng],) if last[eng] is not None else ())
                        tt2 = S.op(eng, lambda e, a=a, b2=b2, tmp=tmp: e.tensor_tensor(
                            tmp, a, b2, ALU.subtract), waits=pw, nosync=True, tag="fw_s")
                        ta = S.op(eng, lambda e, a=a, b2=b2: e.tensor_tensor(
                            a, a, b2, ALU.add), waits=(tt2,), nosync=True, tag="fw_a")
                        last[eng] = S.op(eng, lambda e, b2=b2, tmp=tmp: e.tensor_copy(
                            b2, tmp), waits=(ta,), nosync=True, tag="fw_b")
        barrier = [last["vector"], last["gpsimd"]]
        h *= 2
    S.op("vector", lambda e: e.nop(), waits=tuple(t for t in barrier if t is not None),
         tag="fw_end")
    return out_loc


def emit_fwht_h128(ctx, tiles, scale=None):
    S = ctx.sched
    in_tok = S.tail("main")
    psum_free = [None, None, None, None]
    evs = []
    for idx, tile in enumerate(tiles):
        tw = tile.shape[1]
        s4 = idx % 4
        pa = ctx.pA if s4 % 2 == 0 else ctx.pB
        ps = pa[:, (s4 // 2) * 512:(s4 // 2) * 512 + tw]
        tok = S.op("tensor", lambda e, tile=tile, tw=tw, ps=ps: e.matmul(
            ps, ctx.h128[:, :], tile[:, 0:tw], start=True, stop=True),
            waits=(psum_free[s4], in_tok if idx < 4 else None),
            stream="mm", tag="fw_h128")
        psum_free[s4] = S.op("scalar", lambda e, tile=tile, tw=tw, ps=ps: e.copy(
            tile[:, 0:tw], ps), waits=(tok,), nosync=True, tag="fw_ev")
    S.op("vector", lambda e: e.nop(),
         waits=tuple(t for t in psum_free if t is not None), tag="fwh_end")


def _decl_l0_ffn(nc, ctx):
    ctx.d_wup = nc.dram_tensor("l0f_wup", [64, 128, 2048], F32, kind="ExternalInput").ap()
    ctx.d_wgate = nc.dram_tensor("l0f_wgate", [64, 128, 2048], F32, kind="ExternalInput").ap()
    ctx.d_wdown = nc.dram_tensor("l0f_wdown", [16, 128, 8192], F32, kind="ExternalInput").ap()
    ctx.d_suug = nc.dram_tensor("l0f_suug", [1, 2048], F32, kind="ExternalInput").ap()
    ctx.d_sudp = nc.dram_tensor("l0f_sudp", [128, 64], F32, kind="ExternalInput").ap()
    ctx.d_wsu_up = nc.dram_tensor("l0f_wsu_up", [128, 64], F32, kind="ExternalInput").ap()
    ctx.d_wsu_g = nc.dram_tensor("l0f_wsu_g", [128, 64], F32, kind="ExternalInput").ap()
    ctx.d_wsdp = nc.dram_tensor("l0f_wsdp", [128, 16], F32, kind="ExternalInput").ap()


def emit_l0_ffn(ctx):
    S = ctx.sched

    su_row = t_row_bf(ctx, 0, 2048)
    S.dma("gpsimd", lambda e: e.dma_start(out=su_row, in_=ctx.d_suug[:, :]))
    wsg64 = ctx.sm1[:, 0:64]
    comb_up = ctx.sm1[:, 64:128]
    wdp64 = ctx.sm1[:, 128:144]
    sudp = ctx.sm2[:, 0:64]
    wsuup = ctx.sm2[:, 64:128]
    S.dma("sync", lambda e: e.dma_start(out=sudp, in_=ctx.d_sudp[:, :]))
    S.dma("sync", lambda e: e.dma_start(out=wsuup, in_=ctx.d_wsu_up[:, :]))
    S.dma("sync", lambda e: e.dma_start(out=ctx.sm2[:, 128:192], in_=ctx.d_wsu_g[:, :]))
    S.dma("sync", lambda e: e.dma_start(out=ctx.sm2[:, 192:208], in_=ctx.d_wsdp[:, :]))
    S.op("vector", lambda e: e.tensor_scalar(
        wsg64, ctx.sm2[:, 128:192], 1.0 / np.sqrt(2048.0), None, ALU.mult))
    S.op("vector", lambda e: e.scalar_tensor_tensor(
        comb_up, wsuup, 1.0 / np.sqrt(2048.0), sudp, ALU.mult, ALU.mult))
    S.op("vector", lambda e: e.tensor_scalar(
        wdp64, ctx.sm2[:, 192:208], 1.0 / np.sqrt(8192.0), None, ALU.mult))

    wreg_free = getattr(ctx, "wreg_free", [None, None])
    for tp in range(2):
        tb = tp * 512
        emit_ln(ctx, su_row, ctx.rA, 0, tw=512, tbase=tb)
        ztiles = [ctx.rA[:, c * 512:(c + 1) * 512] for c in range(KD)]
        scratch = [ctx.rA[:, 8192 + c * 512:8192 + (c + 1) * 512] for c in range(KD)]
        if getattr(ctx, "old2048", False):
            emit_fwht_combine(ctx, ztiles)
            emit_fwht_h128(ctx, ztiles)
        else:
            ztiles = emit_fwht_folded(ctx, ztiles, scratch)


        in_tok = S.tail("main")
        last_use = [wreg_free[0], wreg_free[0]]
        ld_tok = [None, None]
        psum_free = [None, None]
        act = []
        for o in range(64):
            reg = ctx.rB if o < 32 else ctx.rC
            act.append(reg[:, (o % 32) * 512:(o % 32) * 512 + 512])
        for o in range(64):
            s = o % 2
            wup = ctx.wreg[:, s * 4096:s * 4096 + 2048]
            wgt = ctx.wreg[:, s * 4096 + 2048:s * 4096 + 4096]
            S.dma("gpsimd", lambda e, o=o, wup=wup: e.dma_start(
                out=wup, in_=ctx.d_wup[o, :, :]), waits=(last_use[s],),
                stream="wl%d" % s, sem="wsl%d" % s, tag="wld_up")
            ld_tok[s] = S.dma("gpsimd", lambda e, o=o, wgt=wgt: e.dma_start(
                out=wgt, in_=ctx.d_wgate[o, :, :]),
                stream="wl%d" % s, sem="wsl%d" % s, tag="wld_gt")
            pu = ctx.pA[:, s * 1024:s * 1024 + 512]
            pg = ctx.pA[:, s * 1024 + 512:s * 1024 + 1024]
            for k in range(KD):
                w0 = (ld_tok[s], psum_free[s], in_tok if o < 2 else None) if k == 0 else ()
                tok_up = S.op("tensor", lambda e, k=k, wup=wup, pu=pu, zt=ztiles[k]: e.matmul(
                    pu, wup[:, k * 128:(k + 1) * 128], zt,
                    start=(k == 0), stop=(k == KD - 1)),
                    waits=w0, stream="mm", tag="up_mm")
            for k in range(KD):
                tok = S.op("tensor", lambda e, k=k, wgt=wgt, pg=pg, zt=ztiles[k]: e.matmul(
                    pg, wgt[:, k * 128:(k + 1) * 128], zt,
                    start=(k == 0), stop=(k == KD - 1)), stream="mm", tag="gt_mm")
            last_use[s] = tok
            S.op("scalar", lambda e, o=o, pg=pg: e.activation(
                ctx.trans[:, 2048:2560], pg, AF.Silu, scale=wsg64[:, o:o + 1]),
                waits=(tok,), tag="silu")
            psum_free[s] = S.op("vector", lambda e, o=o, pu=pu, at=act[o]: e.scalar_tensor_tensor(
                at, ctx.trans[:, 2048:2560], comb_up[:, o:o + 1], pu,
                ALU.mult, ALU.mult), waits=(tok_up,), tag="actmul")

        if getattr(ctx, "dbg_fold", False) and tp == 0:
            for c in range(8):
                emit_dbg(ctx, act[c], 128, 512)
        emit_fwht_combine(ctx, act)
        emit_fwht_h128(ctx, act)


        in_tok = S.tail("main")
        last_use_d = [None, None]
        ld_tok_d = [None, None]
        psum_free_d = [None, None]
        for o in range(KD):
            s = o % 2
            wd = ctx.wreg[:, s * 8192:s * 8192 + 8192]
            if o == 0:
                w8 = (last_use[0], last_use[1])
            elif o == 1:
                w8 = (wreg_free[1],)
            else:
                w8 = (last_use_d[s],)
            ld_tok_d[s] = S.dma("gpsimd", lambda e, o=o, wd=wd: e.dma_start(
                out=wd, in_=ctx.d_wdown[o, :, :]), waits=w8,
                stream="wd%d" % s, sem="wsd%d" % s, tag="wld_dn")
            pd = ctx.pB[:, s * 512:s * 512 + 512]
            for k in range(64):
                w0 = (ld_tok_d[s], psum_free_d[s], in_tok if o < 2 else None) if k == 0 else ()
                tok = S.op("tensor", lambda e, k=k, wd=wd, pd=pd, at=act[k]: e.matmul(
                    pd, wd[:, k * 128:(k + 1) * 128], at,
                    start=(k == 0), stop=(k == 63)),
                    waits=w0, stream="mm", tag="dn_mm")
            last_use_d[s] = tok
            xt = x_tile(ctx, o, (tb, tb + 512))
            psum_free_d[s] = S.op("vector", lambda e, o=o, pd=pd, xt=xt: e.scalar_tensor_tensor(
                xt, pd, wdp64[:, o:o + 1], xt, ALU.mult, ALU.add),
                waits=(tok,), tag="dn_res")
        wreg_free = [last_use_d[0], last_use_d[1]]
    ctx.wreg_free = wreg_free


def _decl_l0_attn(nc, ctx):
    ctx.d_wq = nc.dram_tensor("l0a_wq", [16, 128, 2048], F32, kind="ExternalInput").ap()
    ctx.d_wk = nc.dram_tensor("l0a_wk", [16, 128, 2048], F32, kind="ExternalInput").ap()
    ctx.d_wv = nc.dram_tensor("l0a_wv", [16, 128, 2048], F32, kind="ExternalInput").ap()
    ctx.d_wo = nc.dram_tensor("l0a_wo", [16, 128, 2048], F32, kind="ExternalInput").ap()
    ctx.d_wfq = nc.dram_tensor("l0a_wfq", [32, 64, 64], F32, kind="ExternalInput").ap()
    ctx.d_wfk = nc.dram_tensor("l0a_wfk", [32, 64, 64], F32, kind="ExternalInput").ap()
    ctx.d_blk = nc.dram_tensor("l0a_blk", [128, 4128], BF16)
    ctx.d_ag = nc.dram_tensor("l0a_ag", [NCORES * 128, 4128], BF16)


L0ATTN_STAGE = 5
L0ATTN_SUB = 9


def emit_l0_attn(ctx):
    S = ctx.sched
    wreg_free = getattr(ctx, "wreg_free", [None, None])

    S.dma("gpsimd", lambda e: e.dma_start(out=ctx.mask[:, :], in_=ctx.d_mask[:, :]))

    su_row = emit_ones_row(ctx, 0, 2048)
    emit_ln(ctx, su_row, ctx.rA, 0, tw=T)
    ztiles = [ctx.rA[:, c * T:(c + 1) * T] for c in range(KD)]

    vtmp = ctx.trans[:, 2048:3072]
    in_tok = S.tail("main")
    last_use = [wreg_free[0], wreg_free[0]]
    ld_tok = [None, None]
    psum_free = [None, None, None, None]
    pBb8 = ctx.pB[:, 0:512].bitcast(BF16)
    trev_tok = None
    ev_toks = []
    for which, wdram in (("q", ctx.d_wq), ("k", ctx.d_wk), ("v", ctx.d_wv)):
        for o in range(KD):
            s = o % 2
            wsl = ctx.wreg[:, s * 2048:s * 2048 + 2048]
            ld_tok[s] = S.dma("gpsimd", lambda e, o=o, wsl=wsl, wdram=wdram: e.dma_start(
                out=wsl, in_=wdram[o, :, :]), waits=(last_use[s],),
                stream="qv%d" % s, sem="qvs%d" % s, tag="wld_" + which)
            ev2 = []
            for tt in range(2):
                sl = s * 2 + tt
                ps = ctx.pA[:, sl * 512:sl * 512 + 512]
                for k in range(KD):
                    w0 = (ld_tok[s], psum_free[sl],
                          in_tok if (which == "q" and o < 2) else None) if k == 0 else ()
                    tok = S.op("tensor", lambda e, k=k, wsl=wsl, ps=ps, tt=tt: e.matmul(
                        ps, wsl[:, k * 128:(k + 1) * 128],
                        ztiles[k][:, tt * 512:(tt + 1) * 512],
                        start=(k == 0), stop=(k == KD - 1)),
                        waits=w0, stream="mm", tag=which + "_mm")
                if which == "q":
                    dst = ctx.rB[:, o * T + tt * 512:o * T + (tt + 1) * 512]
                elif which == "k":
                    dst = ctx.rC[:, o * T + tt * 512:o * T + (tt + 1) * 512]
                else:
                    dst = vtmp[:, tt * 512:(tt + 1) * 512]
                wv = (tok, trev_tok) if which == "v" else (tok,)
                psum_free[sl] = S.op("scalar", lambda e, dst=dst, ps=ps: e.copy(
                    dst, ps), waits=wv, nosync=True, tag=which + "_ev")
                ev2.append(psum_free[sl])
            last_use[s] = tok
            if which == "v":
                ttoks = []
                for m in range(8):
                    pt = pBb8[:, m * 128:(m + 1) * 128]
                    ttoks.append(S.op("tensor", lambda e, m=m, pt=pt: e.transpose(
                        pt, vtmp[:, m * 128:(m + 1) * 128],
                        ctx.identb[:, :]),
                        waits=(ev2[0], ev2[1], trev_tok) if m == 0 else (),
                        stream="mm", nosync=True, tag="v_tr"))
                trev_tok = S.op("vector", lambda e, o=o: e.tensor_copy(
                    ctx.rD[:, :].rearrange("p (m t) -> p m t", t=2048)[:, :, o * 128:(o + 1) * 128],
                    pBb8.rearrange("p (m c) -> p m c", c=128)),
                    waits=(ttoks[-1],), nosync=True, tag="v_trev")
                ev_toks.append(trev_tok)
            else:
                ev_toks.extend(ev2)
    S.op("vector", lambda e: e.nop(),
         waits=(max(t for t in psum_free if t is not None), trev_tok),
         tag="qkv_anchor")

    if L0ATTN_STAGE < 2:
        ctx.wreg_free = [last_use[0], last_use[1]]
        return
    wfq_sb = ctx.wreg[:, 0:2048]
    wfk_sb = ctx.wreg[:, 2048:4096]
    ms_tok = S.op("vector", lambda e: e.memset(ctx.wreg[:, 0:4096], 0.0),
                  waits=(last_use[0], last_use[1]), tag="fm_ms")
    for wsb, wdram, tagn in ((wfq_sb, ctx.d_wfq, "wld_fq"), (wfk_sb, ctx.d_wfk, "wld_fk")):
        for half in range(2):
            fm_ld = S.dma("gpsimd", lambda e, wsb=wsb, wdram=wdram, half=half: e.dma_start(
                out=wsb[half * 64:half * 64 + 64, :].rearrange(
                    "d (c g) -> d c g", g=128)[:, :, half * 64:half * 64 + 64],
                in_=wdram.rearrange("(c two) d f -> two d c f", two=2)[half]),
                waits=(ms_tok,) if (half == 0 and wsb is wfq_sb) else (),
                stream="qv0", sem="qvs0", tag=tagn)

    first_fm = True
    for b in range(2):
        for c in range(KD):
            ps = ctx.pA[:, (c % 4) * 512:(c % 4) * 512 + 512]
            mm_op(ctx, 0, lambda e, b=b, c=c, ps=ps: e.matmul(
                ps, wfq_sb[:, c * 128:(c + 1) * 128],
                ctx.rB[:, c * T + b * 512:c * T + b * 512 + 512],
                start=True, stop=True),
                waits=(fm_ld,) if first_fm else (), tag="fq_mm")
            first_fm = False
            S.op("scalar", lambda e, b=b, c=c, ps=ps: e.activation(
                ctx.rB[:, c * T + b * 512:c * T + b * 512 + 512], ps, AF.Exp),
                tag="fq_exp")

    if L0ATTN_STAGE < 3:
        ctx.wreg_free = [last_use[0], last_use[1]]
        return
    fk_tm = ctx.trans[:, 0:2048]
    sums = ctx.sm2[:, 0:32]
    rcps = ctx.sm2[:, 32:64]
    for b in range(2):
        for jc in range(4):
            for c in range(KD):
                mm_op(ctx, 0, lambda e, b=b, jc=jc, c=c: e.matmul(
                    ctx.pA[:, c * 128:(c + 1) * 128],
                    ctx.rC[:, c * T + b * 512 + jc * 128:
                           c * T + b * 512 + (jc + 1) * 128],
                    wfk_sb[:, c * 128:(c + 1) * 128],
                    start=True, stop=True), tag="fk_mm")
            S.op("scalar", lambda e: e.activation(fk_tm, ctx.pA[:, :], AF.Exp),
                 tag="fk_exp")
            if L0ATTN_SUB < 2:
                continue
            S.op("vector", lambda e: e.reduce_sum(
                sums, fk_tm.rearrange("p (h f) -> p h f", h=32), axis=AX.X),
                tag="fk_sum")
            S.op("vector", lambda e: e.reciprocal(rcps, sums), tag="fk_rcp")
            S.op("vector", lambda e: e.tensor_tensor(
                fk_tm.rearrange("p (h f) -> p h f", h=32),
                fk_tm.rearrange("p (h f) -> p h f", h=32),
                rcps.unsqueeze(2).broadcast_to([128, 32, 64]), ALU.mult),
                tag="fk_nrm")
            if L0ATTN_SUB < 3:
                continue
            for hh in range(16):
                S.op("tensor", lambda e, hh=hh: e.matmul(
                    ctx.pA[:, 1024 + hh:1024 + hh + 1],
                    fk_tm[:, hh * 128:(hh + 1) * 128], ctx.onecol[:, :],
                    start=True, stop=True), tag="kp_mm")
            S.op("vector", lambda e, b=b, jc=jc: e.scalar_tensor_tensor(
                ctx.sm2[:, 64 + b * 16:80 + b * 16], ctx.pA[:, 1024:1040], 1.0,
                ctx.sm2[:, 64 + b * 16:80 + b * 16], ALU.mult, ALU.add)
                if jc > 0 else e.tensor_copy(
                ctx.sm2[:, 64 + b * 16:80 + b * 16], ctx.pA[:, 1024:1040]),
                tag="kp_acc")
            if L0ATTN_SUB < 4:
                continue
            for hh in range(16):
                pt = ctx.pA[:, (hh % 2) * 512:(hh % 2) * 512 + 64].bitcast(BF16)
                S.op("tensor", lambda e, hh=hh, pt=pt: e.transpose(
                    pt[:, 0:128], fk_tm[:, hh * 128:(hh + 1) * 128],
                    ctx.identb[:, :]), tag="fk_tr")
                S.op("vector", lambda e, hh=hh, b=b, jc=jc, pt=pt: e.tensor_copy(
                    ctx.rC[:, hh * T + b * 512 + jc * 128:hh * T + b * 512 + (jc + 1) * 128],
                    pt[:, 0:128]), tag="fk_trev")
            if L0ATTN_SUB < 5:
                continue
            for h in range(32):
                S.op("tensor", lambda e, b=b, jc=jc, h=h: e.matmul(
                    ctx.pB[(h % 2) * 64:(h % 2) * 64 + 64, h * 64:(h + 1) * 64],
                    fk_tm[:, h * 64:(h + 1) * 64],
                    ctx.rD[:, (b * 4 + jc) * 2048 + h * 64:(b * 4 + jc) * 2048 + (h + 1) * 64],
                    start=True, stop=True), tag="sd_mm")
            S.op("vector", lambda e, b=b, jc=jc: e.scalar_tensor_tensor(
                ctx.dsb[:, b * 2064:b * 2064 + 2048], ctx.pB[:, 0:2048], 1.0,
                ctx.dsb[:, b * 2064:b * 2064 + 2048], ALU.mult, ALU.add)
                if jc > 0 else e.tensor_copy(
                ctx.dsb[:, b * 2064:b * 2064 + 2048], ctx.pB[:, 0:2048]),
                tag="sd_acc")
        if L0ATTN_SUB < 5:
            continue
        S.op("vector", lambda e, b=b: e.tensor_copy(
            ctx.dsb[:, b * 2064 + 2048:b * 2064 + 2064],
            ctx.sm2[:, 64 + b * 16:80 + b * 16]), tag="kp_exp")

    if L0ATTN_STAGE < 4:
        ctx.wreg_free = [last_use[0], last_use[1]]
        return
    S.dma("sync", lambda e: e.dma_start(out=ctx.d_blk[:, :], in_=ctx.dsb[:, :]),
          tag="blk_out")
    S.op("gpsimd", lambda e: e.collective_compute(
        "AllGather", ALU.bypass, replica_groups=[list(range(NCORES))],
        ins=[ctx.d_blk.ap().opt()], outs=[ctx.d_ag.ap().opt()]), sem="coll", tag="ag")
    for half in range(2):
        S.op("vector", lambda e: e.memset(ctx.accf[:, :], 0.0), tag="accz")
        for r in range(NCORES):
            S.dma("sync", lambda e, r=r, half=half: e.dma_start(
                out=ctx.dbuf[:, 0:2064],
                in_=ctx.d_ag[r * 128:(r + 1) * 128, half * 2064:(half + 1) * 2064]),
                tag="ag_ld")
            S.op("vector", lambda e, r=r: e.scalar_tensor_tensor(
                ctx.accf[:, :], ctx.dbuf[:, 0:2064],
                ctx.pmask[:, r:r + 1], ctx.accf[:, :],
                ALU.mult, ALU.add), tag="ag_acc")
        S.op("vector", lambda e, half=half: e.tensor_copy(
            ctx.acc[:, half * 2064:(half + 1) * 2064], ctx.accf[:, 0:2064]),
            tag="acc_bf")

    if getattr(ctx, "dbg_acc", False):
        emit_dbg(ctx, ctx.acc[:, 0:4096], 128, 4096)
    if L0ATTN_STAGE < 5:
        ctx.wreg_free = [last_use[0], last_use[1]]
        return
    den_sb = ctx.transf[:, 1664:1668]
    rec_sb = ctx.transf[:, 1668:1672]
    am_off = [0, 512, 896, 1152]
    for b in range(2):
        for h in range(32):
            pb_ = (h % 2) * 64
            cch = (h // 2) * T + b * 512
            spref = ctx.acc[pb_:pb_ + 64, b * 2064 + h * 64:b * 2064 + h * 64 + 64]
            kprev = ctx.acc[pb_:pb_ + 64, b * 2064 + 2048 + (h // 2):b * 2064 + 2048 + (h // 2) + 1]
            fq_ap = ctx.rB[pb_:pb_ + 64, cch:cch + 512]
            fk_ap = ctx.rC[pb_:pb_ + 64, cch:cch + 512]
            for jc in range(4):
                nw = 512 - jc * 128
                aps = ctx.pA[:, (jc % 2) * 512:(jc % 2) * 512 + nw]
                mm_op(ctx, pb_, lambda e, jc=jc, nw=nw, aps=aps, fk_ap=fk_ap, fq_ap=fq_ap: e.matmul(
                    aps, fk_ap[:, jc * 128:(jc + 1) * 128], fq_ap[:, jc * 128:512],
                    start=True, stop=True), tag="ap_mm")
                am = ctx.trans[:, 2048 + am_off[jc]:2048 + am_off[jc] + nw]
                S.op("vector", lambda e, jc=jc, aps=aps, am=am: e.tensor_tensor(
                    am[:, 0:128], aps[:, 0:128], ctx.mask[:, 0:128], ALU.mult),
                    tag="am_diag")
                if nw > 128:
                    S.op("vector", lambda e, jc=jc, nw=nw, aps=aps, am=am: e.tensor_copy(
                        am[:, 128:nw], aps[:, 128:nw]), tag="am_off")
            for ic in range(4):
                yps = ctx.pB[:, ic * 64:(ic + 1) * 64]
                first = True
                for jc in range(ic + 1):
                    am = ctx.trans[:, 2048 + am_off[jc] + (ic - jc) * 128:
                                   2048 + am_off[jc] + (ic - jc + 1) * 128]
                    mm_op(ctx, 0, lambda e, am=am, b=b, jc=jc, h=h, yps=yps, first=first: e.matmul(
                        yps, am,
                        ctx.rD[:, (b * 4 + jc) * 2048 + h * 64:(b * 4 + jc) * 2048 + (h + 1) * 64],
                        start=first, stop=False), tag="y_intra")
                    first = False
                mm_op(ctx, pb_, lambda e, ic=ic, yps=yps, fq_ap=fq_ap, spref=spref: e.matmul(
                    yps, fq_ap[:, ic * 128:(ic + 1) * 128], spref,
                    start=False, stop=True), tag="y_inter")
                dps = ctx.pB[:, 256 + ic:256 + ic + 1]
                mm_op(ctx, pb_, lambda e, ic=ic, dps=dps, fq_ap=fq_ap, kprev=kprev: e.matmul(
                    dps, fq_ap[:, ic * 128:(ic + 1) * 128], kprev,
                    start=True, stop=False), tag="dn_kp")
                for jc in range(ic + 1):
                    am = ctx.trans[:, 2048 + am_off[jc] + (ic - jc) * 128:
                                   2048 + am_off[jc] + (ic - jc + 1) * 128]
                    mm_op(ctx, 0, lambda e, am=am, dps=dps, jc=jc, ic=ic: e.matmul(
                        dps, am, ctx.onecol[:, :],
                        start=False, stop=(jc == ic)), tag="dn_am")
            S.op("vector", lambda e: e.tensor_copy(den_sb, ctx.pB[:, 256:260]),
                 tag="dn_ev")
            S.op("vector", lambda e: e.reciprocal(rec_sb, den_sb), tag="dn_rc")
            for ic in range(4):
                S.op("vector", lambda e, ic=ic, b=b, h=h: e.tensor_scalar(
                    ctx.rA[:, (b * 4 + ic) * 2048 + h * 64:(b * 4 + ic) * 2048 + (h + 1) * 64],
                    ctx.pB[:, ic * 64:(ic + 1) * 64], rec_sb[:, ic:ic + 1], None,
                    ALU.mult), tag="y_div")

    for m in range(8):
        for c in range(KD):
            pt = ctx.pA[:, (c % 2) * 512:(c % 2) * 512 + 64].bitcast(BF16)
            mm_op(ctx, 0, lambda e, m=m, c=c, pt=pt: e.transpose(
                pt[:, 0:128], ctx.rA[:, m * 2048 + c * 128:m * 2048 + (c + 1) * 128],
                ctx.identb[:, :]), tag="y_tr")
            tcol = (m // 4) * 512 + (m % 4) * 128
            S.op("vector", lambda e, m=m, c=c, pt=pt, tcol=tcol: e.tensor_copy(
                ctx.rB[:, c * T + tcol:c * T + tcol + 128], pt[:, 0:128]),
                tag="y_trev")

    in_tok5 = S.tail("main")
    last2 = [None, None]
    ld2 = [None, None]
    psum_free5 = [None, None, None, None]
    for o in range(KD):
        s = o % 2
        wsl = ctx.wreg[:, s * 2048:s * 2048 + 2048]
        w8 = (in_tok5,) if o < 2 else (last2[s],)
        ld2[s] = S.dma("gpsimd", lambda e, o=o, wsl=wsl: e.dma_start(
            out=wsl, in_=ctx.d_wo[o, :, :]), waits=w8,
            stream="qv%d" % s, sem="qvs%d" % s, tag="wld_wo")
        for tt in range(2):
            sl = s * 2 + tt
            ps = ctx.pA[:, sl * 512:sl * 512 + 512]
            for kc in range(KD):
                w0 = (ld2[s], psum_free5[sl],
                      in_tok5 if o < 2 else None) if kc == 0 else ()
                tok = S.op("tensor", lambda e, kc=kc, wsl=wsl, ps=ps, tt=tt: e.matmul(
                    ps, wsl[:, kc * 128:(kc + 1) * 128],
                    ctx.rB[:, kc * T + tt * 512:kc * T + (tt + 1) * 512],
                    start=(kc == 0), stop=(kc == KD - 1)),
                    waits=w0, stream="mm", tag="wo_mm")
            xt = x_tile(ctx, o, (tt * 512, (tt + 1) * 512))
            psum_free5[sl] = S.op("vector", lambda e, ps=ps, xt=xt: e.tensor_tensor(
                xt, xt, ps, ALU.add), waits=(tok,), tag="wo_res")
        last2[s] = tok
    ctx.wreg_free = [last2[0], last2[1]]


def _decl_l1_attn(nc, ctx):
    ctx.d_wqkv = nc.dram_tensor("l1a_wqkv", [24, 128, 2048], F32, kind="ExternalInput").ap()
    ctx.d_wproj = nc.dram_tensor("l1a_wproj", [16, 128, 2048], F32, kind="ExternalInput").ap()


def emit_l1_attn(ctx):
    S = ctx.sched
    wreg_free = getattr(ctx, "wreg_free", [None, None])

    su_row = emit_ones_row(ctx, 0, 2048)
    emit_ln(ctx, su_row, ctx.rA, 0, tw=T)
    ztiles = [ctx.rA[:, c * T:(c + 1) * T] for c in range(KD)]

    def qkv_tile(o):
        reg = ctx.rB if o < 16 else ctx.rC
        return reg[:, (o % 16) * T:(o % 16) * T + T]

    in_tok = S.tail("main")
    last_use = [wreg_free[0], wreg_free[0]]
    ld_tok = [None, None]
    psum_free = [None, None, None, None]
    for o in range(24):
        s = o % 2
        wsl = ctx.wreg[:, s * 2048:s * 2048 + 2048]
        ld_tok[s] = S.dma("gpsimd", lambda e, o=o, wsl=wsl: e.dma_start(
            out=wsl, in_=ctx.d_wqkv[o, :, :]), waits=(last_use[s],),
            stream="qk%d" % s, sem="qks%d" % s, tag="wld_qkv")
        for tt in range(2):
            sl = s * 2 + tt
            ps = ctx.pA[:, sl * 512:sl * 512 + 512]
            for k in range(KD):
                w0 = (ld_tok[s], psum_free[sl],
                      in_tok if o < 2 else None) if k == 0 else ()
                tok = S.op("tensor", lambda e, k=k, wsl=wsl, ps=ps, tt=tt: e.matmul(
                    ps, wsl[:, k * 128:(k + 1) * 128],
                    ztiles[k][:, tt * 512:(tt + 1) * 512],
                    start=(k == 0), stop=(k == KD - 1)),
                    waits=w0, stream="mm", tag="qkv_mm")
            psum_free[sl] = S.op("scalar", lambda e, o=o, tt=tt, ps=ps: e.copy(
                qkv_tile(o)[:, tt * 512:(tt + 1) * 512], ps),
                waits=(tok,), nosync=True, tag="qkv_ev")
        last_use[s] = tok
    S.op("vector", lambda e: e.nop(),
         waits=tuple(t for t in psum_free if t is not None), tag="qkv_end")

    dv = ctx.trans[:, 2048:2560]
    prod = ctx.trans[:, 2560:3072]
    rDf = ctx.rD[:, :].bitcast(F32)
    s0_sb = rDf[0:1, 4096:6144]
    sd_sb = rDf[0:1, 6144:8192]
    p1row = ctx.rD[0:1, 8192:10240]
    sc = 1.0 / np.sqrt(128.0)
    for g in range(4):
        kt = qkv_tile(g * 6 + 4)
        vt = qkv_tile(g * 6 + 5)
        S.op("vector", lambda e, vt=vt: e.tensor_tensor(
            dv, vt[:, 512:1024], vt[:, 0:512], ALU.subtract), tag="dv")
        for qi in range(4):
            qt = qkv_tile(g * 6 + qi)
            S.op("vector", lambda e, qt=qt, kt=kt: e.tensor_tensor(
                prod, qt[:, 512:1024], kt[:, 0:512], ALU.mult), tag="p0")
            S.op("tensor", lambda e, qi=qi: e.matmul(
                ctx.pA[0:1, qi * 512:qi * 512 + 512], ctx.onecol[:, :], prod,
                start=True, stop=True), tag="s0mm")
            S.op("vector", lambda e, qt=qt, kt=kt: e.tensor_tensor(
                prod, qt[:, 512:1024], kt[:, 512:1024], ALU.mult), tag="p1")
            S.op("tensor", lambda e, qi=qi: e.matmul(
                ctx.pA[64:65, qi * 512:qi * 512 + 512], ctx.onecol[:, :], prod,
                start=True, stop=True), tag="s1mm")
        S.op("vector", lambda e: e.tensor_scalar(
            s0_sb, ctx.pA[0:1, 0:2048], sc, None, ALU.mult), tag="s0ev")
        S.op("vector", lambda e: e.scalar_tensor_tensor(
            sd_sb, ctx.pA[64:65, 0:2048], sc, s0_sb,
            ALU.mult, ALU.subtract), tag="sdiff")
        S.op("scalar", lambda e: e.activation(
            p1row, sd_sb, AF.Sigmoid), tag="sig")
        for qi in range(4):
            h = g * 4 + qi
            pbc = ctx.pB[:, qi * 512:qi * 512 + 512]
            S.op("tensor", lambda e, qi=qi, pbc=pbc: e.matmul(
                pbc, ctx.onerow[:, :], p1row[0:1, qi * 512:qi * 512 + 512],
                start=True, stop=True), tag="pbc")
            S.op("vector", lambda e, pbc=pbc: e.tensor_tensor(
                prod, pbc, dv, ALU.mult), tag="odv")
            ot = ctx.rD[:, h * 512:(h + 1) * 512]
            S.op("vector", lambda e, ot=ot, vt=vt: e.tensor_tensor(
                ot, prod, vt[:, 0:512], ALU.add), tag="oadd")

    in_tok2 = S.tail("main")
    last2 = [last_use[0], last_use[1]]
    ld2 = [None, None]
    psum_free2 = [None, None, None, None]
    for o in range(KD):
        s = o % 2
        wsl = ctx.wreg[:, s * 2048:s * 2048 + 2048]
        ld2[s] = S.dma("gpsimd", lambda e, o=o, wsl=wsl: e.dma_start(
            out=wsl, in_=ctx.d_wproj[o, :, :]), waits=(last2[s],),
            stream="qk%d" % s, sem="qks%d" % s, tag="wld_prj")
        for tt in range(2):
            sl = s * 2 + tt
            ps = ctx.pA[:, sl * 512:sl * 512 + 512]
            for hc in range(KD):
                g = hc // 4
                if tt == 0:
                    rhs = qkv_tile(g * 6 + 5)[:, 0:512]
                else:
                    rhs = ctx.rD[:, hc * 512:(hc + 1) * 512]
                w0 = (ld2[s], psum_free2[sl],
                      in_tok2 if o < 2 else None) if hc == 0 else ()
                tok = S.op("tensor", lambda e, hc=hc, wsl=wsl, ps=ps, rhs=rhs: e.matmul(
                    ps, wsl[:, hc * 128:(hc + 1) * 128], rhs,
                    start=(hc == 0), stop=(hc == KD - 1)),
                    waits=w0, stream="mm", tag="prj_mm")
            xt = x_tile(ctx, o, (tt * 512, (tt + 1) * 512))
            psum_free2[sl] = S.op("vector", lambda e, ps=ps, xt=xt: e.tensor_tensor(
                xt, xt, ps, ALU.add), waits=(tok,), tag="prj_res")
        last2[s] = tok
    ctx.wreg_free = [last2[0], last2[1]]


def _decl_l1_ffn(nc, ctx):
    ctx.d_wg = nc.dram_tensor("l1f_wg", [16, 128, 8], F32, kind="ExternalInput").ap()
    ctx.d_wgu = nc.dram_tensor("l1f_wgu", [8, 16, 128, 2048], F32, kind="ExternalInput").ap()
    ctx.d_wdn = nc.dram_tensor("l1f_wdn", [8, 16, 128, 1024], F32, kind="ExternalInput").ap()
    ctx.d_wshgu = nc.dram_tensor("l1f_wshgu", [64, 128, 2048], F32, kind="ExternalInput").ap()
    ctx.d_wshdn = nc.dram_tensor("l1f_wshdn", [16, 128, 4096], F32, kind="ExternalInput").ap()


L1FFN_PARTS = ("moe", "shared")


def emit_l1_ffn(ctx):
    S = ctx.sched
    wreg_free = getattr(ctx, "wreg_free", [None, None])

    su_row = emit_ones_row(ctx, 0, 2048)
    emit_ln(ctx, su_row, ctx.rA, 0, tw=T)
    ztiles = [ctx.rA[:, c * T:(c + 1) * T] for c in range(KD)]

    do_moe = "moe" in L1FFN_PARTS
    do_shared = "shared" in L1FFN_PARTS
    wg_sb = ctx.trans[:, 2048:2176]
    S.dma("gpsimd", lambda e: e.dma_start(
        out=wg_sb.rearrange("p (k u) -> p k u", u=8),
        in_=ctx.d_wg.rearrange("k p u -> p k u")), tag="wld_rt")
    compfm = ctx.rD[0:8, 2048:3072]
    comb_tm = ctx.trans[:, 2176:2184]
    pBb = ctx.pB[:, 512:1024].bitcast(BF16)
    m1 = ctx.sm2[:, 208:209]
    m2 = ctx.sm2[:, 209:210]
    nm1 = ctx.sm2[:, 210:211]
    ds = ctx.sm2[:, 211:212]
    tmp8 = ctx.sm2[:, 212:220]
    l2 = ctx.sm2[:, 220:228]
    g8 = ctx.sm2[:, 228:236]
    sel = ctx.sm2[:, 236:244]
    rc = ctx.sm2[:, 244:245]
    for m in range(8 if do_moe else 0):
        lps = ctx.pA[:, m * 8:(m + 1) * 8]
        for k in range(KD):
            S.op("tensor", lambda e, m=m, k=k, lps=lps: e.matmul(
                lps, ztiles[k][:, m * 128:(m + 1) * 128], wg_sb[:, k * 8:(k + 1) * 8],
                start=(k == 0), stop=(k == KD - 1)), tag="rt_mm")
        S.op("vector", lambda e, lps=lps: e.reduce_max(m1, lps, axis=AX.X))
        S.op("vector", lambda e, lps=lps: e.tensor_scalar(tmp8, lps, m1, None, ALU.is_ge))
        S.op("vector", lambda e, lps=lps: e.scalar_tensor_tensor(
            l2, tmp8, -1e30, lps, ALU.mult, ALU.add))
        S.op("vector", lambda e: e.reduce_max(m2, l2, axis=AX.X))
        S.op("vector", lambda e: e.tensor_scalar(nm1, m1, -1.0, None, ALU.mult))
        S.op("scalar", lambda e, lps=lps: e.activation(g8, lps, AF.Exp, bias=nm1))
        S.op("vector", lambda e, lps=lps: e.tensor_scalar(sel, lps, m2, None, ALU.is_ge))
        S.op("vector", lambda e: e.tensor_tensor(g8, g8, sel, ALU.mult))
        S.op("vector", lambda e: e.reduce_sum(ds, g8, axis=AX.X))
        S.op("vector", lambda e: e.reciprocal(rc, ds))
        S.op("vector", lambda e: e.tensor_scalar(comb_tm, g8, rc, None, ALU.mult))
        S.op("tensor", lambda e, m=m: e.transpose(
            pBb[0:8, m * 128:(m + 1) * 128], comb_tm, ctx.identb[:, :]), tag="rt_tr")
        S.op("vector", lambda e, m=m: e.tensor_copy(
            compfm[:, m * 128:(m + 1) * 128], pBb[0:8, m * 128:(m + 1) * 128]))

    cbc_sb = ctx.trans[:, 2176:3200]
    silu_t = ctx.trans[:, 3200:4224]
    t1 = ctx.trans[:, 4224:5248]
    ce_row = ctx.rD[0:1, 1024:2048]

    def act_tile(e, k):
        return ctx.rB[:, k * 1024:(k + 1) * 1024]

    last_gu = [wreg_free[0], wreg_free[0]]
    last_dn = [None, wreg_free[1]]
    ld_tok = [None, None]
    if do_moe:
        in_tok = S.tail("main")
        psum_free_gu = [None, None]
        psum_free_dn = [None, None]
        cbc_last = None
        dn_ld = [None, None]
        for e in range(8):
            ce_tok = S.dma("sync", lambda e_, e=e: e_.dma_start(
                out=ce_row, in_=compfm[e:e + 1, 0:1024]),
                waits=(cbc_last,), tag="ce_dma")
            cbc_mms = []
            for i2 in range(2):
                pcb = ctx.pB[:, 1024 + i2 * 512:1536 + i2 * 512]
                cbc_mms.append(S.op("tensor", lambda e_, pcb=pcb, i2=i2: e_.matmul(
                    pcb, ctx.onerow[:, :], ce_row[0:1, i2 * 512:(i2 + 1) * 512],
                    start=True, stop=True),
                    waits=(ce_tok, cbc_last) if i2 == 0 else (),
                    stream="mm", tag="cbc_mm"))
            cbc_ev = S.op("vector", lambda e_: e_.tensor_copy(
                cbc_sb, ctx.pB[:, 1024:2048]), waits=(cbc_mms[1],), tag="cbc_ev")
            for j in range(8):
                s = (e * 8 + j) % 2
                wga = ctx.wreg[:, s * 4096:s * 4096 + 2048]
                wua = ctx.wreg[:, s * 4096 + 2048:s * 4096 + 4096]
                S.dma("gpsimd", lambda e_, e=e, j=j, wga=wga: e_.dma_start(
                    out=wga, in_=ctx.d_wgu[e, j, :, :]), waits=(last_gu[s],),
                    stream="gu%d" % s, sem="gus%d" % s, tag="wld_g")
                ld_tok[s] = S.dma("gpsimd", lambda e_, e=e, j=j, wua=wua: e_.dma_start(
                    out=wua, in_=ctx.d_wgu[e, j + 8, :, :]),
                    stream="gu%d" % s, sem="gus%d" % s, tag="wld_u")
                pg = ctx.pA[:, 0:1024]
                pu = ctx.pA[:, 1024:2048]
                for tt in range(2):
                    for k in range(KD):
                        w0 = (ld_tok[s], psum_free_gu[0],
                              in_tok if (e == 0 and j == 0) else None) \
                            if (k == 0 and tt == 0) else ()
                        tokg = S.op("tensor", lambda e_, k=k, tt=tt, wga=wga, pg=pg: e_.matmul(
                            pg[:, tt * 512:(tt + 1) * 512],
                            wga[:, k * 128:(k + 1) * 128],
                            ztiles[k][:, tt * 512:(tt + 1) * 512],
                            start=(k == 0), stop=(k == KD - 1)),
                            waits=w0, stream="mm", tag="gu_g_mm")
                for tt in range(2):
                    for k in range(KD):
                        w0 = (psum_free_gu[1],) if (k == 0 and tt == 0) else ()
                        tok = S.op("tensor", lambda e_, k=k, tt=tt, wua=wua, pu=pu: e_.matmul(
                            pu[:, tt * 512:(tt + 1) * 512],
                            wua[:, k * 128:(k + 1) * 128],
                            ztiles[k][:, tt * 512:(tt + 1) * 512],
                            start=(k == 0), stop=(k == KD - 1)),
                            waits=w0, stream="mm", tag="gu_u_mm")
                last_gu[s] = tok
                psum_free_gu[0] = S.op("scalar", lambda e_, pg=pg: e_.activation(
                    silu_t, pg, AF.Silu), waits=(tokg,), tag="moe_silu")
                tok_t1 = S.op("vector", lambda e_, pu=pu: e_.tensor_tensor(
                    t1, pu, cbc_sb, ALU.mult), waits=(tok, cbc_ev), tag="moe_t1")
                psum_free_gu[1] = tok_t1
                act_done = S.op("vector", lambda e_, e=e, j=j: e_.tensor_tensor(
                    act_tile(e, j), silu_t, t1, ALU.mult), tag="moe_act")
            w_init = wreg_free[1]
            for o in range(KD):
                s = o % 2
                wdn = ctx.wreg[:, 8192 + s * 4096:8192 + s * 4096 + 1024]
                dn_prev = last_dn[s]
                dn_ld[s] = S.dma("gpsimd", lambda e_, e=e, o=o, wdn=wdn: e_.dma_start(
                    out=wdn, in_=ctx.d_wdn[e, o, :, :]),
                    waits=(w_init,) if (e == 0 and o < 2) else (dn_prev,),
                    stream="dn%d" % s, sem="dns%d" % s, tag="wld_dn")
                pd = ctx.pB[:, 0:1024]
                for tt in range(2):
                    for k in range(8):
                        w0 = (dn_ld[s], psum_free_dn[0],
                              act_done if o == 0 else None) \
                            if (k == 0 and tt == 0) else ()
                        tok = S.op("tensor", lambda e_, e=e, k=k, tt=tt, wdn=wdn, pd=pd: e_.matmul(
                            pd[:, tt * 512:(tt + 1) * 512],
                            wdn[:, k * 128:(k + 1) * 128],
                            act_tile(e, k)[:, tt * 512:(tt + 1) * 512],
                            start=(k == 0), stop=(k == 7)),
                            waits=w0, stream="mm", tag="dn_mm")
                last_dn[s] = tok
                xt = x_tile(ctx, o)
                psum_free_dn[0] = S.op("vector", lambda e_, pd=pd, xt=xt: e_.tensor_tensor(
                    xt, xt, pd, ALU.add), waits=(tok,), tag="dn_res")
            cbc_last = psum_free_dn[0]
    def sh_tile(o):
        reg = ctx.rB if o < 16 else ctx.rC
        return reg[:, (o % 16) * T:(o % 16) * T + T]

    if not do_moe:
        last_dn = [wreg_free[0], wreg_free[1]]
    in_tok3 = S.tail("main")
    last_sh = [last_dn[0], last_dn[0]]
    ld_sh = [None, None]
    psum_free_sh = [None, None, None, None]
    for o in range(32 if do_shared else 0):
        s = o % 2
        wga = ctx.wreg[:, s * 4096:s * 4096 + 2048]
        wua = ctx.wreg[:, s * 4096 + 2048:s * 4096 + 4096]
        S.dma("gpsimd", lambda e_, o=o, wga=wga: e_.dma_start(
            out=wga, in_=ctx.d_wshgu[o, :, :]), waits=(last_sh[s],),
            stream="sg%d" % s, sem="sgs%d" % s, tag="wld_shg")
        ld_sh[s] = S.dma("gpsimd", lambda e_, o=o, wua=wua: e_.dma_start(
            out=wua, in_=ctx.d_wshgu[o + 32, :, :]),
            stream="sg%d" % s, sem="sgs%d" % s, tag="wld_shu")
        for tt in range(2):
            sl = s * 2 + tt
            pg = ctx.pA[:, s * 1024 + tt * 512:s * 1024 + (tt + 1) * 512]
            pu = ctx.pB[:, s * 1024 + tt * 512:s * 1024 + (tt + 1) * 512]
            for k in range(KD):
                w0 = (ld_sh[s], psum_free_sh[sl],
                      in_tok3 if o < 2 else None) if k == 0 else ()
                S.op("tensor", lambda e_, k=k, wga=wga, pg=pg, tt=tt: e_.matmul(
                    pg, wga[:, k * 128:(k + 1) * 128],
                    ztiles[k][:, tt * 512:(tt + 1) * 512],
                    start=(k == 0), stop=(k == KD - 1)),
                    waits=w0, stream="mm", tag="sh_g_mm")
            for k in range(KD):
                tok = S.op("tensor", lambda e_, k=k, wua=wua, pu=pu, tt=tt: e_.matmul(
                    pu, wua[:, k * 128:(k + 1) * 128],
                    ztiles[k][:, tt * 512:(tt + 1) * 512],
                    start=(k == 0), stop=(k == KD - 1)), stream="mm", tag="sh_u_mm")
            last_sh[s] = tok
            S.op("scalar", lambda e_, pg=pg, tt=tt: e_.activation(
                silu_t[:, tt * 512:(tt + 1) * 512], pg, AF.Silu),
                waits=(tok,), tag="sh_silu")
            psum_free_sh[sl] = S.op("vector", lambda e_, o=o, tt=tt, pu=pu: e_.tensor_tensor(
                sh_tile(o)[:, tt * 512:(tt + 1) * 512],
                silu_t[:, tt * 512:(tt + 1) * 512], pu, ALU.mult),
                tag="sh_act")
    in_tok4 = S.tail("main")
    last_sd = [None, None]
    psum_free_sd = [None, None, None, None]
    for o in range(KD if do_shared else 0):
        s = o % 2
        wsd = ctx.wreg[:, s * 8192:s * 8192 + 4096]
        if o == 0:
            w8 = (last_sh[0], last_sh[1])
        elif o == 1:
            w8 = (last_dn[1],) if last_dn[1] is not None else ()
        else:
            w8 = (last_sd[s],)
        ldd = S.dma("gpsimd", lambda e_, o=o, wsd=wsd: e_.dma_start(
            out=wsd, in_=ctx.d_wshdn[o, :, :]), waits=w8,
            stream="sd%d" % s, sem="sds%d" % s, tag="wld_shd")
        for tt in range(2):
            sl = s * 2 + tt
            pd = ctx.pA[:, (s * 2 + tt) * 512:(s * 2 + tt) * 512 + 512]
            for k in range(32):
                w0 = (ldd, psum_free_sd[sl],
                      in_tok4 if o < 2 else None) if k == 0 else ()
                tok = S.op("tensor", lambda e_, k=k, wsd=wsd, pd=pd, tt=tt: e_.matmul(
                    pd, wsd[:, k * 128:(k + 1) * 128],
                    sh_tile(k)[:, tt * 512:(tt + 1) * 512],
                    start=(k == 0), stop=(k == 31)),
                    waits=w0, stream="mm", tag="shd_mm")
            xt = x_tile(ctx, o, (tt * 512, (tt + 1) * 512))
            psum_free_sd[sl] = S.op("vector", lambda e_, pd=pd, xt=xt: e_.tensor_tensor(
                xt, xt, pd, ALU.add), waits=(tok,), tag="shd_res")
        last_sd[s] = tok
    if do_shared:
        ctx.wreg_free = [last_sd[0], last_sd[1]]
    else:
        ctx.wreg_free = [last_dn[0], last_dn[1]]


def _prep_consts():
    h128 = make_hadamard(128).astype(np.float32)
    ident = np.eye(128, dtype=np.float32)
    mask = (np.arange(128)[:, None] <= np.arange(512)[None, :]).astype(np.float32)
    return {"c_h128": h128, "c_h128n": -h128, "c_ident": ident, "c_mask": mask}


def _prep_pmask(core):
    pm = np.zeros((128, 8), dtype=np.float32)
    pm[:, :core] = 1.0
    return pm


def _prep_l0_attn(inp):
    return {
        "l0a_wq": tile_fm(np.ascontiguousarray(np.asarray(inp["L0_attn_Wq"]).T), D, D),
        "l0a_wk": tile_fm(np.ascontiguousarray(np.asarray(inp["L0_attn_Wk"]).T), D, D),
        "l0a_wv": tile_fm(np.ascontiguousarray(np.asarray(inp["L0_attn_Wv"]).T), D, D),
        "l0a_wo": tile_fm(np.ascontiguousarray(np.asarray(inp["L0_attn_Wo"]).T), D, D),
        "l0a_wfq": np.asarray(inp["L0_attn_Wfq"]),
        "l0a_wfk": np.asarray(inp["L0_attn_Wfk"]),
    }


def _prep_l1_attn(inp):
    wqkv = np.asarray(inp["L1_attn_weight_qkv"])
    wproj = np.asarray(inp["L1_attn_weight_proj"])
    return {
        "l1a_wqkv": tile_fm(np.ascontiguousarray(wqkv.T), D, 3072),
        "l1a_wproj": tile_fm(np.ascontiguousarray(wproj.T), D, D),
    }


def _prep_l1_ffn(inp):
    wg = np.asarray(inp["L1_ffn_w_gate"])
    wgu = np.asarray(inp["L1_ffn_w_gate_up"])
    wdn = np.asarray(inp["L1_ffn_w_down"])
    wshgu = np.asarray(inp["L1_ffn_w_sh_gate_up"])
    wshdn = np.asarray(inp["L1_ffn_w_sh_down"])
    return {
        "l1f_wg": np.ascontiguousarray(wg.reshape(16, 128, 8)),
        "l1f_wgu": np.stack([tile_fm(wgu[e], D, D) for e in range(8)]),
        "l1f_wdn": np.stack([tile_fm(wdn[e], 1024, D) for e in range(8)]),
        "l1f_wshgu": tile_fm(wshgu, D, 8192),
        "l1f_wshdn": tile_fm(wshdn, 4096, D),
    }


def _prep_l0_ffn(inp):
    up = np.asarray(inp["L0_ffn_W_up"])
    gate = np.asarray(inp["L0_ffn_W_gate"])
    down = np.asarray(inp["L0_ffn_W_down"])
    wsc = np.asarray(inp["L0_ffn_Wscale_ug"])
    return {
        "l0f_wup": tile_fm(np.ascontiguousarray(up.T), D, 8192),
        "l0f_wgate": tile_fm(np.ascontiguousarray(gate.T), D, 8192),
        "l0f_wdown": tile_fm(np.ascontiguousarray(down.T), 8192, D),
        "l0f_suug": np.asarray(inp["L0_ffn_SU_ug"]).reshape(1, D),
        "l0f_sudp": pmajor(np.asarray(inp["L0_ffn_SU_dp"])),
        "l0f_wsu_up": pmajor(wsc[:8192]),
        "l0f_wsu_g": pmajor(wsc[8192:]),
        "l0f_wsdp": pmajor(np.asarray(inp["L0_ffn_Wscale_dp"])),
    }


DBG_ACC = False
OLD8192 = False
OLD2048 = False
DBG_FOLD = False


def build_program(blocks):
    nc = bass.Bass("TRN2", target_bir_lowering=False)
    es = contextlib.ExitStack()
    ctx = Ctx()
    with es:
        _alloc_common(nc, es, ctx)
        ctx.dbg_acc = DBG_ACC
        ctx.old8192 = OLD8192
        ctx.old2048 = OLD2048
        ctx.dbg_fold = DBG_FOLD
        if "l0_attn" in blocks:
            _decl_l0_attn(nc, ctx)
        if "l0_ffn" in blocks:
            _decl_l0_ffn(nc, ctx)
        if "l1_attn" in blocks:
            _decl_l1_attn(nc, ctx)
        if "l1_ffn" in blocks:
            _decl_l1_ffn(nc, ctx)
        for b in blocks:
            if b == "l0_attn":
                emit_l0_attn(ctx)
            elif b == "l0_ffn":
                emit_l0_ffn(ctx)
            elif b == "l1_attn":
                emit_l1_attn(ctx)
            elif b == "l1_ffn":
                emit_l1_ffn(ctx)
        _emit_store_out(ctx)
        block = es.enter_context(nc.Block())
        ctx.sched.build(nc, block, es)
    return nc


def shard_x(x):
    shards = []
    for c in range(NCORES):
        xs = np.concatenate([x[0, c * TB:(c + 1) * TB, :],
                             x[1, c * TB:(c + 1) * TB, :]], axis=0)
        shards.append(np.ascontiguousarray(xs.T))
    return shards


def unshard_x(outs):
    x = np.empty((2, NCORES * TB, D), dtype=np.float32)
    for c in range(NCORES):
        o = np.asarray(outs[c])
        x[0, c * TB:(c + 1) * TB, :] = o[:, 0:TB].T
        x[1, c * TB:(c + 1) * TB, :] = o[:, TB:2 * TB].T
    return x


KERNEL_BLOCKS = BLOCKS_ALL


def kernel(**inputs):
    x = np.asarray(inputs["L0_attn_hidden_states"])
    consts = _prep_consts()
    w = {}
    if "l0_attn" in KERNEL_BLOCKS:
        w.update(_prep_l0_attn(inputs))
    if "l0_ffn" in KERNEL_BLOCKS:
        w.update(_prep_l0_ffn(inputs))
    if "l1_attn" in KERNEL_BLOCKS:
        w.update(_prep_l1_attn(inputs))
    if "l1_ffn" in KERNEL_BLOCKS:
        w.update(_prep_l1_ffn(inputs))
    nc = build_program(KERNEL_BLOCKS)
    shards = shard_x(x)
    in_maps = []
    for c in range(NCORES):
        m = {"x_fm": shards[c], "c_pmask": _prep_pmask(c)}
        m.update(consts)
        m.update(w)
        in_maps.append(m)
    res = run_bass_kernel_spmd(nc, in_maps, core_ids=list(range(NCORES)))
    global LAST_RESULTS
    LAST_RESULTS = res
    return unshard_x([r["out_fm"] for r in res.results])


LAST_RESULTS = None


# version 08
# speedup vs baseline: 1.1640x; 1.0121x over previous
import contextlib
import numpy as np
import ml_dtypes
import concourse.bass as bass
import concourse.mybir as mybir
from concourse.bass_utils import run_bass_kernel_spmd

F32 = mybir.dt.float32
BF16 = mybir.dt.bfloat16
AF = mybir.ActivationFunctionType
ALU = mybir.AluOpType
AX = mybir.AxisListType

NCORES = 8
D = 2048
KD = D // 128
T = 1024
TB = 512
BF = ml_dtypes.bfloat16

BLOCKS_ALL = ("l0_attn", "l0_ffn", "l1_attn", "l1_ffn")


class Sched:

    def __init__(self):
        self.ops = []
        self.force = set()
        self._tail = {}

    def tail(self, stream="main"):
        return self._tail.get(stream)

    def op(self, eng, fn, waits=(), stream="main", sem=None, inc=1, tag="",
           nosync=False):
        tok = len(self.ops)
        for w in waits:
            if w is not None:
                self.force.add(w)
        self.ops.append(dict(eng=eng, fn=fn, inc=inc, sem=sem or eng,
                             stream=stream, waits=[w for w in waits if w is not None],
                             tag=tag, nosync=nosync))
        self._tail[stream] = tok
        return tok

    def dma(self, eng, fn, waits=(), stream="main", sem=None, tag="", nosync=False):
        return self.op(eng, fn, waits=waits, stream=stream,
                       sem=sem or (eng + "_dma"), inc=16, tag=tag, nosync=nosync)

    def build(self, nc, block, es):
        n = len(self.ops)
        prev_in_stream = {}
        chain_wait = [None] * n
        for i, o in enumerate(self.ops):
            p = prev_in_stream.get(o["stream"])
            if p is not None and not o.get("nosync"):
                po = self.ops[p]
                if po["eng"] != o["eng"]:
                    chain_wait[i] = p
                elif (o["inc"] != 16 and po["inc"] != 16
                      and o["eng"] in ("vector", "scalar", "gpsimd")):
                    chain_wait[i] = p
            prev_in_stream[o["stream"]] = i
        last_of_stream = set(prev_in_stream.values())
        do_inc = [False] * n
        for i, o in enumerate(self.ops):
            if o["inc"] == 16 or i in self.force or i in last_of_stream:
                do_inc[i] = True
        for i in range(n):
            if chain_wait[i] is not None:
                do_inc[chain_wait[i]] = True
        sem_names = sorted({o["sem"] for i, o in enumerate(self.ops) if do_inc[i]})
        sems = {s: es.enter_context(nc.semaphore("sem_" + s)) for s in sem_names}
        cum = {s: 0 for s in sem_names}
        val_after = [0] * n
        for i, o in enumerate(self.ops):
            if do_inc[i]:
                cum[o["sem"]] += o["inc"]
                val_after[i] = cum[o["sem"]]
        resolved = [[] for _ in range(n)]
        for i, o in enumerate(self.ops):
            tgts = list(o["waits"])
            if chain_wait[i] is not None:
                tgts.append(chain_wait[i])
            seen = {}
            for tkn in tgts:
                assert do_inc[tkn], (i, tkn, self.ops[tkn])
                s = self.ops[tkn]["sem"]
                seen[s] = max(seen.get(s, 0), val_after[tkn])
            resolved[i] = [(sems[s], v) for s, v in sorted(seen.items())]

        def make_body(label):
            def body(eng):
                for i, o in enumerate(self.ops):
                    if o["eng"] != label:
                        continue
                    for s, v in resolved[i]:
                        eng.wait_ge(s, v)
                    inst = o["fn"](eng)
                    if do_inc[i]:
                        inst.then_inc(sems[o["sem"]], o["inc"])
            return body

        used = {o["eng"] for o in self.ops}
        for label, attr in [("tensor", block.tensor), ("vector", block.vector),
                            ("scalar", block.scalar), ("gpsimd", block.gpsimd),
                            ("sync", block.sync)]:
            if label in used:
                attr(make_body(label))


def tile_fm(wt, kdim, odim):
    kch, och = kdim // 128, odim // 128
    return np.ascontiguousarray(
        wt.reshape(kch, 128, och, 128).transpose(2, 1, 0, 3).reshape(och, 128, kch * 128))


def pmajor(v):
    n = v.shape[0] // 128
    return np.ascontiguousarray(v.reshape(n, 128).T)


def make_hadamard(n):
    h = np.array([[1.0]], dtype=np.float32)
    while h.shape[0] < n:
        h = np.block([[h, h], [h, -h]])
    return h


class Ctx:
    pass


def _alloc_common(nc, es, ctx):
    S = ctx.sched = Sched()
    ctx.nc = nc
    ctx.es = es

    ctx.d_x = nc.dram_tensor("x_fm", [D, T], F32, kind="ExternalInput").ap()
    ctx.d_out = nc.dram_tensor("out_fm", [D, T], F32, kind="ExternalOutput").ap()
    ctx.d_h128 = nc.dram_tensor("c_h128", [128, 128], F32, kind="ExternalInput").ap()
    ctx.d_h128n = nc.dram_tensor("c_h128n", [128, 128], F32, kind="ExternalInput").ap()
    ctx.d_ident = nc.dram_tensor("c_ident", [128, 128], F32, kind="ExternalInput").ap()
    ctx.d_mask = nc.dram_tensor("c_mask", [128, 512], F32, kind="ExternalInput").ap()
    ctx.d_pmask = nc.dram_tensor("c_pmask", [128, 8], F32, kind="ExternalInput").ap()
    ctx.d_dbg = nc.dram_tensor("dbg", [128, 8192], F32, kind="ExternalOutput").ap()
    ctx.dbg_off = [0]

    ctx.x = es.enter_context(nc.sbuf_tensor("x", [128, KD * T], BF16))
    ctx.rA = es.enter_context(nc.sbuf_tensor("rA", [128, 16384], BF16))
    ctx.rB = es.enter_context(nc.sbuf_tensor("rB", [128, 16384], BF16))
    ctx.rC = es.enter_context(nc.sbuf_tensor("rC", [128, 16384], BF16))
    ctx.rD = es.enter_context(nc.sbuf_tensor("rD", [128, 16384], BF16))
    ctx.wreg = es.enter_context(nc.sbuf_tensor("wreg", [128, 16384], BF16))
    ctx.trans = es.enter_context(nc.sbuf_tensor("trans", [128, 6656], BF16))
    ctx.dsb = ctx.wreg[:, 4096:8224]
    ctx.accf = ctx.wreg[:, 4096:8224].bitcast(F32)
    ctx.dbuf = ctx.wreg[:, 8224:10288]
    ctx.acc = ctx.wreg[:, 10288:14416]
    ctx.mask = ctx.wreg[:, 14416:14928]
    ctx.sm1 = es.enter_context(nc.sbuf_tensor("sm1", [128, 192], F32))
    ctx.sm2 = es.enter_context(nc.sbuf_tensor("sm2", [128, 256], F32))
    ctx.h128 = es.enter_context(nc.sbuf_tensor("h128", [128, 128], BF16))
    ctx.h128n = es.enter_context(nc.sbuf_tensor("h128n", [128, 128], BF16))
    ctx.identb = es.enter_context(nc.sbuf_tensor("identb", [128, 128], BF16))
    ctx.onecol = es.enter_context(nc.sbuf_tensor("onecol", [128, 1], BF16))
    ctx.onerow = es.enter_context(nc.sbuf_tensor("onerow", [1, 128], BF16))
    ctx.pmask = es.enter_context(nc.sbuf_tensor("pmask", [128, 8], F32))

    ctx.pA = es.enter_context(nc.psum_tensor("pA", [128, 2048], F32))
    ctx.pB = es.enter_context(nc.psum_tensor("pB", [128, 2048], F32))

    ctx.transf = ctx.trans[:, :].bitcast(F32)

    S.dma("gpsimd", lambda e: e.dma_start(out=ctx.h128[:, :], in_=ctx.d_h128[:, :]))
    S.dma("gpsimd", lambda e: e.dma_start(out=ctx.h128n[:, :], in_=ctx.d_h128n[:, :]))
    S.dma("gpsimd", lambda e: e.dma_start(out=ctx.identb[:, :], in_=ctx.d_ident[:, :]))
    S.dma("sync", lambda e: e.dma_start(out=ctx.pmask[:, :], in_=ctx.d_pmask[:, :]))
    S.op("vector", lambda e: e.memset(ctx.onecol[:, :], 1.0))
    S.op("vector", lambda e: e.memset(ctx.onerow[:, :], 1.0))
    for c in range(KD):
        S.dma("gpsimd", lambda e, c=c: e.dma_start(
            out=ctx.x[:, c * T:(c + 1) * T], in_=ctx.d_x[c * 128:(c + 1) * 128, :]))


def _emit_store_out(ctx):
    S = ctx.sched
    for c in range(KD):
        S.dma("gpsimd", lambda e, c=c: e.dma_start(
            out=ctx.d_out[c * 128:(c + 1) * 128, :], in_=ctx.x[:, c * T:(c + 1) * T]))
    S.op("scalar", lambda e: e.nop(), tag="fence")


def emit_dbg(ctx, ap, rows, cols):
    off = ctx.dbg_off[0]
    ctx.sched.dma("gpsimd", lambda e: e.dma_start(
        out=ctx.d_dbg[0:rows, off:off + cols], in_=ap), tag="dbg")
    ctx.dbg_off[0] += cols


def x_tile(ctx, c, tcols=None):
    if tcols is None:
        return ctx.x[:, c * T:(c + 1) * T]
    return ctx.x[:, c * T + tcols[0]:c * T + tcols[1]]


def t_row_bf(ctx, off, w):
    return ctx.trans[0:1, off:off + w]


def t_tile_bf(ctx, off, w, p0=0, p1=128):
    return ctx.trans[p0:p1, off:off + w]


def emit_ln(ctx, su_row, out_region, out_off, tw=T, tbase=0):
    S = ctx.sched
    nt = tw // 512
    x2t = ctx.trans[:, 2048:2560]

    def xt(c, i):
        return ctx.x[:, c * T + tbase + i * 512: c * T + tbase + (i + 1) * 512]

    for i in range(nt):
        psx = ctx.pA[0:1, 1024 + i * 512:1536 + i * 512]
        psq = ctx.pB[0:1, 1024 + i * 512:1536 + i * 512]
        for c in range(KD):
            S.op("scalar", lambda e, c=c, i=i: e.activation(
                x2t, xt(c, i), AF.Square), tag="ln_sq")
            S.op("tensor", lambda e, c=c, i=i, psx=psx: e.matmul(
                psx, ctx.onecol[:, :], xt(c, i), start=(c == 0), stop=(c == KD - 1)),
                tag="ln_sx")
            S.op("tensor", lambda e, c=c, psq=psq: e.matmul(
                psq, ctx.onecol[:, :], x2t, start=(c == 0), stop=(c == KD - 1)),
                tag="ln_sx2")
    mu_ps = ctx.pA[0:1, 1024:1024 + tw]
    var_ps = ctx.pB[0:1, 1024:1024 + tw]
    musq = ctx.transf[0:1, 1280:1280 + tw]
    rstd_bf = t_row_bf(ctx, 4608, tw)
    nmr_bf = t_row_bf(ctx, 4608 + tw, tw)
    S.op("vector", lambda e: e.tensor_scalar(mu_ps, mu_ps, 1.0 / D, None, ALU.mult))
    S.op("vector", lambda e: e.tensor_scalar(var_ps, var_ps, 1.0 / D, 1e-5, ALU.mult, ALU.add))
    S.op("scalar", lambda e: e.activation(musq, mu_ps, AF.Square))
    S.op("vector", lambda e: e.tensor_tensor(var_ps, var_ps, musq, ALU.subtract))
    S.op("scalar", lambda e: e.activation(var_ps, var_ps, AF.Ln))
    S.op("vector", lambda e: e.tensor_scalar(var_ps, var_ps, -0.5, None, ALU.mult))
    S.op("scalar", lambda e: e.activation(rstd_bf, var_ps, AF.Exp))
    S.op("vector", lambda e: e.scalar_tensor_tensor(
        nmr_bf, mu_ps, -1.0, rstd_bf, ALU.mult, ALU.mult))
    for c in range(KD):
        su_c = su_row[0:1, c * 128:(c + 1) * 128]
        for i in range(nt):
            pa = ctx.pA[:, (c % 2) * 512:(c % 2) * 512 + 512]
            pb = ctx.pB[:, (c % 2) * 512:(c % 2) * 512 + 512]
            S.op("tensor", lambda e, pa=pa, su_c=su_c, i=i: e.matmul(
                pa, su_c, rstd_bf[0:1, i * 512:(i + 1) * 512], start=True, stop=True),
                tag="ln_abc")
            S.op("tensor", lambda e, pb=pb, su_c=su_c, i=i: e.matmul(
                pb, su_c, nmr_bf[0:1, i * 512:(i + 1) * 512], start=True, stop=True),
                tag="ln_bbc")
            ot = out_region[:, out_off + c * tw + i * 512: out_off + c * tw + (i + 1) * 512]
            S.op("vector", lambda e, c=c, i=i, ot=ot, pa=pa: e.tensor_tensor(
                ot, xt(c, i), pa, ALU.mult), tag="ln_za")
            S.op("vector", lambda e, ot=ot, pb=pb: e.tensor_tensor(
                ot, ot, pb, ALU.add), tag="ln_zb")


def mm_op(ctx, base, fn, **kw):
    need_nop = getattr(ctx, "_mm_base", None) is not None and ctx._mm_base != base
    ctx._mm_base = base
    if need_nop:
        fn2 = lambda e: (e.nop(), fn(e))[1]
    else:
        fn2 = fn
    return ctx.sched.op("tensor", fn2, **kw)


def emit_ones_row(ctx, off, w):
    S = ctx.sched
    row = t_row_bf(ctx, off, w)
    S.op("vector", lambda e: e.memset(row, 1.0))
    return row


def emit_fwht_combine(ctx, tiles):
    S = ctx.sched
    n = len(tiles)
    barrier = [S.tail("main"), None]
    h = 1
    while h < n:
        first = {"vector": True, "gpsimd": True}
        last = {"vector": None, "gpsimd": None}
        idx = 0
        for base in range(0, n, 2 * h):
            for j in range(h):
                a = tiles[base + j]
                b = tiles[base + j + h]
                eng = "vector" if idx % 2 == 0 else "gpsimd"
                idx += 1
                w0 = tuple(t for t in barrier if t is not None) if first[eng] else ()
                first[eng] = False
                if eng == "vector":
                    ta = S.op(eng, lambda e, a=a, b=b: e.tensor_tensor(a, a, b, ALU.add),
                              waits=w0, nosync=True, tag="fw_a")
                    last[eng] = S.op(eng, lambda e, a=a, b=b: e.scalar_tensor_tensor(
                        b, b, -2.0, a, ALU.mult, ALU.add),
                        waits=(ta,), nosync=True, tag="fw_b")
                else:
                    tmp = ctx.trans[:, 2048:2560]
                    tt = S.op(eng, lambda e, a=a, b=b, tmp=tmp: e.tensor_tensor(
                        tmp, a, b, ALU.subtract), waits=w0, nosync=True, tag="fw_s")
                    ta = S.op(eng, lambda e, a=a, b=b: e.tensor_tensor(
                        a, a, b, ALU.add), waits=(tt,), nosync=True, tag="fw_a")
                    last[eng] = S.op(eng, lambda e, b=b, tmp=tmp: e.tensor_copy(
                        b, tmp), waits=(ta,), nosync=True, tag="fw_b")
        barrier = [last["vector"], last["gpsimd"]]
        h *= 2
    S.op("vector", lambda e: e.nop(), waits=tuple(t for t in barrier if t is not None),
         tag="fw_end")


H16S = make_hadamard(16)


def emit_fwht_folded(ctx, tiles, scratch):
    S = ctx.sched
    n = len(tiles)
    ngroups = n // 16
    in_tok = S.tail("main")
    psum_free = [None, None, None, None]
    out_loc = [None] * n
    ev_last = None
    prev_sign = [None]
    for g in range(ngroups):
        for bp in range(16):
            o = g * 16 + bp
            dst = scratch[bp] if g == 0 else tiles[(g - 1) * 16 + bp]
            out_loc[o] = dst
            s4 = o % 4
            pa = ctx.pA if s4 % 2 == 0 else ctx.pB
            ps = pa[:, (s4 // 2) * 512:(s4 // 2) * 512 + 512]
            border = sorted(range(16), key=lambda b: -H16S[bp, b])
            for bi, b in enumerate(border):
                w0 = (psum_free[s4], in_tok if o < 4 else None) if bi == 0 else ()
                hmat = ctx.h128 if H16S[bp, b] > 0 else ctx.h128n
                tok = S.op("tensor", lambda e, hmat=hmat, ps=ps, bi=bi, src=tiles[g * 16 + b]: e.matmul(
                    ps, hmat[:, :], src, start=(bi == 0), stop=(bi == 15)),
                    waits=w0, stream="mm", tag="fwf_mm")
            psum_free[s4] = ev_last = S.op(
                "scalar", lambda e, dst=dst, ps=ps: e.copy(dst, ps),
                waits=(tok,), nosync=True, tag="fwf_ev")
    S.op("vector", lambda e: e.nop(), waits=(ev_last,), tag="fwf_end")
    if ngroups == 1:
        return [out_loc[b] for b in range(16)]
    barrier = [S.tail("main"), None]
    h = 1
    idx = 0
    while h < ngroups:
        first = {"vector": True, "gpsimd": True}
        last = {"vector": None, "gpsimd": None}
        for base in range(0, ngroups, 2 * h):
            for j in range(h):
                for bp in range(16):
                    a = out_loc[(base + j) * 16 + bp]
                    b2 = out_loc[(base + j + h) * 16 + bp]
                    eng = "vector" if idx % 2 == 0 else "gpsimd"
                    idx += 1
                    w0 = tuple(t for t in barrier if t is not None) if first[eng] else ()
                    first[eng] = False
                    if eng == "vector":
                        ta = S.op(eng, lambda e, a=a, b2=b2: e.tensor_tensor(
                            a, a, b2, ALU.add), waits=w0, nosync=True, tag="fw_a")
                        last[eng] = S.op(eng, lambda e, a=a, b2=b2: e.scalar_tensor_tensor(
                            b2, b2, -2.0, a, ALU.mult, ALU.add),
                            waits=(ta,), nosync=True, tag="fw_b")
                    else:
                        tmp = ctx.trans[:, 2048:2560]
                        pw = w0 + ((last[eng],) if last[eng] is not None else ())
                        tt2 = S.op(eng, lambda e, a=a, b2=b2, tmp=tmp: e.tensor_tensor(
                            tmp, a, b2, ALU.subtract), waits=pw, nosync=True, tag="fw_s")
                        ta = S.op(eng, lambda e, a=a, b2=b2: e.tensor_tensor(
                            a, a, b2, ALU.add), waits=(tt2,), nosync=True, tag="fw_a")
                        last[eng] = S.op(eng, lambda e, b2=b2, tmp=tmp: e.tensor_copy(
                            b2, tmp), waits=(ta,), nosync=True, tag="fw_b")
        barrier = [last["vector"], last["gpsimd"]]
        h *= 2
    S.op("vector", lambda e: e.nop(), waits=tuple(t for t in barrier if t is not None),
         tag="fw_end")
    return out_loc


def emit_fwht_combine_wide(ctx):
    S = ctx.sched
    for h in (1, 2, 4, 8, 16):
        w = h * 512
        for reg in (ctx.rB, ctx.rC):
            va = reg[:, 0:16384].rearrange(
                "p (m two hh) -> p m two hh", two=2, hh=w)[:, :, 0, :]
            vb = reg[:, 0:16384].rearrange(
                "p (m two hh) -> p m two hh", two=2, hh=w)[:, :, 1, :]
            S.op("vector", lambda e, va=va, vb=vb: e.tensor_tensor(
                va, va, vb, ALU.add), tag="fww_a")
            S.op("vector", lambda e, va=va, vb=vb: e.scalar_tensor_tensor(
                vb, vb, -2.0, va, ALU.mult, ALU.add), tag="fww_b")
    a32 = ctx.rB[:, 0:16384]
    b32 = ctx.rC[:, 0:16384]
    S.op("vector", lambda e: e.tensor_tensor(a32, a32, b32, ALU.add), tag="fww_a")
    S.op("vector", lambda e: e.scalar_tensor_tensor(
        b32, b32, -2.0, a32, ALU.mult, ALU.add), tag="fww_b")


def emit_fwht_h128(ctx, tiles, scale=None):
    S = ctx.sched
    in_tok = S.tail("main")
    psum_free = [None, None, None, None]
    evs = []
    for idx, tile in enumerate(tiles):
        tw = tile.shape[1]
        s4 = idx % 4
        pa = ctx.pA if s4 % 2 == 0 else ctx.pB
        ps = pa[:, (s4 // 2) * 512:(s4 // 2) * 512 + tw]
        tok = S.op("tensor", lambda e, tile=tile, tw=tw, ps=ps: e.matmul(
            ps, ctx.h128[:, :], tile[:, 0:tw], start=True, stop=True),
            waits=(psum_free[s4], in_tok if idx < 4 else None),
            stream="mm", tag="fw_h128")
        psum_free[s4] = S.op("scalar", lambda e, tile=tile, tw=tw, ps=ps: e.copy(
            tile[:, 0:tw], ps), waits=(tok,), nosync=True, tag="fw_ev")
    S.op("vector", lambda e: e.nop(),
         waits=tuple(t for t in psum_free if t is not None), tag="fwh_end")


def _decl_l0_ffn(nc, ctx):
    ctx.d_wup = nc.dram_tensor("l0f_wup", [64, 128, 2048], F32, kind="ExternalInput").ap()
    ctx.d_wgate = nc.dram_tensor("l0f_wgate", [64, 128, 2048], F32, kind="ExternalInput").ap()
    ctx.d_wdown = nc.dram_tensor("l0f_wdown", [16, 128, 8192], F32, kind="ExternalInput").ap()
    ctx.d_suug = nc.dram_tensor("l0f_suug", [1, 2048], F32, kind="ExternalInput").ap()
    ctx.d_sudp = nc.dram_tensor("l0f_sudp", [128, 64], F32, kind="ExternalInput").ap()
    ctx.d_wsu_up = nc.dram_tensor("l0f_wsu_up", [128, 64], F32, kind="ExternalInput").ap()
    ctx.d_wsu_g = nc.dram_tensor("l0f_wsu_g", [128, 64], F32, kind="ExternalInput").ap()
    ctx.d_wsdp = nc.dram_tensor("l0f_wsdp", [128, 16], F32, kind="ExternalInput").ap()


def emit_l0_ffn(ctx):
    S = ctx.sched

    su_row = t_row_bf(ctx, 0, 2048)
    S.dma("gpsimd", lambda e: e.dma_start(out=su_row, in_=ctx.d_suug[:, :]))
    wsg64 = ctx.sm1[:, 0:64]
    comb_up = ctx.sm1[:, 64:128]
    wdp64 = ctx.sm1[:, 128:144]
    sudp = ctx.sm2[:, 0:64]
    wsuup = ctx.sm2[:, 64:128]
    S.dma("sync", lambda e: e.dma_start(out=sudp, in_=ctx.d_sudp[:, :]))
    S.dma("sync", lambda e: e.dma_start(out=wsuup, in_=ctx.d_wsu_up[:, :]))
    S.dma("sync", lambda e: e.dma_start(out=ctx.sm2[:, 128:192], in_=ctx.d_wsu_g[:, :]))
    S.dma("sync", lambda e: e.dma_start(out=ctx.sm2[:, 192:208], in_=ctx.d_wsdp[:, :]))
    S.op("vector", lambda e: e.tensor_scalar(
        wsg64, ctx.sm2[:, 128:192], 1.0 / np.sqrt(2048.0), None, ALU.mult))
    S.op("vector", lambda e: e.scalar_tensor_tensor(
        comb_up, wsuup, 1.0 / np.sqrt(2048.0), sudp, ALU.mult, ALU.mult))
    S.op("vector", lambda e: e.tensor_scalar(
        wdp64, ctx.sm2[:, 192:208], 1.0 / np.sqrt(8192.0), None, ALU.mult))

    wreg_free = getattr(ctx, "wreg_free", [None, None])
    for tp in range(2):
        tb = tp * 512
        emit_ln(ctx, su_row, ctx.rA, 0, tw=512, tbase=tb)
        ztiles = [ctx.rA[:, c * 512:(c + 1) * 512] for c in range(KD)]
        scratch = [ctx.rA[:, 8192 + c * 512:8192 + (c + 1) * 512] for c in range(KD)]
        if getattr(ctx, "old2048", False):
            emit_fwht_combine(ctx, ztiles)
            emit_fwht_h128(ctx, ztiles)
        else:
            ztiles = emit_fwht_folded(ctx, ztiles, scratch)


        in_tok = S.tail("main")
        last_use = [wreg_free[0], wreg_free[0]]
        ld_tok = [None, None]
        psum_free = [None, None]
        act = []
        for o in range(64):
            reg = ctx.rB if o < 32 else ctx.rC
            act.append(reg[:, (o % 32) * 512:(o % 32) * 512 + 512])
        for o in range(64):
            s = o % 2
            wup = ctx.wreg[:, s * 4096:s * 4096 + 2048]
            wgt = ctx.wreg[:, s * 4096 + 2048:s * 4096 + 4096]
            S.dma("gpsimd", lambda e, o=o, wup=wup: e.dma_start(
                out=wup, in_=ctx.d_wup[o, :, :]), waits=(last_use[s],),
                stream="wl%d" % s, sem="wsl%d" % s, tag="wld_up")
            ld_tok[s] = S.dma("gpsimd", lambda e, o=o, wgt=wgt: e.dma_start(
                out=wgt, in_=ctx.d_wgate[o, :, :]),
                stream="wl%d" % s, sem="wsl%d" % s, tag="wld_gt")
            pu = ctx.pA[:, s * 1024:s * 1024 + 512]
            pg = ctx.pA[:, s * 1024 + 512:s * 1024 + 1024]
            for k in range(KD):
                w0 = (ld_tok[s], psum_free[s], in_tok if o < 2 else None) if k == 0 else ()
                tok_up = S.op("tensor", lambda e, k=k, wup=wup, pu=pu, zt=ztiles[k]: e.matmul(
                    pu, wup[:, k * 128:(k + 1) * 128], zt,
                    start=(k == 0), stop=(k == KD - 1)),
                    waits=w0, stream="mm", tag="up_mm")
            for k in range(KD):
                tok = S.op("tensor", lambda e, k=k, wgt=wgt, pg=pg, zt=ztiles[k]: e.matmul(
                    pg, wgt[:, k * 128:(k + 1) * 128], zt,
                    start=(k == 0), stop=(k == KD - 1)), stream="mm", tag="gt_mm")
            last_use[s] = tok
            S.op("scalar", lambda e, o=o, pg=pg: e.activation(
                ctx.trans[:, 2048:2560], pg, AF.Silu, scale=wsg64[:, o:o + 1]),
                waits=(tok,), tag="silu")
            psum_free[s] = S.op("vector", lambda e, o=o, pu=pu, at=act[o]: e.scalar_tensor_tensor(
                at, ctx.trans[:, 2048:2560], comb_up[:, o:o + 1], pu,
                ALU.mult, ALU.mult), waits=(tok_up,), tag="actmul")

        if getattr(ctx, "dbg_fold", False) and tp == 0:
            for c in range(8):
                emit_dbg(ctx, act[c], 128, 512)
        emit_fwht_combine_wide(ctx)
        emit_fwht_h128(ctx, act)


        in_tok = S.tail("main")
        last_use_d = [None, None]
        ld_tok_d = [None, None]
        psum_free_d = [None, None]
        for o in range(KD):
            s = o % 2
            wd = ctx.wreg[:, s * 8192:s * 8192 + 8192]
            if o == 0:
                w8 = (last_use[0], last_use[1])
            elif o == 1:
                w8 = (wreg_free[1],)
            else:
                w8 = (last_use_d[s],)
            ld_tok_d[s] = S.dma("gpsimd", lambda e, o=o, wd=wd: e.dma_start(
                out=wd, in_=ctx.d_wdown[o, :, :]), waits=w8,
                stream="wd%d" % s, sem="wsd%d" % s, tag="wld_dn")
            pd = ctx.pB[:, s * 512:s * 512 + 512]
            for k in range(64):
                w0 = (ld_tok_d[s], psum_free_d[s], in_tok if o < 2 else None) if k == 0 else ()
                tok = S.op("tensor", lambda e, k=k, wd=wd, pd=pd, at=act[k]: e.matmul(
                    pd, wd[:, k * 128:(k + 1) * 128], at,
                    start=(k == 0), stop=(k == 63)),
                    waits=w0, stream="mm", tag="dn_mm")
            last_use_d[s] = tok
            xt = x_tile(ctx, o, (tb, tb + 512))
            psum_free_d[s] = S.op("vector", lambda e, o=o, pd=pd, xt=xt: e.scalar_tensor_tensor(
                xt, pd, wdp64[:, o:o + 1], xt, ALU.mult, ALU.add),
                waits=(tok,), tag="dn_res")
        wreg_free = [last_use_d[0], last_use_d[1]]
    ctx.wreg_free = wreg_free


def _decl_l0_attn(nc, ctx):
    ctx.d_wq = nc.dram_tensor("l0a_wq", [16, 128, 2048], F32, kind="ExternalInput").ap()
    ctx.d_wk = nc.dram_tensor("l0a_wk", [16, 128, 2048], F32, kind="ExternalInput").ap()
    ctx.d_wv = nc.dram_tensor("l0a_wv", [16, 128, 2048], F32, kind="ExternalInput").ap()
    ctx.d_wo = nc.dram_tensor("l0a_wo", [16, 128, 2048], F32, kind="ExternalInput").ap()
    ctx.d_wfq = nc.dram_tensor("l0a_wfq", [32, 64, 64], F32, kind="ExternalInput").ap()
    ctx.d_wfk = nc.dram_tensor("l0a_wfk", [32, 64, 64], F32, kind="ExternalInput").ap()
    ctx.d_blk = nc.dram_tensor("l0a_blk", [128, 4128], BF16)
    ctx.d_ag = nc.dram_tensor("l0a_ag", [NCORES * 128, 4128], BF16)


L0ATTN_STAGE = 5
L0ATTN_SUB = 9


def emit_l0_attn(ctx):
    S = ctx.sched
    wreg_free = getattr(ctx, "wreg_free", [None, None])

    S.dma("gpsimd", lambda e: e.dma_start(out=ctx.mask[:, :], in_=ctx.d_mask[:, :]))

    su_row = emit_ones_row(ctx, 0, 2048)
    emit_ln(ctx, su_row, ctx.rA, 0, tw=T)
    ztiles = [ctx.rA[:, c * T:(c + 1) * T] for c in range(KD)]

    vtmp = ctx.trans[:, 2048:3072]
    in_tok = S.tail("main")
    last_use = [wreg_free[0], wreg_free[0]]
    ld_tok = [None, None]
    psum_free = [None, None, None, None]
    pBb8 = ctx.pB[:, 0:512].bitcast(BF16)
    trev_tok = None
    ev_toks = []
    for which, wdram in (("q", ctx.d_wq), ("k", ctx.d_wk), ("v", ctx.d_wv)):
        for o in range(KD):
            s = o % 2
            wsl = ctx.wreg[:, s * 2048:s * 2048 + 2048]
            ld_tok[s] = S.dma("gpsimd", lambda e, o=o, wsl=wsl, wdram=wdram: e.dma_start(
                out=wsl, in_=wdram[o, :, :]), waits=(last_use[s],),
                stream="qv%d" % s, sem="qvs%d" % s, tag="wld_" + which)
            ev2 = []
            for tt in range(2):
                sl = s * 2 + tt
                ps = ctx.pA[:, sl * 512:sl * 512 + 512]
                for k in range(KD):
                    w0 = (ld_tok[s], psum_free[sl],
                          in_tok if (which == "q" and o < 2) else None) if k == 0 else ()
                    tok = S.op("tensor", lambda e, k=k, wsl=wsl, ps=ps, tt=tt: e.matmul(
                        ps, wsl[:, k * 128:(k + 1) * 128],
                        ztiles[k][:, tt * 512:(tt + 1) * 512],
                        start=(k == 0), stop=(k == KD - 1)),
                        waits=w0, stream="mm", tag=which + "_mm")
                if which == "q":
                    dst = ctx.rB[:, o * T + tt * 512:o * T + (tt + 1) * 512]
                elif which == "k":
                    dst = ctx.rC[:, o * T + tt * 512:o * T + (tt + 1) * 512]
                else:
                    dst = vtmp[:, tt * 512:(tt + 1) * 512]
                wv = (tok, trev_tok) if which == "v" else (tok,)
                psum_free[sl] = S.op("scalar", lambda e, dst=dst, ps=ps: e.copy(
                    dst, ps), waits=wv, nosync=True, tag=which + "_ev")
                ev2.append(psum_free[sl])
            last_use[s] = tok
            if which == "v":
                ttoks = []
                for m in range(8):
                    pt = pBb8[:, m * 128:(m + 1) * 128]
                    ttoks.append(S.op("tensor", lambda e, m=m, pt=pt: e.transpose(
                        pt, vtmp[:, m * 128:(m + 1) * 128],
                        ctx.identb[:, :]),
                        waits=(ev2[0], ev2[1], trev_tok) if m == 0 else (),
                        stream="mm", nosync=True, tag="v_tr"))
                trev_tok = S.op("vector", lambda e, o=o: e.tensor_copy(
                    ctx.rD[:, :].rearrange("p (m t) -> p m t", t=2048)[:, :, o * 128:(o + 1) * 128],
                    pBb8.rearrange("p (m c) -> p m c", c=128)),
                    waits=(ttoks[-1],), nosync=True, tag="v_trev")
                ev_toks.append(trev_tok)
            else:
                ev_toks.extend(ev2)
    S.op("vector", lambda e: e.nop(),
         waits=(max(t for t in psum_free if t is not None), trev_tok),
         tag="qkv_anchor")

    if L0ATTN_STAGE < 2:
        ctx.wreg_free = [last_use[0], last_use[1]]
        return
    wfq_sb = ctx.wreg[:, 0:2048]
    wfk_sb = ctx.wreg[:, 2048:4096]
    ms_tok = S.op("vector", lambda e: e.memset(ctx.wreg[:, 0:4096], 0.0),
                  waits=(last_use[0], last_use[1]), tag="fm_ms")
    for wsb, wdram, tagn in ((wfq_sb, ctx.d_wfq, "wld_fq"), (wfk_sb, ctx.d_wfk, "wld_fk")):
        for half in range(2):
            fm_ld = S.dma("gpsimd", lambda e, wsb=wsb, wdram=wdram, half=half: e.dma_start(
                out=wsb[half * 64:half * 64 + 64, :].rearrange(
                    "d (c g) -> d c g", g=128)[:, :, half * 64:half * 64 + 64],
                in_=wdram.rearrange("(c two) d f -> two d c f", two=2)[half]),
                waits=(ms_tok,) if (half == 0 and wsb is wfq_sb) else (),
                stream="qv0", sem="qvs0", tag=tagn)

    first_fm = True
    for b in range(2):
        for c in range(KD):
            ps = ctx.pA[:, (c % 4) * 512:(c % 4) * 512 + 512]
            mm_op(ctx, 0, lambda e, b=b, c=c, ps=ps: e.matmul(
                ps, wfq_sb[:, c * 128:(c + 1) * 128],
                ctx.rB[:, c * T + b * 512:c * T + b * 512 + 512],
                start=True, stop=True),
                waits=(fm_ld,) if first_fm else (), tag="fq_mm")
            first_fm = False
            S.op("scalar", lambda e, b=b, c=c, ps=ps: e.activation(
                ctx.rB[:, c * T + b * 512:c * T + b * 512 + 512], ps, AF.Exp),
                tag="fq_exp")

    if L0ATTN_STAGE < 3:
        ctx.wreg_free = [last_use[0], last_use[1]]
        return
    fk_tm = ctx.trans[:, 0:2048]
    sums = ctx.sm2[:, 0:32]
    rcps = ctx.sm2[:, 32:64]
    for b in range(2):
        for jc in range(4):
            for c in range(KD):
                mm_op(ctx, 0, lambda e, b=b, jc=jc, c=c: e.matmul(
                    ctx.pA[:, c * 128:(c + 1) * 128],
                    ctx.rC[:, c * T + b * 512 + jc * 128:
                           c * T + b * 512 + (jc + 1) * 128],
                    wfk_sb[:, c * 128:(c + 1) * 128],
                    start=True, stop=True), tag="fk_mm")
            S.op("scalar", lambda e: e.activation(fk_tm, ctx.pA[:, :], AF.Exp),
                 tag="fk_exp")
            if L0ATTN_SUB < 2:
                continue
            S.op("vector", lambda e: e.reduce_sum(
                sums, fk_tm.rearrange("p (h f) -> p h f", h=32), axis=AX.X),
                tag="fk_sum")
            S.op("vector", lambda e: e.reciprocal(rcps, sums), tag="fk_rcp")
            S.op("vector", lambda e: e.tensor_tensor(
                fk_tm.rearrange("p (h f) -> p h f", h=32),
                fk_tm.rearrange("p (h f) -> p h f", h=32),
                rcps.unsqueeze(2).broadcast_to([128, 32, 64]), ALU.mult),
                tag="fk_nrm")
            if L0ATTN_SUB < 3:
                continue
            for hh in range(16):
                S.op("tensor", lambda e, hh=hh: e.matmul(
                    ctx.pA[:, 1024 + hh:1024 + hh + 1],
                    fk_tm[:, hh * 128:(hh + 1) * 128], ctx.onecol[:, :],
                    start=True, stop=True), tag="kp_mm")
            S.op("vector", lambda e, b=b, jc=jc: e.scalar_tensor_tensor(
                ctx.sm2[:, 64 + b * 16:80 + b * 16], ctx.pA[:, 1024:1040], 1.0,
                ctx.sm2[:, 64 + b * 16:80 + b * 16], ALU.mult, ALU.add)
                if jc > 0 else e.tensor_copy(
                ctx.sm2[:, 64 + b * 16:80 + b * 16], ctx.pA[:, 1024:1040]),
                tag="kp_acc")
            if L0ATTN_SUB < 4:
                continue
            for hh in range(16):
                pt = ctx.pA[:, (hh % 2) * 512:(hh % 2) * 512 + 64].bitcast(BF16)
                S.op("tensor", lambda e, hh=hh, pt=pt: e.transpose(
                    pt[:, 0:128], fk_tm[:, hh * 128:(hh + 1) * 128],
                    ctx.identb[:, :]), tag="fk_tr")
                S.op("vector", lambda e, hh=hh, b=b, jc=jc, pt=pt: e.tensor_copy(
                    ctx.rC[:, hh * T + b * 512 + jc * 128:hh * T + b * 512 + (jc + 1) * 128],
                    pt[:, 0:128]), tag="fk_trev")
            if L0ATTN_SUB < 5:
                continue
            for h in range(32):
                S.op("tensor", lambda e, b=b, jc=jc, h=h: e.matmul(
                    ctx.pB[(h % 2) * 64:(h % 2) * 64 + 64, h * 64:(h + 1) * 64],
                    fk_tm[:, h * 64:(h + 1) * 64],
                    ctx.rD[:, (b * 4 + jc) * 2048 + h * 64:(b * 4 + jc) * 2048 + (h + 1) * 64],
                    start=True, stop=True), tag="sd_mm")
            S.op("vector", lambda e, b=b, jc=jc: e.scalar_tensor_tensor(
                ctx.dsb[:, b * 2064:b * 2064 + 2048], ctx.pB[:, 0:2048], 1.0,
                ctx.dsb[:, b * 2064:b * 2064 + 2048], ALU.mult, ALU.add)
                if jc > 0 else e.tensor_copy(
                ctx.dsb[:, b * 2064:b * 2064 + 2048], ctx.pB[:, 0:2048]),
                tag="sd_acc")
        if L0ATTN_SUB < 5:
            continue
        S.op("vector", lambda e, b=b: e.tensor_copy(
            ctx.dsb[:, b * 2064 + 2048:b * 2064 + 2064],
            ctx.sm2[:, 64 + b * 16:80 + b * 16]), tag="kp_exp")

    if L0ATTN_STAGE < 4:
        ctx.wreg_free = [last_use[0], last_use[1]]
        return
    S.dma("sync", lambda e: e.dma_start(out=ctx.d_blk[:, :], in_=ctx.dsb[:, :]),
          tag="blk_out")
    S.op("gpsimd", lambda e: e.collective_compute(
        "AllGather", ALU.bypass, replica_groups=[list(range(NCORES))],
        ins=[ctx.d_blk.ap().opt()], outs=[ctx.d_ag.ap().opt()]), sem="coll", tag="ag")
    for half in range(2):
        S.op("vector", lambda e: e.memset(ctx.accf[:, :], 0.0), tag="accz")
        for r in range(NCORES):
            S.dma("sync", lambda e, r=r, half=half: e.dma_start(
                out=ctx.dbuf[:, 0:2064],
                in_=ctx.d_ag[r * 128:(r + 1) * 128, half * 2064:(half + 1) * 2064]),
                tag="ag_ld")
            S.op("vector", lambda e, r=r: e.scalar_tensor_tensor(
                ctx.accf[:, :], ctx.dbuf[:, 0:2064],
                ctx.pmask[:, r:r + 1], ctx.accf[:, :],
                ALU.mult, ALU.add), tag="ag_acc")
        S.op("vector", lambda e, half=half: e.tensor_copy(
            ctx.acc[:, half * 2064:(half + 1) * 2064], ctx.accf[:, 0:2064]),
            tag="acc_bf")

    if getattr(ctx, "dbg_acc", False):
        emit_dbg(ctx, ctx.acc[:, 0:4096], 128, 4096)
    if L0ATTN_STAGE < 5:
        ctx.wreg_free = [last_use[0], last_use[1]]
        return
    den_sb = ctx.transf[:, 1664:1668]
    rec_sb = ctx.transf[:, 1668:1672]
    am_off = [0, 512, 896, 1152]
    for b in range(2):
        for h in range(32):
            pb_ = (h % 2) * 64
            cch = (h // 2) * T + b * 512
            spref = ctx.acc[pb_:pb_ + 64, b * 2064 + h * 64:b * 2064 + h * 64 + 64]
            kprev = ctx.acc[pb_:pb_ + 64, b * 2064 + 2048 + (h // 2):b * 2064 + 2048 + (h // 2) + 1]
            fq_ap = ctx.rB[pb_:pb_ + 64, cch:cch + 512]
            fk_ap = ctx.rC[pb_:pb_ + 64, cch:cch + 512]
            for jc in range(4):
                nw = 512 - jc * 128
                aps = ctx.pA[:, (jc % 2) * 512:(jc % 2) * 512 + nw]
                mm_op(ctx, pb_, lambda e, jc=jc, nw=nw, aps=aps, fk_ap=fk_ap, fq_ap=fq_ap: e.matmul(
                    aps, fk_ap[:, jc * 128:(jc + 1) * 128], fq_ap[:, jc * 128:512],
                    start=True, stop=True), tag="ap_mm")
                am = ctx.trans[:, 2048 + am_off[jc]:2048 + am_off[jc] + nw]
                S.op("vector", lambda e, jc=jc, aps=aps, am=am: e.tensor_tensor(
                    am[:, 0:128], aps[:, 0:128], ctx.mask[:, 0:128], ALU.mult),
                    tag="am_diag")
                if nw > 128:
                    S.op("vector", lambda e, jc=jc, nw=nw, aps=aps, am=am: e.tensor_copy(
                        am[:, 128:nw], aps[:, 128:nw]), tag="am_off")
            for ic in range(4):
                yps = ctx.pB[:, ic * 64:(ic + 1) * 64]
                first = True
                for jc in range(ic + 1):
                    am = ctx.trans[:, 2048 + am_off[jc] + (ic - jc) * 128:
                                   2048 + am_off[jc] + (ic - jc + 1) * 128]
                    mm_op(ctx, 0, lambda e, am=am, b=b, jc=jc, h=h, yps=yps, first=first: e.matmul(
                        yps, am,
                        ctx.rD[:, (b * 4 + jc) * 2048 + h * 64:(b * 4 + jc) * 2048 + (h + 1) * 64],
                        start=first, stop=False), tag="y_intra")
                    first = False
                mm_op(ctx, pb_, lambda e, ic=ic, yps=yps, fq_ap=fq_ap, spref=spref: e.matmul(
                    yps, fq_ap[:, ic * 128:(ic + 1) * 128], spref,
                    start=False, stop=True), tag="y_inter")
                dps = ctx.pB[:, 256 + ic:256 + ic + 1]
                mm_op(ctx, pb_, lambda e, ic=ic, dps=dps, fq_ap=fq_ap, kprev=kprev: e.matmul(
                    dps, fq_ap[:, ic * 128:(ic + 1) * 128], kprev,
                    start=True, stop=False), tag="dn_kp")
                for jc in range(ic + 1):
                    am = ctx.trans[:, 2048 + am_off[jc] + (ic - jc) * 128:
                                   2048 + am_off[jc] + (ic - jc + 1) * 128]
                    mm_op(ctx, 0, lambda e, am=am, dps=dps, jc=jc, ic=ic: e.matmul(
                        dps, am, ctx.onecol[:, :],
                        start=False, stop=(jc == ic)), tag="dn_am")
            S.op("vector", lambda e: e.tensor_copy(den_sb, ctx.pB[:, 256:260]),
                 tag="dn_ev")
            S.op("vector", lambda e: e.reciprocal(rec_sb, den_sb), tag="dn_rc")
            for ic in range(4):
                S.op("vector", lambda e, ic=ic, b=b, h=h: e.tensor_scalar(
                    ctx.rA[:, (b * 4 + ic) * 2048 + h * 64:(b * 4 + ic) * 2048 + (h + 1) * 64],
                    ctx.pB[:, ic * 64:(ic + 1) * 64], rec_sb[:, ic:ic + 1], None,
                    ALU.mult), tag="y_div")

    for m in range(8):
        for c in range(KD):
            pt = ctx.pA[:, (c % 2) * 512:(c % 2) * 512 + 64].bitcast(BF16)
            mm_op(ctx, 0, lambda e, m=m, c=c, pt=pt: e.transpose(
                pt[:, 0:128], ctx.rA[:, m * 2048 + c * 128:m * 2048 + (c + 1) * 128],
                ctx.identb[:, :]), tag="y_tr")
            tcol = (m // 4) * 512 + (m % 4) * 128
            S.op("vector", lambda e, m=m, c=c, pt=pt, tcol=tcol: e.tensor_copy(
                ctx.rB[:, c * T + tcol:c * T + tcol + 128], pt[:, 0:128]),
                tag="y_trev")

    in_tok5 = S.tail("main")
    last2 = [None, None]
    ld2 = [None, None]
    psum_free5 = [None, None, None, None]
    for o in range(KD):
        s = o % 2
        wsl = ctx.wreg[:, s * 2048:s * 2048 + 2048]
        w8 = (in_tok5,) if o < 2 else (last2[s],)
        ld2[s] = S.dma("gpsimd", lambda e, o=o, wsl=wsl: e.dma_start(
            out=wsl, in_=ctx.d_wo[o, :, :]), waits=w8,
            stream="qv%d" % s, sem="qvs%d" % s, tag="wld_wo")
        for tt in range(2):
            sl = s * 2 + tt
            ps = ctx.pA[:, sl * 512:sl * 512 + 512]
            for kc in range(KD):
                w0 = (ld2[s], psum_free5[sl],
                      in_tok5 if o < 2 else None) if kc == 0 else ()
                tok = S.op("tensor", lambda e, kc=kc, wsl=wsl, ps=ps, tt=tt: e.matmul(
                    ps, wsl[:, kc * 128:(kc + 1) * 128],
                    ctx.rB[:, kc * T + tt * 512:kc * T + (tt + 1) * 512],
                    start=(kc == 0), stop=(kc == KD - 1)),
                    waits=w0, stream="mm", tag="wo_mm")
            xt = x_tile(ctx, o, (tt * 512, (tt + 1) * 512))
            psum_free5[sl] = S.op("vector", lambda e, ps=ps, xt=xt: e.tensor_tensor(
                xt, xt, ps, ALU.add), waits=(tok,), tag="wo_res")
        last2[s] = tok
    ctx.wreg_free = [last2[0], last2[1]]


def _decl_l1_attn(nc, ctx):
    ctx.d_wqkv = nc.dram_tensor("l1a_wqkv", [24, 128, 2048], F32, kind="ExternalInput").ap()
    ctx.d_wproj = nc.dram_tensor("l1a_wproj", [16, 128, 2048], F32, kind="ExternalInput").ap()


def emit_l1_attn(ctx):
    S = ctx.sched
    wreg_free = getattr(ctx, "wreg_free", [None, None])

    su_row = emit_ones_row(ctx, 0, 2048)
    emit_ln(ctx, su_row, ctx.rA, 0, tw=T)
    ztiles = [ctx.rA[:, c * T:(c + 1) * T] for c in range(KD)]

    def qkv_tile(o):
        reg = ctx.rB if o < 16 else ctx.rC
        return reg[:, (o % 16) * T:(o % 16) * T + T]

    in_tok = S.tail("main")
    last_use = [wreg_free[0], wreg_free[0]]
    ld_tok = [None, None]
    psum_free = [None, None, None, None]
    for o in range(24):
        s = o % 2
        wsl = ctx.wreg[:, s * 2048:s * 2048 + 2048]
        ld_tok[s] = S.dma("gpsimd", lambda e, o=o, wsl=wsl: e.dma_start(
            out=wsl, in_=ctx.d_wqkv[o, :, :]), waits=(last_use[s],),
            stream="qk%d" % s, sem="qks%d" % s, tag="wld_qkv")
        for tt in range(2):
            sl = s * 2 + tt
            ps = ctx.pA[:, sl * 512:sl * 512 + 512]
            for k in range(KD):
                w0 = (ld_tok[s], psum_free[sl],
                      in_tok if o < 2 else None) if k == 0 else ()
                tok = S.op("tensor", lambda e, k=k, wsl=wsl, ps=ps, tt=tt: e.matmul(
                    ps, wsl[:, k * 128:(k + 1) * 128],
                    ztiles[k][:, tt * 512:(tt + 1) * 512],
                    start=(k == 0), stop=(k == KD - 1)),
                    waits=w0, stream="mm", tag="qkv_mm")
            psum_free[sl] = S.op("scalar", lambda e, o=o, tt=tt, ps=ps: e.copy(
                qkv_tile(o)[:, tt * 512:(tt + 1) * 512], ps),
                waits=(tok,), nosync=True, tag="qkv_ev")
        last_use[s] = tok
    S.op("vector", lambda e: e.nop(),
         waits=tuple(t for t in psum_free if t is not None), tag="qkv_end")

    dv = ctx.trans[:, 2048:2560]
    prod = ctx.trans[:, 2560:3072]
    rDf = ctx.rD[:, :].bitcast(F32)
    s0_sb = rDf[0:1, 4096:6144]
    sd_sb = rDf[0:1, 6144:8192]
    p1row = ctx.rD[0:1, 8192:10240]
    sc = 1.0 / np.sqrt(128.0)
    for g in range(4):
        kt = qkv_tile(g * 6 + 4)
        vt = qkv_tile(g * 6 + 5)
        S.op("vector", lambda e, vt=vt: e.tensor_tensor(
            dv, vt[:, 512:1024], vt[:, 0:512], ALU.subtract), tag="dv")
        for qi in range(4):
            qt = qkv_tile(g * 6 + qi)
            S.op("vector", lambda e, qt=qt, kt=kt: e.tensor_tensor(
                prod, qt[:, 512:1024], kt[:, 0:512], ALU.mult), tag="p0")
            S.op("tensor", lambda e, qi=qi: e.matmul(
                ctx.pA[0:1, qi * 512:qi * 512 + 512], ctx.onecol[:, :], prod,
                start=True, stop=True), tag="s0mm")
            S.op("vector", lambda e, qt=qt, kt=kt: e.tensor_tensor(
                prod, qt[:, 512:1024], kt[:, 512:1024], ALU.mult), tag="p1")
            S.op("tensor", lambda e, qi=qi: e.matmul(
                ctx.pA[64:65, qi * 512:qi * 512 + 512], ctx.onecol[:, :], prod,
                start=True, stop=True), tag="s1mm")
        S.op("vector", lambda e: e.tensor_scalar(
            s0_sb, ctx.pA[0:1, 0:2048], sc, None, ALU.mult), tag="s0ev")
        S.op("vector", lambda e: e.scalar_tensor_tensor(
            sd_sb, ctx.pA[64:65, 0:2048], sc, s0_sb,
            ALU.mult, ALU.subtract), tag="sdiff")
        S.op("scalar", lambda e: e.activation(
            p1row, sd_sb, AF.Sigmoid), tag="sig")
        for qi in range(4):
            h = g * 4 + qi
            pbc = ctx.pB[:, qi * 512:qi * 512 + 512]
            S.op("tensor", lambda e, qi=qi, pbc=pbc: e.matmul(
                pbc, ctx.onerow[:, :], p1row[0:1, qi * 512:qi * 512 + 512],
                start=True, stop=True), tag="pbc")
            S.op("vector", lambda e, pbc=pbc: e.tensor_tensor(
                prod, pbc, dv, ALU.mult), tag="odv")
            ot = ctx.rD[:, h * 512:(h + 1) * 512]
            S.op("vector", lambda e, ot=ot, vt=vt: e.tensor_tensor(
                ot, prod, vt[:, 0:512], ALU.add), tag="oadd")

    in_tok2 = S.tail("main")
    last2 = [last_use[0], last_use[1]]
    ld2 = [None, None]
    psum_free2 = [None, None, None, None]
    for o in range(KD):
        s = o % 2
        wsl = ctx.wreg[:, s * 2048:s * 2048 + 2048]
        ld2[s] = S.dma("gpsimd", lambda e, o=o, wsl=wsl: e.dma_start(
            out=wsl, in_=ctx.d_wproj[o, :, :]), waits=(last2[s],),
            stream="qk%d" % s, sem="qks%d" % s, tag="wld_prj")
        for tt in range(2):
            sl = s * 2 + tt
            ps = ctx.pA[:, sl * 512:sl * 512 + 512]
            for hc in range(KD):
                g = hc // 4
                if tt == 0:
                    rhs = qkv_tile(g * 6 + 5)[:, 0:512]
                else:
                    rhs = ctx.rD[:, hc * 512:(hc + 1) * 512]
                w0 = (ld2[s], psum_free2[sl],
                      in_tok2 if o < 2 else None) if hc == 0 else ()
                tok = S.op("tensor", lambda e, hc=hc, wsl=wsl, ps=ps, rhs=rhs: e.matmul(
                    ps, wsl[:, hc * 128:(hc + 1) * 128], rhs,
                    start=(hc == 0), stop=(hc == KD - 1)),
                    waits=w0, stream="mm", tag="prj_mm")
            xt = x_tile(ctx, o, (tt * 512, (tt + 1) * 512))
            psum_free2[sl] = S.op("vector", lambda e, ps=ps, xt=xt: e.tensor_tensor(
                xt, xt, ps, ALU.add), waits=(tok,), tag="prj_res")
        last2[s] = tok
    ctx.wreg_free = [last2[0], last2[1]]


def _decl_l1_ffn(nc, ctx):
    ctx.d_wg = nc.dram_tensor("l1f_wg", [16, 128, 8], F32, kind="ExternalInput").ap()
    ctx.d_wgu = nc.dram_tensor("l1f_wgu", [8, 16, 128, 2048], F32, kind="ExternalInput").ap()
    ctx.d_wdn = nc.dram_tensor("l1f_wdn", [8, 16, 128, 1024], F32, kind="ExternalInput").ap()
    ctx.d_wshgu = nc.dram_tensor("l1f_wshgu", [64, 128, 2048], F32, kind="ExternalInput").ap()
    ctx.d_wshdn = nc.dram_tensor("l1f_wshdn", [16, 128, 4096], F32, kind="ExternalInput").ap()


L1FFN_PARTS = ("moe", "shared")


def emit_l1_ffn(ctx):
    S = ctx.sched
    wreg_free = getattr(ctx, "wreg_free", [None, None])

    su_row = emit_ones_row(ctx, 0, 2048)
    emit_ln(ctx, su_row, ctx.rA, 0, tw=T)
    ztiles = [ctx.rA[:, c * T:(c + 1) * T] for c in range(KD)]

    do_moe = "moe" in L1FFN_PARTS
    do_shared = "shared" in L1FFN_PARTS
    wg_sb = ctx.trans[:, 2048:2176]
    S.dma("gpsimd", lambda e: e.dma_start(
        out=wg_sb.rearrange("p (k u) -> p k u", u=8),
        in_=ctx.d_wg.rearrange("k p u -> p k u")), tag="wld_rt")
    compfm = ctx.rD[0:8, 2048:3072]
    comb_tm = ctx.trans[:, 2176:2184]
    pBb = ctx.pB[:, 512:1024].bitcast(BF16)
    m1 = ctx.sm2[:, 208:209]
    m2 = ctx.sm2[:, 209:210]
    nm1 = ctx.sm2[:, 210:211]
    ds = ctx.sm2[:, 211:212]
    tmp8 = ctx.sm2[:, 212:220]
    l2 = ctx.sm2[:, 220:228]
    g8 = ctx.sm2[:, 228:236]
    sel = ctx.sm2[:, 236:244]
    rc = ctx.sm2[:, 244:245]
    for m in range(8 if do_moe else 0):
        lps = ctx.pA[:, m * 8:(m + 1) * 8]
        for k in range(KD):
            S.op("tensor", lambda e, m=m, k=k, lps=lps: e.matmul(
                lps, ztiles[k][:, m * 128:(m + 1) * 128], wg_sb[:, k * 8:(k + 1) * 8],
                start=(k == 0), stop=(k == KD - 1)), tag="rt_mm")
        S.op("vector", lambda e, lps=lps: e.reduce_max(m1, lps, axis=AX.X))
        S.op("vector", lambda e, lps=lps: e.tensor_scalar(tmp8, lps, m1, None, ALU.is_ge))
        S.op("vector", lambda e, lps=lps: e.scalar_tensor_tensor(
            l2, tmp8, -1e30, lps, ALU.mult, ALU.add))
        S.op("vector", lambda e: e.reduce_max(m2, l2, axis=AX.X))
        S.op("vector", lambda e: e.tensor_scalar(nm1, m1, -1.0, None, ALU.mult))
        S.op("scalar", lambda e, lps=lps: e.activation(g8, lps, AF.Exp, bias=nm1))
        S.op("vector", lambda e, lps=lps: e.tensor_scalar(sel, lps, m2, None, ALU.is_ge))
        S.op("vector", lambda e: e.tensor_tensor(g8, g8, sel, ALU.mult))
        S.op("vector", lambda e: e.reduce_sum(ds, g8, axis=AX.X))
        S.op("vector", lambda e: e.reciprocal(rc, ds))
        S.op("vector", lambda e: e.tensor_scalar(comb_tm, g8, rc, None, ALU.mult))
        S.op("tensor", lambda e, m=m: e.transpose(
            pBb[0:8, m * 128:(m + 1) * 128], comb_tm, ctx.identb[:, :]), tag="rt_tr")
        S.op("vector", lambda e, m=m: e.tensor_copy(
            compfm[:, m * 128:(m + 1) * 128], pBb[0:8, m * 128:(m + 1) * 128]))

    cbc_sb = ctx.trans[:, 2176:3200]
    silu_t = ctx.trans[:, 3200:4224]
    t1 = ctx.trans[:, 4224:5248]
    ce_row = ctx.rD[0:1, 1024:2048]

    def act_tile(e, k):
        return ctx.rB[:, k * 1024:(k + 1) * 1024]

    last_gu = [wreg_free[0], wreg_free[0]]
    last_dn = [None, wreg_free[1]]
    ld_tok = [None, None]
    if do_moe:
        in_tok = S.tail("main")
        psum_free_gu = [None, None]
        psum_free_dn = [None, None]
        cbc_last = None
        dn_ld = [None, None]
        for e in range(8):
            ce_tok = S.dma("sync", lambda e_, e=e: e_.dma_start(
                out=ce_row, in_=compfm[e:e + 1, 0:1024]),
                waits=(cbc_last,), tag="ce_dma")
            cbc_mms = []
            for i2 in range(2):
                pcb = ctx.pB[:, 1024 + i2 * 512:1536 + i2 * 512]
                cbc_mms.append(S.op("tensor", lambda e_, pcb=pcb, i2=i2: e_.matmul(
                    pcb, ctx.onerow[:, :], ce_row[0:1, i2 * 512:(i2 + 1) * 512],
                    start=True, stop=True),
                    waits=(ce_tok, cbc_last) if i2 == 0 else (),
                    stream="mm", tag="cbc_mm"))
            cbc_ev = S.op("vector", lambda e_: e_.tensor_copy(
                cbc_sb, ctx.pB[:, 1024:2048]), waits=(cbc_mms[1],), tag="cbc_ev")
            for j in range(8):
                s = (e * 8 + j) % 2
                wga = ctx.wreg[:, s * 4096:s * 4096 + 2048]
                wua = ctx.wreg[:, s * 4096 + 2048:s * 4096 + 4096]
                S.dma("gpsimd", lambda e_, e=e, j=j, wga=wga: e_.dma_start(
                    out=wga, in_=ctx.d_wgu[e, j, :, :]), waits=(last_gu[s],),
                    stream="gu%d" % s, sem="gus%d" % s, tag="wld_g")
                ld_tok[s] = S.dma("gpsimd", lambda e_, e=e, j=j, wua=wua: e_.dma_start(
                    out=wua, in_=ctx.d_wgu[e, j + 8, :, :]),
                    stream="gu%d" % s, sem="gus%d" % s, tag="wld_u")
                pg = ctx.pA[:, 0:1024]
                pu = ctx.pA[:, 1024:2048]
                for tt in range(2):
                    for k in range(KD):
                        w0 = (ld_tok[s], psum_free_gu[0],
                              in_tok if (e == 0 and j == 0) else None) \
                            if (k == 0 and tt == 0) else ()
                        tokg = S.op("tensor", lambda e_, k=k, tt=tt, wga=wga, pg=pg: e_.matmul(
                            pg[:, tt * 512:(tt + 1) * 512],
                            wga[:, k * 128:(k + 1) * 128],
                            ztiles[k][:, tt * 512:(tt + 1) * 512],
                            start=(k == 0), stop=(k == KD - 1)),
                            waits=w0, stream="mm", tag="gu_g_mm")
                for tt in range(2):
                    for k in range(KD):
                        w0 = (psum_free_gu[1],) if (k == 0 and tt == 0) else ()
                        tok = S.op("tensor", lambda e_, k=k, tt=tt, wua=wua, pu=pu: e_.matmul(
                            pu[:, tt * 512:(tt + 1) * 512],
                            wua[:, k * 128:(k + 1) * 128],
                            ztiles[k][:, tt * 512:(tt + 1) * 512],
                            start=(k == 0), stop=(k == KD - 1)),
                            waits=w0, stream="mm", tag="gu_u_mm")
                last_gu[s] = tok
                psum_free_gu[0] = S.op("scalar", lambda e_, pg=pg: e_.activation(
                    silu_t, pg, AF.Silu), waits=(tokg,), tag="moe_silu")
                tok_t1 = S.op("vector", lambda e_, pu=pu: e_.tensor_tensor(
                    t1, pu, cbc_sb, ALU.mult), waits=(tok, cbc_ev), tag="moe_t1")
                psum_free_gu[1] = tok_t1
                act_done = S.op("vector", lambda e_, e=e, j=j: e_.tensor_tensor(
                    act_tile(e, j), silu_t, t1, ALU.mult), tag="moe_act")
            w_init = wreg_free[1]
            for o in range(KD):
                s = o % 2
                wdn = ctx.wreg[:, 8192 + s * 4096:8192 + s * 4096 + 1024]
                dn_prev = last_dn[s]
                dn_ld[s] = S.dma("gpsimd", lambda e_, e=e, o=o, wdn=wdn: e_.dma_start(
                    out=wdn, in_=ctx.d_wdn[e, o, :, :]),
                    waits=(w_init,) if (e == 0 and o < 2) else (dn_prev,),
                    stream="dn%d" % s, sem="dns%d" % s, tag="wld_dn")
                pd = ctx.pB[:, 0:1024]
                for tt in range(2):
                    for k in range(8):
                        w0 = (dn_ld[s], psum_free_dn[0],
                              act_done if o == 0 else None) \
                            if (k == 0 and tt == 0) else ()
                        tok = S.op("tensor", lambda e_, e=e, k=k, tt=tt, wdn=wdn, pd=pd: e_.matmul(
                            pd[:, tt * 512:(tt + 1) * 512],
                            wdn[:, k * 128:(k + 1) * 128],
                            act_tile(e, k)[:, tt * 512:(tt + 1) * 512],
                            start=(k == 0), stop=(k == 7)),
                            waits=w0, stream="mm", tag="dn_mm")
                last_dn[s] = tok
                xt = x_tile(ctx, o)
                psum_free_dn[0] = S.op("vector", lambda e_, pd=pd, xt=xt: e_.tensor_tensor(
                    xt, xt, pd, ALU.add), waits=(tok,), tag="dn_res")
            cbc_last = psum_free_dn[0]
    def sh_tile(o):
        reg = ctx.rB if o < 16 else ctx.rC
        return reg[:, (o % 16) * T:(o % 16) * T + T]

    if not do_moe:
        last_dn = [wreg_free[0], wreg_free[1]]
    in_tok3 = S.tail("main")
    last_sh = [last_dn[0], last_dn[0]]
    ld_sh = [None, None]
    psum_free_sh = [None, None, None, None]
    for o in range(32 if do_shared else 0):
        s = o % 2
        wga = ctx.wreg[:, s * 4096:s * 4096 + 2048]
        wua = ctx.wreg[:, s * 4096 + 2048:s * 4096 + 4096]
        S.dma("gpsimd", lambda e_, o=o, wga=wga: e_.dma_start(
            out=wga, in_=ctx.d_wshgu[o, :, :]), waits=(last_sh[s],),
            stream="sg%d" % s, sem="sgs%d" % s, tag="wld_shg")
        ld_sh[s] = S.dma("gpsimd", lambda e_, o=o, wua=wua: e_.dma_start(
            out=wua, in_=ctx.d_wshgu[o + 32, :, :]),
            stream="sg%d" % s, sem="sgs%d" % s, tag="wld_shu")
        for tt in range(2):
            sl = s * 2 + tt
            pg = ctx.pA[:, s * 1024 + tt * 512:s * 1024 + (tt + 1) * 512]
            pu = ctx.pB[:, s * 1024 + tt * 512:s * 1024 + (tt + 1) * 512]
            for k in range(KD):
                w0 = (ld_sh[s], psum_free_sh[sl],
                      in_tok3 if o < 2 else None) if k == 0 else ()
                S.op("tensor", lambda e_, k=k, wga=wga, pg=pg, tt=tt: e_.matmul(
                    pg, wga[:, k * 128:(k + 1) * 128],
                    ztiles[k][:, tt * 512:(tt + 1) * 512],
                    start=(k == 0), stop=(k == KD - 1)),
                    waits=w0, stream="mm", tag="sh_g_mm")
            for k in range(KD):
                tok = S.op("tensor", lambda e_, k=k, wua=wua, pu=pu, tt=tt: e_.matmul(
                    pu, wua[:, k * 128:(k + 1) * 128],
                    ztiles[k][:, tt * 512:(tt + 1) * 512],
                    start=(k == 0), stop=(k == KD - 1)), stream="mm", tag="sh_u_mm")
            last_sh[s] = tok
            S.op("scalar", lambda e_, pg=pg, tt=tt: e_.activation(
                silu_t[:, tt * 512:(tt + 1) * 512], pg, AF.Silu),
                waits=(tok,), tag="sh_silu")
            psum_free_sh[sl] = S.op("vector", lambda e_, o=o, tt=tt, pu=pu: e_.tensor_tensor(
                sh_tile(o)[:, tt * 512:(tt + 1) * 512],
                silu_t[:, tt * 512:(tt + 1) * 512], pu, ALU.mult),
                tag="sh_act")
    in_tok4 = S.tail("main")
    last_sd = [None, None]
    psum_free_sd = [None, None, None, None]
    for o in range(KD if do_shared else 0):
        s = o % 2
        wsd = ctx.wreg[:, s * 8192:s * 8192 + 4096]
        if o == 0:
            w8 = (last_sh[0], last_sh[1])
        elif o == 1:
            w8 = (last_dn[1],) if last_dn[1] is not None else ()
        else:
            w8 = (last_sd[s],)
        ldd = S.dma("gpsimd", lambda e_, o=o, wsd=wsd: e_.dma_start(
            out=wsd, in_=ctx.d_wshdn[o, :, :]), waits=w8,
            stream="sd%d" % s, sem="sds%d" % s, tag="wld_shd")
        for tt in range(2):
            sl = s * 2 + tt
            pd = ctx.pA[:, (s * 2 + tt) * 512:(s * 2 + tt) * 512 + 512]
            for k in range(32):
                w0 = (ldd, psum_free_sd[sl],
                      in_tok4 if o < 2 else None) if k == 0 else ()
                tok = S.op("tensor", lambda e_, k=k, wsd=wsd, pd=pd, tt=tt: e_.matmul(
                    pd, wsd[:, k * 128:(k + 1) * 128],
                    sh_tile(k)[:, tt * 512:(tt + 1) * 512],
                    start=(k == 0), stop=(k == 31)),
                    waits=w0, stream="mm", tag="shd_mm")
            xt = x_tile(ctx, o, (tt * 512, (tt + 1) * 512))
            psum_free_sd[sl] = S.op("vector", lambda e_, pd=pd, xt=xt: e_.tensor_tensor(
                xt, xt, pd, ALU.add), waits=(tok,), tag="shd_res")
        last_sd[s] = tok
    if do_shared:
        ctx.wreg_free = [last_sd[0], last_sd[1]]
    else:
        ctx.wreg_free = [last_dn[0], last_dn[1]]


def _prep_consts():
    h128 = make_hadamard(128).astype(np.float32)
    ident = np.eye(128, dtype=np.float32)
    mask = (np.arange(128)[:, None] <= np.arange(512)[None, :]).astype(np.float32)
    return {"c_h128": h128, "c_h128n": -h128, "c_ident": ident, "c_mask": mask}


def _prep_pmask(core):
    pm = np.zeros((128, 8), dtype=np.float32)
    pm[:, :core] = 1.0
    return pm


def _prep_l0_attn(inp):
    return {
        "l0a_wq": tile_fm(np.ascontiguousarray(np.asarray(inp["L0_attn_Wq"]).T), D, D),
        "l0a_wk": tile_fm(np.ascontiguousarray(np.asarray(inp["L0_attn_Wk"]).T), D, D),
        "l0a_wv": tile_fm(np.ascontiguousarray(np.asarray(inp["L0_attn_Wv"]).T), D, D),
        "l0a_wo": tile_fm(np.ascontiguousarray(np.asarray(inp["L0_attn_Wo"]).T), D, D),
        "l0a_wfq": np.asarray(inp["L0_attn_Wfq"]),
        "l0a_wfk": np.asarray(inp["L0_attn_Wfk"]),
    }


def _prep_l1_attn(inp):
    wqkv = np.asarray(inp["L1_attn_weight_qkv"])
    wproj = np.asarray(inp["L1_attn_weight_proj"])
    return {
        "l1a_wqkv": tile_fm(np.ascontiguousarray(wqkv.T), D, 3072),
        "l1a_wproj": tile_fm(np.ascontiguousarray(wproj.T), D, D),
    }


def _prep_l1_ffn(inp):
    wg = np.asarray(inp["L1_ffn_w_gate"])
    wgu = np.asarray(inp["L1_ffn_w_gate_up"])
    wdn = np.asarray(inp["L1_ffn_w_down"])
    wshgu = np.asarray(inp["L1_ffn_w_sh_gate_up"])
    wshdn = np.asarray(inp["L1_ffn_w_sh_down"])
    return {
        "l1f_wg": np.ascontiguousarray(wg.reshape(16, 128, 8)),
        "l1f_wgu": np.stack([tile_fm(wgu[e], D, D) for e in range(8)]),
        "l1f_wdn": np.stack([tile_fm(wdn[e], 1024, D) for e in range(8)]),
        "l1f_wshgu": tile_fm(wshgu, D, 8192),
        "l1f_wshdn": tile_fm(wshdn, 4096, D),
    }


def _prep_l0_ffn(inp):
    up = np.asarray(inp["L0_ffn_W_up"])
    gate = np.asarray(inp["L0_ffn_W_gate"])
    down = np.asarray(inp["L0_ffn_W_down"])
    wsc = np.asarray(inp["L0_ffn_Wscale_ug"])
    return {
        "l0f_wup": tile_fm(np.ascontiguousarray(up.T), D, 8192),
        "l0f_wgate": tile_fm(np.ascontiguousarray(gate.T), D, 8192),
        "l0f_wdown": tile_fm(np.ascontiguousarray(down.T), 8192, D),
        "l0f_suug": np.asarray(inp["L0_ffn_SU_ug"]).reshape(1, D),
        "l0f_sudp": pmajor(np.asarray(inp["L0_ffn_SU_dp"])),
        "l0f_wsu_up": pmajor(wsc[:8192]),
        "l0f_wsu_g": pmajor(wsc[8192:]),
        "l0f_wsdp": pmajor(np.asarray(inp["L0_ffn_Wscale_dp"])),
    }


DBG_ACC = False
OLD8192 = False
OLD2048 = False
DBG_FOLD = False


def build_program(blocks):
    nc = bass.Bass("TRN2", target_bir_lowering=False)
    es = contextlib.ExitStack()
    ctx = Ctx()
    with es:
        _alloc_common(nc, es, ctx)
        ctx.dbg_acc = DBG_ACC
        ctx.old8192 = OLD8192
        ctx.old2048 = OLD2048
        ctx.dbg_fold = DBG_FOLD
        if "l0_attn" in blocks:
            _decl_l0_attn(nc, ctx)
        if "l0_ffn" in blocks:
            _decl_l0_ffn(nc, ctx)
        if "l1_attn" in blocks:
            _decl_l1_attn(nc, ctx)
        if "l1_ffn" in blocks:
            _decl_l1_ffn(nc, ctx)
        for b in blocks:
            if b == "l0_attn":
                emit_l0_attn(ctx)
            elif b == "l0_ffn":
                emit_l0_ffn(ctx)
            elif b == "l1_attn":
                emit_l1_attn(ctx)
            elif b == "l1_ffn":
                emit_l1_ffn(ctx)
        _emit_store_out(ctx)
        block = es.enter_context(nc.Block())
        ctx.sched.build(nc, block, es)
    return nc


def shard_x(x):
    shards = []
    for c in range(NCORES):
        xs = np.concatenate([x[0, c * TB:(c + 1) * TB, :],
                             x[1, c * TB:(c + 1) * TB, :]], axis=0)
        shards.append(np.ascontiguousarray(xs.T))
    return shards


def unshard_x(outs):
    x = np.empty((2, NCORES * TB, D), dtype=np.float32)
    for c in range(NCORES):
        o = np.asarray(outs[c])
        x[0, c * TB:(c + 1) * TB, :] = o[:, 0:TB].T
        x[1, c * TB:(c + 1) * TB, :] = o[:, TB:2 * TB].T
    return x


KERNEL_BLOCKS = BLOCKS_ALL


def kernel(**inputs):
    x = np.asarray(inputs["L0_attn_hidden_states"])
    consts = _prep_consts()
    w = {}
    if "l0_attn" in KERNEL_BLOCKS:
        w.update(_prep_l0_attn(inputs))
    if "l0_ffn" in KERNEL_BLOCKS:
        w.update(_prep_l0_ffn(inputs))
    if "l1_attn" in KERNEL_BLOCKS:
        w.update(_prep_l1_attn(inputs))
    if "l1_ffn" in KERNEL_BLOCKS:
        w.update(_prep_l1_ffn(inputs))
    nc = build_program(KERNEL_BLOCKS)
    shards = shard_x(x)
    in_maps = []
    for c in range(NCORES):
        m = {"x_fm": shards[c], "c_pmask": _prep_pmask(c)}
        m.update(consts)
        m.update(w)
        in_maps.append(m)
    res = run_bass_kernel_spmd(nc, in_maps, core_ids=list(range(NCORES)))
    global LAST_RESULTS
    LAST_RESULTS = res
    return unshard_x([r["out_fm"] for r in res.results])


LAST_RESULTS = None


# version 09
# speedup vs baseline: 1.2469x; 1.0712x over previous
import contextlib
import numpy as np
import ml_dtypes
import concourse.bass as bass
import concourse.mybir as mybir
from concourse.bass_utils import run_bass_kernel_spmd

F32 = mybir.dt.float32
BF16 = mybir.dt.bfloat16
AF = mybir.ActivationFunctionType
ALU = mybir.AluOpType
AX = mybir.AxisListType

NCORES = 8
D = 2048
KD = D // 128
T = 1024
TB = 512
BF = ml_dtypes.bfloat16

BLOCKS_ALL = ("l0_attn", "l0_ffn", "l1_attn", "l1_ffn")


class Sched:

    def __init__(self):
        self.ops = []
        self.force = set()
        self._tail = {}

    def tail(self, stream="main"):
        return self._tail.get(stream)

    def op(self, eng, fn, waits=(), stream="main", sem=None, inc=1, tag="",
           nosync=False):
        tok = len(self.ops)
        for w in waits:
            if w is not None:
                self.force.add(w)
        self.ops.append(dict(eng=eng, fn=fn, inc=inc, sem=sem or eng,
                             stream=stream, waits=[w for w in waits if w is not None],
                             tag=tag, nosync=nosync))
        self._tail[stream] = tok
        return tok

    def dma(self, eng, fn, waits=(), stream="main", sem=None, tag="", nosync=False):
        return self.op(eng, fn, waits=waits, stream=stream,
                       sem=sem or (eng + "_dma"), inc=16, tag=tag, nosync=nosync)

    def build(self, nc, block, es):
        n = len(self.ops)
        prev_in_stream = {}
        chain_wait = [None] * n
        for i, o in enumerate(self.ops):
            p = prev_in_stream.get(o["stream"])
            if p is not None and not o.get("nosync"):
                po = self.ops[p]
                if po["eng"] != o["eng"]:
                    chain_wait[i] = p
                elif (o["inc"] != 16 and po["inc"] != 16
                      and o["eng"] in ("vector", "scalar", "gpsimd")):
                    chain_wait[i] = p
            prev_in_stream[o["stream"]] = i
        last_of_stream = set(prev_in_stream.values())
        do_inc = [False] * n
        for i, o in enumerate(self.ops):
            if o["inc"] == 16 or i in self.force or i in last_of_stream:
                do_inc[i] = True
        for i in range(n):
            if chain_wait[i] is not None:
                do_inc[chain_wait[i]] = True
        sem_names = sorted({o["sem"] for i, o in enumerate(self.ops) if do_inc[i]})
        sems = {s: es.enter_context(nc.semaphore("sem_" + s)) for s in sem_names}
        cum = {s: 0 for s in sem_names}
        val_after = [0] * n
        for i, o in enumerate(self.ops):
            if do_inc[i]:
                cum[o["sem"]] += o["inc"]
                val_after[i] = cum[o["sem"]]
        resolved = [[] for _ in range(n)]
        for i, o in enumerate(self.ops):
            tgts = list(o["waits"])
            if chain_wait[i] is not None:
                tgts.append(chain_wait[i])
            seen = {}
            for tkn in tgts:
                assert do_inc[tkn], (i, tkn, self.ops[tkn])
                s = self.ops[tkn]["sem"]
                seen[s] = max(seen.get(s, 0), val_after[tkn])
            resolved[i] = [(sems[s], v) for s, v in sorted(seen.items())]

        def make_body(label):
            def body(eng):
                for i, o in enumerate(self.ops):
                    if o["eng"] != label:
                        continue
                    for s, v in resolved[i]:
                        eng.wait_ge(s, v)
                    inst = o["fn"](eng)
                    if do_inc[i]:
                        inst.then_inc(sems[o["sem"]], o["inc"])
            return body

        used = {o["eng"] for o in self.ops}
        for label, attr in [("tensor", block.tensor), ("vector", block.vector),
                            ("scalar", block.scalar), ("gpsimd", block.gpsimd),
                            ("sync", block.sync)]:
            if label in used:
                attr(make_body(label))


def tile_fm(wt, kdim, odim):
    kch, och = kdim // 128, odim // 128
    return np.ascontiguousarray(
        wt.reshape(kch, 128, och, 128).transpose(2, 1, 0, 3).reshape(och, 128, kch * 128))


def pmajor(v):
    n = v.shape[0] // 128
    return np.ascontiguousarray(v.reshape(n, 128).T)


def make_hadamard(n):
    h = np.array([[1.0]], dtype=np.float32)
    while h.shape[0] < n:
        h = np.block([[h, h], [h, -h]])
    return h


class Ctx:
    pass


def _alloc_common(nc, es, ctx):
    S = ctx.sched = Sched()
    ctx.nc = nc
    ctx.es = es

    ctx.d_x = nc.dram_tensor("x_fm", [D, T], F32, kind="ExternalInput").ap()
    ctx.d_out = nc.dram_tensor("out_fm", [D, T], F32, kind="ExternalOutput").ap()
    ctx.d_h128 = nc.dram_tensor("c_h128", [128, 128], F32, kind="ExternalInput").ap()
    ctx.d_h128n = nc.dram_tensor("c_h128n", [128, 128], F32, kind="ExternalInput").ap()
    ctx.d_ident = nc.dram_tensor("c_ident", [128, 128], F32, kind="ExternalInput").ap()
    ctx.d_mask = nc.dram_tensor("c_mask", [128, 512], F32, kind="ExternalInput").ap()
    ctx.d_pmask = nc.dram_tensor("c_pmask", [128, 8], F32, kind="ExternalInput").ap()
    ctx.d_dbg = nc.dram_tensor("dbg", [128, 8192], F32, kind="ExternalOutput").ap()
    ctx.dbg_off = [0]

    ctx.x = es.enter_context(nc.sbuf_tensor("x", [128, KD * T], BF16))
    ctx.rA = es.enter_context(nc.sbuf_tensor("rA", [128, 16384], BF16))
    ctx.rB = es.enter_context(nc.sbuf_tensor("rB", [128, 16384], BF16))
    ctx.rC = es.enter_context(nc.sbuf_tensor("rC", [128, 16384], BF16))
    ctx.rD = es.enter_context(nc.sbuf_tensor("rD", [128, 16384], BF16))
    ctx.wreg = es.enter_context(nc.sbuf_tensor("wreg", [128, 16384], BF16))
    ctx.trans = es.enter_context(nc.sbuf_tensor("trans", [128, 6656], BF16))
    ctx.dsb = ctx.wreg[:, 4096:8224]
    ctx.accf = ctx.wreg[:, 4096:8224].bitcast(F32)
    ctx.dbuf = ctx.wreg[:, 8224:10288]
    ctx.acc = ctx.wreg[:, 10288:14416]
    ctx.mask = ctx.wreg[:, 14416:14928]
    ctx.sm1 = es.enter_context(nc.sbuf_tensor("sm1", [128, 192], F32))
    ctx.sm2 = es.enter_context(nc.sbuf_tensor("sm2", [128, 256], F32))
    ctx.h128 = es.enter_context(nc.sbuf_tensor("h128", [128, 128], BF16))
    ctx.h128n = es.enter_context(nc.sbuf_tensor("h128n", [128, 128], BF16))
    ctx.identb = es.enter_context(nc.sbuf_tensor("identb", [128, 128], BF16))
    ctx.onecol = es.enter_context(nc.sbuf_tensor("onecol", [128, 1], BF16))
    ctx.onerow = es.enter_context(nc.sbuf_tensor("onerow", [1, 128], BF16))
    ctx.pmask = es.enter_context(nc.sbuf_tensor("pmask", [128, 8], F32))

    ctx.pA = es.enter_context(nc.psum_tensor("pA", [128, 2048], F32))
    ctx.pB = es.enter_context(nc.psum_tensor("pB", [128, 2048], F32))

    ctx.transf = ctx.trans[:, :].bitcast(F32)

    S.dma("gpsimd", lambda e: e.dma_start(out=ctx.h128[:, :], in_=ctx.d_h128[:, :]))
    S.dma("gpsimd", lambda e: e.dma_start(out=ctx.h128n[:, :], in_=ctx.d_h128n[:, :]))
    S.dma("gpsimd", lambda e: e.dma_start(out=ctx.identb[:, :], in_=ctx.d_ident[:, :]))
    S.dma("sync", lambda e: e.dma_start(out=ctx.pmask[:, :], in_=ctx.d_pmask[:, :]))
    S.op("vector", lambda e: e.memset(ctx.onecol[:, :], 1.0))
    S.op("vector", lambda e: e.memset(ctx.onerow[:, :], 1.0))
    for c in range(KD):
        S.dma("gpsimd", lambda e, c=c: e.dma_start(
            out=ctx.x[:, c * T:(c + 1) * T], in_=ctx.d_x[c * 128:(c + 1) * 128, :]))


def _emit_store_out(ctx):
    S = ctx.sched
    for c in range(KD):
        S.dma("gpsimd", lambda e, c=c: e.dma_start(
            out=ctx.d_out[c * 128:(c + 1) * 128, :], in_=ctx.x[:, c * T:(c + 1) * T]))
    S.op("scalar", lambda e: e.nop(), tag="fence")


def emit_dbg(ctx, ap, rows, cols):
    off = ctx.dbg_off[0]
    ctx.sched.dma("gpsimd", lambda e: e.dma_start(
        out=ctx.d_dbg[0:rows, off:off + cols], in_=ap), tag="dbg")
    ctx.dbg_off[0] += cols


def x_tile(ctx, c, tcols=None):
    if tcols is None:
        return ctx.x[:, c * T:(c + 1) * T]
    return ctx.x[:, c * T + tcols[0]:c * T + tcols[1]]


def t_row_bf(ctx, off, w):
    return ctx.trans[0:1, off:off + w]


def t_tile_bf(ctx, off, w, p0=0, p1=128):
    return ctx.trans[p0:p1, off:off + w]


def emit_ln(ctx, su_row, out_region, out_off, tw=T, tbase=0):
    S = ctx.sched
    nt = tw // 512
    x2t = ctx.trans[:, 2048:2560]

    def xt(c, i):
        return ctx.x[:, c * T + tbase + i * 512: c * T + tbase + (i + 1) * 512]

    for i in range(nt):
        psx = ctx.pA[0:1, 1024 + i * 512:1536 + i * 512]
        psq = ctx.pB[0:1, 1024 + i * 512:1536 + i * 512]
        for c in range(KD):
            S.op("scalar", lambda e, c=c, i=i: e.activation(
                x2t, xt(c, i), AF.Square), tag="ln_sq")
            S.op("tensor", lambda e, c=c, i=i, psx=psx: e.matmul(
                psx, ctx.onecol[:, :], xt(c, i), start=(c == 0), stop=(c == KD - 1)),
                tag="ln_sx")
            S.op("tensor", lambda e, c=c, psq=psq: e.matmul(
                psq, ctx.onecol[:, :], x2t, start=(c == 0), stop=(c == KD - 1)),
                tag="ln_sx2")
    mu_ps = ctx.pA[0:1, 1024:1024 + tw]
    var_ps = ctx.pB[0:1, 1024:1024 + tw]
    musq = ctx.transf[0:1, 1280:1280 + tw]
    rstd_bf = t_row_bf(ctx, 4608, tw)
    nmr_bf = t_row_bf(ctx, 4608 + tw, tw)
    S.op("vector", lambda e: e.tensor_scalar(mu_ps, mu_ps, 1.0 / D, None, ALU.mult))
    S.op("vector", lambda e: e.tensor_scalar(var_ps, var_ps, 1.0 / D, 1e-5, ALU.mult, ALU.add))
    S.op("scalar", lambda e: e.activation(musq, mu_ps, AF.Square))
    S.op("vector", lambda e: e.tensor_tensor(var_ps, var_ps, musq, ALU.subtract))
    S.op("scalar", lambda e: e.activation(var_ps, var_ps, AF.Ln))
    S.op("vector", lambda e: e.tensor_scalar(var_ps, var_ps, -0.5, None, ALU.mult))
    S.op("scalar", lambda e: e.activation(rstd_bf, var_ps, AF.Exp))
    S.op("vector", lambda e: e.scalar_tensor_tensor(
        nmr_bf, mu_ps, -1.0, rstd_bf, ALU.mult, ALU.mult))
    for c in range(KD):
        su_c = su_row[0:1, c * 128:(c + 1) * 128]
        for i in range(nt):
            pa = ctx.pA[:, (c % 2) * 512:(c % 2) * 512 + 512]
            pb = ctx.pB[:, (c % 2) * 512:(c % 2) * 512 + 512]
            S.op("tensor", lambda e, pa=pa, su_c=su_c, i=i: e.matmul(
                pa, su_c, rstd_bf[0:1, i * 512:(i + 1) * 512], start=True, stop=True),
                tag="ln_abc")
            S.op("tensor", lambda e, pb=pb, su_c=su_c, i=i: e.matmul(
                pb, su_c, nmr_bf[0:1, i * 512:(i + 1) * 512], start=True, stop=True),
                tag="ln_bbc")
            ot = out_region[:, out_off + c * tw + i * 512: out_off + c * tw + (i + 1) * 512]
            S.op("vector", lambda e, c=c, i=i, ot=ot, pa=pa: e.tensor_tensor(
                ot, xt(c, i), pa, ALU.mult), tag="ln_za")
            S.op("vector", lambda e, ot=ot, pb=pb: e.tensor_tensor(
                ot, ot, pb, ALU.add), tag="ln_zb")


def mm_op(ctx, base, fn, **kw):
    need_nop = getattr(ctx, "_mm_base", None) is not None and ctx._mm_base != base
    ctx._mm_base = base
    if need_nop:
        fn2 = lambda e: (e.nop(), fn(e))[1]
    else:
        fn2 = fn
    return ctx.sched.op("tensor", fn2, **kw)


def emit_ones_row(ctx, off, w):
    S = ctx.sched
    row = t_row_bf(ctx, off, w)
    S.op("vector", lambda e: e.memset(row, 1.0))
    return row


def emit_fwht_combine(ctx, tiles):
    S = ctx.sched
    n = len(tiles)
    barrier = [S.tail("main"), None]
    h = 1
    while h < n:
        first = {"vector": True, "gpsimd": True}
        last = {"vector": None, "gpsimd": None}
        idx = 0
        for base in range(0, n, 2 * h):
            for j in range(h):
                a = tiles[base + j]
                b = tiles[base + j + h]
                eng = "vector" if idx % 2 == 0 else "gpsimd"
                idx += 1
                w0 = tuple(t for t in barrier if t is not None) if first[eng] else ()
                first[eng] = False
                if eng == "vector":
                    ta = S.op(eng, lambda e, a=a, b=b: e.tensor_tensor(a, a, b, ALU.add),
                              waits=w0, nosync=True, tag="fw_a")
                    last[eng] = S.op(eng, lambda e, a=a, b=b: e.scalar_tensor_tensor(
                        b, b, -2.0, a, ALU.mult, ALU.add),
                        waits=(ta,), nosync=True, tag="fw_b")
                else:
                    tmp = ctx.trans[:, 2048:2560]
                    tt = S.op(eng, lambda e, a=a, b=b, tmp=tmp: e.tensor_tensor(
                        tmp, a, b, ALU.subtract), waits=w0, nosync=True, tag="fw_s")
                    ta = S.op(eng, lambda e, a=a, b=b: e.tensor_tensor(
                        a, a, b, ALU.add), waits=(tt,), nosync=True, tag="fw_a")
                    last[eng] = S.op(eng, lambda e, b=b, tmp=tmp: e.tensor_copy(
                        b, tmp), waits=(ta,), nosync=True, tag="fw_b")
        barrier = [last["vector"], last["gpsimd"]]
        h *= 2
    S.op("vector", lambda e: e.nop(), waits=tuple(t for t in barrier if t is not None),
         tag="fw_end")


H16S = make_hadamard(16)


def emit_fwht_folded(ctx, tiles, scratch):
    S = ctx.sched
    n = len(tiles)
    ngroups = n // 16
    in_tok = S.tail("main")
    psum_free = [None, None, None, None]
    out_loc = [None] * n
    ev_last = None
    prev_sign = [None]
    for g in range(ngroups):
        for bp in range(16):
            o = g * 16 + bp
            dst = scratch[bp] if g == 0 else tiles[(g - 1) * 16 + bp]
            out_loc[o] = dst
            s4 = o % 4
            pa = ctx.pA if s4 % 2 == 0 else ctx.pB
            ps = pa[:, (s4 // 2) * 512:(s4 // 2) * 512 + 512]
            border = sorted(range(16), key=lambda b: -H16S[bp, b])
            for bi, b in enumerate(border):
                w0 = (psum_free[s4], in_tok if o < 4 else None) if bi == 0 else ()
                hmat = ctx.h128 if H16S[bp, b] > 0 else ctx.h128n
                tok = S.op("tensor", lambda e, hmat=hmat, ps=ps, bi=bi, src=tiles[g * 16 + b]: e.matmul(
                    ps, hmat[:, :], src, start=(bi == 0), stop=(bi == 15)),
                    waits=w0, stream="mm", tag="fwf_mm")
            psum_free[s4] = ev_last = S.op(
                "scalar", lambda e, dst=dst, ps=ps: e.copy(dst, ps),
                waits=(tok,), nosync=True, tag="fwf_ev")
    S.op("vector", lambda e: e.nop(), waits=(ev_last,), tag="fwf_end")
    if ngroups == 1:
        return [out_loc[b] for b in range(16)]
    barrier = [S.tail("main"), None]
    h = 1
    idx = 0
    while h < ngroups:
        first = {"vector": True, "gpsimd": True}
        last = {"vector": None, "gpsimd": None}
        for base in range(0, ngroups, 2 * h):
            for j in range(h):
                for bp in range(16):
                    a = out_loc[(base + j) * 16 + bp]
                    b2 = out_loc[(base + j + h) * 16 + bp]
                    eng = "vector" if idx % 2 == 0 else "gpsimd"
                    idx += 1
                    w0 = tuple(t for t in barrier if t is not None) if first[eng] else ()
                    first[eng] = False
                    if eng == "vector":
                        ta = S.op(eng, lambda e, a=a, b2=b2: e.tensor_tensor(
                            a, a, b2, ALU.add), waits=w0, nosync=True, tag="fw_a")
                        last[eng] = S.op(eng, lambda e, a=a, b2=b2: e.scalar_tensor_tensor(
                            b2, b2, -2.0, a, ALU.mult, ALU.add),
                            waits=(ta,), nosync=True, tag="fw_b")
                    else:
                        tmp = ctx.trans[:, 2048:2560]
                        pw = w0 + ((last[eng],) if last[eng] is not None else ())
                        tt2 = S.op(eng, lambda e, a=a, b2=b2, tmp=tmp: e.tensor_tensor(
                            tmp, a, b2, ALU.subtract), waits=pw, nosync=True, tag="fw_s")
                        ta = S.op(eng, lambda e, a=a, b2=b2: e.tensor_tensor(
                            a, a, b2, ALU.add), waits=(tt2,), nosync=True, tag="fw_a")
                        last[eng] = S.op(eng, lambda e, b2=b2, tmp=tmp: e.tensor_copy(
                            b2, tmp), waits=(ta,), nosync=True, tag="fw_b")
        barrier = [last["vector"], last["gpsimd"]]
        h *= 2
    S.op("vector", lambda e: e.nop(), waits=tuple(t for t in barrier if t is not None),
         tag="fw_end")
    return out_loc


def emit_fwht_combine_wide(ctx):
    S = ctx.sched
    for h in (1, 2, 4, 8, 16):
        w = h * 512
        for reg in (ctx.rB, ctx.rC):
            va = reg[:, 0:16384].rearrange(
                "p (m two hh) -> p m two hh", two=2, hh=w)[:, :, 0, :]
            vb = reg[:, 0:16384].rearrange(
                "p (m two hh) -> p m two hh", two=2, hh=w)[:, :, 1, :]
            S.op("vector", lambda e, va=va, vb=vb: e.tensor_tensor(
                va, va, vb, ALU.add), tag="fww_a")
            S.op("vector", lambda e, va=va, vb=vb: e.scalar_tensor_tensor(
                vb, vb, -2.0, va, ALU.mult, ALU.add), tag="fww_b")
    a32 = ctx.rB[:, 0:16384]
    b32 = ctx.rC[:, 0:16384]
    S.op("vector", lambda e: e.tensor_tensor(a32, a32, b32, ALU.add), tag="fww_a")
    S.op("vector", lambda e: e.scalar_tensor_tensor(
        b32, b32, -2.0, a32, ALU.mult, ALU.add), tag="fww_b")


def emit_fwht_h128(ctx, tiles, scale=None):
    S = ctx.sched
    in_tok = S.tail("main")
    psum_free = [None, None, None, None]
    evs = []
    for idx, tile in enumerate(tiles):
        tw = tile.shape[1]
        s4 = idx % 4
        pa = ctx.pA if s4 % 2 == 0 else ctx.pB
        ps = pa[:, (s4 // 2) * 512:(s4 // 2) * 512 + tw]
        tok = S.op("tensor", lambda e, tile=tile, tw=tw, ps=ps: e.matmul(
            ps, ctx.h128[:, :], tile[:, 0:tw], start=True, stop=True),
            waits=(psum_free[s4], in_tok if idx < 4 else None),
            stream="mm", tag="fw_h128")
        psum_free[s4] = S.op("scalar", lambda e, tile=tile, tw=tw, ps=ps: e.copy(
            tile[:, 0:tw], ps), waits=(tok,), nosync=True, tag="fw_ev")
    S.op("vector", lambda e: e.nop(),
         waits=tuple(t for t in psum_free if t is not None), tag="fwh_end")


def _decl_l0_ffn(nc, ctx):
    ctx.d_wupgate = nc.dram_tensor("l0f_wupgate", [32, 128, 8192], F32, kind="ExternalInput").ap()
    ctx.d_wdown = nc.dram_tensor("l0f_wdown", [16, 128, 8192], F32, kind="ExternalInput").ap()
    ctx.d_suug = nc.dram_tensor("l0f_suug", [1, 2048], F32, kind="ExternalInput").ap()
    ctx.d_sudp = nc.dram_tensor("l0f_sudp", [128, 64], F32, kind="ExternalInput").ap()
    ctx.d_wsu_up = nc.dram_tensor("l0f_wsu_up", [128, 64], F32, kind="ExternalInput").ap()
    ctx.d_wsu_g = nc.dram_tensor("l0f_wsu_g", [128, 64], F32, kind="ExternalInput").ap()
    ctx.d_wsdp = nc.dram_tensor("l0f_wsdp", [128, 16], F32, kind="ExternalInput").ap()


def emit_l0_ffn(ctx):
    S = ctx.sched

    su_row = t_row_bf(ctx, 0, 2048)
    S.dma("gpsimd", lambda e: e.dma_start(out=su_row, in_=ctx.d_suug[:, :]))
    wsg64 = ctx.sm1[:, 0:64]
    comb_up = ctx.sm1[:, 64:128]
    wdp64 = ctx.sm1[:, 128:144]
    sudp = ctx.sm2[:, 0:64]
    wsuup = ctx.sm2[:, 64:128]
    S.dma("sync", lambda e: e.dma_start(out=sudp, in_=ctx.d_sudp[:, :]))
    S.dma("sync", lambda e: e.dma_start(out=wsuup, in_=ctx.d_wsu_up[:, :]))
    S.dma("sync", lambda e: e.dma_start(out=ctx.sm2[:, 128:192], in_=ctx.d_wsu_g[:, :]))
    S.dma("sync", lambda e: e.dma_start(out=ctx.sm2[:, 192:208], in_=ctx.d_wsdp[:, :]))
    S.op("vector", lambda e: e.tensor_scalar(
        wsg64, ctx.sm2[:, 128:192], 1.0 / np.sqrt(2048.0), None, ALU.mult))
    S.op("vector", lambda e: e.scalar_tensor_tensor(
        comb_up, wsuup, 1.0 / np.sqrt(2048.0), sudp, ALU.mult, ALU.mult))
    S.op("vector", lambda e: e.tensor_scalar(
        wdp64, ctx.sm2[:, 192:208], 1.0 / np.sqrt(8192.0), None, ALU.mult))

    wreg_free = getattr(ctx, "wreg_free", [None, None])
    for tp in range(2):
        tb = tp * 512
        emit_ln(ctx, su_row, ctx.rA, 0, tw=512, tbase=tb)
        ztiles = [ctx.rA[:, c * 512:(c + 1) * 512] for c in range(KD)]
        scratch = [ctx.rA[:, 8192 + c * 512:8192 + (c + 1) * 512] for c in range(KD)]
        if getattr(ctx, "old2048", False):
            emit_fwht_combine(ctx, ztiles)
            emit_fwht_h128(ctx, ztiles)
        else:
            ztiles = emit_fwht_folded(ctx, ztiles, scratch)


        in_tok = S.tail("main")
        last_use = [wreg_free[0], wreg_free[1]]
        ld_tok = [None, None]
        psum_free = [[None, None], [None, None]]
        act = []
        for o in range(64):
            reg = ctx.rB if o < 32 else ctx.rC
            act.append(reg[:, (o % 32) * 512:(o % 32) * 512 + 512])
        for oo in range(32):
            s = oo % 2
            slab = ctx.wreg[:, s * 8192:(s + 1) * 8192]
            ld_tok[s] = S.dma("gpsimd", lambda e, oo=oo, slab=slab: e.dma_start(
                out=slab, in_=ctx.d_wupgate[oo, :, :]), waits=(last_use[s],),
                stream="wl%d" % s, sem="wsl%d" % s, tag="wld_ug")
            pool = ctx.pA if s == 0 else ctx.pB
            for sub in range(2):
                o = 2 * oo + sub
                pu = pool[:, sub * 1024:sub * 1024 + 512]
                pg = pool[:, sub * 1024 + 512:sub * 1024 + 1024]
                for k in range(KD):
                    w0 = (ld_tok[s], psum_free[s][sub],
                          in_tok if oo < 2 else None) if k == 0 else ()
                    tok_up = S.op("tensor", lambda e, k=k, slab=slab, sub=sub, pu=pu, zt=ztiles[k]: e.matmul(
                        pu, slab[:, sub * 4096 + k * 128:sub * 4096 + (k + 1) * 128], zt,
                        start=(k == 0), stop=(k == KD - 1)),
                        waits=w0, stream="mm", tag="up_mm")
                for k in range(KD):
                    tok = S.op("tensor", lambda e, k=k, slab=slab, sub=sub, pg=pg, zt=ztiles[k]: e.matmul(
                        pg, slab[:, sub * 4096 + 2048 + k * 128:sub * 4096 + 2048 + (k + 1) * 128], zt,
                        start=(k == 0), stop=(k == KD - 1)), stream="mm", tag="gt_mm")
                last_use[s] = tok
                S.op("scalar", lambda e, o=o, pg=pg: e.activation(
                    ctx.trans[:, 2048:2560], pg, AF.Silu, scale=wsg64[:, o:o + 1]),
                    waits=(tok,), tag="silu")
                psum_free[s][sub] = S.op("vector", lambda e, o=o, pu=pu, at=act[o]: e.scalar_tensor_tensor(
                    at, ctx.trans[:, 2048:2560], comb_up[:, o:o + 1], pu,
                    ALU.mult, ALU.mult), waits=(tok_up,), tag="actmul")

        if getattr(ctx, "dbg_fold", False) and tp == 0:
            for c in range(8):
                emit_dbg(ctx, act[c], 128, 512)
        emit_fwht_combine_wide(ctx)
        emit_fwht_h128(ctx, act)


        in_tok = S.tail("main")
        last_use_d = [None, None]
        ld_tok_d = [None, None]
        psum_free_d = [None, None]
        for o in range(KD):
            s = o % 2
            wd = ctx.wreg[:, s * 8192:s * 8192 + 8192]
            if o == 0:
                w8 = (last_use[0],)
            elif o == 1:
                w8 = (last_use[1],)
            else:
                w8 = (last_use_d[s],)
            ld_tok_d[s] = S.dma("gpsimd", lambda e, o=o, wd=wd: e.dma_start(
                out=wd, in_=ctx.d_wdown[o, :, :]), waits=w8,
                stream="wd%d" % s, sem="wsd%d" % s, tag="wld_dn")
            pd = ctx.pB[:, s * 512:s * 512 + 512]
            for k in range(64):
                w0 = (ld_tok_d[s], psum_free_d[s], in_tok if o < 2 else None) if k == 0 else ()
                tok = S.op("tensor", lambda e, k=k, wd=wd, pd=pd, at=act[k]: e.matmul(
                    pd, wd[:, k * 128:(k + 1) * 128], at,
                    start=(k == 0), stop=(k == 63)),
                    waits=w0, stream="mm", tag="dn_mm")
            last_use_d[s] = tok
            xt = x_tile(ctx, o, (tb, tb + 512))
            psum_free_d[s] = S.op("vector", lambda e, o=o, pd=pd, xt=xt: e.scalar_tensor_tensor(
                xt, pd, wdp64[:, o:o + 1], xt, ALU.mult, ALU.add),
                waits=(tok,), tag="dn_res")
        wreg_free = [last_use_d[0], last_use_d[1]]
    ctx.wreg_free = wreg_free


def _decl_l0_attn(nc, ctx):
    ctx.d_wq = nc.dram_tensor("l0a_wq", [16, 128, 2048], F32, kind="ExternalInput").ap()
    ctx.d_wk = nc.dram_tensor("l0a_wk", [16, 128, 2048], F32, kind="ExternalInput").ap()
    ctx.d_wv = nc.dram_tensor("l0a_wv", [16, 128, 2048], F32, kind="ExternalInput").ap()
    ctx.d_wo = nc.dram_tensor("l0a_wo", [16, 128, 2048], F32, kind="ExternalInput").ap()
    ctx.d_wfq = nc.dram_tensor("l0a_wfq", [32, 64, 64], F32, kind="ExternalInput").ap()
    ctx.d_wfk = nc.dram_tensor("l0a_wfk", [32, 64, 64], F32, kind="ExternalInput").ap()
    ctx.d_blk = nc.dram_tensor("l0a_blk", [128, 4128], BF16)
    ctx.d_ag = nc.dram_tensor("l0a_ag", [NCORES * 128, 4128], BF16)


L0ATTN_STAGE = 5
L0ATTN_SUB = 9


def emit_l0_attn(ctx):
    S = ctx.sched
    wreg_free = getattr(ctx, "wreg_free", [None, None])

    S.dma("gpsimd", lambda e: e.dma_start(out=ctx.mask[:, :], in_=ctx.d_mask[:, :]))

    su_row = emit_ones_row(ctx, 0, 2048)
    emit_ln(ctx, su_row, ctx.rA, 0, tw=T)
    ztiles = [ctx.rA[:, c * T:(c + 1) * T] for c in range(KD)]

    vtmp = ctx.trans[:, 2048:3072]
    in_tok = S.tail("main")
    last_use = [wreg_free[0], wreg_free[0]]
    ld_tok = [None, None]
    psum_free = [None, None, None, None]
    pBb8 = ctx.pB[:, 0:512].bitcast(BF16)
    trev_tok = None
    ev_toks = []
    for which, wdram in (("q", ctx.d_wq), ("k", ctx.d_wk), ("v", ctx.d_wv)):
        for o in range(KD):
            s = o % 2
            wsl = ctx.wreg[:, s * 2048:s * 2048 + 2048]
            ld_tok[s] = S.dma("gpsimd", lambda e, o=o, wsl=wsl, wdram=wdram: e.dma_start(
                out=wsl, in_=wdram[o, :, :]), waits=(last_use[s],),
                stream="qv%d" % s, sem="qvs%d" % s, tag="wld_" + which)
            ev2 = []
            for tt in range(2):
                sl = s * 2 + tt
                ps = ctx.pA[:, sl * 512:sl * 512 + 512]
                for k in range(KD):
                    w0 = (ld_tok[s], psum_free[sl],
                          in_tok if (which == "q" and o < 2) else None) if k == 0 else ()
                    tok = S.op("tensor", lambda e, k=k, wsl=wsl, ps=ps, tt=tt: e.matmul(
                        ps, wsl[:, k * 128:(k + 1) * 128],
                        ztiles[k][:, tt * 512:(tt + 1) * 512],
                        start=(k == 0), stop=(k == KD - 1)),
                        waits=w0, stream="mm", tag=which + "_mm")
                if which == "q":
                    dst = ctx.rB[:, o * T + tt * 512:o * T + (tt + 1) * 512]
                elif which == "k":
                    dst = ctx.rC[:, o * T + tt * 512:o * T + (tt + 1) * 512]
                else:
                    dst = vtmp[:, tt * 512:(tt + 1) * 512]
                wv = (tok, trev_tok) if which == "v" else (tok,)
                psum_free[sl] = S.op("scalar", lambda e, dst=dst, ps=ps: e.copy(
                    dst, ps), waits=wv, nosync=True, tag=which + "_ev")
                ev2.append(psum_free[sl])
            last_use[s] = tok
            if which == "v":
                ttoks = []
                for m in range(8):
                    pt = pBb8[:, m * 128:(m + 1) * 128]
                    ttoks.append(S.op("tensor", lambda e, m=m, pt=pt: e.transpose(
                        pt, vtmp[:, m * 128:(m + 1) * 128],
                        ctx.identb[:, :]),
                        waits=(ev2[0], ev2[1], trev_tok) if m == 0 else (),
                        stream="mm", nosync=True, tag="v_tr"))
                trev_tok = S.op("vector", lambda e, o=o: e.tensor_copy(
                    ctx.rD[:, :].rearrange("p (m t) -> p m t", t=2048)[:, :, o * 128:(o + 1) * 128],
                    pBb8.rearrange("p (m c) -> p m c", c=128)),
                    waits=(ttoks[-1],), nosync=True, tag="v_trev")
                ev_toks.append(trev_tok)
            else:
                ev_toks.extend(ev2)
    S.op("vector", lambda e: e.nop(),
         waits=(max(t for t in psum_free if t is not None), trev_tok),
         tag="qkv_anchor")

    if L0ATTN_STAGE < 2:
        ctx.wreg_free = [last_use[0], last_use[1]]
        return
    wfq_sb = ctx.wreg[:, 0:2048]
    wfk_sb = ctx.wreg[:, 2048:4096]
    ms_tok = S.op("vector", lambda e: e.memset(ctx.wreg[:, 0:4096], 0.0),
                  waits=(last_use[0], last_use[1]), tag="fm_ms")
    for wsb, wdram, tagn in ((wfq_sb, ctx.d_wfq, "wld_fq"), (wfk_sb, ctx.d_wfk, "wld_fk")):
        for half in range(2):
            fm_ld = S.dma("gpsimd", lambda e, wsb=wsb, wdram=wdram, half=half: e.dma_start(
                out=wsb[half * 64:half * 64 + 64, :].rearrange(
                    "d (c g) -> d c g", g=128)[:, :, half * 64:half * 64 + 64],
                in_=wdram.rearrange("(c two) d f -> two d c f", two=2)[half]),
                waits=(ms_tok,) if (half == 0 and wsb is wfq_sb) else (),
                stream="qv0", sem="qvs0", tag=tagn)

    first_fm = True
    for b in range(2):
        for c in range(KD):
            ps = ctx.pA[:, (c % 4) * 512:(c % 4) * 512 + 512]
            mm_op(ctx, 0, lambda e, b=b, c=c, ps=ps: e.matmul(
                ps, wfq_sb[:, c * 128:(c + 1) * 128],
                ctx.rB[:, c * T + b * 512:c * T + b * 512 + 512],
                start=True, stop=True),
                waits=(fm_ld,) if first_fm else (), tag="fq_mm")
            first_fm = False
            S.op("scalar", lambda e, b=b, c=c, ps=ps: e.activation(
                ctx.rB[:, c * T + b * 512:c * T + b * 512 + 512], ps, AF.Exp),
                tag="fq_exp")

    if L0ATTN_STAGE < 3:
        ctx.wreg_free = [last_use[0], last_use[1]]
        return
    fk_tm = ctx.trans[:, 0:2048]
    sums = ctx.sm2[:, 0:32]
    rcps = ctx.sm2[:, 32:64]
    for b in range(2):
        for jc in range(4):
            for c in range(KD):
                mm_op(ctx, 0, lambda e, b=b, jc=jc, c=c: e.matmul(
                    ctx.pA[:, c * 128:(c + 1) * 128],
                    ctx.rC[:, c * T + b * 512 + jc * 128:
                           c * T + b * 512 + (jc + 1) * 128],
                    wfk_sb[:, c * 128:(c + 1) * 128],
                    start=True, stop=True), tag="fk_mm")
            S.op("scalar", lambda e: e.activation(fk_tm, ctx.pA[:, :], AF.Exp),
                 tag="fk_exp")
            if L0ATTN_SUB < 2:
                continue
            S.op("vector", lambda e: e.reduce_sum(
                sums, fk_tm.rearrange("p (h f) -> p h f", h=32), axis=AX.X),
                tag="fk_sum")
            S.op("vector", lambda e: e.reciprocal(rcps, sums), tag="fk_rcp")
            S.op("vector", lambda e: e.tensor_tensor(
                fk_tm.rearrange("p (h f) -> p h f", h=32),
                fk_tm.rearrange("p (h f) -> p h f", h=32),
                rcps.unsqueeze(2).broadcast_to([128, 32, 64]), ALU.mult),
                tag="fk_nrm")
            if L0ATTN_SUB < 3:
                continue
            for hh in range(16):
                S.op("tensor", lambda e, hh=hh: e.matmul(
                    ctx.pA[:, 1024 + hh:1024 + hh + 1],
                    fk_tm[:, hh * 128:(hh + 1) * 128], ctx.onecol[:, :],
                    start=True, stop=True), tag="kp_mm")
            S.op("vector", lambda e, b=b, jc=jc: e.scalar_tensor_tensor(
                ctx.sm2[:, 64 + b * 16:80 + b * 16], ctx.pA[:, 1024:1040], 1.0,
                ctx.sm2[:, 64 + b * 16:80 + b * 16], ALU.mult, ALU.add)
                if jc > 0 else e.tensor_copy(
                ctx.sm2[:, 64 + b * 16:80 + b * 16], ctx.pA[:, 1024:1040]),
                tag="kp_acc")
            if L0ATTN_SUB < 4:
                continue
            for hh in range(16):
                pt = ctx.pA[:, (hh % 2) * 512:(hh % 2) * 512 + 64].bitcast(BF16)
                S.op("tensor", lambda e, hh=hh, pt=pt: e.transpose(
                    pt[:, 0:128], fk_tm[:, hh * 128:(hh + 1) * 128],
                    ctx.identb[:, :]), tag="fk_tr")
                S.op("vector", lambda e, hh=hh, b=b, jc=jc, pt=pt: e.tensor_copy(
                    ctx.rC[:, hh * T + b * 512 + jc * 128:hh * T + b * 512 + (jc + 1) * 128],
                    pt[:, 0:128]), tag="fk_trev")
            if L0ATTN_SUB < 5:
                continue
            for h in range(32):
                S.op("tensor", lambda e, b=b, jc=jc, h=h: e.matmul(
                    ctx.pB[(h % 2) * 64:(h % 2) * 64 + 64, h * 64:(h + 1) * 64],
                    fk_tm[:, h * 64:(h + 1) * 64],
                    ctx.rD[:, (b * 4 + jc) * 2048 + h * 64:(b * 4 + jc) * 2048 + (h + 1) * 64],
                    start=True, stop=True), tag="sd_mm")
            S.op("vector", lambda e, b=b, jc=jc: e.scalar_tensor_tensor(
                ctx.dsb[:, b * 2064:b * 2064 + 2048], ctx.pB[:, 0:2048], 1.0,
                ctx.dsb[:, b * 2064:b * 2064 + 2048], ALU.mult, ALU.add)
                if jc > 0 else e.tensor_copy(
                ctx.dsb[:, b * 2064:b * 2064 + 2048], ctx.pB[:, 0:2048]),
                tag="sd_acc")
        if L0ATTN_SUB < 5:
            continue
        S.op("vector", lambda e, b=b: e.tensor_copy(
            ctx.dsb[:, b * 2064 + 2048:b * 2064 + 2064],
            ctx.sm2[:, 64 + b * 16:80 + b * 16]), tag="kp_exp")

    if L0ATTN_STAGE < 4:
        ctx.wreg_free = [last_use[0], last_use[1]]
        return
    S.dma("sync", lambda e: e.dma_start(out=ctx.d_blk[:, :], in_=ctx.dsb[:, :]),
          tag="blk_out")
    S.op("gpsimd", lambda e: e.collective_compute(
        "AllGather", ALU.bypass, replica_groups=[list(range(NCORES))],
        ins=[ctx.d_blk.ap().opt()], outs=[ctx.d_ag.ap().opt()]), sem="coll", tag="ag")
    for half in range(2):
        S.op("vector", lambda e: e.memset(ctx.accf[:, :], 0.0), tag="accz")
        for r in range(NCORES):
            S.dma("sync", lambda e, r=r, half=half: e.dma_start(
                out=ctx.dbuf[:, 0:2064],
                in_=ctx.d_ag[r * 128:(r + 1) * 128, half * 2064:(half + 1) * 2064]),
                tag="ag_ld")
            S.op("vector", lambda e, r=r: e.scalar_tensor_tensor(
                ctx.accf[:, :], ctx.dbuf[:, 0:2064],
                ctx.pmask[:, r:r + 1], ctx.accf[:, :],
                ALU.mult, ALU.add), tag="ag_acc")
        S.op("vector", lambda e, half=half: e.tensor_copy(
            ctx.acc[:, half * 2064:(half + 1) * 2064], ctx.accf[:, 0:2064]),
            tag="acc_bf")

    if getattr(ctx, "dbg_acc", False):
        emit_dbg(ctx, ctx.acc[:, 0:4096], 128, 4096)
    if L0ATTN_STAGE < 5:
        ctx.wreg_free = [last_use[0], last_use[1]]
        return
    den_sb = ctx.transf[:, 1664:1668]
    rec_sb = ctx.transf[:, 1668:1672]
    am_off = [0, 512, 896, 1152]
    for b in range(2):
        for h in range(32):
            pb_ = (h % 2) * 64
            cch = (h // 2) * T + b * 512
            spref = ctx.acc[pb_:pb_ + 64, b * 2064 + h * 64:b * 2064 + h * 64 + 64]
            kprev = ctx.acc[pb_:pb_ + 64, b * 2064 + 2048 + (h // 2):b * 2064 + 2048 + (h // 2) + 1]
            fq_ap = ctx.rB[pb_:pb_ + 64, cch:cch + 512]
            fk_ap = ctx.rC[pb_:pb_ + 64, cch:cch + 512]
            for jc in range(4):
                nw = 512 - jc * 128
                aps = ctx.pA[:, (jc % 2) * 512:(jc % 2) * 512 + nw]
                mm_op(ctx, pb_, lambda e, jc=jc, nw=nw, aps=aps, fk_ap=fk_ap, fq_ap=fq_ap: e.matmul(
                    aps, fk_ap[:, jc * 128:(jc + 1) * 128], fq_ap[:, jc * 128:512],
                    start=True, stop=True), tag="ap_mm")
                am = ctx.trans[:, 2048 + am_off[jc]:2048 + am_off[jc] + nw]
                S.op("vector", lambda e, jc=jc, aps=aps, am=am: e.tensor_tensor(
                    am[:, 0:128], aps[:, 0:128], ctx.mask[:, 0:128], ALU.mult),
                    tag="am_diag")
                if nw > 128:
                    S.op("vector", lambda e, jc=jc, nw=nw, aps=aps, am=am: e.tensor_copy(
                        am[:, 128:nw], aps[:, 128:nw]), tag="am_off")
            for ic in range(4):
                yps = ctx.pB[:, ic * 64:(ic + 1) * 64]
                first = True
                for jc in range(ic + 1):
                    am = ctx.trans[:, 2048 + am_off[jc] + (ic - jc) * 128:
                                   2048 + am_off[jc] + (ic - jc + 1) * 128]
                    mm_op(ctx, 0, lambda e, am=am, b=b, jc=jc, h=h, yps=yps, first=first: e.matmul(
                        yps, am,
                        ctx.rD[:, (b * 4 + jc) * 2048 + h * 64:(b * 4 + jc) * 2048 + (h + 1) * 64],
                        start=first, stop=False), tag="y_intra")
                    first = False
                mm_op(ctx, pb_, lambda e, ic=ic, yps=yps, fq_ap=fq_ap, spref=spref: e.matmul(
                    yps, fq_ap[:, ic * 128:(ic + 1) * 128], spref,
                    start=False, stop=True), tag="y_inter")
                dps = ctx.pB[:, 256 + ic:256 + ic + 1]
                mm_op(ctx, pb_, lambda e, ic=ic, dps=dps, fq_ap=fq_ap, kprev=kprev: e.matmul(
                    dps, fq_ap[:, ic * 128:(ic + 1) * 128], kprev,
                    start=True, stop=False), tag="dn_kp")
                for jc in range(ic + 1):
                    am = ctx.trans[:, 2048 + am_off[jc] + (ic - jc) * 128:
                                   2048 + am_off[jc] + (ic - jc + 1) * 128]
                    mm_op(ctx, 0, lambda e, am=am, dps=dps, jc=jc, ic=ic: e.matmul(
                        dps, am, ctx.onecol[:, :],
                        start=False, stop=(jc == ic)), tag="dn_am")
            S.op("vector", lambda e: e.tensor_copy(den_sb, ctx.pB[:, 256:260]),
                 tag="dn_ev")
            S.op("vector", lambda e: e.reciprocal(rec_sb, den_sb), tag="dn_rc")
            for ic in range(4):
                S.op("vector", lambda e, ic=ic, b=b, h=h: e.tensor_scalar(
                    ctx.rA[:, (b * 4 + ic) * 2048 + h * 64:(b * 4 + ic) * 2048 + (h + 1) * 64],
                    ctx.pB[:, ic * 64:(ic + 1) * 64], rec_sb[:, ic:ic + 1], None,
                    ALU.mult), tag="y_div")

    for m in range(8):
        for c in range(KD):
            pt = ctx.pA[:, (c % 2) * 512:(c % 2) * 512 + 64].bitcast(BF16)
            mm_op(ctx, 0, lambda e, m=m, c=c, pt=pt: e.transpose(
                pt[:, 0:128], ctx.rA[:, m * 2048 + c * 128:m * 2048 + (c + 1) * 128],
                ctx.identb[:, :]), tag="y_tr")
            tcol = (m // 4) * 512 + (m % 4) * 128
            S.op("vector", lambda e, m=m, c=c, pt=pt, tcol=tcol: e.tensor_copy(
                ctx.rB[:, c * T + tcol:c * T + tcol + 128], pt[:, 0:128]),
                tag="y_trev")

    in_tok5 = S.tail("main")
    last2 = [None, None]
    ld2 = [None, None]
    psum_free5 = [None, None, None, None]
    for o in range(KD):
        s = o % 2
        wsl = ctx.wreg[:, s * 2048:s * 2048 + 2048]
        w8 = (in_tok5,) if o < 2 else (last2[s],)
        ld2[s] = S.dma("gpsimd", lambda e, o=o, wsl=wsl: e.dma_start(
            out=wsl, in_=ctx.d_wo[o, :, :]), waits=w8,
            stream="qv%d" % s, sem="qvs%d" % s, tag="wld_wo")
        for tt in range(2):
            sl = s * 2 + tt
            ps = ctx.pA[:, sl * 512:sl * 512 + 512]
            for kc in range(KD):
                w0 = (ld2[s], psum_free5[sl],
                      in_tok5 if o < 2 else None) if kc == 0 else ()
                tok = S.op("tensor", lambda e, kc=kc, wsl=wsl, ps=ps, tt=tt: e.matmul(
                    ps, wsl[:, kc * 128:(kc + 1) * 128],
                    ctx.rB[:, kc * T + tt * 512:kc * T + (tt + 1) * 512],
                    start=(kc == 0), stop=(kc == KD - 1)),
                    waits=w0, stream="mm", tag="wo_mm")
            xt = x_tile(ctx, o, (tt * 512, (tt + 1) * 512))
            psum_free5[sl] = S.op("vector", lambda e, ps=ps, xt=xt: e.tensor_tensor(
                xt, xt, ps, ALU.add), waits=(tok,), tag="wo_res")
        last2[s] = tok
    ctx.wreg_free = [last2[0], last2[1]]


def _decl_l1_attn(nc, ctx):
    ctx.d_wqkv = nc.dram_tensor("l1a_wqkv", [24, 128, 2048], F32, kind="ExternalInput").ap()
    ctx.d_wproj = nc.dram_tensor("l1a_wproj", [16, 128, 2048], F32, kind="ExternalInput").ap()


def emit_l1_attn(ctx):
    S = ctx.sched
    wreg_free = getattr(ctx, "wreg_free", [None, None])

    su_row = emit_ones_row(ctx, 0, 2048)
    emit_ln(ctx, su_row, ctx.rA, 0, tw=T)
    ztiles = [ctx.rA[:, c * T:(c + 1) * T] for c in range(KD)]

    def qkv_tile(o):
        reg = ctx.rB if o < 16 else ctx.rC
        return reg[:, (o % 16) * T:(o % 16) * T + T]

    in_tok = S.tail("main")
    last_use = [wreg_free[0], wreg_free[0]]
    ld_tok = [None, None]
    psum_free = [None, None, None, None]
    for o in range(24):
        s = o % 2
        wsl = ctx.wreg[:, s * 2048:s * 2048 + 2048]
        ld_tok[s] = S.dma("gpsimd", lambda e, o=o, wsl=wsl: e.dma_start(
            out=wsl, in_=ctx.d_wqkv[o, :, :]), waits=(last_use[s],),
            stream="qk%d" % s, sem="qks%d" % s, tag="wld_qkv")
        for tt in range(2):
            sl = s * 2 + tt
            ps = ctx.pA[:, sl * 512:sl * 512 + 512]
            for k in range(KD):
                w0 = (ld_tok[s], psum_free[sl],
                      in_tok if o < 2 else None) if k == 0 else ()
                tok = S.op("tensor", lambda e, k=k, wsl=wsl, ps=ps, tt=tt: e.matmul(
                    ps, wsl[:, k * 128:(k + 1) * 128],
                    ztiles[k][:, tt * 512:(tt + 1) * 512],
                    start=(k == 0), stop=(k == KD - 1)),
                    waits=w0, stream="mm", tag="qkv_mm")
            psum_free[sl] = S.op("scalar", lambda e, o=o, tt=tt, ps=ps: e.copy(
                qkv_tile(o)[:, tt * 512:(tt + 1) * 512], ps),
                waits=(tok,), nosync=True, tag="qkv_ev")
        last_use[s] = tok
    S.op("vector", lambda e: e.nop(),
         waits=tuple(t for t in psum_free if t is not None), tag="qkv_end")

    dv = ctx.trans[:, 2048:2560]
    prod = ctx.trans[:, 2560:3072]
    rDf = ctx.rD[:, :].bitcast(F32)
    s0_sb = rDf[0:1, 4096:6144]
    sd_sb = rDf[0:1, 6144:8192]
    p1row = ctx.rD[0:1, 8192:10240]
    sc = 1.0 / np.sqrt(128.0)
    for g in range(4):
        kt = qkv_tile(g * 6 + 4)
        vt = qkv_tile(g * 6 + 5)
        S.op("vector", lambda e, vt=vt: e.tensor_tensor(
            dv, vt[:, 512:1024], vt[:, 0:512], ALU.subtract), tag="dv")
        for qi in range(4):
            qt = qkv_tile(g * 6 + qi)
            S.op("vector", lambda e, qt=qt, kt=kt: e.tensor_tensor(
                prod, qt[:, 512:1024], kt[:, 0:512], ALU.mult), tag="p0")
            S.op("tensor", lambda e, qi=qi: e.matmul(
                ctx.pA[0:1, qi * 512:qi * 512 + 512], ctx.onecol[:, :], prod,
                start=True, stop=True), tag="s0mm")
            S.op("vector", lambda e, qt=qt, kt=kt: e.tensor_tensor(
                prod, qt[:, 512:1024], kt[:, 512:1024], ALU.mult), tag="p1")
            S.op("tensor", lambda e, qi=qi: e.matmul(
                ctx.pA[64:65, qi * 512:qi * 512 + 512], ctx.onecol[:, :], prod,
                start=True, stop=True), tag="s1mm")
        S.op("vector", lambda e: e.tensor_scalar(
            s0_sb, ctx.pA[0:1, 0:2048], sc, None, ALU.mult), tag="s0ev")
        S.op("vector", lambda e: e.scalar_tensor_tensor(
            sd_sb, ctx.pA[64:65, 0:2048], sc, s0_sb,
            ALU.mult, ALU.subtract), tag="sdiff")
        S.op("scalar", lambda e: e.activation(
            p1row, sd_sb, AF.Sigmoid), tag="sig")
        for qi in range(4):
            h = g * 4 + qi
            pbc = ctx.pB[:, qi * 512:qi * 512 + 512]
            S.op("tensor", lambda e, qi=qi, pbc=pbc: e.matmul(
                pbc, ctx.onerow[:, :], p1row[0:1, qi * 512:qi * 512 + 512],
                start=True, stop=True), tag="pbc")
            S.op("vector", lambda e, pbc=pbc: e.tensor_tensor(
                prod, pbc, dv, ALU.mult), tag="odv")
            ot = ctx.rD[:, h * 512:(h + 1) * 512]
            S.op("vector", lambda e, ot=ot, vt=vt: e.tensor_tensor(
                ot, prod, vt[:, 0:512], ALU.add), tag="oadd")

    in_tok2 = S.tail("main")
    last2 = [last_use[0], last_use[1]]
    ld2 = [None, None]
    psum_free2 = [None, None, None, None]
    for o in range(KD):
        s = o % 2
        wsl = ctx.wreg[:, s * 2048:s * 2048 + 2048]
        ld2[s] = S.dma("gpsimd", lambda e, o=o, wsl=wsl: e.dma_start(
            out=wsl, in_=ctx.d_wproj[o, :, :]), waits=(last2[s],),
            stream="qk%d" % s, sem="qks%d" % s, tag="wld_prj")
        for tt in range(2):
            sl = s * 2 + tt
            ps = ctx.pA[:, sl * 512:sl * 512 + 512]
            for hc in range(KD):
                g = hc // 4
                if tt == 0:
                    rhs = qkv_tile(g * 6 + 5)[:, 0:512]
                else:
                    rhs = ctx.rD[:, hc * 512:(hc + 1) * 512]
                w0 = (ld2[s], psum_free2[sl],
                      in_tok2 if o < 2 else None) if hc == 0 else ()
                tok = S.op("tensor", lambda e, hc=hc, wsl=wsl, ps=ps, rhs=rhs: e.matmul(
                    ps, wsl[:, hc * 128:(hc + 1) * 128], rhs,
                    start=(hc == 0), stop=(hc == KD - 1)),
                    waits=w0, stream="mm", tag="prj_mm")
            xt = x_tile(ctx, o, (tt * 512, (tt + 1) * 512))
            psum_free2[sl] = S.op("vector", lambda e, ps=ps, xt=xt: e.tensor_tensor(
                xt, xt, ps, ALU.add), waits=(tok,), tag="prj_res")
        last2[s] = tok
    ctx.wreg_free = [last2[0], last2[1]]


def _decl_l1_ffn(nc, ctx):
    ctx.d_wg = nc.dram_tensor("l1f_wg", [16, 128, 8], F32, kind="ExternalInput").ap()
    ctx.d_wgu = nc.dram_tensor("l1f_wgu", [8, 16, 128, 2048], F32, kind="ExternalInput").ap()
    ctx.d_wdn = nc.dram_tensor("l1f_wdn", [8, 16, 128, 1024], F32, kind="ExternalInput").ap()
    ctx.d_wshgu = nc.dram_tensor("l1f_wshgu", [64, 128, 2048], F32, kind="ExternalInput").ap()
    ctx.d_wshdn = nc.dram_tensor("l1f_wshdn", [16, 128, 4096], F32, kind="ExternalInput").ap()


L1FFN_PARTS = ("moe", "shared")


def emit_l1_ffn(ctx):
    S = ctx.sched
    wreg_free = getattr(ctx, "wreg_free", [None, None])

    su_row = emit_ones_row(ctx, 0, 2048)
    emit_ln(ctx, su_row, ctx.rA, 0, tw=T)
    ztiles = [ctx.rA[:, c * T:(c + 1) * T] for c in range(KD)]

    do_moe = "moe" in L1FFN_PARTS
    do_shared = "shared" in L1FFN_PARTS
    wg_sb = ctx.trans[:, 2048:2176]
    S.dma("gpsimd", lambda e: e.dma_start(
        out=wg_sb.rearrange("p (k u) -> p k u", u=8),
        in_=ctx.d_wg.rearrange("k p u -> p k u")), tag="wld_rt")
    compfm = ctx.rD[0:8, 2048:3072]
    comb_tm = ctx.trans[:, 2176:2184]
    pBb = ctx.pB[:, 512:1024].bitcast(BF16)
    m1 = ctx.sm2[:, 208:209]
    m2 = ctx.sm2[:, 209:210]
    nm1 = ctx.sm2[:, 210:211]
    ds = ctx.sm2[:, 211:212]
    tmp8 = ctx.sm2[:, 212:220]
    l2 = ctx.sm2[:, 220:228]
    g8 = ctx.sm2[:, 228:236]
    sel = ctx.sm2[:, 236:244]
    rc = ctx.sm2[:, 244:245]
    for m in range(8 if do_moe else 0):
        lps = ctx.pA[:, m * 8:(m + 1) * 8]
        for k in range(KD):
            S.op("tensor", lambda e, m=m, k=k, lps=lps: e.matmul(
                lps, ztiles[k][:, m * 128:(m + 1) * 128], wg_sb[:, k * 8:(k + 1) * 8],
                start=(k == 0), stop=(k == KD - 1)), tag="rt_mm")
        S.op("vector", lambda e, lps=lps: e.reduce_max(m1, lps, axis=AX.X))
        S.op("vector", lambda e, lps=lps: e.tensor_scalar(tmp8, lps, m1, None, ALU.is_ge))
        S.op("vector", lambda e, lps=lps: e.scalar_tensor_tensor(
            l2, tmp8, -1e30, lps, ALU.mult, ALU.add))
        S.op("vector", lambda e: e.reduce_max(m2, l2, axis=AX.X))
        S.op("vector", lambda e: e.tensor_scalar(nm1, m1, -1.0, None, ALU.mult))
        S.op("scalar", lambda e, lps=lps: e.activation(g8, lps, AF.Exp, bias=nm1))
        S.op("vector", lambda e, lps=lps: e.tensor_scalar(sel, lps, m2, None, ALU.is_ge))
        S.op("vector", lambda e: e.tensor_tensor(g8, g8, sel, ALU.mult))
        S.op("vector", lambda e: e.reduce_sum(ds, g8, axis=AX.X))
        S.op("vector", lambda e: e.reciprocal(rc, ds))
        S.op("vector", lambda e: e.tensor_scalar(comb_tm, g8, rc, None, ALU.mult))
        S.op("tensor", lambda e, m=m: e.transpose(
            pBb[0:8, m * 128:(m + 1) * 128], comb_tm, ctx.identb[:, :]), tag="rt_tr")
        S.op("vector", lambda e, m=m: e.tensor_copy(
            compfm[:, m * 128:(m + 1) * 128], pBb[0:8, m * 128:(m + 1) * 128]))

    cbc_sb = ctx.trans[:, 2176:3200]
    silu_t = ctx.trans[:, 3200:4224]
    t1 = ctx.trans[:, 4224:5248]
    ce_row = ctx.rD[0:1, 1024:2048]

    def act_tile(e, k):
        return ctx.rB[:, k * 1024:(k + 1) * 1024]

    last_gu = [wreg_free[0], wreg_free[0]]
    last_dn = [None, wreg_free[1]]
    ld_tok = [None, None]
    if do_moe:
        in_tok = S.tail("main")
        psum_free_gu = [None, None]
        psum_free_dn = [None, None]
        cbc_last = None
        dn_ld = [None, None]
        for e in range(8):
            ce_tok = S.dma("sync", lambda e_, e=e: e_.dma_start(
                out=ce_row, in_=compfm[e:e + 1, 0:1024]),
                waits=(cbc_last,), tag="ce_dma")
            cbc_mms = []
            for i2 in range(2):
                pcb = ctx.pB[:, 1024 + i2 * 512:1536 + i2 * 512]
                cbc_mms.append(S.op("tensor", lambda e_, pcb=pcb, i2=i2: e_.matmul(
                    pcb, ctx.onerow[:, :], ce_row[0:1, i2 * 512:(i2 + 1) * 512],
                    start=True, stop=True),
                    waits=(ce_tok, cbc_last) if i2 == 0 else (),
                    stream="mm", tag="cbc_mm"))
            cbc_ev = S.op("vector", lambda e_: e_.tensor_copy(
                cbc_sb, ctx.pB[:, 1024:2048]), waits=(cbc_mms[1],), tag="cbc_ev")
            for j in range(8):
                s = (e * 8 + j) % 2
                wga = ctx.wreg[:, s * 4096:s * 4096 + 2048]
                wua = ctx.wreg[:, s * 4096 + 2048:s * 4096 + 4096]
                S.dma("gpsimd", lambda e_, e=e, j=j, wga=wga: e_.dma_start(
                    out=wga, in_=ctx.d_wgu[e, j, :, :]), waits=(last_gu[s],),
                    stream="gu%d" % s, sem="gus%d" % s, tag="wld_g")
                ld_tok[s] = S.dma("gpsimd", lambda e_, e=e, j=j, wua=wua: e_.dma_start(
                    out=wua, in_=ctx.d_wgu[e, j + 8, :, :]),
                    stream="gu%d" % s, sem="gus%d" % s, tag="wld_u")
                pg = ctx.pA[:, 0:1024]
                pu = ctx.pA[:, 1024:2048]
                for tt in range(2):
                    for k in range(KD):
                        w0 = (ld_tok[s], psum_free_gu[0],
                              in_tok if (e == 0 and j == 0) else None) \
                            if (k == 0 and tt == 0) else ()
                        tokg = S.op("tensor", lambda e_, k=k, tt=tt, wga=wga, pg=pg: e_.matmul(
                            pg[:, tt * 512:(tt + 1) * 512],
                            wga[:, k * 128:(k + 1) * 128],
                            ztiles[k][:, tt * 512:(tt + 1) * 512],
                            start=(k == 0), stop=(k == KD - 1)),
                            waits=w0, stream="mm", tag="gu_g_mm")
                for tt in range(2):
                    for k in range(KD):
                        w0 = (psum_free_gu[1],) if (k == 0 and tt == 0) else ()
                        tok = S.op("tensor", lambda e_, k=k, tt=tt, wua=wua, pu=pu: e_.matmul(
                            pu[:, tt * 512:(tt + 1) * 512],
                            wua[:, k * 128:(k + 1) * 128],
                            ztiles[k][:, tt * 512:(tt + 1) * 512],
                            start=(k == 0), stop=(k == KD - 1)),
                            waits=w0, stream="mm", tag="gu_u_mm")
                last_gu[s] = tok
                psum_free_gu[0] = S.op("scalar", lambda e_, pg=pg: e_.activation(
                    silu_t, pg, AF.Silu), waits=(tokg,), tag="moe_silu")
                tok_t1 = S.op("vector", lambda e_, pu=pu: e_.tensor_tensor(
                    t1, pu, cbc_sb, ALU.mult), waits=(tok, cbc_ev), tag="moe_t1")
                psum_free_gu[1] = tok_t1
                act_done = S.op("vector", lambda e_, e=e, j=j: e_.tensor_tensor(
                    act_tile(e, j), silu_t, t1, ALU.mult), tag="moe_act")
            w_init = wreg_free[1]
            for o in range(KD):
                s = o % 2
                wdn = ctx.wreg[:, 8192 + s * 4096:8192 + s * 4096 + 1024]
                dn_prev = last_dn[s]
                dn_ld[s] = S.dma("gpsimd", lambda e_, e=e, o=o, wdn=wdn: e_.dma_start(
                    out=wdn, in_=ctx.d_wdn[e, o, :, :]),
                    waits=(w_init,) if (e == 0 and o < 2) else (dn_prev,),
                    stream="dn%d" % s, sem="dns%d" % s, tag="wld_dn")
                pd = ctx.pB[:, 0:1024]
                for tt in range(2):
                    for k in range(8):
                        w0 = (dn_ld[s], psum_free_dn[0],
                              act_done if o == 0 else None) \
                            if (k == 0 and tt == 0) else ()
                        tok = S.op("tensor", lambda e_, e=e, k=k, tt=tt, wdn=wdn, pd=pd: e_.matmul(
                            pd[:, tt * 512:(tt + 1) * 512],
                            wdn[:, k * 128:(k + 1) * 128],
                            act_tile(e, k)[:, tt * 512:(tt + 1) * 512],
                            start=(k == 0), stop=(k == 7)),
                            waits=w0, stream="mm", tag="dn_mm")
                last_dn[s] = tok
                xt = x_tile(ctx, o)
                psum_free_dn[0] = S.op("vector", lambda e_, pd=pd, xt=xt: e_.tensor_tensor(
                    xt, xt, pd, ALU.add), waits=(tok,), tag="dn_res")
            cbc_last = psum_free_dn[0]
    def sh_tile(o):
        reg = ctx.rB if o < 16 else ctx.rC
        return reg[:, (o % 16) * T:(o % 16) * T + T]

    if not do_moe:
        last_dn = [wreg_free[0], wreg_free[1]]
    in_tok3 = S.tail("main")
    last_sh = [last_dn[0], last_dn[0]]
    ld_sh = [None, None]
    psum_free_sh = [None, None, None, None]
    for o in range(32 if do_shared else 0):
        s = o % 2
        wga = ctx.wreg[:, s * 4096:s * 4096 + 2048]
        wua = ctx.wreg[:, s * 4096 + 2048:s * 4096 + 4096]
        S.dma("gpsimd", lambda e_, o=o, wga=wga: e_.dma_start(
            out=wga, in_=ctx.d_wshgu[o, :, :]), waits=(last_sh[s],),
            stream="sg%d" % s, sem="sgs%d" % s, tag="wld_shg")
        ld_sh[s] = S.dma("gpsimd", lambda e_, o=o, wua=wua: e_.dma_start(
            out=wua, in_=ctx.d_wshgu[o + 32, :, :]),
            stream="sg%d" % s, sem="sgs%d" % s, tag="wld_shu")
        for tt in range(2):
            sl = s * 2 + tt
            pg = ctx.pA[:, s * 1024 + tt * 512:s * 1024 + (tt + 1) * 512]
            pu = ctx.pB[:, s * 1024 + tt * 512:s * 1024 + (tt + 1) * 512]
            for k in range(KD):
                w0 = (ld_sh[s], psum_free_sh[sl],
                      in_tok3 if o < 2 else None) if k == 0 else ()
                S.op("tensor", lambda e_, k=k, wga=wga, pg=pg, tt=tt: e_.matmul(
                    pg, wga[:, k * 128:(k + 1) * 128],
                    ztiles[k][:, tt * 512:(tt + 1) * 512],
                    start=(k == 0), stop=(k == KD - 1)),
                    waits=w0, stream="mm", tag="sh_g_mm")
            for k in range(KD):
                tok = S.op("tensor", lambda e_, k=k, wua=wua, pu=pu, tt=tt: e_.matmul(
                    pu, wua[:, k * 128:(k + 1) * 128],
                    ztiles[k][:, tt * 512:(tt + 1) * 512],
                    start=(k == 0), stop=(k == KD - 1)), stream="mm", tag="sh_u_mm")
            last_sh[s] = tok
            S.op("scalar", lambda e_, pg=pg, tt=tt: e_.activation(
                silu_t[:, tt * 512:(tt + 1) * 512], pg, AF.Silu),
                waits=(tok,), tag="sh_silu")
            psum_free_sh[sl] = S.op("vector", lambda e_, o=o, tt=tt, pu=pu: e_.tensor_tensor(
                sh_tile(o)[:, tt * 512:(tt + 1) * 512],
                silu_t[:, tt * 512:(tt + 1) * 512], pu, ALU.mult),
                tag="sh_act")
    in_tok4 = S.tail("main")
    last_sd = [None, None]
    psum_free_sd = [None, None, None, None]
    for o in range(KD if do_shared else 0):
        s = o % 2
        wsd = ctx.wreg[:, s * 8192:s * 8192 + 4096]
        if o == 0:
            w8 = (last_sh[0], last_sh[1])
        elif o == 1:
            w8 = (last_dn[1],) if last_dn[1] is not None else ()
        else:
            w8 = (last_sd[s],)
        ldd = S.dma("gpsimd", lambda e_, o=o, wsd=wsd: e_.dma_start(
            out=wsd, in_=ctx.d_wshdn[o, :, :]), waits=w8,
            stream="sd%d" % s, sem="sds%d" % s, tag="wld_shd")
        for tt in range(2):
            sl = s * 2 + tt
            pd = ctx.pA[:, (s * 2 + tt) * 512:(s * 2 + tt) * 512 + 512]
            for k in range(32):
                w0 = (ldd, psum_free_sd[sl],
                      in_tok4 if o < 2 else None) if k == 0 else ()
                tok = S.op("tensor", lambda e_, k=k, wsd=wsd, pd=pd, tt=tt: e_.matmul(
                    pd, wsd[:, k * 128:(k + 1) * 128],
                    sh_tile(k)[:, tt * 512:(tt + 1) * 512],
                    start=(k == 0), stop=(k == 31)),
                    waits=w0, stream="mm", tag="shd_mm")
            xt = x_tile(ctx, o, (tt * 512, (tt + 1) * 512))
            psum_free_sd[sl] = S.op("vector", lambda e_, pd=pd, xt=xt: e_.tensor_tensor(
                xt, xt, pd, ALU.add), waits=(tok,), tag="shd_res")
        last_sd[s] = tok
    if do_shared:
        ctx.wreg_free = [last_sd[0], last_sd[1]]
    else:
        ctx.wreg_free = [last_dn[0], last_dn[1]]


def _prep_consts():
    h128 = make_hadamard(128).astype(np.float32)
    ident = np.eye(128, dtype=np.float32)
    mask = (np.arange(128)[:, None] <= np.arange(512)[None, :]).astype(np.float32)
    return {"c_h128": h128, "c_h128n": -h128, "c_ident": ident, "c_mask": mask}


def _prep_pmask(core):
    pm = np.zeros((128, 8), dtype=np.float32)
    pm[:, :core] = 1.0
    return pm


def _prep_l0_attn(inp):
    return {
        "l0a_wq": tile_fm(np.ascontiguousarray(np.asarray(inp["L0_attn_Wq"]).T), D, D),
        "l0a_wk": tile_fm(np.ascontiguousarray(np.asarray(inp["L0_attn_Wk"]).T), D, D),
        "l0a_wv": tile_fm(np.ascontiguousarray(np.asarray(inp["L0_attn_Wv"]).T), D, D),
        "l0a_wo": tile_fm(np.ascontiguousarray(np.asarray(inp["L0_attn_Wo"]).T), D, D),
        "l0a_wfq": np.asarray(inp["L0_attn_Wfq"]),
        "l0a_wfk": np.asarray(inp["L0_attn_Wfk"]),
    }


def _prep_l1_attn(inp):
    wqkv = np.asarray(inp["L1_attn_weight_qkv"])
    wproj = np.asarray(inp["L1_attn_weight_proj"])
    return {
        "l1a_wqkv": tile_fm(np.ascontiguousarray(wqkv.T), D, 3072),
        "l1a_wproj": tile_fm(np.ascontiguousarray(wproj.T), D, D),
    }


def _prep_l1_ffn(inp):
    wg = np.asarray(inp["L1_ffn_w_gate"])
    wgu = np.asarray(inp["L1_ffn_w_gate_up"])
    wdn = np.asarray(inp["L1_ffn_w_down"])
    wshgu = np.asarray(inp["L1_ffn_w_sh_gate_up"])
    wshdn = np.asarray(inp["L1_ffn_w_sh_down"])
    return {
        "l1f_wg": np.ascontiguousarray(wg.reshape(16, 128, 8)),
        "l1f_wgu": np.stack([tile_fm(wgu[e], D, D) for e in range(8)]),
        "l1f_wdn": np.stack([tile_fm(wdn[e], 1024, D) for e in range(8)]),
        "l1f_wshgu": tile_fm(wshgu, D, 8192),
        "l1f_wshdn": tile_fm(wshdn, 4096, D),
    }


def _prep_l0_ffn(inp):
    up = np.asarray(inp["L0_ffn_W_up"])
    gate = np.asarray(inp["L0_ffn_W_gate"])
    down = np.asarray(inp["L0_ffn_W_down"])
    wsc = np.asarray(inp["L0_ffn_Wscale_ug"])
    upt = tile_fm(np.ascontiguousarray(up.T), D, 8192)
    gatet = tile_fm(np.ascontiguousarray(gate.T), D, 8192)
    wupgate = np.concatenate([
        upt[0::2], gatet[0::2], upt[1::2], gatet[1::2]], axis=2)
    wupgate = np.ascontiguousarray(
        np.concatenate([upt.reshape(32, 2, 128, 2048)[:, 0],
                        gatet.reshape(32, 2, 128, 2048)[:, 0],
                        upt.reshape(32, 2, 128, 2048)[:, 1],
                        gatet.reshape(32, 2, 128, 2048)[:, 1]], axis=2))
    return {
        "l0f_wupgate": wupgate,
        "l0f_wdown": tile_fm(np.ascontiguousarray(down.T), 8192, D),
        "l0f_suug": np.asarray(inp["L0_ffn_SU_ug"]).reshape(1, D),
        "l0f_sudp": pmajor(np.asarray(inp["L0_ffn_SU_dp"])),
        "l0f_wsu_up": pmajor(wsc[:8192]),
        "l0f_wsu_g": pmajor(wsc[8192:]),
        "l0f_wsdp": pmajor(np.asarray(inp["L0_ffn_Wscale_dp"])),
    }


DBG_ACC = False
OLD8192 = False
OLD2048 = False
DBG_FOLD = False


def build_program(blocks):
    nc = bass.Bass("TRN2", target_bir_lowering=False)
    es = contextlib.ExitStack()
    ctx = Ctx()
    with es:
        _alloc_common(nc, es, ctx)
        ctx.dbg_acc = DBG_ACC
        ctx.old8192 = OLD8192
        ctx.old2048 = OLD2048
        ctx.dbg_fold = DBG_FOLD
        if "l0_attn" in blocks:
            _decl_l0_attn(nc, ctx)
        if "l0_ffn" in blocks:
            _decl_l0_ffn(nc, ctx)
        if "l1_attn" in blocks:
            _decl_l1_attn(nc, ctx)
        if "l1_ffn" in blocks:
            _decl_l1_ffn(nc, ctx)
        for b in blocks:
            if b == "l0_attn":
                emit_l0_attn(ctx)
            elif b == "l0_ffn":
                emit_l0_ffn(ctx)
            elif b == "l1_attn":
                emit_l1_attn(ctx)
            elif b == "l1_ffn":
                emit_l1_ffn(ctx)
        _emit_store_out(ctx)
        block = es.enter_context(nc.Block())
        ctx.sched.build(nc, block, es)
    return nc


def shard_x(x):
    shards = []
    for c in range(NCORES):
        xs = np.concatenate([x[0, c * TB:(c + 1) * TB, :],
                             x[1, c * TB:(c + 1) * TB, :]], axis=0)
        shards.append(np.ascontiguousarray(xs.T))
    return shards


def unshard_x(outs):
    x = np.empty((2, NCORES * TB, D), dtype=np.float32)
    for c in range(NCORES):
        o = np.asarray(outs[c])
        x[0, c * TB:(c + 1) * TB, :] = o[:, 0:TB].T
        x[1, c * TB:(c + 1) * TB, :] = o[:, TB:2 * TB].T
    return x


KERNEL_BLOCKS = BLOCKS_ALL


def kernel(**inputs):
    x = np.asarray(inputs["L0_attn_hidden_states"])
    consts = _prep_consts()
    w = {}
    if "l0_attn" in KERNEL_BLOCKS:
        w.update(_prep_l0_attn(inputs))
    if "l0_ffn" in KERNEL_BLOCKS:
        w.update(_prep_l0_ffn(inputs))
    if "l1_attn" in KERNEL_BLOCKS:
        w.update(_prep_l1_attn(inputs))
    if "l1_ffn" in KERNEL_BLOCKS:
        w.update(_prep_l1_ffn(inputs))
    nc = build_program(KERNEL_BLOCKS)
    shards = shard_x(x)
    in_maps = []
    for c in range(NCORES):
        m = {"x_fm": shards[c], "c_pmask": _prep_pmask(c)}
        m.update(consts)
        m.update(w)
        in_maps.append(m)
    res = run_bass_kernel_spmd(nc, in_maps, core_ids=list(range(NCORES)))
    global LAST_RESULTS
    LAST_RESULTS = res
    return unshard_x([r["out_fm"] for r in res.results])


LAST_RESULTS = None


# version 10 (final)
# speedup vs baseline: 1.3090x; 1.0498x over previous
import contextlib
import numpy as np
import ml_dtypes
import concourse.bass as bass
import concourse.mybir as mybir
from concourse.bass_utils import run_bass_kernel_spmd

F32 = mybir.dt.float32
BF16 = mybir.dt.bfloat16
AF = mybir.ActivationFunctionType
ALU = mybir.AluOpType
AX = mybir.AxisListType

NCORES = 8
D = 2048
KD = D // 128
T = 1024
TB = 512
BF = ml_dtypes.bfloat16

BLOCKS_ALL = ("l0_attn", "l0_ffn", "l1_attn", "l1_ffn")


class Sched:

    def __init__(self):
        self.ops = []
        self.force = set()
        self._tail = {}

    def tail(self, stream="main"):
        return self._tail.get(stream)

    def op(self, eng, fn, waits=(), stream="main", sem=None, inc=1, tag="",
           nosync=False):
        tok = len(self.ops)
        for w in waits:
            if w is not None:
                self.force.add(w)
        self.ops.append(dict(eng=eng, fn=fn, inc=inc, sem=sem or eng,
                             stream=stream, waits=[w for w in waits if w is not None],
                             tag=tag, nosync=nosync))
        self._tail[stream] = tok
        return tok

    def dma(self, eng, fn, waits=(), stream="main", sem=None, tag="", nosync=False):
        return self.op(eng, fn, waits=waits, stream=stream,
                       sem=sem or (eng + "_dma"), inc=16, tag=tag, nosync=nosync)

    def build(self, nc, block, es):
        n = len(self.ops)
        prev_in_stream = {}
        chain_wait = [None] * n
        for i, o in enumerate(self.ops):
            p = prev_in_stream.get(o["stream"])
            if p is not None and not o.get("nosync"):
                po = self.ops[p]
                if po["eng"] != o["eng"]:
                    chain_wait[i] = p
                elif (o["inc"] != 16 and po["inc"] != 16
                      and o["eng"] in ("vector", "scalar", "gpsimd")):
                    chain_wait[i] = p
            prev_in_stream[o["stream"]] = i
        last_of_stream = set(prev_in_stream.values())
        do_inc = [False] * n
        for i, o in enumerate(self.ops):
            if o["inc"] == 16 or i in self.force or i in last_of_stream:
                do_inc[i] = True
        for i in range(n):
            if chain_wait[i] is not None:
                do_inc[chain_wait[i]] = True
        sem_names = sorted({o["sem"] for i, o in enumerate(self.ops) if do_inc[i]})
        sems = {s: es.enter_context(nc.semaphore("sem_" + s)) for s in sem_names}
        cum = {s: 0 for s in sem_names}
        val_after = [0] * n
        for i, o in enumerate(self.ops):
            if do_inc[i]:
                cum[o["sem"]] += o["inc"]
                val_after[i] = cum[o["sem"]]
        resolved = [[] for _ in range(n)]
        for i, o in enumerate(self.ops):
            tgts = list(o["waits"])
            if chain_wait[i] is not None:
                tgts.append(chain_wait[i])
            seen = {}
            for tkn in tgts:
                assert do_inc[tkn], (i, tkn, self.ops[tkn])
                s = self.ops[tkn]["sem"]
                seen[s] = max(seen.get(s, 0), val_after[tkn])
            resolved[i] = [(sems[s], v) for s, v in sorted(seen.items())]

        def make_body(label):
            def body(eng):
                for i, o in enumerate(self.ops):
                    if o["eng"] != label:
                        continue
                    for s, v in resolved[i]:
                        eng.wait_ge(s, v)
                    inst = o["fn"](eng)
                    if do_inc[i]:
                        inst.then_inc(sems[o["sem"]], o["inc"])
            return body

        used = {o["eng"] for o in self.ops}
        for label, attr in [("tensor", block.tensor), ("vector", block.vector),
                            ("scalar", block.scalar), ("gpsimd", block.gpsimd),
                            ("sync", block.sync)]:
            if label in used:
                attr(make_body(label))


def tile_fm(wt, kdim, odim):
    kch, och = kdim // 128, odim // 128
    return np.ascontiguousarray(
        wt.reshape(kch, 128, och, 128).transpose(2, 1, 0, 3).reshape(och, 128, kch * 128))


def pmajor(v):
    n = v.shape[0] // 128
    return np.ascontiguousarray(v.reshape(n, 128).T)


def make_hadamard(n):
    h = np.array([[1.0]], dtype=np.float32)
    while h.shape[0] < n:
        h = np.block([[h, h], [h, -h]])
    return h


class Ctx:
    pass


def _alloc_common(nc, es, ctx):
    S = ctx.sched = Sched()
    ctx.nc = nc
    ctx.es = es

    ctx.d_x = nc.dram_tensor("x_fm", [D, T], F32, kind="ExternalInput").ap()
    ctx.d_out = nc.dram_tensor("out_fm", [D, T], F32, kind="ExternalOutput").ap()
    ctx.d_h128 = nc.dram_tensor("c_h128", [128, 128], F32, kind="ExternalInput").ap()
    ctx.d_h128n = nc.dram_tensor("c_h128n", [128, 128], F32, kind="ExternalInput").ap()
    ctx.d_ident = nc.dram_tensor("c_ident", [128, 128], F32, kind="ExternalInput").ap()
    ctx.d_mask = nc.dram_tensor("c_mask", [128, 512], F32, kind="ExternalInput").ap()
    ctx.d_pmask = nc.dram_tensor("c_pmask", [128, 8], F32, kind="ExternalInput").ap()
    ctx.d_dbg = nc.dram_tensor("dbg", [128, 8192], F32, kind="ExternalOutput").ap()
    ctx.dbg_off = [0]

    ctx.x = es.enter_context(nc.sbuf_tensor("x", [128, KD * T], BF16))
    ctx.rA = es.enter_context(nc.sbuf_tensor("rA", [128, 16384], BF16))
    ctx.rB = es.enter_context(nc.sbuf_tensor("rB", [128, 16384], BF16))
    ctx.rC = es.enter_context(nc.sbuf_tensor("rC", [128, 16384], BF16))
    ctx.rD = es.enter_context(nc.sbuf_tensor("rD", [128, 16384], BF16))
    ctx.wreg = es.enter_context(nc.sbuf_tensor("wreg", [128, 16384], BF16))
    ctx.trans = es.enter_context(nc.sbuf_tensor("trans", [128, 6656], BF16))
    ctx.dsb = ctx.wreg[:, 4096:8224]
    ctx.accf = ctx.wreg[:, 4096:8224].bitcast(F32)
    ctx.dbuf = ctx.wreg[:, 8224:10288]
    ctx.acc = ctx.wreg[:, 10288:14416]
    ctx.mask = ctx.wreg[:, 14416:14928]
    ctx.sm1 = es.enter_context(nc.sbuf_tensor("sm1", [128, 192], F32))
    ctx.sm2 = es.enter_context(nc.sbuf_tensor("sm2", [128, 256], F32))
    ctx.h128 = es.enter_context(nc.sbuf_tensor("h128", [128, 128], BF16))
    ctx.h128n = es.enter_context(nc.sbuf_tensor("h128n", [128, 128], BF16))
    ctx.identb = es.enter_context(nc.sbuf_tensor("identb", [128, 128], BF16))
    ctx.onecol = es.enter_context(nc.sbuf_tensor("onecol", [128, 1], BF16))
    ctx.onerow = es.enter_context(nc.sbuf_tensor("onerow", [1, 128], BF16))
    ctx.pmask = es.enter_context(nc.sbuf_tensor("pmask", [128, 8], F32))

    ctx.pA = es.enter_context(nc.psum_tensor("pA", [128, 2048], F32))
    ctx.pB = es.enter_context(nc.psum_tensor("pB", [128, 2048], F32))

    ctx.transf = ctx.trans[:, :].bitcast(F32)

    S.dma("gpsimd", lambda e: e.dma_start(out=ctx.h128[:, :], in_=ctx.d_h128[:, :]))
    S.dma("gpsimd", lambda e: e.dma_start(out=ctx.h128n[:, :], in_=ctx.d_h128n[:, :]))
    S.dma("gpsimd", lambda e: e.dma_start(out=ctx.identb[:, :], in_=ctx.d_ident[:, :]))
    S.dma("sync", lambda e: e.dma_start(out=ctx.pmask[:, :], in_=ctx.d_pmask[:, :]))
    S.op("vector", lambda e: e.memset(ctx.onecol[:, :], 1.0))
    S.op("vector", lambda e: e.memset(ctx.onerow[:, :], 1.0))
    for c in range(KD):
        S.dma("gpsimd", lambda e, c=c: e.dma_start(
            out=ctx.x[:, c * T:(c + 1) * T], in_=ctx.d_x[c * 128:(c + 1) * 128, :]))


def _emit_store_out(ctx):
    S = ctx.sched
    for c in range(KD):
        S.dma("gpsimd", lambda e, c=c: e.dma_start(
            out=ctx.d_out[c * 128:(c + 1) * 128, :], in_=ctx.x[:, c * T:(c + 1) * T]))
    S.op("scalar", lambda e: e.nop(), tag="fence")


def emit_dbg(ctx, ap, rows, cols):
    off = ctx.dbg_off[0]
    ctx.sched.dma("gpsimd", lambda e: e.dma_start(
        out=ctx.d_dbg[0:rows, off:off + cols], in_=ap), tag="dbg")
    ctx.dbg_off[0] += cols


def x_tile(ctx, c, tcols=None):
    if tcols is None:
        return ctx.x[:, c * T:(c + 1) * T]
    return ctx.x[:, c * T + tcols[0]:c * T + tcols[1]]


def t_row_bf(ctx, off, w):
    return ctx.trans[0:1, off:off + w]


def t_tile_bf(ctx, off, w, p0=0, p1=128):
    return ctx.trans[p0:p1, off:off + w]


def emit_ln(ctx, su_row, out_region, out_off, tw=T, tbase=0):
    S = ctx.sched
    nt = tw // 512
    x2t = ctx.trans[:, 2048:2560]

    def xt(c, i):
        return ctx.x[:, c * T + tbase + i * 512: c * T + tbase + (i + 1) * 512]

    for i in range(nt):
        psx = ctx.pA[0:1, 1024 + i * 512:1536 + i * 512]
        psq = ctx.pB[0:1, 1024 + i * 512:1536 + i * 512]
        for c in range(KD):
            S.op("scalar", lambda e, c=c, i=i: e.activation(
                x2t, xt(c, i), AF.Square), tag="ln_sq")
            S.op("tensor", lambda e, c=c, i=i, psx=psx: e.matmul(
                psx, ctx.onecol[:, :], xt(c, i), start=(c == 0), stop=(c == KD - 1)),
                tag="ln_sx")
            S.op("tensor", lambda e, c=c, psq=psq: e.matmul(
                psq, ctx.onecol[:, :], x2t, start=(c == 0), stop=(c == KD - 1)),
                tag="ln_sx2")
    mu_ps = ctx.pA[0:1, 1024:1024 + tw]
    var_ps = ctx.pB[0:1, 1024:1024 + tw]
    musq = ctx.transf[0:1, 1280:1280 + tw]
    rstd_bf = t_row_bf(ctx, 4608, tw)
    nmr_bf = t_row_bf(ctx, 4608 + tw, tw)
    S.op("vector", lambda e: e.tensor_scalar(mu_ps, mu_ps, 1.0 / D, None, ALU.mult))
    S.op("vector", lambda e: e.tensor_scalar(var_ps, var_ps, 1.0 / D, 1e-5, ALU.mult, ALU.add))
    S.op("scalar", lambda e: e.activation(musq, mu_ps, AF.Square))
    S.op("vector", lambda e: e.tensor_tensor(var_ps, var_ps, musq, ALU.subtract))
    S.op("scalar", lambda e: e.activation(var_ps, var_ps, AF.Ln))
    S.op("vector", lambda e: e.tensor_scalar(var_ps, var_ps, -0.5, None, ALU.mult))
    S.op("scalar", lambda e: e.activation(rstd_bf, var_ps, AF.Exp))
    S.op("vector", lambda e: e.scalar_tensor_tensor(
        nmr_bf, mu_ps, -1.0, rstd_bf, ALU.mult, ALU.mult))
    for c in range(KD):
        su_c = su_row[0:1, c * 128:(c + 1) * 128]
        for i in range(nt):
            pa = ctx.pA[:, (c % 2) * 512:(c % 2) * 512 + 512]
            pb = ctx.pB[:, (c % 2) * 512:(c % 2) * 512 + 512]
            S.op("tensor", lambda e, pa=pa, su_c=su_c, i=i: e.matmul(
                pa, su_c, rstd_bf[0:1, i * 512:(i + 1) * 512], start=True, stop=True),
                tag="ln_abc")
            S.op("tensor", lambda e, pb=pb, su_c=su_c, i=i: e.matmul(
                pb, su_c, nmr_bf[0:1, i * 512:(i + 1) * 512], start=True, stop=True),
                tag="ln_bbc")
            ot = out_region[:, out_off + c * tw + i * 512: out_off + c * tw + (i + 1) * 512]
            S.op("vector", lambda e, c=c, i=i, ot=ot, pa=pa: e.tensor_tensor(
                ot, xt(c, i), pa, ALU.mult), tag="ln_za")
            S.op("vector", lambda e, ot=ot, pb=pb: e.tensor_tensor(
                ot, ot, pb, ALU.add), tag="ln_zb")


def mm_op(ctx, base, fn, **kw):
    need_nop = getattr(ctx, "_mm_base", None) is not None and ctx._mm_base != base
    ctx._mm_base = base
    if need_nop:
        fn2 = lambda e: (e.nop(), fn(e))[1]
    else:
        fn2 = fn
    return ctx.sched.op("tensor", fn2, **kw)


def emit_ones_row(ctx, off, w):
    S = ctx.sched
    row = t_row_bf(ctx, off, w)
    S.op("vector", lambda e: e.memset(row, 1.0))
    return row


def emit_fwht_combine(ctx, tiles):
    S = ctx.sched
    n = len(tiles)
    barrier = [S.tail("main"), None]
    h = 1
    while h < n:
        first = {"vector": True, "gpsimd": True}
        last = {"vector": None, "gpsimd": None}
        idx = 0
        for base in range(0, n, 2 * h):
            for j in range(h):
                a = tiles[base + j]
                b = tiles[base + j + h]
                eng = "vector" if idx % 2 == 0 else "gpsimd"
                idx += 1
                w0 = tuple(t for t in barrier if t is not None) if first[eng] else ()
                first[eng] = False
                if eng == "vector":
                    ta = S.op(eng, lambda e, a=a, b=b: e.tensor_tensor(a, a, b, ALU.add),
                              waits=w0, nosync=True, tag="fw_a")
                    last[eng] = S.op(eng, lambda e, a=a, b=b: e.scalar_tensor_tensor(
                        b, b, -2.0, a, ALU.mult, ALU.add),
                        waits=(ta,), nosync=True, tag="fw_b")
                else:
                    tmp = ctx.trans[:, 2048:2560]
                    tt = S.op(eng, lambda e, a=a, b=b, tmp=tmp: e.tensor_tensor(
                        tmp, a, b, ALU.subtract), waits=w0, nosync=True, tag="fw_s")
                    ta = S.op(eng, lambda e, a=a, b=b: e.tensor_tensor(
                        a, a, b, ALU.add), waits=(tt,), nosync=True, tag="fw_a")
                    last[eng] = S.op(eng, lambda e, b=b, tmp=tmp: e.tensor_copy(
                        b, tmp), waits=(ta,), nosync=True, tag="fw_b")
        barrier = [last["vector"], last["gpsimd"]]
        h *= 2
    S.op("vector", lambda e: e.nop(), waits=tuple(t for t in barrier if t is not None),
         tag="fw_end")


H16S = make_hadamard(16)


def emit_fwht_folded(ctx, tiles, scratch):
    S = ctx.sched
    n = len(tiles)
    ngroups = n // 16
    in_tok = S.tail("main")
    psum_free = [None, None, None, None]
    out_loc = [None] * n
    ev_last = None
    prev_sign = [None]
    for g in range(ngroups):
        for bp in range(16):
            o = g * 16 + bp
            dst = scratch[bp] if g == 0 else tiles[(g - 1) * 16 + bp]
            out_loc[o] = dst
            s4 = o % 4
            pa = ctx.pA if s4 % 2 == 0 else ctx.pB
            ps = pa[:, (s4 // 2) * 512:(s4 // 2) * 512 + 512]
            border = sorted(range(16), key=lambda b: -H16S[bp, b])
            for bi, b in enumerate(border):
                w0 = (psum_free[s4], in_tok if o < 4 else None) if bi == 0 else ()
                hmat = ctx.h128 if H16S[bp, b] > 0 else ctx.h128n
                tok = S.op("tensor", lambda e, hmat=hmat, ps=ps, bi=bi, src=tiles[g * 16 + b]: e.matmul(
                    ps, hmat[:, :], src, start=(bi == 0), stop=(bi == 15)),
                    waits=w0, stream="mm", tag="fwf_mm")
            psum_free[s4] = ev_last = S.op(
                "scalar", lambda e, dst=dst, ps=ps: e.copy(dst, ps),
                waits=(tok,), nosync=True, tag="fwf_ev")
    S.op("vector", lambda e: e.nop(), waits=(ev_last,), tag="fwf_end")
    if ngroups == 1:
        return [out_loc[b] for b in range(16)]
    barrier = [S.tail("main"), None]
    h = 1
    idx = 0
    while h < ngroups:
        first = {"vector": True, "gpsimd": True}
        last = {"vector": None, "gpsimd": None}
        for base in range(0, ngroups, 2 * h):
            for j in range(h):
                for bp in range(16):
                    a = out_loc[(base + j) * 16 + bp]
                    b2 = out_loc[(base + j + h) * 16 + bp]
                    eng = "vector" if idx % 2 == 0 else "gpsimd"
                    idx += 1
                    w0 = tuple(t for t in barrier if t is not None) if first[eng] else ()
                    first[eng] = False
                    if eng == "vector":
                        ta = S.op(eng, lambda e, a=a, b2=b2: e.tensor_tensor(
                            a, a, b2, ALU.add), waits=w0, nosync=True, tag="fw_a")
                        last[eng] = S.op(eng, lambda e, a=a, b2=b2: e.scalar_tensor_tensor(
                            b2, b2, -2.0, a, ALU.mult, ALU.add),
                            waits=(ta,), nosync=True, tag="fw_b")
                    else:
                        tmp = ctx.trans[:, 2048:2560]
                        pw = w0 + ((last[eng],) if last[eng] is not None else ())
                        tt2 = S.op(eng, lambda e, a=a, b2=b2, tmp=tmp: e.tensor_tensor(
                            tmp, a, b2, ALU.subtract), waits=pw, nosync=True, tag="fw_s")
                        ta = S.op(eng, lambda e, a=a, b2=b2: e.tensor_tensor(
                            a, a, b2, ALU.add), waits=(tt2,), nosync=True, tag="fw_a")
                        last[eng] = S.op(eng, lambda e, b2=b2, tmp=tmp: e.tensor_copy(
                            b2, tmp), waits=(ta,), nosync=True, tag="fw_b")
        barrier = [last["vector"], last["gpsimd"]]
        h *= 2
    S.op("vector", lambda e: e.nop(), waits=tuple(t for t in barrier if t is not None),
         tag="fw_end")
    return out_loc


def emit_fwht_combine_wide(ctx):
    S = ctx.sched
    for h in (1, 2, 4, 8, 16):
        w = h * 512
        for reg in (ctx.rB, ctx.rC):
            va = reg[:, 0:16384].rearrange(
                "p (m two hh) -> p m two hh", two=2, hh=w)[:, :, 0, :]
            vb = reg[:, 0:16384].rearrange(
                "p (m two hh) -> p m two hh", two=2, hh=w)[:, :, 1, :]
            S.op("vector", lambda e, va=va, vb=vb: e.tensor_tensor(
                va, va, vb, ALU.add), tag="fww_a")
            S.op("vector", lambda e, va=va, vb=vb: e.scalar_tensor_tensor(
                vb, vb, -2.0, va, ALU.mult, ALU.add), tag="fww_b")
    a32 = ctx.rB[:, 0:16384]
    b32 = ctx.rC[:, 0:16384]
    S.op("vector", lambda e: e.tensor_tensor(a32, a32, b32, ALU.add), tag="fww_a")
    S.op("vector", lambda e: e.scalar_tensor_tensor(
        b32, b32, -2.0, a32, ALU.mult, ALU.add), tag="fww_b")


def emit_fwht_h128(ctx, tiles, scale=None):
    S = ctx.sched
    in_tok = S.tail("main")
    psum_free = [None, None, None, None]
    evs = []
    for idx, tile in enumerate(tiles):
        tw = tile.shape[1]
        s4 = idx % 4
        pa = ctx.pA if s4 % 2 == 0 else ctx.pB
        ps = pa[:, (s4 // 2) * 512:(s4 // 2) * 512 + tw]
        tok = S.op("tensor", lambda e, tile=tile, tw=tw, ps=ps: e.matmul(
            ps, ctx.h128[:, :], tile[:, 0:tw], start=True, stop=True),
            waits=(psum_free[s4], in_tok if idx < 4 else None),
            stream="mm", tag="fw_h128")
        psum_free[s4] = S.op("scalar", lambda e, tile=tile, tw=tw, ps=ps: e.copy(
            tile[:, 0:tw], ps), waits=(tok,), nosync=True, tag="fw_ev")
    S.op("vector", lambda e: e.nop(),
         waits=tuple(t for t in psum_free if t is not None), tag="fwh_end")


def _decl_l0_ffn(nc, ctx):
    ctx.d_wupgate = nc.dram_tensor("l0f_wupgate", [32, 128, 8192], F32, kind="ExternalInput").ap()
    ctx.d_wdown = nc.dram_tensor("l0f_wdown", [16, 128, 8192], F32, kind="ExternalInput").ap()
    ctx.d_suug = nc.dram_tensor("l0f_suug", [1, 2048], F32, kind="ExternalInput").ap()
    ctx.d_sudp = nc.dram_tensor("l0f_sudp", [128, 64], F32, kind="ExternalInput").ap()
    ctx.d_wsu_up = nc.dram_tensor("l0f_wsu_up", [128, 64], F32, kind="ExternalInput").ap()
    ctx.d_wsu_g = nc.dram_tensor("l0f_wsu_g", [128, 64], F32, kind="ExternalInput").ap()
    ctx.d_wsdp = nc.dram_tensor("l0f_wsdp", [128, 16], F32, kind="ExternalInput").ap()


def emit_l0_ffn(ctx):
    S = ctx.sched

    su_row = t_row_bf(ctx, 0, 2048)
    S.dma("gpsimd", lambda e: e.dma_start(out=su_row, in_=ctx.d_suug[:, :]))
    wsg64 = ctx.sm1[:, 0:64]
    comb_up = ctx.sm1[:, 64:128]
    wdp64 = ctx.sm1[:, 128:144]
    sudp = ctx.sm2[:, 0:64]
    wsuup = ctx.sm2[:, 64:128]
    S.dma("sync", lambda e: e.dma_start(out=sudp, in_=ctx.d_sudp[:, :]))
    S.dma("sync", lambda e: e.dma_start(out=wsuup, in_=ctx.d_wsu_up[:, :]))
    S.dma("sync", lambda e: e.dma_start(out=ctx.sm2[:, 128:192], in_=ctx.d_wsu_g[:, :]))
    S.dma("sync", lambda e: e.dma_start(out=ctx.sm2[:, 192:208], in_=ctx.d_wsdp[:, :]))
    S.op("vector", lambda e: e.tensor_scalar(
        wsg64, ctx.sm2[:, 128:192], 1.0 / np.sqrt(2048.0), None, ALU.mult))
    S.op("vector", lambda e: e.scalar_tensor_tensor(
        comb_up, wsuup, 1.0 / np.sqrt(2048.0), sudp, ALU.mult, ALU.mult))
    S.op("vector", lambda e: e.tensor_scalar(
        wdp64, ctx.sm2[:, 192:208], 1.0 / np.sqrt(8192.0), None, ALU.mult))

    wreg_free = getattr(ctx, "wreg_free", [None, None])
    for tp in range(2):
        tb = tp * 512
        emit_ln(ctx, su_row, ctx.rA, 0, tw=512, tbase=tb)
        ztiles = [ctx.rA[:, c * 512:(c + 1) * 512] for c in range(KD)]
        scratch = [ctx.rA[:, 8192 + c * 512:8192 + (c + 1) * 512] for c in range(KD)]
        if getattr(ctx, "old2048", False):
            emit_fwht_combine(ctx, ztiles)
            emit_fwht_h128(ctx, ztiles)
        else:
            ztiles = emit_fwht_folded(ctx, ztiles, scratch)


        in_tok = S.tail("main")
        last_use = [wreg_free[0], wreg_free[1]]
        ld_tok = [None, None]
        psum_free = [[None, None], [None, None]]
        act = []
        for o in range(64):
            reg = ctx.rB if o < 32 else ctx.rC
            act.append(reg[:, (o % 32) * 512:(o % 32) * 512 + 512])
        for oo in range(32):
            s = oo % 2
            slab = ctx.wreg[:, s * 8192:(s + 1) * 8192]
            ld_tok[s] = S.dma("gpsimd", lambda e, oo=oo, slab=slab: e.dma_start(
                out=slab, in_=ctx.d_wupgate[oo, :, :]), waits=(last_use[s],),
                stream="wl%d" % s, sem="wsl%d" % s, tag="wld_ug")
            pool = ctx.pA if s == 0 else ctx.pB
            for sub in range(2):
                o = 2 * oo + sub
                pu = pool[:, sub * 1024:sub * 1024 + 512]
                pg = pool[:, sub * 1024 + 512:sub * 1024 + 1024]
                for k in range(KD):
                    w0 = (ld_tok[s], psum_free[s][sub],
                          in_tok if oo < 2 else None) if k == 0 else ()
                    tok_up = S.op("tensor", lambda e, k=k, slab=slab, sub=sub, pu=pu, zt=ztiles[k]: e.matmul(
                        pu, slab[:, sub * 4096 + k * 128:sub * 4096 + (k + 1) * 128], zt,
                        start=(k == 0), stop=(k == KD - 1)),
                        waits=w0, stream="mm", tag="up_mm")
                for k in range(KD):
                    tok = S.op("tensor", lambda e, k=k, slab=slab, sub=sub, pg=pg, zt=ztiles[k]: e.matmul(
                        pg, slab[:, sub * 4096 + 2048 + k * 128:sub * 4096 + 2048 + (k + 1) * 128], zt,
                        start=(k == 0), stop=(k == KD - 1)), stream="mm", tag="gt_mm")
                last_use[s] = tok
                S.op("scalar", lambda e, o=o, pg=pg: e.activation(
                    ctx.trans[:, 2048:2560], pg, AF.Silu, scale=wsg64[:, o:o + 1]),
                    waits=(tok,), tag="silu")
                psum_free[s][sub] = S.op("vector", lambda e, o=o, pu=pu, at=act[o]: e.scalar_tensor_tensor(
                    at, ctx.trans[:, 2048:2560], comb_up[:, o:o + 1], pu,
                    ALU.mult, ALU.mult), waits=(tok_up,), tag="actmul")

        if getattr(ctx, "dbg_fold", False) and tp == 0:
            for c in range(8):
                emit_dbg(ctx, act[c], 128, 512)
        emit_fwht_combine_wide(ctx)
        emit_fwht_h128(ctx, act)


        in_tok = S.tail("main")
        last_use_d = [None, None]
        ld_tok_d = [None, None]
        psum_free_d = [None, None]
        for o in range(KD):
            s = o % 2
            wd = ctx.wreg[:, s * 8192:s * 8192 + 8192]
            if o == 0:
                w8 = (last_use[0],)
            elif o == 1:
                w8 = (last_use[1],)
            else:
                w8 = (last_use_d[s],)
            ld_tok_d[s] = S.dma("gpsimd", lambda e, o=o, wd=wd: e.dma_start(
                out=wd, in_=ctx.d_wdown[o, :, :]), waits=w8,
                stream="wd%d" % s, sem="wsd%d" % s, tag="wld_dn")
            pd = ctx.pB[:, s * 512:s * 512 + 512]
            for k in range(64):
                w0 = (ld_tok_d[s], psum_free_d[s], in_tok if o < 2 else None) if k == 0 else ()
                tok = S.op("tensor", lambda e, k=k, wd=wd, pd=pd, at=act[k]: e.matmul(
                    pd, wd[:, k * 128:(k + 1) * 128], at,
                    start=(k == 0), stop=(k == 63)),
                    waits=w0, stream="mm", tag="dn_mm")
            last_use_d[s] = tok
            xt = x_tile(ctx, o, (tb, tb + 512))
            psum_free_d[s] = S.op("vector", lambda e, o=o, pd=pd, xt=xt: e.scalar_tensor_tensor(
                xt, pd, wdp64[:, o:o + 1], xt, ALU.mult, ALU.add),
                waits=(tok,), tag="dn_res")
        wreg_free = [last_use_d[0], last_use_d[1]]
    ctx.wreg_free = wreg_free


def _decl_l0_attn(nc, ctx):
    ctx.d_wq = nc.dram_tensor("l0a_wq", [16, 128, 2048], F32, kind="ExternalInput").ap()
    ctx.d_wk = nc.dram_tensor("l0a_wk", [16, 128, 2048], F32, kind="ExternalInput").ap()
    ctx.d_wv = nc.dram_tensor("l0a_wv", [16, 128, 2048], F32, kind="ExternalInput").ap()
    ctx.d_wo = nc.dram_tensor("l0a_wo", [16, 128, 2048], F32, kind="ExternalInput").ap()
    ctx.d_wfq = nc.dram_tensor("l0a_wfq", [32, 64, 64], F32, kind="ExternalInput").ap()
    ctx.d_wfk = nc.dram_tensor("l0a_wfk", [32, 64, 64], F32, kind="ExternalInput").ap()
    ctx.d_blk = nc.dram_tensor("l0a_blk", [128, 4128], BF16)
    ctx.d_ag = nc.dram_tensor("l0a_ag", [NCORES * 128, 4128], BF16)


L0ATTN_STAGE = 5
L0ATTN_SUB = 9


def emit_l0_attn(ctx):
    S = ctx.sched
    wreg_free = getattr(ctx, "wreg_free", [None, None])

    S.dma("gpsimd", lambda e: e.dma_start(out=ctx.mask[:, :], in_=ctx.d_mask[:, :]))

    su_row = emit_ones_row(ctx, 0, 2048)
    emit_ln(ctx, su_row, ctx.rA, 0, tw=T)
    ztiles = [ctx.rA[:, c * T:(c + 1) * T] for c in range(KD)]

    vtmp = ctx.trans[:, 2048:3072]
    in_tok = S.tail("main")
    last_use = [wreg_free[0], wreg_free[0]]
    ld_tok = [None, None]
    psum_free = [None, None, None, None]
    pBb8 = ctx.pB[:, 0:512].bitcast(BF16)
    trev_tok = None
    ev_toks = []
    for which, wdram in (("q", ctx.d_wq), ("k", ctx.d_wk), ("v", ctx.d_wv)):
        for o in range(KD):
            s = o % 2
            wsl = ctx.wreg[:, s * 2048:s * 2048 + 2048]
            ld_tok[s] = S.dma("gpsimd", lambda e, o=o, wsl=wsl, wdram=wdram: e.dma_start(
                out=wsl, in_=wdram[o, :, :]), waits=(last_use[s],),
                stream="qv%d" % s, sem="qvs%d" % s, tag="wld_" + which)
            ev2 = []
            for tt in range(2):
                sl = s * 2 + tt
                ps = ctx.pA[:, sl * 512:sl * 512 + 512]
                for k in range(KD):
                    w0 = (ld_tok[s], psum_free[sl],
                          in_tok if (which == "q" and o < 2) else None) if k == 0 else ()
                    tok = S.op("tensor", lambda e, k=k, wsl=wsl, ps=ps, tt=tt: e.matmul(
                        ps, wsl[:, k * 128:(k + 1) * 128],
                        ztiles[k][:, tt * 512:(tt + 1) * 512],
                        start=(k == 0), stop=(k == KD - 1)),
                        waits=w0, stream="mm", tag=which + "_mm")
                if which == "q":
                    dst = ctx.rB[:, o * T + tt * 512:o * T + (tt + 1) * 512]
                elif which == "k":
                    dst = ctx.rC[:, o * T + tt * 512:o * T + (tt + 1) * 512]
                else:
                    dst = vtmp[:, tt * 512:(tt + 1) * 512]
                wv = (tok, trev_tok) if which == "v" else (tok,)
                psum_free[sl] = S.op("scalar", lambda e, dst=dst, ps=ps: e.copy(
                    dst, ps), waits=wv, nosync=True, tag=which + "_ev")
                ev2.append(psum_free[sl])
            last_use[s] = tok
            if which == "v":
                ttoks = []
                for m in range(8):
                    pt = pBb8[:, m * 128:(m + 1) * 128]
                    ttoks.append(S.op("tensor", lambda e, m=m, pt=pt: e.transpose(
                        pt, vtmp[:, m * 128:(m + 1) * 128],
                        ctx.identb[:, :]),
                        waits=(ev2[0], ev2[1], trev_tok) if m == 0 else (),
                        stream="mm", nosync=True, tag="v_tr"))
                trev_tok = S.op("vector", lambda e, o=o: e.tensor_copy(
                    ctx.rD[:, :].rearrange("p (m t) -> p m t", t=2048)[:, :, o * 128:(o + 1) * 128],
                    pBb8.rearrange("p (m c) -> p m c", c=128)),
                    waits=(ttoks[-1],), nosync=True, tag="v_trev")
                ev_toks.append(trev_tok)
            else:
                ev_toks.extend(ev2)
    S.op("vector", lambda e: e.nop(),
         waits=(max(t for t in psum_free if t is not None), trev_tok),
         tag="qkv_anchor")

    if L0ATTN_STAGE < 2:
        ctx.wreg_free = [last_use[0], last_use[1]]
        return
    wfq_sb = ctx.wreg[:, 0:2048]
    wfk_sb = ctx.wreg[:, 2048:4096]
    ms_tok = S.op("vector", lambda e: e.memset(ctx.wreg[:, 0:4096], 0.0),
                  waits=(last_use[0], last_use[1]), tag="fm_ms")
    for wsb, wdram, tagn in ((wfq_sb, ctx.d_wfq, "wld_fq"), (wfk_sb, ctx.d_wfk, "wld_fk")):
        for half in range(2):
            fm_ld = S.dma("gpsimd", lambda e, wsb=wsb, wdram=wdram, half=half: e.dma_start(
                out=wsb[half * 64:half * 64 + 64, :].rearrange(
                    "d (c g) -> d c g", g=128)[:, :, half * 64:half * 64 + 64],
                in_=wdram.rearrange("(c two) d f -> two d c f", two=2)[half]),
                waits=(ms_tok,) if (half == 0 and wsb is wfq_sb) else (),
                stream="qv0", sem="qvs0", tag=tagn)

    first_fm = True
    for b in range(2):
        for c in range(KD):
            ps = ctx.pA[:, (c % 4) * 512:(c % 4) * 512 + 512]
            mm_op(ctx, 0, lambda e, b=b, c=c, ps=ps: e.matmul(
                ps, wfq_sb[:, c * 128:(c + 1) * 128],
                ctx.rB[:, c * T + b * 512:c * T + b * 512 + 512],
                start=True, stop=True),
                waits=(fm_ld,) if first_fm else (), tag="fq_mm")
            first_fm = False
            S.op("scalar", lambda e, b=b, c=c, ps=ps: e.activation(
                ctx.rB[:, c * T + b * 512:c * T + b * 512 + 512], ps, AF.Exp),
                tag="fq_exp")

    if L0ATTN_STAGE < 3:
        ctx.wreg_free = [last_use[0], last_use[1]]
        return
    fk_tm = ctx.trans[:, 0:2048]
    sums = ctx.sm2[:, 0:32]
    rcps = ctx.sm2[:, 32:64]
    for b in range(2):
        for jc in range(4):
            for c in range(KD):
                mm_op(ctx, 0, lambda e, b=b, jc=jc, c=c: e.matmul(
                    ctx.pA[:, c * 128:(c + 1) * 128],
                    ctx.rC[:, c * T + b * 512 + jc * 128:
                           c * T + b * 512 + (jc + 1) * 128],
                    wfk_sb[:, c * 128:(c + 1) * 128],
                    start=True, stop=True), tag="fk_mm")
            S.op("scalar", lambda e: e.activation(fk_tm, ctx.pA[:, :], AF.Exp),
                 tag="fk_exp")
            if L0ATTN_SUB < 2:
                continue
            S.op("vector", lambda e: e.reduce_sum(
                sums, fk_tm.rearrange("p (h f) -> p h f", h=32), axis=AX.X),
                tag="fk_sum")
            S.op("vector", lambda e: e.reciprocal(rcps, sums), tag="fk_rcp")
            S.op("vector", lambda e: e.tensor_tensor(
                fk_tm.rearrange("p (h f) -> p h f", h=32),
                fk_tm.rearrange("p (h f) -> p h f", h=32),
                rcps.unsqueeze(2).broadcast_to([128, 32, 64]), ALU.mult),
                tag="fk_nrm")
            if L0ATTN_SUB < 3:
                continue
            for hh in range(16):
                S.op("tensor", lambda e, hh=hh: e.matmul(
                    ctx.pA[:, 1024 + hh:1024 + hh + 1],
                    fk_tm[:, hh * 128:(hh + 1) * 128], ctx.onecol[:, :],
                    start=True, stop=True), tag="kp_mm")
            S.op("vector", lambda e, b=b, jc=jc: e.scalar_tensor_tensor(
                ctx.sm2[:, 64 + b * 16:80 + b * 16], ctx.pA[:, 1024:1040], 1.0,
                ctx.sm2[:, 64 + b * 16:80 + b * 16], ALU.mult, ALU.add)
                if jc > 0 else e.tensor_copy(
                ctx.sm2[:, 64 + b * 16:80 + b * 16], ctx.pA[:, 1024:1040]),
                tag="kp_acc")
            if L0ATTN_SUB < 4:
                continue
            pAb = ctx.pA[:, 0:1024].bitcast(BF16)
            for hh in range(16):
                S.op("tensor", lambda e, hh=hh, pAb=pAb: e.transpose(
                    pAb[:, hh * 128:(hh + 1) * 128], fk_tm[:, hh * 128:(hh + 1) * 128],
                    ctx.identb[:, :]), tag="fk_tr")
            S.op("vector", lambda e, b=b, jc=jc, pAb=pAb: e.tensor_copy(
                ctx.rC[:, :].rearrange("p (hh t) -> p hh t", t=T)[
                    :, :, b * 512 + jc * 128:b * 512 + (jc + 1) * 128],
                pAb.rearrange("p (hh c) -> p hh c", c=128)), tag="fk_trev")
            if L0ATTN_SUB < 5:
                continue
            for h in range(32):
                S.op("tensor", lambda e, b=b, jc=jc, h=h: e.matmul(
                    ctx.pB[(h % 2) * 64:(h % 2) * 64 + 64, h * 64:(h + 1) * 64],
                    fk_tm[:, h * 64:(h + 1) * 64],
                    ctx.rD[:, (b * 4 + jc) * 2048 + h * 64:(b * 4 + jc) * 2048 + (h + 1) * 64],
                    start=True, stop=True), tag="sd_mm")
            S.op("vector", lambda e, b=b, jc=jc: e.scalar_tensor_tensor(
                ctx.dsb[:, b * 2064:b * 2064 + 2048], ctx.pB[:, 0:2048], 1.0,
                ctx.dsb[:, b * 2064:b * 2064 + 2048], ALU.mult, ALU.add)
                if jc > 0 else e.tensor_copy(
                ctx.dsb[:, b * 2064:b * 2064 + 2048], ctx.pB[:, 0:2048]),
                tag="sd_acc")
        if L0ATTN_SUB < 5:
            continue
        S.op("vector", lambda e, b=b: e.tensor_copy(
            ctx.dsb[:, b * 2064 + 2048:b * 2064 + 2064],
            ctx.sm2[:, 64 + b * 16:80 + b * 16]), tag="kp_exp")

    if L0ATTN_STAGE < 4:
        ctx.wreg_free = [last_use[0], last_use[1]]
        return
    S.dma("sync", lambda e: e.dma_start(out=ctx.d_blk[:, :], in_=ctx.dsb[:, :]),
          tag="blk_out")
    S.op("gpsimd", lambda e: e.collective_compute(
        "AllGather", ALU.bypass, replica_groups=[list(range(NCORES))],
        ins=[ctx.d_blk.ap().opt()], outs=[ctx.d_ag.ap().opt()]), sem="coll", tag="ag")
    for half in range(2):
        S.op("vector", lambda e: e.memset(ctx.accf[:, :], 0.0), tag="accz")
        for r in range(NCORES):
            S.dma("sync", lambda e, r=r, half=half: e.dma_start(
                out=ctx.dbuf[:, 0:2064],
                in_=ctx.d_ag[r * 128:(r + 1) * 128, half * 2064:(half + 1) * 2064]),
                tag="ag_ld")
            S.op("vector", lambda e, r=r: e.scalar_tensor_tensor(
                ctx.accf[:, :], ctx.dbuf[:, 0:2064],
                ctx.pmask[:, r:r + 1], ctx.accf[:, :],
                ALU.mult, ALU.add), tag="ag_acc")
        S.op("vector", lambda e, half=half: e.tensor_copy(
            ctx.acc[:, half * 2064:(half + 1) * 2064], ctx.accf[:, 0:2064]),
            tag="acc_bf")
    in_attn = S.tail("main")

    if getattr(ctx, "dbg_acc", False):
        emit_dbg(ctx, ctx.acc[:, 0:4096], 128, 4096)
    if L0ATTN_STAGE < 5:
        ctx.wreg_free = [last_use[0], last_use[1]]
        return
    am_off = [0, 512, 896, 1152]
    ydiv_tok = [None, None]
    amev_hist = {}
    for pi, (b, h) in enumerate([(b, h) for b in range(2) for h in range(32)]):
        pp = pi % 2
        pb_ = (h % 2) * 64
        cch = (h // 2) * T + b * 512
        spref = ctx.acc[pb_:pb_ + 64, b * 2064 + h * 64:b * 2064 + h * 64 + 64]
        kprev = ctx.acc[pb_:pb_ + 64, b * 2064 + 2048 + (h // 2):b * 2064 + 2048 + (h // 2) + 1]
        fq_ap = ctx.rB[pb_:pb_ + 64, cch:cch + 512]
        fk_ap = ctx.rC[pb_:pb_ + 64, cch:cch + 512]
        den_sb = ctx.sm2[:, 96 + pp * 8:100 + pp * 8]
        rec_sb = ctx.sm2[:, 100 + pp * 8:104 + pp * 8]
        am_base = pp * 1280
        amev = [None] * 4
        for jc in range(4):
            nw = 512 - jc * 128
            aps = ctx.pA[:, pp * 1024 + (jc % 2) * 512:pp * 1024 + (jc % 2) * 512 + nw]
            w0 = [amev_hist.get((pp, jc))]
            if jc >= 2:
                w0.append(amev[jc - 2])
            if jc == 0:
                w0.append(ydiv_tok[pp])
            if pi < 2:
                w0.append(in_attn)
            tok = mm_op(ctx, pb_, lambda e, jc=jc, nw=nw, aps=aps, fk_ap=fk_ap, fq_ap=fq_ap: e.matmul(
                aps, fk_ap[:, jc * 128:(jc + 1) * 128], fq_ap[:, jc * 128:512],
                start=True, stop=True), waits=tuple(w0), stream="mm", tag="ap_mm")
            am = ctx.trans[:, am_base + am_off[jc]:am_base + am_off[jc] + nw]
            amev[jc] = S.op("vector", lambda e, jc=jc, nw=nw, aps=aps, am=am: e.tensor_tensor(
                am, aps, ctx.mask[:, 0:nw], ALU.mult), waits=(tok,), tag="am_ev")
            amev_hist[(pp, jc)] = amev[jc]
        dn_last = None
        for ic in range(4):
            yps = ctx.pB[:, pp * 512 + ic * 64:pp * 512 + (ic + 1) * 64]
            first = True
            for jc in range(ic + 1):
                am = ctx.trans[:, am_base + am_off[jc] + (ic - jc) * 128:
                               am_base + am_off[jc] + (ic - jc + 1) * 128]
                w0 = (amev[ic], ydiv_tok[pp] if ic == 0 else None) if first else ()
                mm_op(ctx, 0, lambda e, am=am, b=b, jc=jc, h=h, yps=yps, first=first: e.matmul(
                    yps, am,
                    ctx.rD[:, (b * 4 + jc) * 2048 + h * 64:(b * 4 + jc) * 2048 + (h + 1) * 64],
                    start=first, stop=False), waits=w0, stream="mm", tag="y_intra")
                first = False
            mm_op(ctx, pb_, lambda e, ic=ic, yps=yps, fq_ap=fq_ap, spref=spref: e.matmul(
                yps, fq_ap[:, ic * 128:(ic + 1) * 128], spref,
                start=False, stop=True), stream="mm", tag="y_inter")
            dps = ctx.pB[:, pp * 512 + 256 + ic:pp * 512 + 256 + ic + 1]
            mm_op(ctx, pb_, lambda e, ic=ic, dps=dps, fq_ap=fq_ap, kprev=kprev: e.matmul(
                dps, fq_ap[:, ic * 128:(ic + 1) * 128], kprev,
                start=True, stop=False), stream="mm", tag="dn_kp")
            for jc in range(ic + 1):
                am = ctx.trans[:, am_base + am_off[jc] + (ic - jc) * 128:
                               am_base + am_off[jc] + (ic - jc + 1) * 128]
                dn_last = mm_op(ctx, 0, lambda e, am=am, dps=dps, jc=jc, ic=ic: e.matmul(
                    dps, am, ctx.onecol[:, :],
                    start=False, stop=(jc == ic)), stream="mm", tag="dn_am")
        dn_ev = S.op("vector", lambda e, pp=pp, den_sb=den_sb: e.tensor_copy(
            den_sb, ctx.pB[:, pp * 512 + 256:pp * 512 + 260]),
            waits=(dn_last,), tag="dn_ev")
        S.op("vector", lambda e, den_sb=den_sb, rec_sb=rec_sb: e.reciprocal(
            rec_sb, den_sb), tag="dn_rc")
        ydiv_tok[pp] = S.op("vector", lambda e, b=b, h=h, pp=pp, rec_sb=rec_sb: e.tensor_tensor(
            ctx.rA[:, :].rearrange("p (m t) -> p m t", t=2048)[
                :, b * 4:b * 4 + 4, h * 64:(h + 1) * 64],
            ctx.pB[:, pp * 512:pp * 512 + 256].rearrange("p (m t) -> p m t", t=64),
            rec_sb.unsqueeze(2).broadcast_to([128, 4, 64]), ALU.mult),
            tag="y_div")
    S.op("vector", lambda e: e.nop(), waits=(ydiv_tok[0], ydiv_tok[1]), tag="pair_end")

    for m in range(8):
        for c in range(KD):
            pt = ctx.pA[:, (c % 2) * 512:(c % 2) * 512 + 64].bitcast(BF16)
            mm_op(ctx, 0, lambda e, m=m, c=c, pt=pt: e.transpose(
                pt[:, 0:128], ctx.rA[:, m * 2048 + c * 128:m * 2048 + (c + 1) * 128],
                ctx.identb[:, :]), tag="y_tr")
            tcol = (m // 4) * 512 + (m % 4) * 128
            S.op("vector", lambda e, m=m, c=c, pt=pt, tcol=tcol: e.tensor_copy(
                ctx.rB[:, c * T + tcol:c * T + tcol + 128], pt[:, 0:128]),
                tag="y_trev")

    in_tok5 = S.tail("main")
    last2 = [None, None]
    ld2 = [None, None]
    psum_free5 = [None, None, None, None]
    for o in range(KD):
        s = o % 2
        wsl = ctx.wreg[:, s * 2048:s * 2048 + 2048]
        w8 = (in_tok5,) if o < 2 else (last2[s],)
        ld2[s] = S.dma("gpsimd", lambda e, o=o, wsl=wsl: e.dma_start(
            out=wsl, in_=ctx.d_wo[o, :, :]), waits=w8,
            stream="qv%d" % s, sem="qvs%d" % s, tag="wld_wo")
        for tt in range(2):
            sl = s * 2 + tt
            ps = ctx.pA[:, sl * 512:sl * 512 + 512]
            for kc in range(KD):
                w0 = (ld2[s], psum_free5[sl],
                      in_tok5 if o < 2 else None) if kc == 0 else ()
                tok = S.op("tensor", lambda e, kc=kc, wsl=wsl, ps=ps, tt=tt: e.matmul(
                    ps, wsl[:, kc * 128:(kc + 1) * 128],
                    ctx.rB[:, kc * T + tt * 512:kc * T + (tt + 1) * 512],
                    start=(kc == 0), stop=(kc == KD - 1)),
                    waits=w0, stream="mm", tag="wo_mm")
            xt = x_tile(ctx, o, (tt * 512, (tt + 1) * 512))
            psum_free5[sl] = S.op("vector", lambda e, ps=ps, xt=xt: e.tensor_tensor(
                xt, xt, ps, ALU.add), waits=(tok,), tag="wo_res")
        last2[s] = tok
    ctx.wreg_free = [last2[0], last2[1]]


def _decl_l1_attn(nc, ctx):
    ctx.d_wqkv = nc.dram_tensor("l1a_wqkv", [24, 128, 2048], F32, kind="ExternalInput").ap()
    ctx.d_wproj = nc.dram_tensor("l1a_wproj", [16, 128, 2048], F32, kind="ExternalInput").ap()


def emit_l1_attn(ctx):
    S = ctx.sched
    wreg_free = getattr(ctx, "wreg_free", [None, None])

    su_row = emit_ones_row(ctx, 0, 2048)
    emit_ln(ctx, su_row, ctx.rA, 0, tw=T)
    ztiles = [ctx.rA[:, c * T:(c + 1) * T] for c in range(KD)]

    def qkv_tile(o):
        reg = ctx.rB if o < 16 else ctx.rC
        return reg[:, (o % 16) * T:(o % 16) * T + T]

    in_tok = S.tail("main")
    last_use = [wreg_free[0], wreg_free[0]]
    ld_tok = [None, None]
    psum_free = [None, None, None, None]
    for o in range(24):
        s = o % 2
        wsl = ctx.wreg[:, s * 2048:s * 2048 + 2048]
        ld_tok[s] = S.dma("gpsimd", lambda e, o=o, wsl=wsl: e.dma_start(
            out=wsl, in_=ctx.d_wqkv[o, :, :]), waits=(last_use[s],),
            stream="qk%d" % s, sem="qks%d" % s, tag="wld_qkv")
        for tt in range(2):
            sl = s * 2 + tt
            ps = ctx.pA[:, sl * 512:sl * 512 + 512]
            for k in range(KD):
                w0 = (ld_tok[s], psum_free[sl],
                      in_tok if o < 2 else None) if k == 0 else ()
                tok = S.op("tensor", lambda e, k=k, wsl=wsl, ps=ps, tt=tt: e.matmul(
                    ps, wsl[:, k * 128:(k + 1) * 128],
                    ztiles[k][:, tt * 512:(tt + 1) * 512],
                    start=(k == 0), stop=(k == KD - 1)),
                    waits=w0, stream="mm", tag="qkv_mm")
            psum_free[sl] = S.op("scalar", lambda e, o=o, tt=tt, ps=ps: e.copy(
                qkv_tile(o)[:, tt * 512:(tt + 1) * 512], ps),
                waits=(tok,), nosync=True, tag="qkv_ev")
        last_use[s] = tok
    S.op("vector", lambda e: e.nop(),
         waits=tuple(t for t in psum_free if t is not None), tag="qkv_end")

    dv = ctx.trans[:, 2048:2560]
    prod = ctx.trans[:, 2560:3072]
    rDf = ctx.rD[:, :].bitcast(F32)
    s0_sb = rDf[0:1, 4096:6144]
    sd_sb = rDf[0:1, 6144:8192]
    p1row = ctx.rD[0:1, 8192:10240]
    sc = 1.0 / np.sqrt(128.0)
    for g in range(4):
        kt = qkv_tile(g * 6 + 4)
        vt = qkv_tile(g * 6 + 5)
        S.op("vector", lambda e, vt=vt: e.tensor_tensor(
            dv, vt[:, 512:1024], vt[:, 0:512], ALU.subtract), tag="dv")
        for qi in range(4):
            qt = qkv_tile(g * 6 + qi)
            S.op("vector", lambda e, qt=qt, kt=kt: e.tensor_tensor(
                prod, qt[:, 512:1024], kt[:, 0:512], ALU.mult), tag="p0")
            S.op("tensor", lambda e, qi=qi: e.matmul(
                ctx.pA[0:1, qi * 512:qi * 512 + 512], ctx.onecol[:, :], prod,
                start=True, stop=True), tag="s0mm")
            S.op("vector", lambda e, qt=qt, kt=kt: e.tensor_tensor(
                prod, qt[:, 512:1024], kt[:, 512:1024], ALU.mult), tag="p1")
            S.op("tensor", lambda e, qi=qi: e.matmul(
                ctx.pA[64:65, qi * 512:qi * 512 + 512], ctx.onecol[:, :], prod,
                start=True, stop=True), tag="s1mm")
        S.op("vector", lambda e: e.tensor_scalar(
            s0_sb, ctx.pA[0:1, 0:2048], sc, None, ALU.mult), tag="s0ev")
        S.op("vector", lambda e: e.scalar_tensor_tensor(
            sd_sb, ctx.pA[64:65, 0:2048], sc, s0_sb,
            ALU.mult, ALU.subtract), tag="sdiff")
        S.op("scalar", lambda e: e.activation(
            p1row, sd_sb, AF.Sigmoid), tag="sig")
        for qi in range(4):
            h = g * 4 + qi
            pbc = ctx.pB[:, qi * 512:qi * 512 + 512]
            S.op("tensor", lambda e, qi=qi, pbc=pbc: e.matmul(
                pbc, ctx.onerow[:, :], p1row[0:1, qi * 512:qi * 512 + 512],
                start=True, stop=True), tag="pbc")
            S.op("vector", lambda e, pbc=pbc: e.tensor_tensor(
                prod, pbc, dv, ALU.mult), tag="odv")
            ot = ctx.rD[:, h * 512:(h + 1) * 512]
            S.op("vector", lambda e, ot=ot, vt=vt: e.tensor_tensor(
                ot, prod, vt[:, 0:512], ALU.add), tag="oadd")

    in_tok2 = S.tail("main")
    last2 = [last_use[0], last_use[1]]
    ld2 = [None, None]
    psum_free2 = [None, None, None, None]
    for o in range(KD):
        s = o % 2
        wsl = ctx.wreg[:, s * 2048:s * 2048 + 2048]
        ld2[s] = S.dma("gpsimd", lambda e, o=o, wsl=wsl: e.dma_start(
            out=wsl, in_=ctx.d_wproj[o, :, :]), waits=(last2[s],),
            stream="qk%d" % s, sem="qks%d" % s, tag="wld_prj")
        for tt in range(2):
            sl = s * 2 + tt
            ps = ctx.pA[:, sl * 512:sl * 512 + 512]
            for hc in range(KD):
                g = hc // 4
                if tt == 0:
                    rhs = qkv_tile(g * 6 + 5)[:, 0:512]
                else:
                    rhs = ctx.rD[:, hc * 512:(hc + 1) * 512]
                w0 = (ld2[s], psum_free2[sl],
                      in_tok2 if o < 2 else None) if hc == 0 else ()
                tok = S.op("tensor", lambda e, hc=hc, wsl=wsl, ps=ps, rhs=rhs: e.matmul(
                    ps, wsl[:, hc * 128:(hc + 1) * 128], rhs,
                    start=(hc == 0), stop=(hc == KD - 1)),
                    waits=w0, stream="mm", tag="prj_mm")
            xt = x_tile(ctx, o, (tt * 512, (tt + 1) * 512))
            psum_free2[sl] = S.op("vector", lambda e, ps=ps, xt=xt: e.tensor_tensor(
                xt, xt, ps, ALU.add), waits=(tok,), tag="prj_res")
        last2[s] = tok
    ctx.wreg_free = [last2[0], last2[1]]


def _decl_l1_ffn(nc, ctx):
    ctx.d_wg = nc.dram_tensor("l1f_wg", [16, 128, 8], F32, kind="ExternalInput").ap()
    ctx.d_wgu = nc.dram_tensor("l1f_wgu", [8, 16, 128, 2048], F32, kind="ExternalInput").ap()
    ctx.d_wdn = nc.dram_tensor("l1f_wdn", [8, 16, 128, 1024], F32, kind="ExternalInput").ap()
    ctx.d_wshgu = nc.dram_tensor("l1f_wshgu", [64, 128, 2048], F32, kind="ExternalInput").ap()
    ctx.d_wshdn = nc.dram_tensor("l1f_wshdn", [16, 128, 4096], F32, kind="ExternalInput").ap()


L1FFN_PARTS = ("moe", "shared")


def emit_l1_ffn(ctx):
    S = ctx.sched
    wreg_free = getattr(ctx, "wreg_free", [None, None])

    su_row = emit_ones_row(ctx, 0, 2048)
    emit_ln(ctx, su_row, ctx.rA, 0, tw=T)
    ztiles = [ctx.rA[:, c * T:(c + 1) * T] for c in range(KD)]

    do_moe = "moe" in L1FFN_PARTS
    do_shared = "shared" in L1FFN_PARTS
    wg_sb = ctx.trans[:, 2048:2176]
    S.dma("gpsimd", lambda e: e.dma_start(
        out=wg_sb.rearrange("p (k u) -> p k u", u=8),
        in_=ctx.d_wg.rearrange("k p u -> p k u")), tag="wld_rt")
    compfm = ctx.rD[0:8, 2048:3072]
    comb_tm = ctx.trans[:, 2176:2184]
    pBb = ctx.pB[:, 512:1024].bitcast(BF16)
    m1 = ctx.sm2[:, 208:209]
    m2 = ctx.sm2[:, 209:210]
    nm1 = ctx.sm2[:, 210:211]
    ds = ctx.sm2[:, 211:212]
    tmp8 = ctx.sm2[:, 212:220]
    l2 = ctx.sm2[:, 220:228]
    g8 = ctx.sm2[:, 228:236]
    sel = ctx.sm2[:, 236:244]
    rc = ctx.sm2[:, 244:245]
    for m in range(8 if do_moe else 0):
        lps = ctx.pA[:, m * 8:(m + 1) * 8]
        for k in range(KD):
            S.op("tensor", lambda e, m=m, k=k, lps=lps: e.matmul(
                lps, ztiles[k][:, m * 128:(m + 1) * 128], wg_sb[:, k * 8:(k + 1) * 8],
                start=(k == 0), stop=(k == KD - 1)), tag="rt_mm")
        S.op("vector", lambda e, lps=lps: e.reduce_max(m1, lps, axis=AX.X))
        S.op("vector", lambda e, lps=lps: e.tensor_scalar(tmp8, lps, m1, None, ALU.is_ge))
        S.op("vector", lambda e, lps=lps: e.scalar_tensor_tensor(
            l2, tmp8, -1e30, lps, ALU.mult, ALU.add))
        S.op("vector", lambda e: e.reduce_max(m2, l2, axis=AX.X))
        S.op("vector", lambda e: e.tensor_scalar(nm1, m1, -1.0, None, ALU.mult))
        S.op("scalar", lambda e, lps=lps: e.activation(g8, lps, AF.Exp, bias=nm1))
        S.op("vector", lambda e, lps=lps: e.tensor_scalar(sel, lps, m2, None, ALU.is_ge))
        S.op("vector", lambda e: e.tensor_tensor(g8, g8, sel, ALU.mult))
        S.op("vector", lambda e: e.reduce_sum(ds, g8, axis=AX.X))
        S.op("vector", lambda e: e.reciprocal(rc, ds))
        S.op("vector", lambda e: e.tensor_scalar(comb_tm, g8, rc, None, ALU.mult))
        S.op("tensor", lambda e, m=m: e.transpose(
            pBb[0:8, m * 128:(m + 1) * 128], comb_tm, ctx.identb[:, :]), tag="rt_tr")
        S.op("vector", lambda e, m=m: e.tensor_copy(
            compfm[:, m * 128:(m + 1) * 128], pBb[0:8, m * 128:(m + 1) * 128]))

    cbc_sb = ctx.trans[:, 2176:3200]
    silu_t = ctx.trans[:, 3200:4224]
    t1 = ctx.trans[:, 4224:5248]
    ce_row = ctx.rD[0:1, 1024:2048]

    def act_tile(e, k):
        return ctx.rB[:, k * 1024:(k + 1) * 1024]

    last_gu = [wreg_free[0], wreg_free[0]]
    last_dn = [None, wreg_free[1]]
    ld_tok = [None, None]
    if do_moe:
        in_tok = S.tail("main")
        psum_free_gu = [None, None]
        psum_free_dn = [None, None]
        cbc_last = None
        dn_ld = [None, None]
        for e in range(8):
            ce_tok = S.dma("sync", lambda e_, e=e: e_.dma_start(
                out=ce_row, in_=compfm[e:e + 1, 0:1024]),
                waits=(cbc_last,), tag="ce_dma")
            cbc_mms = []
            for i2 in range(2):
                pcb = ctx.pB[:, 1024 + i2 * 512:1536 + i2 * 512]
                cbc_mms.append(S.op("tensor", lambda e_, pcb=pcb, i2=i2: e_.matmul(
                    pcb, ctx.onerow[:, :], ce_row[0:1, i2 * 512:(i2 + 1) * 512],
                    start=True, stop=True),
                    waits=(ce_tok, cbc_last) if i2 == 0 else (),
                    stream="mm", tag="cbc_mm"))
            cbc_ev = S.op("vector", lambda e_: e_.tensor_copy(
                cbc_sb, ctx.pB[:, 1024:2048]), waits=(cbc_mms[1],), tag="cbc_ev")
            for j in range(8):
                s = (e * 8 + j) % 2
                wga = ctx.wreg[:, s * 4096:s * 4096 + 2048]
                wua = ctx.wreg[:, s * 4096 + 2048:s * 4096 + 4096]
                S.dma("gpsimd", lambda e_, e=e, j=j, wga=wga: e_.dma_start(
                    out=wga, in_=ctx.d_wgu[e, j, :, :]), waits=(last_gu[s],),
                    stream="gu%d" % s, sem="gus%d" % s, tag="wld_g")
                ld_tok[s] = S.dma("gpsimd", lambda e_, e=e, j=j, wua=wua: e_.dma_start(
                    out=wua, in_=ctx.d_wgu[e, j + 8, :, :]),
                    stream="gu%d" % s, sem="gus%d" % s, tag="wld_u")
                pg = ctx.pA[:, 0:1024]
                pu = ctx.pA[:, 1024:2048]
                for tt in range(2):
                    for k in range(KD):
                        w0 = (ld_tok[s], psum_free_gu[0],
                              in_tok if (e == 0 and j == 0) else None) \
                            if (k == 0 and tt == 0) else ()
                        tokg = S.op("tensor", lambda e_, k=k, tt=tt, wga=wga, pg=pg: e_.matmul(
                            pg[:, tt * 512:(tt + 1) * 512],
                            wga[:, k * 128:(k + 1) * 128],
                            ztiles[k][:, tt * 512:(tt + 1) * 512],
                            start=(k == 0), stop=(k == KD - 1)),
                            waits=w0, stream="mm", tag="gu_g_mm")
                for tt in range(2):
                    for k in range(KD):
                        w0 = (psum_free_gu[1],) if (k == 0 and tt == 0) else ()
                        tok = S.op("tensor", lambda e_, k=k, tt=tt, wua=wua, pu=pu: e_.matmul(
                            pu[:, tt * 512:(tt + 1) * 512],
                            wua[:, k * 128:(k + 1) * 128],
                            ztiles[k][:, tt * 512:(tt + 1) * 512],
                            start=(k == 0), stop=(k == KD - 1)),
                            waits=w0, stream="mm", tag="gu_u_mm")
                last_gu[s] = tok
                psum_free_gu[0] = S.op("scalar", lambda e_, pg=pg: e_.activation(
                    silu_t, pg, AF.Silu), waits=(tokg,), tag="moe_silu")
                tok_t1 = S.op("vector", lambda e_, pu=pu: e_.tensor_tensor(
                    t1, pu, cbc_sb, ALU.mult), waits=(tok, cbc_ev), tag="moe_t1")
                psum_free_gu[1] = tok_t1
                act_done = S.op("vector", lambda e_, e=e, j=j: e_.tensor_tensor(
                    act_tile(e, j), silu_t, t1, ALU.mult), tag="moe_act")
            w_init = wreg_free[1]
            for o in range(KD):
                s = o % 2
                wdn = ctx.wreg[:, 8192 + s * 4096:8192 + s * 4096 + 1024]
                dn_prev = last_dn[s]
                dn_ld[s] = S.dma("gpsimd", lambda e_, e=e, o=o, wdn=wdn: e_.dma_start(
                    out=wdn, in_=ctx.d_wdn[e, o, :, :]),
                    waits=(w_init,) if (e == 0 and o < 2) else (dn_prev,),
                    stream="dn%d" % s, sem="dns%d" % s, tag="wld_dn")
                pd = ctx.pB[:, 0:1024]
                for tt in range(2):
                    for k in range(8):
                        w0 = (dn_ld[s], psum_free_dn[0],
                              act_done if o == 0 else None) \
                            if (k == 0 and tt == 0) else ()
                        tok = S.op("tensor", lambda e_, e=e, k=k, tt=tt, wdn=wdn, pd=pd: e_.matmul(
                            pd[:, tt * 512:(tt + 1) * 512],
                            wdn[:, k * 128:(k + 1) * 128],
                            act_tile(e, k)[:, tt * 512:(tt + 1) * 512],
                            start=(k == 0), stop=(k == 7)),
                            waits=w0, stream="mm", tag="dn_mm")
                last_dn[s] = tok
                xt = x_tile(ctx, o)
                psum_free_dn[0] = S.op("vector", lambda e_, pd=pd, xt=xt: e_.tensor_tensor(
                    xt, xt, pd, ALU.add), waits=(tok,), tag="dn_res")
            cbc_last = psum_free_dn[0]
    def sh_tile(o):
        reg = ctx.rB if o < 16 else ctx.rC
        return reg[:, (o % 16) * T:(o % 16) * T + T]

    if not do_moe:
        last_dn = [wreg_free[0], wreg_free[1]]
    in_tok3 = S.tail("main")
    last_sh = [last_dn[0], last_dn[0]]
    ld_sh = [None, None]
    psum_free_sh = [None, None, None, None]
    for o in range(32 if do_shared else 0):
        s = o % 2
        wga = ctx.wreg[:, s * 4096:s * 4096 + 2048]
        wua = ctx.wreg[:, s * 4096 + 2048:s * 4096 + 4096]
        S.dma("gpsimd", lambda e_, o=o, wga=wga: e_.dma_start(
            out=wga, in_=ctx.d_wshgu[o, :, :]), waits=(last_sh[s],),
            stream="sg%d" % s, sem="sgs%d" % s, tag="wld_shg")
        ld_sh[s] = S.dma("gpsimd", lambda e_, o=o, wua=wua: e_.dma_start(
            out=wua, in_=ctx.d_wshgu[o + 32, :, :]),
            stream="sg%d" % s, sem="sgs%d" % s, tag="wld_shu")
        for tt in range(2):
            sl = s * 2 + tt
            pg = ctx.pA[:, s * 1024 + tt * 512:s * 1024 + (tt + 1) * 512]
            pu = ctx.pB[:, s * 1024 + tt * 512:s * 1024 + (tt + 1) * 512]
            for k in range(KD):
                w0 = (ld_sh[s], psum_free_sh[sl],
                      in_tok3 if o < 2 else None) if k == 0 else ()
                S.op("tensor", lambda e_, k=k, wga=wga, pg=pg, tt=tt: e_.matmul(
                    pg, wga[:, k * 128:(k + 1) * 128],
                    ztiles[k][:, tt * 512:(tt + 1) * 512],
                    start=(k == 0), stop=(k == KD - 1)),
                    waits=w0, stream="mm", tag="sh_g_mm")
            for k in range(KD):
                tok = S.op("tensor", lambda e_, k=k, wua=wua, pu=pu, tt=tt: e_.matmul(
                    pu, wua[:, k * 128:(k + 1) * 128],
                    ztiles[k][:, tt * 512:(tt + 1) * 512],
                    start=(k == 0), stop=(k == KD - 1)), stream="mm", tag="sh_u_mm")
            last_sh[s] = tok
            S.op("scalar", lambda e_, pg=pg, tt=tt: e_.activation(
                silu_t[:, tt * 512:(tt + 1) * 512], pg, AF.Silu),
                waits=(tok,), tag="sh_silu")
            psum_free_sh[sl] = S.op("vector", lambda e_, o=o, tt=tt, pu=pu: e_.tensor_tensor(
                sh_tile(o)[:, tt * 512:(tt + 1) * 512],
                silu_t[:, tt * 512:(tt + 1) * 512], pu, ALU.mult),
                tag="sh_act")
    in_tok4 = S.tail("main")
    last_sd = [None, None]
    psum_free_sd = [None, None, None, None]
    for o in range(KD if do_shared else 0):
        s = o % 2
        wsd = ctx.wreg[:, s * 8192:s * 8192 + 4096]
        if o == 0:
            w8 = (last_sh[0], last_sh[1])
        elif o == 1:
            w8 = (last_dn[1],) if last_dn[1] is not None else ()
        else:
            w8 = (last_sd[s],)
        ldd = S.dma("gpsimd", lambda e_, o=o, wsd=wsd: e_.dma_start(
            out=wsd, in_=ctx.d_wshdn[o, :, :]), waits=w8,
            stream="sd%d" % s, sem="sds%d" % s, tag="wld_shd")
        for tt in range(2):
            sl = s * 2 + tt
            pd = ctx.pA[:, (s * 2 + tt) * 512:(s * 2 + tt) * 512 + 512]
            for k in range(32):
                w0 = (ldd, psum_free_sd[sl],
                      in_tok4 if o < 2 else None) if k == 0 else ()
                tok = S.op("tensor", lambda e_, k=k, wsd=wsd, pd=pd, tt=tt: e_.matmul(
                    pd, wsd[:, k * 128:(k + 1) * 128],
                    sh_tile(k)[:, tt * 512:(tt + 1) * 512],
                    start=(k == 0), stop=(k == 31)),
                    waits=w0, stream="mm", tag="shd_mm")
            xt = x_tile(ctx, o, (tt * 512, (tt + 1) * 512))
            psum_free_sd[sl] = S.op("vector", lambda e_, pd=pd, xt=xt: e_.tensor_tensor(
                xt, xt, pd, ALU.add), waits=(tok,), tag="shd_res")
        last_sd[s] = tok
    if do_shared:
        ctx.wreg_free = [last_sd[0], last_sd[1]]
    else:
        ctx.wreg_free = [last_dn[0], last_dn[1]]


def _prep_consts():
    h128 = make_hadamard(128).astype(np.float32)
    ident = np.eye(128, dtype=np.float32)
    mask = (np.arange(128)[:, None] <= np.arange(512)[None, :]).astype(np.float32)
    return {"c_h128": h128, "c_h128n": -h128, "c_ident": ident, "c_mask": mask}


def _prep_pmask(core):
    pm = np.zeros((128, 8), dtype=np.float32)
    pm[:, :core] = 1.0
    return pm


def _prep_l0_attn(inp):
    return {
        "l0a_wq": tile_fm(np.ascontiguousarray(np.asarray(inp["L0_attn_Wq"]).T), D, D),
        "l0a_wk": tile_fm(np.ascontiguousarray(np.asarray(inp["L0_attn_Wk"]).T), D, D),
        "l0a_wv": tile_fm(np.ascontiguousarray(np.asarray(inp["L0_attn_Wv"]).T), D, D),
        "l0a_wo": tile_fm(np.ascontiguousarray(np.asarray(inp["L0_attn_Wo"]).T), D, D),
        "l0a_wfq": np.asarray(inp["L0_attn_Wfq"]),
        "l0a_wfk": np.asarray(inp["L0_attn_Wfk"]),
    }


def _prep_l1_attn(inp):
    wqkv = np.asarray(inp["L1_attn_weight_qkv"])
    wproj = np.asarray(inp["L1_attn_weight_proj"])
    return {
        "l1a_wqkv": tile_fm(np.ascontiguousarray(wqkv.T), D, 3072),
        "l1a_wproj": tile_fm(np.ascontiguousarray(wproj.T), D, D),
    }


def _prep_l1_ffn(inp):
    wg = np.asarray(inp["L1_ffn_w_gate"])
    wgu = np.asarray(inp["L1_ffn_w_gate_up"])
    wdn = np.asarray(inp["L1_ffn_w_down"])
    wshgu = np.asarray(inp["L1_ffn_w_sh_gate_up"])
    wshdn = np.asarray(inp["L1_ffn_w_sh_down"])
    return {
        "l1f_wg": np.ascontiguousarray(wg.reshape(16, 128, 8)),
        "l1f_wgu": np.stack([tile_fm(wgu[e], D, D) for e in range(8)]),
        "l1f_wdn": np.stack([tile_fm(wdn[e], 1024, D) for e in range(8)]),
        "l1f_wshgu": tile_fm(wshgu, D, 8192),
        "l1f_wshdn": tile_fm(wshdn, 4096, D),
    }


def _prep_l0_ffn(inp):
    up = np.asarray(inp["L0_ffn_W_up"])
    gate = np.asarray(inp["L0_ffn_W_gate"])
    down = np.asarray(inp["L0_ffn_W_down"])
    wsc = np.asarray(inp["L0_ffn_Wscale_ug"])
    upt = tile_fm(np.ascontiguousarray(up.T), D, 8192)
    gatet = tile_fm(np.ascontiguousarray(gate.T), D, 8192)
    wupgate = np.concatenate([
        upt[0::2], gatet[0::2], upt[1::2], gatet[1::2]], axis=2)
    wupgate = np.ascontiguousarray(
        np.concatenate([upt.reshape(32, 2, 128, 2048)[:, 0],
                        gatet.reshape(32, 2, 128, 2048)[:, 0],
                        upt.reshape(32, 2, 128, 2048)[:, 1],
                        gatet.reshape(32, 2, 128, 2048)[:, 1]], axis=2))
    return {
        "l0f_wupgate": wupgate,
        "l0f_wdown": tile_fm(np.ascontiguousarray(down.T), 8192, D),
        "l0f_suug": np.asarray(inp["L0_ffn_SU_ug"]).reshape(1, D),
        "l0f_sudp": pmajor(np.asarray(inp["L0_ffn_SU_dp"])),
        "l0f_wsu_up": pmajor(wsc[:8192]),
        "l0f_wsu_g": pmajor(wsc[8192:]),
        "l0f_wsdp": pmajor(np.asarray(inp["L0_ffn_Wscale_dp"])),
    }


DBG_ACC = False
OLD8192 = False
OLD2048 = False
DBG_FOLD = False


def build_program(blocks):
    nc = bass.Bass("TRN2", target_bir_lowering=False)
    es = contextlib.ExitStack()
    ctx = Ctx()
    with es:
        _alloc_common(nc, es, ctx)
        ctx.dbg_acc = DBG_ACC
        ctx.old8192 = OLD8192
        ctx.old2048 = OLD2048
        ctx.dbg_fold = DBG_FOLD
        if "l0_attn" in blocks:
            _decl_l0_attn(nc, ctx)
        if "l0_ffn" in blocks:
            _decl_l0_ffn(nc, ctx)
        if "l1_attn" in blocks:
            _decl_l1_attn(nc, ctx)
        if "l1_ffn" in blocks:
            _decl_l1_ffn(nc, ctx)
        for b in blocks:
            if b == "l0_attn":
                emit_l0_attn(ctx)
            elif b == "l0_ffn":
                emit_l0_ffn(ctx)
            elif b == "l1_attn":
                emit_l1_attn(ctx)
            elif b == "l1_ffn":
                emit_l1_ffn(ctx)
        _emit_store_out(ctx)
        block = es.enter_context(nc.Block())
        ctx.sched.build(nc, block, es)
    return nc


def shard_x(x):
    shards = []
    for c in range(NCORES):
        xs = np.concatenate([x[0, c * TB:(c + 1) * TB, :],
                             x[1, c * TB:(c + 1) * TB, :]], axis=0)
        shards.append(np.ascontiguousarray(xs.T))
    return shards


def unshard_x(outs):
    x = np.empty((2, NCORES * TB, D), dtype=np.float32)
    for c in range(NCORES):
        o = np.asarray(outs[c])
        x[0, c * TB:(c + 1) * TB, :] = o[:, 0:TB].T
        x[1, c * TB:(c + 1) * TB, :] = o[:, TB:2 * TB].T
    return x


KERNEL_BLOCKS = BLOCKS_ALL


def kernel(**inputs):
    x = np.asarray(inputs["L0_attn_hidden_states"])
    consts = _prep_consts()
    w = {}
    if "l0_attn" in KERNEL_BLOCKS:
        w.update(_prep_l0_attn(inputs))
    if "l0_ffn" in KERNEL_BLOCKS:
        w.update(_prep_l0_ffn(inputs))
    if "l1_attn" in KERNEL_BLOCKS:
        w.update(_prep_l1_attn(inputs))
    if "l1_ffn" in KERNEL_BLOCKS:
        w.update(_prep_l1_ffn(inputs))
    nc = build_program(KERNEL_BLOCKS)
    shards = shard_x(x)
    in_maps = []
    for c in range(NCORES):
        m = {"x_fm": shards[c], "c_pmask": _prep_pmask(c)}
        m.update(consts)
        m.update(w)
        in_maps.append(m)
    res = run_bass_kernel_spmd(nc, in_maps, core_ids=list(range(NCORES)))
    global LAST_RESULTS
    LAST_RESULTS = res
    return unshard_x([r["out_fm"] for r in res.results])


LAST_RESULTS = None
